# version 04
# baseline (speedup 1.0000x reference)
import numpy as np
import ml_dtypes
from contextlib import ExitStack

import concourse.bass as bass
import concourse.mybir as mybir
from concourse.bass_utils import run_bass_kernel_spmd
from concourse.bass_core_rust import IndirectOffsetOnAxis

F32 = mybir.dt.float32
I32 = mybir.dt.int32
BF16 = mybir.dt.bfloat16
AF = mybir.ActivationFunctionType
ALU = mybir.AluOpType
AX = mybir.AxisListType.X
BFNP = ml_dtypes.bfloat16

NCORES = 8
S = 2048
D = 1024
T = 2 * S
NT = T // 128
ND = D // 128
NT_SEQ = S // 128
NEG = -1.0e30

COMPUTE_ENGINES = ("tensor", "vector", "scalar", "gpsimd")
DMA_QUEUES = ("sync", "scalar", "gpsimd")


class _Ev:
    __slots__ = ("idx", "engine", "fn", "deps", "is_dma", "mile", "semval", "tag", "inc", "semkey", "gate")

    def __init__(self, idx, engine, fn, deps, is_dma, tag):
        self.idx = idx
        self.engine = engine
        self.fn = fn
        self.deps = deps
        self.is_dma = is_dma
        self.inc = 16 if is_dma else 1
        self.mile = False
        self.semkey = None
        self.gate = None
        self.semval = None
        self.tag = tag


class Em:

    NSUB = 4

    def __init__(self, nc, ctx):
        self.nc = nc
        self.events = []
        self.res = {}
        self.sem = {}
        for e in COMPUTE_ENGINES:
            self.sem[e] = ctx.enter_context(nc.semaphore(f"sem_{e}"))
        for q in DMA_QUEUES:
            for r in range(self.NSUB):
                self.sem[f"dma_{q}_{r}"] = ctx.enter_context(nc.semaphore(f"sem_dma_{q}_{r}"))

    @staticmethod
    def _excl(k):
        return isinstance(k, tuple) and k and isinstance(k[0], str) and (
            k[0].startswith("bank") or k[0].startswith("ps"))

    def _track(self, idx, reads, writes, deps):
        reads = [k for k in reads]
        writes = [k for k in writes]
        excl = [k for k in reads if self._excl(k)]
        if excl:
            reads = [k for k in reads if not self._excl(k)]
            writes = writes + excl
        dep = set(deps)
        for k in reads:
            r = self.res.get(k)
            if r is not None and r[0] is not None:
                dep.add(r[0])
        for k in writes:
            r = self.res.get(k)
            if r is not None:
                if r[0] is not None:
                    dep.add(r[0])
                dep.update(r[1])
            self.res[k] = [idx, []]
        for k in reads:
            r = self.res.setdefault(k, [None, []])
            r[1].append(idx)
        dep.discard(idx)
        return sorted(dep)

    def op(self, engine, fn, reads=(), writes=(), deps=(), tag=""):
        idx = len(self.events)
        d = self._track(idx, reads, writes, deps)
        self.events.append(_Ev(idx, engine, fn, d, False, tag))
        return idx

    def dma(self, queue, out, in_, reads=(), writes=(), deps=(), tag="dma"):
        idx = len(self.events)
        d = self._track(idx, reads, writes, deps)
        fn = (lambda eng, out=out, in_=in_: eng.dma_start(out=out, in_=in_))
        self.events.append(_Ev(idx, queue, fn, d, True, tag))
        return idx

    def dma_fn(self, queue, fn, reads=(), writes=(), deps=(), tag="dma", inc=16):
        idx = len(self.events)
        d = self._track(idx, reads, writes, deps)
        ev = _Ev(idx, queue, fn, d, True, tag)
        ev.inc = inc
        self.events.append(ev)
        return idx

    def barrier(self):
        last = {}
        for ev in self.events:
            key = ("q:" + ev.engine) if ev.is_dma else ev.engine
            last[key] = ev.idx
        deps = sorted(set(last.values()))
        ids = []
        for e in COMPUTE_ENGINES:
            ids.append(self.op(e, lambda eng: eng.nop(), deps=deps, tag="barrier"))
        for k in self.res:
            self.res[k] = [ids[0], ids[:]]
        return ids

    def finalize(self, block):
        events = self.events
        for ev in events:
            for d in ev.deps:
                de = events[d]
                if de.is_dma or not (de.engine == ev.engine == "tensor" and not ev.is_dma):
                    de.mile = True
        cnt = {}
        qn = {}
        for ev in events:
            if ev.is_dma:
                q = ev.engine
                n = qn.get(q, 0)
                qn[q] = n + 1
                semkey = f"dma_{q}_{n % self.NSUB}"
                ev.semkey = semkey
                prev = cnt.get(semkey, 0)
                ev.gate = (semkey, prev) if prev else None
                cnt[semkey] = prev + ev.inc
                ev.semval = cnt[semkey]
                ev.mile = True
            elif ev.mile:
                semkey = ev.engine
                ev.semkey = semkey
                cnt[semkey] = cnt.get(semkey, 0) + 1
                ev.semval = cnt[semkey]
        self.final_counts = dict(cnt)

        per_engine = {e: [] for e in COMPUTE_ENGINES}
        sync_evs = []
        for ev in events:
            if ev.is_dma and ev.engine == "sync":
                sync_evs.append(ev)
            else:
                per_engine[ev.engine].append(ev)

        sems = self.sem

        def run(eng, engine_name, evs, final=False):
            watermark = {}
            for ev in evs:
                waits = []
                if ev.is_dma and ev.gate is not None:
                    waits.append(ev.gate)
                for d in ev.deps:
                    de = events[d]
                    if (not de.is_dma) and (not ev.is_dma) and de.engine == engine_name == "tensor":
                        continue
                    if de.semval is None:
                        raise RuntimeError(f"dep on non-milestone event {de.idx} {de.tag}")
                    waits.append((de.semkey, de.semval))
                for semkey, val in waits:
                    if watermark.get(semkey, 0) >= val:
                        continue
                    eng.wait_ge(sems[semkey], val)
                    watermark[semkey] = val
                inst = ev.fn(eng)
                if ev.mile:
                    if inst is None:
                        raise RuntimeError(f"milestone event {ev.idx} ({ev.tag}) returned None")
                    inst.then_inc(sems[ev.semkey], ev.inc if ev.is_dma else 1)
            if final:
                for k, v in self.final_counts.items():
                    if k.startswith("dma_") and watermark.get(k, 0) < v:
                        eng.wait_ge(sems[k], v)

        @block.tensor
        def _(tensor):
            run(tensor, "tensor", per_engine["tensor"])

        @block.vector
        def _(vector):
            run(vector, "vector", per_engine["vector"])

        @block.scalar
        def _(scalar):
            run(scalar, "scalar", per_engine["scalar"])

        @block.gpsimd
        def _(gpsimd):
            run(gpsimd, "gpsimd", per_engine["gpsimd"])

        @block.sync
        def _(sync):
            run(sync, "__sync__", sync_evs, final=True)


class Prog:
    def __init__(self, name="prog"):
        self.nc = bass.Bass("TRN2", target_bir_lowering=False, debug=False)
        self.ctx = ExitStack()
        self.em = Em(self.nc, self.ctx)
        self._uniq = 0

    def uniq(self, base):
        self._uniq += 1
        return f"{base}_{self._uniq}"

    def inp(self, name, shape, dtype=F32):
        return self.nc.dram_tensor(name, list(shape), dtype, kind="ExternalInput").ap()

    def out(self, name, shape, dtype=F32):
        return self.nc.dram_tensor(name, list(shape), dtype, kind="ExternalOutput").ap()

    def dram(self, name, shape, dtype=F32):
        return self.nc.dram_tensor(self.uniq(name), list(shape), dtype).ap()

    def sb(self, name, shape, dtype=F32):
        return self.ctx.enter_context(self.nc.sbuf_tensor(self.uniq(name), list(shape), dtype))

    def psum(self, name, shape, dtype=F32):
        return self.ctx.enter_context(self.nc.psum_tensor(self.uniq(name), list(shape), dtype))

    def close_build(self):
        block = self.ctx.enter_context(self.nc.Block())
        self.em.finalize(block)
        self.ctx.close()
        return self.nc


def alloc_ln_scratch(P, tag, npar=2):
    return {
        "sq": P.sb(tag + "_sq", [128, npar, D], F32),
        "s": P.sb(tag + "_s", [128, npar, 1], F32),
        "s2": P.sb(tag + "_s2", [128, npar, 1], F32),
        "mu": P.sb(tag + "_mu", [128, npar, 1], F32),
        "m2": P.sb(tag + "_m2", [128, npar, 1], F32),
        "var": P.sb(tag + "_var", [128, npar, 1], F32),
        "lnv": P.sb(tag + "_lnv", [128, npar, 1], F32),
        "rstd": P.sb(tag + "_rstd", [128, npar, 1], F32),
        "xn": P.sb(tag + "_xn", [128, npar, D], F32),
        "tag": tag,
    }


def emit_ln_tile(P, scr, pa, x_ap, x_key, rms=False, eps=1e-5):
    em = P.em
    kk = (scr["tag"], pa)
    sq = scr["sq"][:, pa, :]
    s_ = scr["s"][:, pa, :]
    s2 = scr["s2"][:, pa, :]
    mu = scr["mu"][:, pa, :]
    m2 = scr["m2"][:, pa, :]
    var = scr["var"][:, pa, :]
    lnv = scr["lnv"][:, pa, :]
    rstd = scr["rstd"][:, pa, :]
    xn = scr["xn"][:, pa, :]

    em.op("scalar", lambda e: e.activation(sq, x_ap, AF.Square, accum_out=s2),
          reads=[x_key], writes=[kk + ("sq",), kk + ("s2",)], tag="ln_sq")
    if not rms:
        em.op("vector", lambda e: e.reduce_sum(s_, x_ap, axis=AX),
              reads=[x_key], writes=[kk + ("s",)], tag="ln_s")
        em.op("scalar", lambda e: e.mul(mu, s_, 1.0 / D),
              reads=[kk + ("s",)], writes=[kk + ("mu",)], tag="ln_mu")
        em.op("vector", lambda e: e.tensor_scalar(m2, mu, mu, -float(eps), ALU.mult, ALU.add),
              reads=[kk + ("mu",)], writes=[kk + ("m2",)], tag="ln_m2")
        em.op("vector", lambda e: e.scalar_tensor_tensor(var, s2, 1.0 / D, m2, ALU.mult, ALU.subtract),
              reads=[kk + ("s2",), kk + ("m2",)], writes=[kk + ("var",)], tag="ln_var")
    else:
        em.op("vector", lambda e: e.tensor_scalar(var, s2, 1.0 / D, float(eps), ALU.mult, ALU.add),
              reads=[kk + ("s2",)], writes=[kk + ("var",)], tag="rms_var")
    em.op("scalar", lambda e: e.activation(lnv, var, AF.Ln),
          reads=[kk + ("var",)], writes=[kk + ("lnv",)], tag="ln_ln")
    em.op("scalar", lambda e: e.activation(rstd, lnv, AF.Exp, scale=-0.5),
          reads=[kk + ("lnv",)], writes=[kk + ("rstd",)], tag="ln_exp")
    if rms:
        em.op("vector", lambda e: e.tensor_scalar(xn, x_ap, rstd, None, ALU.mult),
              reads=[x_key, kk + ("rstd",)], writes=[kk + ("xn",)], tag="rms_apply")
    else:
        em.op("vector", lambda e: e.tensor_scalar(xn, x_ap, mu, rstd, ALU.subtract, ALU.mult),
              reads=[x_key, kk + ("mu",), kk + ("rstd",)], writes=[kk + ("xn",)], tag="ln_apply")
    return kk + ("xn",)


def emit_mm_group(P, psum_ap, psum_key, pairs, reads, n_parts=None, tag="mm",
                  tile_position=None):
    em = P.em

    def fn(e):
        n = len(pairs)
        inst = None
        for i, (lt, rh) in enumerate(pairs):
            inst = e.matmul(psum_ap, lt, rh, start=(i == 0), stop=(i == n - 1),
                            tile_position=tile_position)
        return inst

    return em.op("tensor", fn, reads=list(reads), writes=[psum_key], tag=tag)


def build_l0attn(P=None, V=None, x_in=None, xo_out=None, bank=None):
    standalone = P is None
    if standalone:
        P = Prog("l0attn")
        V = {"Wqkv": P.inp("Wqkv", [D, 3 * D]), "Wproj": P.inp("Wproj", [D, D]),
             "identf": P.inp("identf", [128, 128], F32), "identb": P.inp("identb", [128, 128], BF16),
             "mask_band": P.inp("mask_band", [128, 256], F32),
             "mask_first": P.inp("mask_first", [128, 128], F32)}
        x_in = P.inp("x", [T, D])
        xo_out = P.out("xo", [T, D])
    em = P.em
    nc = P.nc
    wqkv_in = V["Wqkv"]; wproj_in = V["Wproj"]
    identf = V["identf"]; identb_in = V["identb"]
    mask_band = V["mask_band"]; mask_first = V["mask_first"]

    inst_ctx = ExitStack()
    _sb = lambda n, sh, dt=F32: inst_ctx.enter_context(nc.sbuf_tensor(P.uniq(n), list(sh), dt))
    wqkv = _sb("wqkv", [128, ND, 3 * D], BF16)
    wproj = _sb("wproj", [128, ND, D], BF16)
    ident = _sb("ident", [128, 128], F32)
    identb = _sb("identb", [128, 128], BF16)
    mband = _sb("mband", [128, 256], F32)
    mfirst = _sb("mfirst", [128, 128], F32)
    em.dma_fn("gpsimd", lambda e: e.dma_start(
        out=wqkv[:, :, :], in_=wqkv_in.rearrange("(k p) c -> p k c", p=128)),
        writes=[("wqkv", k) for k in range(ND)], tag="ld_wqkv")
    em.dma_fn("gpsimd", lambda e: e.dma_start(
        out=wproj[:, :, :], in_=wproj_in.rearrange("(k p) c -> p k c", p=128)),
        writes=[("wproj", k) for k in range(ND)], tag="ld_wproj")
    em.dma("sync", ident[:, :], identf, writes=[("ident",)], tag="ld_ident")
    em.dma("sync", identb[:, :], identb_in, writes=[("identb",)], tag="ld_identb")
    em.dma("sync", mband[:, :], mask_band, writes=[("mband",)], tag="ld_mb")
    em.dma("sync", mfirst[:, :], mask_first, writes=[("mfirst",)], tag="ld_mf")

    x_cur = _sb("x_cur", [128, 2, D], F32)
    lns = {k: _sb("ln0" + k, [128, 2, D if k in ("sq", "xn") else 1]) for k in
           ("sq", "s", "s2", "mu", "m2", "var", "lnv", "rstd", "xn")}
    lns["tag"] = "ln0"
    xnT = _sb("xnT", [128, 2, ND, 128], BF16)
    qT = _sb("qT", [128, 2, ND, 128], BF16)
    kT = _sb("kT", [128, 2, ND, 128], BF16)
    v_roll = _sb("v_roll", [128, 2, D], BF16)
    kTg = _sb("kTg", [128, ND, 32], BF16)
    vg = _sb("vg", [32, D], BF16)
    probs = _sb("probs", [128, 2, 320], F32)
    probs_bf = _sb("probs_bf", [128, 2, 320], BF16)
    probs_bf2 = _sb("probs_bf2", [128, 2, 320], BF16)
    pT = _sb("pT", [128, 2, 3, 128], BF16)
    nm = _sb("nm", [128, 2, 1], F32)
    sume = _sb("sume", [128, 2, 1], F32)
    rec = _sb("rec", [128, 2, 1], F32)
    yT = _sb("yT", [128, 2, ND, 128], BF16)
    xo_buf = _sb("xo_buf", [128, 2, D], F32)

    if bank is None:
        bank = [P.psum(f"bank{i}", [128, 512], F32) for i in range(8)]
    ps_tr = bank[0][:, 0:128]
    ps_qk = [bank[1][:, 0:128], bank[2][:, 0:128]]
    ps_v = bank[3][:, :]
    ps_sc = bank[4][:, :]
    ps_pT = bank[5][:, :].bitcast(BF16)[:, 0:128]
    ps_av = bank[6][:, 0:128]
    ps_pr = bank[7][:, :]

    for t in range(NT):
        pa = t % 2
        t_loc = t % NT_SEQ

        em.dma("sync", x_cur[:, pa, :], x_in[t * 128:(t + 1) * 128, :],
               writes=[("x", pa)], tag="ld_x")
        xn_key = emit_ln_tile(P, lns, pa, x_cur[:, pa, :], ("x", pa))
        xn = lns["xn"][:, pa, :]

        for k in range(ND):
            em.op("tensor", lambda e, k=k, xn=xn: e.transpose(ps_tr[:, :], xn[:, k * 128:(k + 1) * 128], ident[:, :]),
                  reads=[xn_key, ("ident",)], writes=[("ps_tr",)], tag="tr_xn")
            em.op("vector", lambda e, k=k, pa=pa: e.tensor_copy(xnT[:, pa, k, :], ps_tr[:, :]),
                  reads=[("ps_tr",)], writes=[("xnT", pa, k)], tag="tr_evac")

        xnT_reads = [("xnT", pa, k) for k in range(ND)]
        for hp in range(ND):
            bank = ps_qk[hp % 2]
            bkey = ("ps_qk", hp % 2)
            emit_mm_group(P, bank[:, :], bkey,
                          [(wqkv[:, k, hp * 128:(hp + 1) * 128], xnT[:, pa, k, :]) for k in range(ND)],
                          reads=xnT_reads + [("wqkv", k) for k in range(ND)], tag="mm_q")
            em.op("scalar", lambda e, hp=hp, bank=bank, pa=pa: e.activation(qT[:, pa, hp, :], bank[:, :], AF.Copy, scale=0.125),
                  reads=[bkey], writes=[("qT", pa, hp)], tag="ev_q")
            emit_mm_group(P, bank[:, :], bkey,
                          [(wqkv[:, k, D + hp * 128:D + (hp + 1) * 128], xnT[:, pa, k, :]) for k in range(ND)],
                          reads=xnT_reads + [("wqkv", k) for k in range(ND)], tag="mm_k")
            em.op("scalar", lambda e, hp=hp, bank=bank, pa=pa: e.copy(kT[:, pa, hp, :], bank[:, :]),
                  reads=[bkey], writes=[("kT", pa, hp)], tag="ev_k")
        for c in range(2):
            emit_mm_group(P, ps_v[:, :], ("ps_v",),
                          [(xnT[:, pa, k, :], wqkv[:, k, 2 * D + c * 512:2 * D + (c + 1) * 512]) for k in range(ND)],
                          reads=xnT_reads + [("wqkv", k) for k in range(ND)], tag="mm_v")
            em.op("vector", lambda e, c=c, pa=pa: e.tensor_copy(v_roll[:, pa, c * 512:(c + 1) * 512], ps_v[:, :]),
                  reads=[("ps_v",)], writes=[("v", pa, c)], tag="ev_v")

        em.op("vector",
              lambda e, pa=pa, t_loc=t_loc: e.tensor_copy(kTg[:, :, 2 * t_loc:2 * t_loc + 2],
                                                          kT.ap()[:, pa, :, 0:128:64]),
              reads=[("kT", pa, k) for k in range(ND)], writes=[("kTg", 2 * t_loc), ("kTg", 2 * t_loc + 1)],
              tag="kTg")
        em.dma_fn("scalar",
                  lambda e, t_loc=t_loc, pa=pa: e.dma_start(out=vg[2 * t_loc:2 * t_loc + 2, :], in_=v_roll.ap()[0:128:64, pa, :]),
                  reads=[("v", pa, 0), ("v", pa, 1)],
                  writes=[("vg", 2 * t_loc), ("vg", 2 * t_loc + 1)], tag="vg")

        g = 2 * (t_loc - 1) if t_loc >= 1 else 0
        Wd = 128 if t_loc == 0 else 256 + g
        for h in range(16):
            hp = h // 2
            base = (h % 2) * 64
            q_ap = qT[base:base + 64, pa, hp, :]
            sc_reads = [("qT", pa, hp)]
            pairs = []
            if t_loc == 0:
                pairs.append((q_ap, kT[base:base + 64, pa, hp, :], 0))
                sc_reads.append(("kT", pa, hp))
            else:
                pairs.append((q_ap, kT[base:base + 64, 1 - pa, hp, :], 0))
                pairs.append((q_ap, kT[base:base + 64, pa, hp, :], 128))
                sc_reads += [("kT", 1 - pa, hp), ("kT", pa, hp)]
                if g > 0:
                    pairs.append((q_ap, kTg[base:base + 64, hp, 0:g], 256))
                    sc_reads += [("kTg", i) for i in range(g)]

            def fn_sc(e, pairs=pairs, g=g):
                inst = None
                for (lt, rh, off) in pairs:
                    n = rh.shape[-1]
                    inst = e.matmul(ps_sc[:, off:off + n], lt, rh, start=True, stop=True)
                return inst
            em.op("tensor", fn_sc, reads=sc_reads, writes=[("ps_sc",)], tag="mm_sc")

            pr = probs[:, pa, :]
            if t_loc == 0:
                em.op("vector", lambda e, pr=pr: e.tensor_tensor(pr[:, 0:128], ps_sc[:, 0:128], mfirst[:, :], ALU.add),
                      reads=[("ps_sc",), ("mfirst",)], writes=[("probs", pa)], tag="mask0")
            else:
                em.op("vector", lambda e, pr=pr: e.tensor_tensor(pr[:, 0:256], ps_sc[:, 0:256], mband[:, :], ALU.add),
                      reads=[("ps_sc",), ("mband",)], writes=[("probs", pa)], tag="mask")
                if g > 0:
                    em.op("scalar", lambda e, pr=pr, g=g: e.copy(pr[:, 256:256 + g], ps_sc[:, 256:256 + g]),
                          reads=[("ps_sc",)], writes=[("probs", pa)], tag="cpglob")
            em.op("vector", lambda e, pr=pr, Wd=Wd, pa=pa: e.reduce_max(nm[:, pa, :], pr[:, 0:Wd], axis=AX, negate=True),
                  reads=[("probs", pa)], writes=[("nm", pa)], tag="negmax")
            em.op("scalar", lambda e, pr=pr, Wd=Wd, pa=pa: e.activation(probs_bf[:, pa, 0:Wd], pr[:, 0:Wd], AF.Exp,
                                                                 bias=nm[:, pa, :], accum_out=sume[:, pa, :]),
                  reads=[("probs", pa), ("nm", pa)], writes=[("probs_bf", pa), ("sume", pa)], tag="exp")
            em.op("vector", lambda e, pa=pa: e.reciprocal(rec[:, pa, :], sume[:, pa, :]),
                  reads=[("sume", pa)], writes=[("rec", pa)], tag="recip")
            em.op("vector", lambda e, Wd=Wd, pa=pa: e.tensor_scalar(probs_bf2[:, pa, 0:Wd], probs_bf[:, pa, 0:Wd],
                                                                    rec[:, pa, :], None, ALU.mult),
                  reads=[("probs_bf", pa), ("rec", pa)], writes=[("probs_bf2", pa)], tag="pscale")

            nch = (Wd + 127) // 128
            for c in range(nch):
                w = min(128, Wd - c * 128)
                em.op("tensor", lambda e, c=c, w=w, pa=pa: e.transpose(ps_pT[0:w, :], probs_bf2[:, pa, c * 128:c * 128 + w], identb[:, :]),
                      reads=[("probs_bf2", pa), ("identb",)], writes=[("ps_pT",)], tag="tr_p")
                em.op("scalar", lambda e, c=c, w=w, pa=pa: e.copy(pT[0:w, pa, c, :], ps_pT[0:w, :]),
                      reads=[("ps_pT",)], writes=[("pT", pa, c)], tag="ev_p")

            av_pairs = []
            av_reads = []
            if t_loc == 0:
                av_pairs.append((v_roll[:, pa, h * 64:(h + 1) * 64], pT[:, pa, 0, :]))
                av_reads += [("v", pa, 0), ("v", pa, 1), ("pT", pa, 0)]
            else:
                av_pairs.append((v_roll[:, 1 - pa, h * 64:(h + 1) * 64], pT[:, pa, 0, :]))
                av_pairs.append((v_roll[:, pa, h * 64:(h + 1) * 64], pT[:, pa, 1, :]))
                av_reads += [("v", 1 - pa, 0), ("v", 1 - pa, 1), ("v", pa, 0), ("v", pa, 1),
                             ("pT", pa, 0), ("pT", pa, 1)]
                if g > 0:
                    av_pairs.append((vg[0:g, h * 64:(h + 1) * 64], pT[0:g, pa, 2, :]))
                    av_reads += [("vg", i) for i in range(g)] + [("pT", pa, 2)]
            tp = None if base == 0 else (0, 64)
            emit_mm_group(P, ps_av[base:base + 64, :], ("ps_av", h % 2), av_pairs,
                          reads=av_reads, tag="mm_av", tile_position=tp)
            if h % 2 == 1:
                em.op("vector", lambda e, hp=hp, pa=pa: e.tensor_copy(yT[:, pa, hp, :], ps_av[:, :]),
                      reads=[("ps_av", 0), ("ps_av", 1)], writes=[("yT", pa, hp)], tag="ev_av")

        for c in range(2):
            emit_mm_group(P, ps_pr[:, :], ("ps_pr",),
                          [(yT[:, pa, k, :], wproj[:, k, c * 512:(c + 1) * 512]) for k in range(ND)],
                          reads=[("yT", pa, k) for k in range(ND)] + [("wproj", k) for k in range(ND)],
                          tag="mm_pr")
            em.op("vector", lambda e, c=c, pa=pa: e.tensor_tensor(xo_buf[:, pa, c * 512:(c + 1) * 512],
                                                           ps_pr[:, :], x_cur[:, pa, c * 512:(c + 1) * 512], ALU.add),
                  reads=[("ps_pr",), ("x", pa)], writes=[("xo", pa, c)], tag="resid")
        em.dma("sync", xo_out[t * 128:(t + 1) * 128, :], xo_buf[:, pa, :],
               reads=[("xo", pa, 0), ("xo", pa, 1)], writes=[("xo_st", t)], tag="st_x")

    inst_ctx.close()
    em.barrier()
    return P


def _l0attn_masks():
    r = np.arange(128)[:, None]
    j = np.arange(256)[None, :]
    allowed = (j <= 128 + r) & ((j >= r + 1) | (j % 64 == 0))
    mask_band = np.where(allowed, 0.0, NEG).astype(np.float32)
    jj = np.arange(128)[None, :]
    mask_first = np.where(jj <= r, 0.0, NEG).astype(np.float32)
    return mask_band, mask_first


def run_l0attn(x_full, Wqkv, Wproj):
    P = build_l0attn()
    nc = P.close_build()
    mask_band, mask_first = _l0attn_masks()
    ident = np.eye(128, dtype=np.float32)
    in_maps = []
    xs = x_full.reshape(NCORES, T, D)
    for c in range(NCORES):
        in_maps.append({
            "x": np.ascontiguousarray(xs[c]),
            "Wqkv": Wqkv, "Wproj": Wproj,
            "identf": ident, "identb": np.eye(128, dtype=BFNP),
            "mask_band": mask_band, "mask_first": mask_first,
        })
    res = run_bass_kernel_spmd(nc, in_maps, core_ids=list(range(NCORES)))
    out = np.stack([res.results[c]["xo"] for c in range(NCORES)])
    return out.reshape(16, 2048, D)


CAP = 5120
MOE_PHASES = "ABC"
MOE_NT = NT
MOE_SKIP_CC = False


def build_moe(P=None, V=None, x_in=None, xo_out=None, bank=None):
    standalone = P is None
    if standalone:
        P = Prog("moe")
        V = {"wg": P.inp("wg", [D, 8]), "w1": P.inp("w1", [8, D, 4096]),
             "b1t": P.inp("b1t", [128, 8, 32]), "w2": P.inp("w2", [8, 4096, D]),
             "b2": P.inp("b2", [8, D]), "identf": P.inp("identf", [128, 128], F32),
             "tri": P.inp("tri", [128, 128], F32), "ones128": P.inp("ones128", [128, 1], F32),
             "ones1": P.inp("ones1", [1, 128], F32), "prefmask": P.inp("prefmask", [128, 1], F32),
             "ebase": P.inp("ebase", [1, 8], F32), "identb": P.inp("identb", [128, 128], BF16)}
        x_in = P.inp("x", [T, D])
        xo_out = P.out("xo", [T, D])
    em = P.em
    nc = P.nc
    wg_in = V["wg"]; w1_in = V["w1"]; b1t_in = V["b1t"]; w2_in = V["w2"]; b2_in = V["b2"]
    identf_in = V["identf"]; tri_in = V["tri"]; ones128_in = V["ones128"]
    ones1_in = V["ones1"]; prefmask_in = V["prefmask"]; ebase_in = V["ebase"]

    cc_in = P.dram("cc_in", [16, 128])
    cc_out = P.dram("cc_out", [128, 128])
    CAPL = 768
    NSLOT = 8 * CAPL
    xg_dram = P.dram("xg_moe", [NSLOT + T, D], BF16)
    yg_dram = P.dram("yg_moe", [NSLOT + T, D], BF16)

    inst_ctx = ExitStack()
    _sb = lambda n, sh, dt=F32: inst_ctx.enter_context(nc.sbuf_tensor(P.uniq(n), list(sh), dt))
    coef8 = _sb("coef8", [128, NT, 8], F32)
    coef1 = _sb("coef1", [128, NT, 1], F32)
    slot_i = _sb("slot_i", [128, NT, 1], I32)
    if bank is None:
        bank = [P.psum(f"bank{i}", [128, 512], F32) for i in range(8)]
    BK = lambda i: ("bank", i)


    with ExitStack() as sa:
        A = lambda name, shape, dt=F32: sa.enter_context(nc.sbuf_tensor(P.uniq(name), list(shape), dt))
        wg = A("wg", [128, ND, 8])
        ident = A("ident", [128, 128])
        tri = A("tri", [128, 128])
        ones128 = A("ones128", [128, 1])
        ones1 = A("ones1", [1, 128])
        prefmask = A("prefmask", [128, 1])
        x_cur = A("x_cur", [128, 2, D])
        lns = {
            "sq": A("sq", [128, 2, D]), "s": A("s", [128, 2, 1]), "s2": A("s2", [128, 2, 1]),
            "mu": A("mu", [128, 2, 1]), "m2": A("m2", [128, 2, 1]), "var": A("var", [128, 2, 1]),
            "lnv": A("lnv", [128, 2, 1]), "rstd": A("rstd", [128, 2, 1]),
            "xn": A("xn", [128, 2, D]), "tag": "lnm",
        }
        xg = A("xg", [128, 2, ND, 128])
        lg = A("lg", [128, 2, 8]); nmg = A("nmg", [128, 2, 1]); eg = A("eg", [128, 2, 8])
        sumg = A("sumg", [128, 2, 1]); eq = A("eq", [128, 2, 8]); cum = A("cum", [128, 2, 8])
        m1 = A("m1", [128, 2, 8]); zero8 = A("zero8", [128, 8])
        mask_sb = A("mask_sb", [128, NT, 8]); r1_sb = A("r1_sb", [128, NT, 8])
        g1_sb = A("g1_sb", [128, NT, 1])
        offs_sb = A("offs_sb", [1, NT, 8]); offs_run = A("offs_run", [1, 2, 8])
        tot_sb = A("tot_sb", [1, 2, 8])
        counts_st = A("counts_st", [16, 128]); allc = A("allc", [128, 128])
        base_sb = A("base_sb", [1, 8]); bvec = A("bvec", [1, 2, 8])
        posb = A("posb", [128, 2, 8]); keep = A("keep", [128, 2, 8])
        kc = A("kc", [128, 2, 8]); ks = A("ks", [128, 2, 1]); coefb = A("coefb", [128, 2, 1])
        ebase = A("ebase", [1, 8]); bvec2 = A("bvec2", [1, 2, 8])
        sl8 = A("sl8", [128, 2, 8]); slf = A("slf", [128, 2, 1])
        xntb = A("xntb", [128, D], BF16)

        for k in range(ND):
            em.dma("sync", wg[:, k, :], wg_in[k * 128:(k + 1) * 128, :], writes=[("wg",)], tag="ld_wg")
        em.dma("sync", ident[:, :], identf_in, writes=[("ident",)])
        em.dma("sync", tri[:, :], tri_in, writes=[("tri",)])
        em.dma("sync", ones128[:, :], ones128_in, writes=[("ones128",)])
        em.dma("sync", ones1[:, :], ones1_in, writes=[("ones1",)])
        em.dma("sync", prefmask[:, :], prefmask_in, writes=[("prefmask",)])
        em.dma("sync", ebase[:, :], ebase_in, writes=[("ebase",)], tag="ld_eb")
        zrow = A("zrow", [128, D], BF16)
        em.op("gpsimd", lambda e: e.memset(zrow[:, :], 0.0), writes=[("zrow",)], tag="zrow")
        for zi in range(48):
            em.dma_fn("scalar", lambda e, zi=zi: e.dma_start(
                out=xg_dram[zi * 128:(zi + 1) * 128, :], in_=zrow[:, :]),
                reads=[("zrow",)], writes=[("xgz",)], tag="zxg")
        em.op("gpsimd", lambda e: e.memset(zero8[:, :], 0.0), writes=[("zero8",)], tag="z8")
        em.op("gpsimd", lambda e: e.memset(counts_st[:, :], 0.0), writes=[("counts0",)], tag="z16")
        em.op("gpsimd", lambda e: e.memset(offs_run[:, 0, :], 0.0), writes=[("offs_run", 0)], tag="zoffs")

        for t in range(MOE_NT):
            pa = t % 2
            em.dma("sync", x_cur[:, pa, :], x_in[t * 128:(t + 1) * 128, :], writes=[("x", pa)], tag="ld_x")
            if MOE_PHASES == "A1a":
                continue
            xn_key = emit_ln_tile(P, lns, pa, x_cur[:, pa, :], ("x", pa))
            xn = lns["xn"][:, pa, :]
            em.op("scalar", lambda e, xn=xn: e.copy(xntb[:, :], xn),
                  reads=[xn_key], writes=[("xntb",)], tag="xntb")
            if MOE_PHASES == "A1b":
                continue
            for k in range(ND):
                em.op("tensor", lambda e, k=k, xn=xn: e.transpose(bank[0][:, 0:128], xn[:, k * 128:(k + 1) * 128], ident[:, :]),
                      reads=[xn_key, ("ident",)], writes=[BK(0)], tag="tr_xn")
                em.op("vector", lambda e, k=k, pa=pa: e.tensor_copy(xg[:, pa, k, :], bank[0][:, 0:128]),
                      reads=[BK(0)], writes=[("xg", pa, k)], tag="ev_xg")
            if MOE_PHASES == "A1":
                continue
            emit_mm_group(P, bank[1][:, 0:8], BK(1),
                          [(xg[:, pa, k, :], wg[:, k, :]) for k in range(ND)],
                          reads=[("xg", pa, k) for k in range(ND)] + [("wg",)], tag="mm_g")
            em.op("vector", lambda e, pa=pa: e.tensor_copy(lg[:, pa, :], bank[1][:, 0:8]),
                  reads=[BK(1)], writes=[("lg", pa)], tag="ev_g")
            em.op("vector", lambda e, pa=pa: e.reduce_max(nmg[:, pa, :], lg[:, pa, :], axis=AX, negate=True),
                  reads=[("lg", pa)], writes=[("nmg", pa)], tag="gmax")
            em.op("scalar", lambda e, pa=pa: e.activation(eg[:, pa, :], lg[:, pa, :], AF.Exp,
                                                          bias=nmg[:, pa, :], accum_out=sumg[:, pa, :]),
                  reads=[("lg", pa), ("nmg", pa)], writes=[("eg", pa), ("sumg", pa)], tag="gexp")
            em.op("vector", lambda e, pa=pa, t=t: e.reciprocal(g1_sb[:, t, :], sumg[:, pa, :]),
                  reads=[("sumg", pa)], writes=[("g1", t)], tag="g1")
            em.op("vector", lambda e, pa=pa: e.tensor_scalar(eq[:, pa, :], lg[:, pa, :], nmg[:, pa, :], 0.0,
                                                             ALU.add, ALU.is_equal),
                  reads=[("lg", pa), ("nmg", pa)], writes=[("eq", pa)], tag="eq")
            em.op("vector", lambda e, pa=pa: e.tensor_tensor_scan(cum[:, pa, :], eq[:, pa, :], zero8[:, :], 0.0,
                                                                  ALU.add, ALU.add),
                  reads=[("eq", pa), ("zero8",)], writes=[("cum", pa)], tag="cum")
            em.op("vector", lambda e, pa=pa: e.tensor_scalar(m1[:, pa, :], cum[:, pa, :], 1.0, None, ALU.is_equal),
                  reads=[("cum", pa)], writes=[("m1", pa)], tag="m1")
            em.op("vector", lambda e, pa=pa, t=t: e.tensor_tensor(mask_sb[:, t, :], eq[:, pa, :], m1[:, pa, :], ALU.mult),
                  reads=[("eq", pa), ("m1", pa)], writes=[("mask", t)], tag="mask")
            if MOE_PHASES == "A2":
                continue
            emit_mm_group(P, bank[2][:, 0:8], BK(2), [(tri[:, :], mask_sb[:, t, :])],
                          reads=[("tri",), ("mask", t)], tag="mm_r1")
            em.op("vector", lambda e, t=t: e.tensor_copy(r1_sb[:, t, :], bank[2][:, 0:8]),
                  reads=[BK(2)], writes=[("r1", t)], tag="ev_r1")
            emit_mm_group(P, bank[3][0:1, 0:8], BK(3), [(ones128[:, :], mask_sb[:, t, :])],
                          reads=[("ones128",), ("mask", t)], tag="mm_tot")
            em.op("vector", lambda e, t=t: e.tensor_copy(tot_sb[:, t % 2, :], bank[3][0:1, 0:8]),
                  reads=[BK(3)], writes=[("tot", t % 2)], tag="ev_tot")
            em.op("scalar", lambda e, t=t: e.copy(offs_sb[:, t, :], offs_run[:, t % 2, :]),
                  reads=[("offs_run", t % 2)], writes=[("offs", t)], tag="offs_cp")
            em.op("vector", lambda e, t=t: e.tensor_tensor(offs_run[:, (t + 1) % 2, :], offs_run[:, t % 2, :],
                                                           tot_sb[:, t % 2, :], ALU.add),
                  reads=[("offs_run", t % 2), ("tot", t % 2)], writes=[("offs_run", (t + 1) % 2)], tag="offs_add")
            em.op("vector", lambda e, t=t, pa=pa: e.tensor_tensor(bvec2[:, pa, :], offs_sb[:, t, :], ebase[:, :], ALU.add),
                  reads=[("offs", t), ("ebase",)], writes=[("bvec2", pa)], tag="bvec2")
            emit_mm_group(P, bank[6][:, 0:8], BK(6), [(ones1[:, :], bvec2[:, pa, :])],
                          reads=[("ones1",), ("bvec2", pa)], tag="mm_bc2")
            em.op("vector", lambda e, t=t, pa=pa: e.tensor_tensor(sl8[:, pa, :], r1_sb[:, t, :], bank[6][:, 0:8], ALU.add),
                  reads=[("r1", t), BK(6)], writes=[("sl8", pa)], tag="sl8a")
            em.op("vector", lambda e, t=t, pa=pa: e.tensor_tensor(sl8[:, pa, :], sl8[:, pa, :], mask_sb[:, t, :], ALU.mult),
                  reads=[("sl8", pa), ("mask", t)], writes=[("sl8", pa)], tag="sl8b")
            em.op("vector", lambda e, pa=pa: e.reduce_sum(slf[:, pa, :], sl8[:, pa, :], axis=AX),
                  reads=[("sl8", pa)], writes=[("slf", pa)], tag="slot_f")
            em.op("vector", lambda e, t=t, pa=pa: e.tensor_copy(slot_i[:, t, :], slf[:, pa, :]),
                  reads=[("slf", pa)], writes=[("slot", t)], tag="slot_i")
            em.dma_fn("gpsimd", lambda e, t=t: e.indirect_dma_start(
                xg_dram, IndirectOffsetOnAxis(slot_i[:, t, :], 0), xntb[:, :], None),
                reads=[("xntb",), ("slot", t), ("xgz",)], writes=[("xg", t)], tag="scat")

        if MOE_PHASES in ("A1", "A2", "A3"):
            for t in range(NT):
                em.dma("sync", xo_out[t * 128:(t + 1) * 128, :], x_in[t * 128:(t + 1) * 128, :],
                       writes=[("xo_st", t)], tag="st_dummy")
            return P
        em.op("scalar", lambda e: e.copy(counts_st[0:1, 0:8], offs_run[:, NT % 2, :]),
              reads=[("offs_run", NT % 2), ("counts0",)], writes=[("counts_st",)], tag="cnt_cp")
        em.dma("gpsimd", cc_in, counts_st[:, :],
               reads=[("counts_st",)], writes=[("cc_in",)], tag="cnt_st")
        if MOE_SKIP_CC:
            for r in range(NCORES):
                em.dma("gpsimd", cc_out[16 * r:16 * (r + 1), :], cc_in,
                       reads=[("cc_in",)], writes=[("cc_out",)], tag="fake_ag")
        else:
            em.dma_fn("gpsimd",
                      lambda e: e.collective_compute("AllGather", ALU.bypass,
                                                     replica_groups=[list(range(NCORES))],
                                                     ins=[cc_in.opt()],
                                                     outs=[cc_out.opt()]),
                      reads=[("cc_in",)], writes=[("cc_out",)], tag="allgather", inc=1)
        em.dma("gpsimd", allc[:, :], cc_out, reads=[("cc_out",)], writes=[("allc",)], tag="cnt_ld")
        emit_mm_group(P, bank[4][0:1, 0:8], BK(4), [(prefmask[:, :], allc[:, 0:8])],
                      reads=[("prefmask",), ("allc",)], tag="mm_base")
        em.op("vector", lambda e: e.tensor_copy(base_sb[:, :], bank[4][0:1, 0:8]),
              reads=[BK(4)], writes=[("base",)], tag="ev_base")

        for t in range(NT):
            pa = t % 2
            em.op("vector", lambda e, t=t, pa=pa: e.tensor_tensor(bvec[:, pa, :], offs_sb[:, t, :], base_sb[:, :], ALU.add),
                  reads=[("offs", t), ("base",)], writes=[("bvec", pa)], tag="bvec")
            emit_mm_group(P, bank[5][:, 0:8], BK(5), [(ones1[:, :], bvec[:, pa, :])],
                          reads=[("ones1",), ("bvec", pa)], tag="mm_bc")
            em.op("vector", lambda e, t=t, pa=pa: e.tensor_tensor(posb[:, pa, :], r1_sb[:, t, :], bank[5][:, 0:8], ALU.add),
                  reads=[("r1", t), BK(5)], writes=[("posb", pa)], tag="posb")
            em.op("vector", lambda e, pa=pa: e.tensor_scalar(keep[:, pa, :], posb[:, pa, :], float(CAP), None, ALU.is_le),
                  reads=[("posb", pa)], writes=[("keep", pa)], tag="keep")
            em.op("vector", lambda e, t=t, pa=pa: e.tensor_tensor(kc[:, pa, :], mask_sb[:, t, :], keep[:, pa, :], ALU.mult),
                  reads=[("mask", t), ("keep", pa)], writes=[("kc", pa)], tag="kc")
            em.op("vector", lambda e, pa=pa: e.reduce_sum(ks[:, pa, :], kc[:, pa, :], axis=AX),
                  reads=[("kc", pa)], writes=[("ks", pa)], tag="ks")
            em.op("vector", lambda e, t=t, pa=pa: e.tensor_scalar(coefb[:, pa, :], g1_sb[:, t, :], ks[:, pa, :], 0.5,
                                                                  ALU.mult, ALU.mult),
                  reads=[("g1", t), ("ks", pa)], writes=[("coef", pa)], tag="coef")
            em.op("vector", lambda e, t=t, pa=pa: e.tensor_scalar(coef8[:, t, :], mask_sb[:, t, :], coefb[:, pa, :], None,
                                                                  ALU.mult),
                  reads=[("mask", t), ("coef", pa)], writes=[("coef8", t)], tag="coef8")
            em.op("scalar", lambda e, t=t, pa=pa: e.copy(coef1[:, t, :], coefb[:, pa, :]),
                  reads=[("coef", pa)], writes=[("coef1", t)], tag="coef1")
    em.barrier()

    if "B" not in MOE_PHASES:
        for t in range(NT):
            em.dma("sync", xo_out[t * 128:(t + 1) * 128, :], x_in[t * 128:(t + 1) * 128, :],
                   writes=[("xo_st", t)], tag="st_dummy")
        return P

    with ExitStack() as sb_:
        A = lambda name, shape, dt=BF16: sb_.enter_context(nc.sbuf_tensor(P.uniq(name), list(shape), dt))
        w1e = A("w1e", [128, ND, 2048])
        w2e = A("w2e", [128, 32, D])
        xg_tok = A("xg_tok", [128, 6, D])
        XgT = A("XgT", [128, ND, 768])
        hn = A("hn", [128, 32, 768])
        ygb = A("ygb", [128, 2, D])
        b1t = A("b1t", [128, 8, 32], F32)
        identB = A("identB", [128, 128], BF16)
        em.dma("sync", b1t[:, :, :], b1t_in, writes=[("b1t",)], tag="ld_b1t")
        em.dma("sync", identB[:, :], V["identb"], writes=[("identB",)], tag="ld_idb")
        NCK = (768 + 511) // 512
        for e_i in range(8):
            for s_ in range(6):
                em.dma_fn("sync", lambda e, e_i=e_i, s_=s_: e.dma_start(
                    out=xg_tok[:, s_, :], in_=xg_dram[e_i * 768 + s_ * 128:e_i * 768 + (s_ + 1) * 128, :]),
                    reads=[("xg", t) for t in range(NT)], writes=[("xg_tok", s_)], tag="ld_xg")
                for k in range(ND):
                    em.op("tensor", lambda e, s_=s_, k=k: e.transpose(
                        bank[0][:, :].bitcast(BF16)[:, 0:128], xg_tok[:, s_, k * 128:(k + 1) * 128], identB[:, :]),
                        reads=[("xg_tok", s_), ("identB",)], writes=[BK(0)], tag="tr_xg")
                    em.op("vector", lambda e, s_=s_, k=k: e.tensor_copy(
                        XgT[:, k, s_ * 128:(s_ + 1) * 128], bank[0][:, :].bitcast(BF16)[:, 0:128]),
                        reads=[BK(0)], writes=[("XgT", k)], tag="ev_xg2")
            for fb in range(2):
                em.dma_fn("gpsimd", lambda e, e_i=e_i, fb=fb: e.dma_start(
                    out=w1e[:, :, :],
                    in_=w1_in[e_i].rearrange("(k p) c -> p k c", p=128)[:, :, fb * 2048:(fb + 1) * 2048]),
                    writes=[("w1e", k) for k in range(ND)], tag="ld_w1")
                w1r = [("w1e", k) for k in range(ND)]
                xgr = [("XgT", k) for k in range(ND)]
                for n in range(NCK):
                    c0 = n * 512
                    cw = min(768, c0 + 512) - c0
                    for m in range(16):
                        bk = m % 4
                        mg = fb * 16 + m
                        emit_mm_group(P, bank[bk][:, 0:cw], BK(bk),
                                      [(w1e[:, k, m * 128:(m + 1) * 128], XgT[:, k, c0:c0 + cw]) for k in range(ND)],
                                      reads=w1r + xgr, tag="mm_w1")
                        em.op("scalar", lambda e, bk=bk, e_i=e_i, mg=mg, c0=c0, cw=cw: e.activation(
                            hn[:, mg, c0:c0 + cw], bank[bk][:, 0:cw], AF.Relu, bias=b1t[:, e_i, mg:mg + 1]),
                            reads=[BK(bk), ("b1t",)], writes=[("hn", mg)], tag="ev_h")
            em.dma_fn("gpsimd", lambda e, e_i=e_i: e.dma_start(
                out=w2e[:, :, :], in_=w2_in[e_i].rearrange("(k p) c -> p k c", p=128)),
                writes=[("w2e", kf) for kf in range(32)], tag="ld_w2")
            w2r = [("w2e", kf) for kf in range(32)]
            for sub in range(6):
                for dc in range(2):
                    bk = 4 + (sub * 2 + dc) % 4
                    emit_mm_group(P, bank[bk][:, :], BK(bk),
                                  [(hn[:, kf, sub * 128:(sub + 1) * 128], w2e[:, kf, dc * 512:(dc + 1) * 512])
                                   for kf in range(32)],
                                  reads=[("hn", kf) for kf in range(32)] + w2r, tag="mm_w2")
                    em.op("vector", lambda e, bk=bk, dc=dc, sub=sub: e.tensor_copy(
                        ygb[:, sub % 2, dc * 512:(dc + 1) * 512], bank[bk][:, :]),
                        reads=[BK(bk)], writes=[("ygb", sub % 2, dc)], tag="ev_y")
                em.dma_fn("sync", lambda e, e_i=e_i, sub=sub: e.dma_start(
                    out=yg_dram[e_i * 768 + sub * 128:e_i * 768 + (sub + 1) * 128, :], in_=ygb[:, sub % 2, :]),
                    reads=[("ygb", sub % 2, 0), ("ygb", sub % 2, 1)], writes=[("yg", e_i, sub)], tag="st_yg")
    em.barrier()

    with ExitStack() as sc:
        A = lambda name, shape, dt=F32: sc.enter_context(nc.sbuf_tensor(P.uniq(name), list(shape), dt))
        x_cur = A("x_cur", [128, 2, D])
        xo_buf = A("xo_buf", [128, 2, D])
        c8T = A("c8T", [8, NT, 128], BF16)
        c8b = A("c8b", [128, 2, 8], BF16)
        b2s = A("b2s", [8, D], BF16)
        identc = A("identc", [128, 128], BF16)
        em.dma("gpsimd", b2s[:, :], b2_in, writes=[("b2s",)], tag="ld_b2")
        em.dma("gpsimd", identc[:, :], identf_in, writes=[("identc",)], tag="ld_idc")
        ytok = sc.enter_context(nc.sbuf_tensor(P.uniq("ytok"), [128, D], BF16))
        for t in range(NT):
            pa = t % 2
            em.dma("sync", x_cur[:, pa, :], x_in[t * 128:(t + 1) * 128, :], writes=[("x", pa)], tag="ld_x2")
            em.dma_fn("gpsimd", lambda e, t=t: e.indirect_dma_start(
                ytok[:, :], None, yg_dram, IndirectOffsetOnAxis(slot_i[:, t, :], 0)),
                reads=[("slot", t)] + [("yg", e2, sub) for e2 in range(8) for sub in range(6)],
                writes=[("ytok",)], tag="gath_y")
            em.op("vector", lambda e, t=t, pa=pa: e.tensor_copy(c8b[:, pa, :], coef8[:, t, :]),
                  reads=[("coef8", t)], writes=[("c8b", pa)], tag="c8cast")
            em.op("tensor", lambda e, t=t, pa=pa: e.transpose(bank[6][:, :].bitcast(BF16)[0:8, 0:128], c8b[:, pa, :], identc[:, :]),
                  reads=[("c8b", pa), ("identc",)], writes=[BK(6)], tag="tr_c8")
            em.op("scalar", lambda e, t=t: e.copy(c8T[:, t, :], bank[6][:, :].bitcast(BF16)[0:8, 0:128]),
                  reads=[BK(6)], writes=[("c8T", t)], tag="ev_c8")
            for dc in range(2):
                emit_mm_group(P, bank[dc][:, :], BK(dc),
                              [(c8T[:, t, :], b2s[:, dc * 512:(dc + 1) * 512])],
                              reads=[("c8T", t), ("b2s",)], tag="mm_b2")
                em.op("vector", lambda e, t=t, pa=pa, dc=dc: e.scalar_tensor_tensor(
                    xo_buf[:, pa, dc * 512:(dc + 1) * 512], ytok[:, dc * 512:(dc + 1) * 512],
                    coef1[:, t, :], x_cur[:, pa, dc * 512:(dc + 1) * 512], ALU.mult, ALU.add),
                    reads=[("x", pa), ("ytok",), ("coef1", t)], writes=[("xo", pa, dc)], tag="resid1")
                em.op("vector", lambda e, pa=pa, dc=dc: e.tensor_tensor(
                    xo_buf[:, pa, dc * 512:(dc + 1) * 512], xo_buf[:, pa, dc * 512:(dc + 1) * 512],
                    bank[dc][:, :], ALU.add),
                    reads=[("xo", pa, dc), BK(dc)], writes=[("xo", pa, dc)], tag="resid2")
            em.dma("sync", xo_out[t * 128:(t + 1) * 128, :], xo_buf[:, pa, :],
                   reads=[("xo", pa, 0), ("xo", pa, 1)], writes=[("xo_st", t)], tag="st_x")
    inst_ctx.close()
    em.barrier()
    return P


def _moe_consts(rank):
    tri = (np.arange(128)[:, None] <= np.arange(128)[None, :]).astype(np.float32)
    ones128 = np.ones((128, 1), np.float32)
    ones1 = np.ones((1, 128), np.float32)
    pm = np.zeros((128, 1), np.float32)
    for r in range(rank):
        pm[16 * r, 0] = 1.0
    return tri, ones128, ones1, pm


def _moe_ebase():
    return (np.arange(8, dtype=np.float32) * 768.0 - 1.0).reshape(1, 8)


def run_moe(x_full, wg, w1, b1, w2, b2):
    P = build_moe()
    nc = P.close_build()
    tri, ones128, ones1, _ = _moe_consts(0)
    ident = np.eye(128, dtype=np.float32)
    b1t = np.ascontiguousarray(b1.reshape(8, 32, 128).transpose(2, 0, 1))
    xs = x_full.reshape(NCORES, T, D)
    in_maps = []
    for c in range(NCORES):
        in_maps.append({
            "x": np.ascontiguousarray(xs[c]), "wg": wg, "w1": w1, "b1t": b1t,
            "w2": w2, "b2": b2, "identf": ident, "tri": tri,
            "ones128": ones128, "ones1": ones1, "prefmask": _moe_consts(c)[3], "ebase": _moe_ebase(),
            "identb": np.eye(128, dtype=BFNP),
        })
    res = run_bass_kernel_spmd(nc, in_maps, core_ids=list(range(NCORES)))
    out = np.stack([res.results[c]["xo"] for c in range(NCORES)])
    return out.reshape(16, 2048, D)


def build_l1attn(P=None, V=None, x_in=None, xo_out=None, bank=None):
    standalone = P is None
    if standalone:
        P = Prog("l1attn")
        V = {"wmixt": P.inp("wmixt", [128, ND]), "WgT": P.inp("WgT", [D, D]),
             "bgt": P.inp("bgt", [128, ND]), "WvT": P.inp("WvT", [D, D]),
             "bvt": P.inp("bvt", [128, ND]), "WdT": P.inp("WdT", [D, D]),
             "bdt": P.inp("bdt", [128, ND]), "wattnt": P.inp("wattnt", [128, ND]),
             "WqkvT": P.inp("WqkvT", [D, 3 * D]), "WoT": P.inp("WoT", [D, D]),
             "wffnt": P.inp("wffnt", [128, ND]), "WgateT": P.inp("WgateT", [D, 4096]),
             "WupT": P.inp("WupT", [D, 4096]), "WoutT": P.inp("WoutT", [4096, D]),
             "identf": P.inp("identf", [128, 128], F32), "identb": P.inp("identb", [128, 128], BF16),
             "mask_band_l1": P.inp("mask_band", [128, 256], F32),
             "mask_first": P.inp("mask_first", [128, 128], F32)}
        x_in = P.inp("x", [T, D])
        xo_out = P.out("xo", [T, D])
    em = P.em
    nc = P.nc
    wmix_in = V["wmixt"]; WgT_in = V["WgT"]; bg_in = V["bgt"]; WvT_in = V["WvT"]
    bv_in = V["bvt"]; WdT_in = V["WdT"]; bd_in = V["bdt"]; wattn_in = V["wattnt"]
    WqkvT_in = V["WqkvT"]; WoT_in = V["WoT"]; wffn_in = V["wffnt"]
    WgateT_in = V["WgateT"]; WupT_in = V["WupT"]; WoutT_in = V["WoutT"]
    identf_in = V["identf"]; identb_in = V["identb"]
    mask_band_in = V["mask_band_l1"]; mask_first_in = V["mask_first"]

    xd0 = P.dram("xd0", [T, D])
    xd1 = P.dram("xd1", [T, D])
    xd2 = P.dram("xd2", [T, D])
    xd3 = P.dram("xd3", [T, D])

    if bank is None:
        bank = [P.psum(f"bank{i}", [128, 512], F32) for i in range(8)]
    BK = lambda i: ("bank", i)
    NCH = 8
    CHT = 512

    with ExitStack() as s0:
        A = lambda n, sh, dt=F32: s0.enter_context(nc.sbuf_tensor(P.uniq(n), list(sh), dt))
        xc0 = A("xc0", [128, 2, D])
        lns0 = {k: A("l0" + k, [128, 2, D if k in ("sq", "xn") else 1]) for k in
               ("sq", "s", "s2", "mu", "m2", "var", "lnv", "rstd", "xn")}
        lns0["tag"] = "lnA"
        for t in range(NT):
            pa = t % 2
            em.dma("sync", xc0[:, pa, :], x_in[t * 128:(t + 1) * 128, :], writes=[("x", pa)], tag="ld_x")
            xk = emit_ln_tile(P, lns0, pa, xc0[:, pa, :], ("x", pa))
            em.dma("sync", xd0[t * 128:(t + 1) * 128, :], lns0["xn"][:, pa, :], reads=[xk],
                   writes=[("xd0", t)], tag="st0")
    em.barrier()

    with ExitStack() as s1:
        A = lambda n, sh, dt=F32: s1.enter_context(nc.sbuf_tensor(P.uniq(n), list(sh), dt))
        WgT = A("WgT", [128, ND, D], BF16)
        WvT = A("WvT", [128, ND, D], BF16)
        WdT = A("WdT", [128, ND, D], BF16)
        wmix = A("wmix", [128, ND])
        bgt = A("bgt", [128, ND])
        bvt = A("bvt", [128, ND])
        bdt = A("bdt", [128, ND])
        identa = A("identa", [128, 128])
        identba = A("identb", [128, 128], BF16)
        xra = A("xra", [128, 2, 4, D])
        rmsa = {k: A("r1" + k, [128, 2, D if k in ("sq", "xn") else 1]) for k in
               ("sq", "s", "s2", "mu", "m2", "var", "lnv", "rstd", "xn")}
        rmsa["tag"] = "rmsA"
        xmTa = A("xmTa", [128, 2, ND, CHT], BF16)
        sg = A("sg", [128, 2, CHT])
        tv = A("tv", [128, 2, CHT])
        sd = A("sd", [128, 2, CHT])
        aa = A("aa", [128, 2, CHT])
        xs = A("xs", [128, 2, CHT], BF16)
        hh = A("hh", [128, 2, CHT], BF16)
        carry = A("carry", [128, ND, 1])
        for (wt, win, wkey) in ((WgT, WgT_in, "WgT"), (WvT, WvT_in, "WvT"), (WdT, WdT_in, "WdT")):
            em.dma_fn("gpsimd", lambda e, wt=wt, win=win: e.dma_start(
                out=wt[:, :, :], in_=win.rearrange("(k p) c -> p k c", p=128)),
                writes=[(wkey,)], tag="ldw")
        em.dma("sync", wmix[:, :], wmix_in, writes=[("wmix",)])
        em.dma("sync", bgt[:, :], bg_in, writes=[("bgt",)])
        em.dma("sync", bvt[:, :], bv_in, writes=[("bvt",)])
        em.dma("sync", bdt[:, :], bd_in, writes=[("bdt",)])
        em.dma("sync", identa[:, :], identf_in, writes=[("identa",)])
        em.dma("sync", identba[:, :], identb_in, writes=[("identb",)])
        for n in range(NCH):
            pa = n % 2
            tok0 = n * CHT
            for s_ in range(4):
                t = n * 4 + s_
                em.dma("sync", xra[:, pa, s_, :], xd0[t * 128:(t + 1) * 128, :],
                       writes=[("xra", pa, s_)], tag="ld_c")
                xk = emit_ln_tile(P, rmsa, pa, xra[:, pa, s_, :], ("xra", pa, s_), rms=True, eps=1e-6)
                xn = rmsa["xn"][:, pa, :]
                for k in range(ND):
                    em.op("tensor", lambda e, k=k, xn=xn: e.transpose(bank[6][:, 0:128], xn[:, k * 128:(k + 1) * 128], identa[:, :]),
                          reads=[xk, ("identa",)], writes=[BK(6)], tag="trm")
                    em.op("vector", lambda e, k=k, pa=pa, s_=s_: e.tensor_scalar(
                        xmTa[:, pa, k, s_ * 128:(s_ + 1) * 128], bank[6][:, 0:128], wmix[:, k:k + 1], None, ALU.mult),
                        reads=[BK(6), ("wmix",)], writes=[("xmTa", pa, k)], tag="evm")
            xmr = [("xmTa", pa, k) for k in range(ND)]
            for m in range(ND):
                for (wt, wkey, bk) in ((WgT, "WgT", 0), (WvT, "WvT", 1), (WdT, "WdT", 2)):
                    emit_mm_group(P, bank[bk][:, :], BK(bk),
                                  [(wt[:, k, m * 128:(m + 1) * 128], xmTa[:, pa, k, :]) for k in range(ND)],
                                  reads=xmr + [(wkey,)], tag="mm_gvd")
                em.op("scalar", lambda e, m=m, pa=pa: e.activation(sg[:, pa, :], bank[0][:, :], AF.Sigmoid,
                                                                   bias=bgt[:, m:m + 1]),
                      reads=[BK(0), ("bgt",)], writes=[("sg", pa)], tag="ev_g")
                em.op("scalar", lambda e, m=m, pa=pa: e.activation(tv[:, pa, :], bank[1][:, :], AF.Tanh,
                                                                   bias=bvt[:, m:m + 1]),
                      reads=[BK(1), ("bvt",)], writes=[("tv", pa)], tag="ev_v")
                em.op("scalar", lambda e, m=m, pa=pa: e.activation(sd[:, pa, :], bank[2][:, :], AF.Sigmoid,
                                                                   bias=bdt[:, m:m + 1]),
                      reads=[BK(2), ("bdt",)], writes=[("sd", pa)], tag="ev_d")
                em.op("vector", lambda e, pa=pa: e.tensor_mul(xs[:, pa, :], sg[:, pa, :], tv[:, pa, :]),
                      reads=[("sg", pa), ("tv", pa)], writes=[("xs", pa)], tag="xs")
                em.op("vector", lambda e, pa=pa: e.tensor_scalar(aa[:, pa, :], sd[:, pa, :], 0.998, 0.001,
                                                                 ALU.mult, ALU.add),
                      reads=[("sd", pa)], writes=[("aa", pa)], tag="aa")
                if n % 4 == 0:
                    em.op("vector", lambda e, m=m, pa=pa: e.tensor_tensor_scan(
                        hh[:, pa, :], aa[:, pa, :], xs[:, pa, :], 0.0, ALU.mult, ALU.add),
                        reads=[("aa", pa), ("xs", pa)], writes=[("hh", pa)], tag="scan")
                else:
                    em.op("vector", lambda e, m=m, pa=pa: e.tensor_tensor_scan(
                        hh[:, pa, :], aa[:, pa, :], xs[:, pa, :], carry[:, m, :], ALU.mult, ALU.add),
                        reads=[("aa", pa), ("xs", pa), ("carry", m)], writes=[("hh", pa)], tag="scan")
                em.op("vector", lambda e, m=m, pa=pa: e.tensor_copy(carry[:, m, :], hh[:, pa, CHT - 1:CHT]),
                      reads=[("hh", pa)], writes=[("carry", m)], tag="carry")
                for s_ in range(4):
                    em.op("tensor", lambda e, m=m, pa=pa, s_=s_: e.transpose(
                        bank[7][:, :].bitcast(BF16)[:, 0:128], hh[:, pa, s_ * 128:(s_ + 1) * 128], identba[:, :]),
                        reads=[("hh", pa), ("identb",)], writes=[BK(7)], tag="trh")
                    em.op("vector", lambda e, m=m, pa=pa, s_=s_: e.tensor_tensor(
                        xra[:, pa, s_, m * 128:(m + 1) * 128], xra[:, pa, s_, m * 128:(m + 1) * 128],
                        bank[7][:, :].bitcast(BF16)[:, 0:128], ALU.add),
                        reads=[BK(7), ("xra", pa, s_)], writes=[("xra", pa, s_)], tag="addh")
            for s_ in range(4):
                t = n * 4 + s_
                em.dma("sync", xd1[t * 128:(t + 1) * 128, :], xra[:, pa, s_, :],
                       reads=[("xra", pa, s_)], writes=[("xd1", t)], tag="st1")
    em.barrier()

    with ExitStack() as s2:
        A = lambda n, sh, dt=F32: s2.enter_context(nc.sbuf_tensor(P.uniq(n), list(sh), dt))
        WqkvT = A("WqkvT", [128, ND, 3 * D], BF16)
        WoT = A("WoT", [128, ND, D], BF16)
        wattn = A("wattn", [128, ND])
        identb2 = A("identb2", [128, 128])
        identbb = A("identbb", [128, 128], BF16)
        mband = A("mband", [128, 256])
        mfirst = A("mfirst", [128, 128])
        xcb = A("xcb", [128, 2, D])
        rmsb = {k: A("r2" + k, [128, 2, D if k in ("sq", "xn") else 1]) for k in
               ("sq", "s", "s2", "mu", "m2", "var", "lnv", "rstd", "xn")}
        rmsb["tag"] = "rmsB"
        xmTb = A("xmTb", [128, 2, ND, 128], BF16)
        qTb = A("qTb", [128, 2, ND, 128], BF16)
        kTb = A("kTb", [128, 2, ND, 128], BF16)
        v_rollb = A("v_rollb", [128, 2, D], BF16)
        probsb = A("probsb", [128, 2, 256])
        probsb_bf = A("probsb_bf", [128, 2, 256], BF16)
        probsb_bf2 = A("probsb_bf2", [128, 2, 256], BF16)
        pTb2 = A("pTb2", [128, 2, 2, 128], BF16)
        nmb = A("nmb", [128, 2, 1])
        sumeb = A("sumeb", [128, 2, 1])
        recb = A("recb", [128, 2, 1])
        yTb = A("yTb", [128, 2, ND, 128], BF16)
        xob = A("xob", [128, 2, D])
        em.dma_fn("gpsimd", lambda e: e.dma_start(
            out=WqkvT[:, :, :], in_=WqkvT_in.rearrange("(k p) c -> p k c", p=128)),
            writes=[("WqkvT", k) for k in range(ND)], tag="ldw")
        em.dma_fn("gpsimd", lambda e: e.dma_start(
            out=WoT[:, :, :], in_=WoT_in.rearrange("(k p) c -> p k c", p=128)),
            writes=[("WoT", k) for k in range(ND)], tag="ldw")
        em.dma("sync", wattn[:, :], wattn_in, writes=[("wattn",)])
        em.dma("sync", identb2[:, :], identf_in, writes=[("identb2",)])
        em.dma("sync", identbb[:, :], identb_in, writes=[("identbb",)])
        em.dma("sync", mband[:, :], mask_band_in, writes=[("mband",)])
        em.dma("sync", mfirst[:, :], mask_first_in, writes=[("mfirst",)])
        wqr = [("WqkvT", k) for k in range(ND)]
        for t in range(NT):
            pa = t % 2
            t_loc = t % NT_SEQ
            em.dma("sync", xcb[:, pa, :], xd1[t * 128:(t + 1) * 128, :], writes=[("x", pa)],
                   deps=[], tag="ld_x")
            xk = emit_ln_tile(P, rmsb, pa, xcb[:, pa, :], ("x", pa), rms=True, eps=1e-6)
            xn = rmsb["xn"][:, pa, :]
            for k in range(ND):
                em.op("tensor", lambda e, k=k, xn=xn: e.transpose(bank[6][:, 0:128], xn[:, k * 128:(k + 1) * 128], identb2[:, :]),
                      reads=[xk, ("identb2",)], writes=[BK(6)], tag="trb")
                em.op("vector", lambda e, k=k, pa=pa: e.tensor_scalar(
                    xmTb[:, pa, k, :], bank[6][:, 0:128], wattn[:, k:k + 1], None, ALU.mult),
                    reads=[BK(6), ("wattn",)], writes=[("xmTb", pa, k)], tag="evb")
            xmr = [("xmTb", pa, k) for k in range(ND)]
            for hp in range(ND):
                bk = hp % 2
                emit_mm_group(P, bank[bk][:, 0:128], BK(bk),
                              [(WqkvT[:, k, hp * 128:(hp + 1) * 128], xmTb[:, pa, k, :]) for k in range(ND)],
                              reads=xmr + wqr, tag="mm_q")
                em.op("scalar", lambda e, hp=hp, bk=bk, pa=pa: e.activation(qTb[:, pa, hp, :], bank[bk][:, 0:128],
                                                                            AF.Copy, scale=0.125),
                      reads=[BK(bk)], writes=[("qTb", pa, hp)], tag="ev_q")
                emit_mm_group(P, bank[bk][:, 0:128], BK(bk),
                              [(WqkvT[:, k, D + hp * 128:D + (hp + 1) * 128], xmTb[:, pa, k, :]) for k in range(ND)],
                              reads=xmr + wqr, tag="mm_k")
                em.op("scalar", lambda e, hp=hp, bk=bk, pa=pa: e.copy(kTb[:, pa, hp, :], bank[bk][:, 0:128]),
                      reads=[BK(bk)], writes=[("kTb", pa, hp)], tag="ev_k")
            for c in range(2):
                emit_mm_group(P, bank[2][:, :], BK(2),
                              [(xmTb[:, pa, k, :], WqkvT[:, k, 2 * D + c * 512:2 * D + (c + 1) * 512]) for k in range(ND)],
                              reads=xmr + wqr, tag="mm_v")
                em.op("vector", lambda e, c=c, pa=pa: e.tensor_copy(v_rollb[:, pa, c * 512:(c + 1) * 512], bank[2][:, :]),
                      reads=[BK(2)], writes=[("v", pa, c)], tag="ev_v")
            Wd_ = 128 if t_loc == 0 else 256
            for h in range(16):
                hp = h // 2
                base = (h % 2) * 64
                q_ap = qTb[base:base + 64, pa, hp, :]

                def fn_sc(e, q_ap=q_ap, pa=pa, hp=hp, base=base, t_loc=t_loc):
                    if t_loc == 0:
                        return e.matmul(bank[3][:, 0:128], q_ap, kTb[base:base + 64, pa, hp, :], start=True, stop=True)
                    e.matmul(bank[3][:, 0:128], q_ap, kTb[base:base + 64, 1 - pa, hp, :], start=True, stop=True)
                    return e.matmul(bank[3][:, 128:256], q_ap, kTb[base:base + 64, pa, hp, :], start=True, stop=True)
                sc_reads = [("qTb", pa, hp), ("kTb", pa, hp)] + ([("kTb", 1 - pa, hp)] if t_loc > 0 else [])
                em.op("tensor", fn_sc, reads=sc_reads, writes=[BK(3)], tag="mm_sc")
                pr = probsb[:, pa, :]
                if t_loc == 0:
                    em.op("vector", lambda e, pr=pr: e.tensor_tensor(pr[:, 0:128], bank[3][:, 0:128], mfirst[:, :], ALU.add),
                          reads=[BK(3), ("mfirst",)], writes=[("probsb", pa)], tag="mask0")
                else:
                    em.op("vector", lambda e, pr=pr: e.tensor_tensor(pr[:, 0:256], bank[3][:, 0:256], mband[:, :], ALU.add),
                          reads=[BK(3), ("mband",)], writes=[("probsb", pa)], tag="mask")
                em.op("vector", lambda e, pr=pr, Wd_=Wd_, pa=pa: e.reduce_max(nmb[:, pa, :], pr[:, 0:Wd_], axis=AX, negate=True),
                      reads=[("probsb", pa)], writes=[("nmb", pa)], tag="negmax")
                em.op("scalar", lambda e, pr=pr, Wd_=Wd_, pa=pa: e.activation(probsb_bf[:, pa, 0:Wd_], pr[:, 0:Wd_], AF.Exp,
                                                                              bias=nmb[:, pa, :], accum_out=sumeb[:, pa, :]),
                      reads=[("probsb", pa), ("nmb", pa)], writes=[("probsb_bf", pa), ("sumeb", pa)], tag="exp")
                em.op("vector", lambda e, pa=pa: e.reciprocal(recb[:, pa, :], sumeb[:, pa, :]),
                      reads=[("sumeb", pa)], writes=[("recb", pa)], tag="recip")
                em.op("vector", lambda e, Wd_=Wd_, pa=pa: e.tensor_scalar(probsb_bf2[:, pa, 0:Wd_], probsb_bf[:, pa, 0:Wd_],
                                                                          recb[:, pa, :], None, ALU.mult),
                      reads=[("probsb_bf", pa), ("recb", pa)], writes=[("probsb_bf2", pa)], tag="pscale")
                nch = Wd_ // 128
                for c in range(nch):
                    em.op("tensor", lambda e, c=c, pa=pa: e.transpose(bank[4][:, :].bitcast(BF16)[:, 0:128],
                                                                      probsb_bf2[:, pa, c * 128:(c + 1) * 128], identbb[:, :]),
                          reads=[("probsb_bf2", pa), ("identbb",)], writes=[BK(4)], tag="tr_p")
                    em.op("scalar", lambda e, c=c, pa=pa: e.copy(pTb2[:, pa, c, :], bank[4][:, :].bitcast(BF16)[:, 0:128]),
                          reads=[BK(4)], writes=[("pTb2", pa, c)], tag="ev_p")
                av_pairs = []
                av_reads = []
                if t_loc == 0:
                    av_pairs.append((v_rollb[:, pa, h * 64:(h + 1) * 64], pTb2[:, pa, 0, :]))
                    av_reads += [("v", pa, 0), ("v", pa, 1), ("pTb2", pa, 0)]
                else:
                    av_pairs.append((v_rollb[:, 1 - pa, h * 64:(h + 1) * 64], pTb2[:, pa, 0, :]))
                    av_pairs.append((v_rollb[:, pa, h * 64:(h + 1) * 64], pTb2[:, pa, 1, :]))
                    av_reads += [("v", 1 - pa, 0), ("v", 1 - pa, 1), ("v", pa, 0), ("v", pa, 1),
                                 ("pTb2", pa, 0), ("pTb2", pa, 1)]
                tp = None if base == 0 else (0, 64)
                emit_mm_group(P, bank[5][base:base + 64, 0:128], ("ps_av", h % 2), av_pairs,
                              reads=av_reads, tag="mm_av", tile_position=tp)
                if h % 2 == 1:
                    em.op("vector", lambda e, hp=hp, pa=pa: e.tensor_copy(yTb[:, pa, hp, :], bank[5][:, 0:128]),
                          reads=[("ps_av", 0), ("ps_av", 1)], writes=[("yTb", pa, hp)], tag="ev_av")
            for c in range(2):
                emit_mm_group(P, bank[6][:, :], BK(6),
                              [(yTb[:, pa, k, :], WoT[:, k, c * 512:(c + 1) * 512]) for k in range(ND)],
                              reads=[("yTb", pa, k) for k in range(ND)] + [("WoT", k) for k in range(ND)],
                              tag="mm_pr")
                em.op("vector", lambda e, c=c, pa=pa: e.tensor_tensor(xob[:, pa, c * 512:(c + 1) * 512],
                                                                      bank[6][:, :], xcb[:, pa, c * 512:(c + 1) * 512], ALU.add),
                      reads=[BK(6), ("x", pa)], writes=[("xo", pa, c)], tag="resid")
            em.dma("sync", xd2[t * 128:(t + 1) * 128, :], xob[:, pa, :],
                   reads=[("xo", pa, 0), ("xo", pa, 1)], writes=[("xd2", t)], tag="st2")
    em.barrier()

    with ExitStack() as s3:
        A = lambda n, sh, dt=F32: s3.enter_context(nc.sbuf_tensor(P.uniq(n), list(sh), dt))
        WoutT = A("WoutT", [128, 32, D], BF16)
        wgate_b = A("wgate_b", [128, 2, ND, 128], BF16)
        wup_b = A("wup_b", [128, 2, ND, 128], BF16)
        wffn = A("wffn", [128, ND])
        identc2 = A("identc2", [128, 128])
        xrc = A("xrc", [128, 2, 4, D])
        rmsc = {k: A("r3" + k, [128, 2, D if k in ("sq", "xn") else 1]) for k in
               ("sq", "s", "s2", "mu", "m2", "var", "lnv", "rstd", "xn")}
        rmsc["tag"] = "rmsC"
        xmTc = A("xmTc", [128, 2, ND, 512], BF16)
        gbufc = A("gbufc", [128, 2, 512])
        pTb = A("pTb", [128, 32, 512], BF16)
        for kf in range(32):
            em.dma_fn("gpsimd", lambda e, kf=kf: e.dma_start(out=WoutT[:, kf, :], in_=WoutT_in[kf * 128:(kf + 1) * 128, :]),
                      writes=[("WoutT",)], tag="ld_wo")
        em.dma("sync", wffn[:, :], wffn_in, writes=[("wffn",)])
        em.dma("sync", identc2[:, :], identf_in, writes=[("identc2",)], tag="ldi")
        for n in range(NCH):
            pa = n % 2
            for s_ in range(4):
                t = n * 4 + s_
                em.dma("sync", xrc[:, pa, s_, :], xd2[t * 128:(t + 1) * 128, :],
                       writes=[("xrc", pa, s_)], tag="ld_c")
                xk = emit_ln_tile(P, rmsc, pa, xrc[:, pa, s_, :], ("xrc", pa, s_), rms=True, eps=1e-6)
                xn = rmsc["xn"][:, pa, :]
                for k in range(ND):
                    em.op("tensor", lambda e, k=k, xn=xn: e.transpose(bank[6][:, 0:128], xn[:, k * 128:(k + 1) * 128], identc2[:, :]),
                          reads=[xk, ("identc2",)], writes=[BK(6)], tag="trc")
                    em.op("vector", lambda e, k=k, pa=pa, s_=s_: e.tensor_scalar(
                        xmTc[:, pa, k, s_ * 128:(s_ + 1) * 128], bank[6][:, 0:128], wffn[:, k:k + 1], None, ALU.mult),
                        reads=[BK(6), ("wffn",)], writes=[("xmTc", pa, k)], tag="evc")
            xmr = [("xmTc", pa, k) for k in range(ND)]
            for m in range(32):
                wpa = m % 2
                em.dma_fn("gpsimd", lambda e, m=m, wpa=wpa: e.dma_start(
                    out=wgate_b[:, wpa, :, :],
                    in_=WgateT_in.rearrange("(k p) c -> p k c", p=128)[:, :, m * 128:(m + 1) * 128]),
                    writes=[("wgate_b", wpa, k) for k in range(ND)], tag="ld_wg")
                em.dma_fn("gpsimd", lambda e, m=m, wpa=wpa: e.dma_start(
                    out=wup_b[:, wpa, :, :],
                    in_=WupT_in.rearrange("(k p) c -> p k c", p=128)[:, :, m * 128:(m + 1) * 128]),
                    writes=[("wup_b", wpa, k) for k in range(ND)], tag="ld_wu")
                bk = m % 2
                emit_mm_group(P, bank[bk][:, :], BK(bk),
                              [(wgate_b[:, wpa, k, :], xmTc[:, pa, k, :]) for k in range(ND)],
                              reads=xmr + [("wgate_b", wpa, k) for k in range(ND)], tag="mm_gate")
                em.op("scalar", lambda e, bk=bk, pa=pa: e.activation(gbufc[:, pa, :], bank[bk][:, :], AF.Silu),
                      reads=[BK(bk)], writes=[("gbufc", pa)], tag="ev_gate")
                emit_mm_group(P, bank[2 + bk][:, :], ("bank", 2 + bk),
                              [(wup_b[:, wpa, k, :], xmTc[:, pa, k, :]) for k in range(ND)],
                              reads=xmr + [("wup_b", wpa, k) for k in range(ND)], tag="mm_up")
                em.op("vector", lambda e, m=m, bk=bk, pa=pa: e.tensor_tensor(pTb[:, m, :], gbufc[:, pa, :],
                                                                             bank[2 + bk][:, :], ALU.mult),
                      reads=[("gbufc", pa), ("bank", 2 + bk)], writes=[("pTb", m)], tag="pmul")
            for sub in range(4):
                for dc in range(2):
                    bko = 4 + (sub % 2) * 2 + dc
                    emit_mm_group(P, bank[bko][:, :], ("bank", bko),
                                  [(pTb[:, kf, sub * 128:(sub + 1) * 128], WoutT[:, kf, dc * 512:(dc + 1) * 512])
                                   for kf in range(32)],
                                  reads=[("pTb", kf) for kf in range(32)] + [("WoutT",)], tag="mm_wout")
                    em.op("vector", lambda e, sub=sub, dc=dc, bko=bko, pa=pa: e.tensor_tensor(
                        xrc[:, pa, sub, dc * 512:(dc + 1) * 512], xrc[:, pa, sub, dc * 512:(dc + 1) * 512],
                        bank[bko][:, :], ALU.add),
                        reads=[("bank", bko), ("xrc", pa, sub)], writes=[("xrc", pa, sub)], tag="residc")
            for s_ in range(4):
                t = n * 4 + s_
                em.dma("sync", xd3[t * 128:(t + 1) * 128, :], xrc[:, pa, s_, :],
                       reads=[("xrc", pa, s_)], writes=[("xd3", t)], tag="st3")
    em.barrier()

    with ExitStack() as s4:
        A = lambda n, sh, dt=F32: s4.enter_context(nc.sbuf_tensor(P.uniq(n), list(sh), dt))
        xcd = A("xcd", [128, 2, D])
        x3cd = A("x3cd", [128, 2, D])
        xod = A("xod", [128, 2, D])
        for t in range(NT):
            pa = t % 2
            em.dma("sync", xcd[:, pa, :], x_in[t * 128:(t + 1) * 128, :], writes=[("x", pa)], tag="ld_x")
            em.dma("sync", x3cd[:, pa, :], xd3[t * 128:(t + 1) * 128, :], writes=[("x3", pa)], tag="ld_x3")
            em.op("vector", lambda e, pa=pa: e.tensor_tensor(xod[:, pa, :], xcd[:, pa, :], x3cd[:, pa, :], ALU.add),
                  reads=[("x", pa), ("x3", pa)], writes=[("xo", pa)], tag="resid")
            em.dma("sync", xo_out[t * 128:(t + 1) * 128, :], xod[:, pa, :],
                   reads=[("xo", pa)], writes=[("xo_st", t)], tag="st_x")
    em.barrier()
    return P


def run_l1attn(x_full, w_rms_mix, Wg, bg, Wv, bv, Wd, bd, w_rms_attn, Wqkv, Wo,
               w_rms_ffn, Wgate, Wup, Wout):
    P = build_l1attn()
    nc = P.close_build()
    pmj = lambda v: np.ascontiguousarray(v.reshape(ND, 128).T)
    r = np.arange(128)[:, None]
    j = np.arange(256)[None, :]
    mband = np.where((j >= r + 1) & (j <= 128 + r), 0.0, NEG).astype(np.float32)
    jj = np.arange(128)[None, :]
    mfirst = np.where(jj <= r, 0.0, NEG).astype(np.float32)
    xs = x_full.reshape(NCORES, T, D)
    base = {
        "wmixt": pmj(w_rms_mix), "WgT": np.ascontiguousarray(Wg.T), "bgt": pmj(bg),
        "WvT": np.ascontiguousarray(Wv.T), "bvt": pmj(bv),
        "WdT": np.ascontiguousarray(Wd.T), "bdt": pmj(bd),
        "wattnt": pmj(w_rms_attn), "WqkvT": np.ascontiguousarray(Wqkv.T),
        "WoT": np.ascontiguousarray(Wo.T), "wffnt": pmj(w_rms_ffn),
        "WgateT": np.ascontiguousarray(Wgate.T), "WupT": np.ascontiguousarray(Wup.T),
        "WoutT": np.ascontiguousarray(Wout.T),
        "identf": np.eye(128, dtype=np.float32), "identb": np.eye(128, dtype=BFNP),
        "mask_band": mband, "mask_first": mfirst,
    }
    in_maps = [dict(base, x=np.ascontiguousarray(xs[c])) for c in range(NCORES)]
    res = run_bass_kernel_spmd(nc, in_maps, core_ids=list(range(NCORES)))
    out = np.stack([res.results[c]["xo"] for c in range(NCORES)])
    return out.reshape(16, 2048, D)


L1FFN_DEPTHS = 6
L1FFN_PARTS = "af"
L1FFN_DEBUG = False


def build_l1ffn(P=None, V=None, x_in=None, xo_out=None, bank=None):
    DEPTH6 = 6
    INNER = 1024
    DI2 = 2730
    standalone = P is None
    if standalone:
        P = Prog("l1ffn")
        V = {"Wqkv6": P.inp("Wqkv", [DEPTH6, D, 3 * INNER]), "Wo6": P.inp("Wo", [DEPTH6, INNER, D]),
             "lng": P.inp("lng", [128, DEPTH6, ND]), "lnb": P.inp("lnb", [128, DEPTH6, ND]),
             "W1": P.inp("W1", [DEPTH6, D, 2 * DI2]), "b1a": P.inp("b1a", [128, DEPTH6, 22]),
             "b1g": P.inp("b1g", [128, DEPTH6, 22]), "W2": P.inp("W2", [DEPTH6, DI2, D]),
             "b26": P.inp("b2", [DEPTH6, D]), "qs": P.inp("qs", [DEPTH6, 64]),
             "ks": P.inp("ks", [DEPTH6, 64]),
             "tab_cs": P.inp("tab_cs", [128, 16, 64], F32),
             "tab_sslo": P.inp("tab_sslo", [128, 16, 32], F32),
             "tab_sshi": P.inp("tab_sshi", [128, 16, 32], F32),
             "tab_ci": P.inp("tab_ci", [128, 16, 64], F32),
             "tab_silo": P.inp("tab_silo", [128, 16, 32], F32),
             "tab_sihi": P.inp("tab_sihi", [128, 16, 32], F32),
             "identf": P.inp("identf", [128, 128], F32), "identb": P.inp("identb", [128, 128], BF16),
             "m_full": P.inp("m_full", [128, 384], F32), "m_1": P.inp("m_1", [128, 384], F32),
             "m_0": P.inp("m_0", [128, 384], F32), "ones128": P.inp("ones128", [128, 1], F32),
             "ones1": P.inp("ones1", [1, 128], F32)}
        x_in = P.inp("x", [T, D])
        xo_out = P.out("xo", [T, D])
    em = P.em
    nc = P.nc
    Wqkv_in = V["Wqkv6"]; Wo_in = V["Wo6"]; lng_in = V["lng"]; lnb_in = V["lnb"]
    W1_in = V["W1"]; b1a_in = V["b1a"]; b1g_in = V["b1g"]; W2_in = V["W2"]; b2_in = V["b26"]
    qs_in = V["qs"]; ks_in = V["ks"]
    tab_cs_in = V["tab_cs"]; tab_sslo_in = V["tab_sslo"]; tab_sshi_in = V["tab_sshi"]
    tab_ci_in = V["tab_ci"]; tab_silo_in = V["tab_silo"]; tab_sihi_in = V["tab_sihi"]
    identf_in = V["identf"]; identb_in = V["identb"]
    m_full_in = V["m_full"]; m_1_in = V["m_1"]; m_0_in = V["m_0"]
    ones128_in = V["ones128"]; ones1_in = V["ones1"]

    xd = [P.dram(f"xd{i}", [T, D]) for i in range(2)]
    dbg_mu = P.out("dbg_mu", [1, 1024]) if L1FFN_DEBUG else None
    dbg_rs = P.out("dbg_rs", [1, 1024]) if L1FFN_DEBUG else None
    dbg_xnf = P.out("dbg_xnf", [128, ND, 1024], BF16) if L1FFN_DEBUG else None
    dbg_pb = P.out("dbg_pb", [128, 22, 1024], BF16) if L1FFN_DEBUG else None

    if bank is None:
        bank = [P.psum(f"bank{i}", [128, 512], F32) for i in range(8)]
    BK = lambda i: ("bank", i)

    inst_ctx = ExitStack()
    _sb = lambda n, sh, dt=F32: inst_ctx.enter_context(nc.sbuf_tensor(P.uniq(n), list(sh), dt))
    ident = _sb("ident", [128, 128], F32)
    identb = _sb("identb", [128, 128], BF16)
    tab_cs = _sb("tab_cs", [128, 16, 64], F32)
    tab_sslo = _sb("tab_sslo", [128, 16, 32], F32)
    tab_sshi = _sb("tab_sshi", [128, 16, 32], F32)
    tab_ci = _sb("tab_ci", [128, 16, 64], F32)
    tab_silo = _sb("tab_silo", [128, 16, 32], F32)
    tab_sihi = _sb("tab_sihi", [128, 16, 32], F32)
    m_full = _sb("m_full", [128, 384], F32)
    m_1 = _sb("m_1", [128, 384], F32)
    m_0 = _sb("m_0", [128, 384], F32)
    ones128 = _sb("ones128", [128, 1], F32)
    ones128b = _sb("ones128b", [128, 1], BF16)
    ones1 = _sb("ones1", [1, 128], F32)
    em.dma_fn("gpsimd", lambda e: e.dma_start(out=ones128b[:, :], in_=ones128_in),
              writes=[("o128b",)], tag="ldcb")
    for ap_s, ap_i, key in ((ident, identf_in, "ident"), (identb, identb_in, "identb"),
                            (tab_cs, tab_cs_in, "tcs"), (tab_sslo, tab_sslo_in, "tsl"),
                            (tab_sshi, tab_sshi_in, "tsh"), (tab_ci, tab_ci_in, "tci"),
                            (tab_silo, tab_silo_in, "til"), (tab_sihi, tab_sihi_in, "tih"),
                            (m_full, m_full_in, "mf"), (m_1, m_1_in, "m1m"), (m_0, m_0_in, "m0m"),
                            (ones128, ones128_in, "o128"), (ones1, ones1_in, "o1")):
        em.dma("sync", ap_s[:, :] if ap_s.shape.__len__() == 2 else ap_s[:, :, :], ap_i,
               writes=[(key,)], tag="ldc")

    with ExitStack() as s0:
        A = lambda n, sh, dt=F32: s0.enter_context(nc.sbuf_tensor(P.uniq(n), list(sh), dt))
        xc0 = A("xc0", [128, 2, D])
        lns0 = {k: A("f0" + k, [128, 2, D if k in ("sq", "xn") else 1]) for k in
                ("sq", "s", "s2", "mu", "m2", "var", "lnv", "rstd", "xn")}
        lns0["tag"] = "lnF"
        for t in range(NT):
            pa = t % 2
            em.dma("sync", xc0[:, pa, :], x_in[t * 128:(t + 1) * 128, :], writes=[("x0", pa)], tag="ld")
            xk = emit_ln_tile(P, lns0, pa, xc0[:, pa, :], ("x0", pa))
            em.dma("sync", xd[0][t * 128:(t + 1) * 128, :], lns0["xn"][:, pa, :], reads=[xk],
                   writes=[("xd0", t)], tag="st")
    em.barrier()

    def emit_attn_depth(i, src, dst):
        with ExitStack() as sa:
            A = lambda n, sh, dt=F32: sa.enter_context(nc.sbuf_tensor(P.uniq(n), list(sh), dt))
            wq = A("wq", [128, ND, 3 * INNER], BF16)
            wo = A("wo", [128, ND, D], BF16)
            qsb = A("qsb", [1, 64])
            ksb = A("ksb", [1, 64])
            qkb2 = A("qkb2", [128, 2, 64])
            Aq = A("Aq", [128, 16, 64])
            Bql = A("Bql", [128, 16, 32])
            Bqh = A("Bqh", [128, 16, 32])
            Ak = A("Ak", [128, 16, 64])
            Bkl = A("Bkl", [128, 16, 32])
            Bkh = A("Bkh", [128, 16, 32])
            xat = A("xat", [128, 2, D])
            xT = A("xT", [128, 2, ND, 128], BF16)
            xsT = A("xsT", [128, 2, 4, 128], BF16)
            qb = A("qb", [128, 2, INNER])
            kb = A("kb", [128, 2, INNER])
            sqq = A("sqq", [128, 2, INNER], BF16)
            s2q = A("s2q", [128, 2, 16])
            s2k = A("s2k", [128, 2, 16])
            rsq = A("rsq", [128, 2, 16])
            rsk = A("rsk", [128, 2, 16])
            tmpq = A("tmpq", [128, 2, INNER])
            tmpv = A("tmpv", [128, 2, INNER])
            q2 = A("q2", [128, 2, INNER], BF16)
            k2 = A("k2", [128, 2, INNER], BF16)
            qT = A("qT", [128, 2, ND, 128], BF16)
            kT = A("kT", [128, ND, 6 * 128], BF16)
            v_roll = A("v_roll", [128, 3, INNER], BF16)
            maskb = A("maskb", [128, 3, 384], BF16)
            probs_bf = A("probs_bf", [128, 4, 384], BF16)
            probs_bf2 = A("probs_bf2", [128, 4, 384], BF16)
            pT = A("pT", [128, 4, 3, 128], BF16)
            nm = A("nm", [128, 4, 1])
            sume = A("sume", [128, 4, 1])
            rec = A("rec", [128, 4, 1])
            yT = A("yT", [128, 2, ND, 128], BF16)
            xob = A("xob", [128, 2, D])
            em.dma_fn("gpsimd", lambda e: e.dma_start(
                out=wq[:, :, :], in_=Wqkv_in[i].rearrange("(k p) c -> p k c", p=128)),
                writes=[("wq", k) for k in range(ND)], tag="ldw")
            em.dma_fn("gpsimd", lambda e: e.dma_start(
                out=wo[:, :, :], in_=Wo_in[i].rearrange("(k p) c -> p k c", p=128)),
                writes=[("wo", k) for k in range(ND)], tag="ldw")
            for mi, m_in in enumerate((m_0_in, m_1_in, m_full_in)):
                em.dma_fn("gpsimd", lambda e, mi=mi, m_in=m_in: e.dma_start(out=maskb[:, mi, :], in_=m_in),
                          writes=[("maskb",)], tag="ldmb")
            em.dma("sync", qsb[:, :], qs_in[i:i + 1, :], writes=[("qsb",)], tag="ldq")
            em.dma("sync", ksb[:, :], ks_in[i:i + 1, :], writes=[("ksb",)], tag="ldk")
            emit_mm_group(P, bank[0][:, 0:64], BK(0), [(ones1[:, :], qsb[:, :])],
                          reads=[("o1",), ("qsb",)], tag="mm_bq")
            em.op("vector", lambda e: e.tensor_copy(qkb2[:, 0, :], bank[0][:, 0:64]),
                  reads=[BK(0)], writes=[("qs_b",)], tag="ev_bq")
            emit_mm_group(P, bank[0][:, 0:64], BK(0), [(ones1[:, :], ksb[:, :])],
                          reads=[("o1",), ("ksb",)], tag="mm_bk")
            em.op("vector", lambda e: e.tensor_copy(qkb2[:, 1, :], bank[0][:, 0:64]),
                  reads=[BK(0)], writes=[("ks_b",)], tag="ev_bk")
            qs_b = qkb2[:, 0, :]
            ks_b = qkb2[:, 1, :]
            em.op("vector", lambda e: e.scalar_tensor_tensor(
                Aq[:, :, :], tab_cs[:, :, :], 8.0,
                qs_b.unsqueeze(1).broadcast_to([128, 16, 64]), ALU.mult, ALU.mult),
                reads=[("tcs",), ("qs_b",)], writes=[("Aq",)], tag="tAq")
            em.op("vector", lambda e: e.scalar_tensor_tensor(
                Bql[:, :, :], tab_sslo[:, :, :], 8.0,
                qs_b[:, 32:64].unsqueeze(1).broadcast_to([128, 16, 32]), ALU.mult, ALU.mult),
                reads=[("tsl",), ("qs_b",)], writes=[("Bql",)], tag="tBql")
            em.op("vector", lambda e: e.scalar_tensor_tensor(
                Bqh[:, :, :], tab_sshi[:, :, :], 8.0,
                qs_b[:, 0:32].unsqueeze(1).broadcast_to([128, 16, 32]), ALU.mult, ALU.mult),
                reads=[("tsh",), ("qs_b",)], writes=[("Bqh",)], tag="tBqh")
            em.op("vector", lambda e: e.tensor_tensor(
                Ak[:, :, :], tab_ci[:, :, :], ks_b.unsqueeze(1).broadcast_to([128, 16, 64]), ALU.mult),
                reads=[("tci",), ("ks_b",)], writes=[("Ak",)], tag="tAk")
            em.op("vector", lambda e: e.tensor_tensor(
                Bkl[:, :, :], tab_silo[:, :, :], ks_b[:, 32:64].unsqueeze(1).broadcast_to([128, 16, 32]), ALU.mult),
                reads=[("til",), ("ks_b",)], writes=[("Bkl",)], tag="tBkl")
            em.op("vector", lambda e: e.tensor_tensor(
                Bkh[:, :, :], tab_sihi[:, :, :], ks_b[:, 0:32].unsqueeze(1).broadcast_to([128, 16, 32]), ALU.mult),
                reads=[("tih",), ("ks_b",)], writes=[("Bkh",)], tag="tBkh")

            wqr = [("wq", k) for k in range(ND)]
            for t in range(NT):
                pa = t % 2
                t_loc = t % NT_SEQ
                r3 = t % 3
                em.dma("sync", xat[:, pa, :], src[t * 128:(t + 1) * 128, :], writes=[("xat", pa)], tag="ldx")
                for k in range(ND):
                    em.op("tensor", lambda e, k=k, pa=pa: e.transpose(
                        bank[6][:, 0:128], xat[:, pa, k * 128:(k + 1) * 128], ident[:, :]),
                        reads=[("xat", pa), ("ident",)], writes=[BK(6)], tag="trx")
                    em.op("vector", lambda e, k=k, pa=pa: e.tensor_copy(xT[:, pa, k, :], bank[6][:, 0:128]),
                          reads=[BK(6)], writes=[("xT", pa, k)], tag="evx")
                for k in range(4, ND):
                    if t_loc == 0:
                        em.op("gpsimd", lambda e, k=k, pa=pa: e.memset(xsT[:, pa, k - 4, 0:1], 0.0),
                              reads=[], writes=[("xsT", pa, k)], tag="z_sft")
                    else:
                        em.op("scalar", lambda e, k=k, pa=pa: e.copy(xsT[:, pa, k - 4, 0:1], xT[:, 1 - pa, k, 127:128]),
                              reads=[("xT", 1 - pa, k)], writes=[("xsT", pa, k)], tag="c_sft0")
                    em.op("scalar", lambda e, k=k, pa=pa: e.copy(xsT[:, pa, k - 4, 1:128], xT[:, pa, k, 0:127]),
                          reads=[("xT", pa, k)], writes=[("xsT", pa, k)], deps=(), tag="c_sft")

                def rhs_k(k, pa=pa):
                    return xT[:, pa, k, :] if k < 4 else xsT[:, pa, k - 4, :]
                xs_reads = [("xT", pa, k) for k in range(4)] + [("xsT", pa, k) for k in range(4, ND)]
                for (dst_b, key, col0) in ((qb, "qb", 0), (kb, "kb", INNER)):
                    for c in range(2):
                        emit_mm_group(P, bank[c][:, :], BK(c),
                                      [(rhs_k(k), wq[:, k, col0 + c * 512:col0 + (c + 1) * 512]) for k in range(ND)],
                                      reads=xs_reads + wqr, tag="mm_qk")
                    em.op("vector", lambda e, dst_b=dst_b, pa=pa: e.tensor_copy(dst_b[:, pa, 0:512], bank[0][:, :]),
                          reads=[BK(0)], writes=[(key, pa)], tag="ev_qk")
                    em.op("vector", lambda e, dst_b=dst_b, pa=pa: e.tensor_copy(dst_b[:, pa, 512:1024], bank[1][:, :]),
                          reads=[BK(1)], writes=[(key, pa)], deps=(), tag="ev_qk2")
                for c in range(2):
                    emit_mm_group(P, bank[2][:, :], BK(2),
                                  [(rhs_k(k), wq[:, k, 2 * INNER + c * 512:2 * INNER + (c + 1) * 512]) for k in range(ND)],
                                  reads=xs_reads + wqr, tag="mm_v")
                    em.op("vector", lambda e, c=c, r3=r3: e.tensor_copy(v_roll[:, r3, c * 512:(c + 1) * 512], bank[2][:, :]),
                          reads=[BK(2)], writes=[("v", r3, c)], tag="ev_v")
                for (buf, key, s2b, rsb, Atab, Bltab, Bhtab, outb, okey) in (
                        (qb, "qb", s2q, rsq, Aq, Bql, Bqh, q2, "q2"),
                        (kb, "kb", s2k, rsk, Ak, Bkl, Bkh, k2, "k2")):
                    em.op("scalar", lambda e, buf=buf, pa=pa, s2b=s2b: e.activation(
                        sqq[:, pa, :], buf[:, pa, :], AF.Square),
                        reads=[(key, pa)], writes=[("sqq", pa)], tag="rsq")
                    em.op("vector", lambda e, pa=pa, s2b=s2b: e.reduce_sum(
                        s2b[:, pa, :], sqq.ap()[:, pa, :].rearrange("p (h d) -> p h d", d=64), axis=AX),
                        reads=[("sqq", pa)], writes=[("s2", key, pa)], tag="rs2")
                    em.op("scalar", lambda e, pa=pa, s2b=s2b, rsb=rsb: e.activation(rsb[:, pa, :], s2b[:, pa, :], AF.Ln),
                          reads=[("s2", key, pa)], writes=[("lnv", key, pa)], tag="rln")
                    em.op("scalar", lambda e, pa=pa, rsb=rsb: e.activation(rsb[:, pa, :], rsb[:, pa, :], AF.Exp, scale=0.5),
                          reads=[("lnv", key, pa)], writes=[("st", key, pa)], tag="rexp")
                    em.op("vector", lambda e, pa=pa, rsb=rsb: e.tensor_scalar(rsb[:, pa, :], rsb[:, pa, :], 1e-12, None, ALU.max),
                          reads=[("st", key, pa)], writes=[("mx", key, pa)], tag="rmax")
                    em.op("vector", lambda e, pa=pa, rsb=rsb: e.reciprocal(rsb[:, pa, :], rsb[:, pa, :]),
                          reads=[("mx", key, pa)], writes=[("rs", key, pa)], tag="rrec")
                    b3 = lambda ap: ap
                    em.op("vector", lambda e, buf=buf, pa=pa, Atab=Atab, t_loc=t_loc: e.tensor_tensor(
                        tmpq[:, pa, :].rearrange("p (h d) -> p h d", d=64),
                        buf[:, pa, :].rearrange("p (h d) -> p h d", d=64),
                        Atab[:, t_loc:t_loc + 1, :].broadcast_to([128, 16, 64]), ALU.mult),
                        reads=[(key, pa), ("Aq",), ("Ak",)], writes=[("tmpq", pa)], tag="rotA")
                    em.op("vector", lambda e, buf=buf, pa=pa, Bltab=Bltab, t_loc=t_loc: e.tensor_tensor(
                        tmpv[:, pa, :].rearrange("p (h d) -> p h d", d=64)[:, :, 0:32],
                        buf[:, pa, :].rearrange("p (h d) -> p h d", d=64)[:, :, 32:64],
                        Bltab[:, t_loc:t_loc + 1, :].broadcast_to([128, 16, 32]), ALU.mult),
                        reads=[(key, pa), ("Bql",), ("Bkl",)], writes=[("tmpv", pa)], tag="rotBl")
                    em.op("vector", lambda e, buf=buf, pa=pa, Bhtab=Bhtab, t_loc=t_loc: e.tensor_tensor(
                        tmpv[:, pa, :].rearrange("p (h d) -> p h d", d=64)[:, :, 32:64],
                        buf[:, pa, :].rearrange("p (h d) -> p h d", d=64)[:, :, 0:32],
                        Bhtab[:, t_loc:t_loc + 1, :].broadcast_to([128, 16, 32]), ALU.mult),
                        reads=[(key, pa), ("Bqh",), ("Bkh",)], writes=[("tmpv", pa)], deps=(), tag="rotBh")
                    em.op("vector", lambda e, pa=pa: e.tensor_tensor(tmpq[:, pa, :], tmpq[:, pa, :], tmpv[:, pa, :], ALU.add),
                          reads=[("tmpq", pa), ("tmpv", pa)], writes=[("tmpq", pa)], tag="rotsum")
                    em.op("vector", lambda e, pa=pa, rsb=rsb, outb=outb: e.tensor_tensor(
                        outb[:, pa, :].rearrange("p (h d) -> p h d", d=64),
                        tmpq[:, pa, :].rearrange("p (h d) -> p h d", d=64),
                        rsb[:, pa, :].unsqueeze(2).broadcast_to([128, 16, 64]), ALU.mult),
                        reads=[("tmpq", pa), ("rs", key, pa)], writes=[(okey, pa)], tag="rotout")
                for k in range(ND):
                    em.op("tensor", lambda e, k=k, pa=pa: e.transpose(
                        bank[6][:, :].bitcast(BF16)[:, 0:128], q2[:, pa, k * 128:(k + 1) * 128], identb[:, :]),
                        reads=[("q2", pa), ("identb",)], writes=[BK(6)], tag="trq")
                    em.op("scalar", lambda e, k=k, pa=pa: e.copy(qT[:, pa, k, :], bank[6][:, :].bitcast(BF16)[:, 0:128]),
                          reads=[BK(6)], writes=[("qT", pa, k)], tag="evq")
                    em.op("tensor", lambda e, k=k, pa=pa: e.transpose(
                        bank[6][:, :].bitcast(BF16)[:, 0:128], k2[:, pa, k * 128:(k + 1) * 128], identb[:, :]),
                        reads=[("k2", pa), ("identb",)], writes=[BK(6)], tag="trk")
                    em.op("scalar", lambda e, k=k, kr=t % 6: e.copy(kT[:, k, kr * 128:(kr + 1) * 128],
                                                                    bank[6][:, :].bitcast(BF16)[:, 0:128]),
                          reads=[BK(6)], writes=[("kT", t % 6, k)], tag="evk")
                kr8 = t % 6
                mi = min(t_loc, 2)
                for h in range(16):
                    hp = h // 2
                    base = (h % 2) * 64
                    hb = h % 4
                    scb = 3 if h % 2 == 0 else 7
                    q_ap = qT[base:base + 64, pa, hp, :]

                    def fn_sc(e, q_ap=q_ap, hp=hp, base=base, t_loc=t_loc, kr8=kr8, mi=mi, scb=scb):
                        bb = bank[scb][:, :].bitcast(BF16)
                        e.matmul(bank[scb][:, 0:384], identb[:, :], maskb[:, mi, :], start=True, stop=False)
                        if t_loc >= 2 and kr8 >= 2:
                            return e.matmul(bank[scb][:, 0:384], q_ap,
                                            kT[base:base + 64, hp, (kr8 - 2) * 128:(kr8 + 1) * 128],
                                            start=False, stop=True)
                        inst = None
                        for bi in range(3):
                            tt = t_loc - 2 + bi
                            if tt < 0:
                                continue
                            kk = (kr8 - 2 + bi) % 6
                            inst = e.matmul(bank[scb][:, bi * 128:(bi + 1) * 128], q_ap,
                                            kT[base:base + 64, hp, kk * 128:(kk + 1) * 128], start=False, stop=True)
                        return inst
                    sc_reads = [("qT", pa, hp), ("maskb",), ("identb",)] +                                [("kT", (kr8 - 2 + bi) % 6, hp) for bi in range(3)]
                    em.op("tensor", fn_sc, reads=sc_reads, writes=[BK(scb)], tag="mm_sc")
                    em.op("vector", lambda e, hb=hb, scb=scb: e.reduce_max(nm[:, hb, :], bank[scb][:, 0:384], axis=AX, negate=True),
                          reads=[BK(scb)], writes=[("nm", hb)], tag="negmax")
                    em.op("scalar", lambda e, hb=hb, scb=scb: e.activation(probs_bf[:, hb, 0:384], bank[scb][:, 0:384], AF.Exp,
                                                                           bias=nm[:, hb, :], accum_out=sume[:, hb, :]),
                          reads=[BK(scb), ("nm", hb)], writes=[("probs_bf", hb), ("sume", hb)], tag="exp")
                    em.op("vector", lambda e, hb=hb: e.reciprocal(rec[:, hb, :], sume[:, hb, :]),
                          reads=[("sume", hb)], writes=[("rec", hb)], tag="recip")
                    em.op("vector", lambda e, hb=hb: e.tensor_scalar(probs_bf2[:, hb, 0:384], probs_bf[:, hb, 0:384],
                                                                     rec[:, hb, :], None, ALU.mult),
                          reads=[("probs_bf", hb), ("rec", hb)], writes=[("probs_bf2", hb)], tag="pscale")

                    def fn_trp(e, hb=hb, h2=h % 2):
                        b4 = bank[4][:, :].bitcast(BF16)
                        inst = None
                        for c in range(3):
                            inst = e.transpose(b4[:, (h2 * 3 + c) * 128:(h2 * 3 + c + 1) * 128],
                                               probs_bf2[:, hb, c * 128:(c + 1) * 128], identb[:, :])
                        return inst
                    em.op("tensor", fn_trp, reads=[("probs_bf2", hb), ("identb",)], writes=[BK(4)], tag="tr_p")
                    em.op("scalar", lambda e, hb=hb, h2=h % 2: e.copy(pT.ap()[:, hb, :, :].opt(),
                                                            bank[4][:, :].bitcast(BF16)[:, h2 * 384:h2 * 384 + 384]),
                          reads=[BK(4)], writes=[("pT", hb)], tag="ev_p")
                    av_pairs = []
                    av_reads = [("pT", hb)]
                    for bi in range(3):
                        tt = t_loc - 2 + bi
                        if tt < 0:
                            continue
                        rr = (r3 + 1 + bi) % 3
                        av_pairs.append((v_roll[:, rr, h * 64:(h + 1) * 64], pT[:, hb, bi, :]))
                        av_reads += [("v", rr, 0), ("v", rr, 1)]
                    tp = None if base == 0 else (0, 64)
                    emit_mm_group(P, bank[5][base:base + 64, 0:128], ("ps_av", h % 2), av_pairs,
                                  reads=av_reads, tag="mm_av", tile_position=tp)
                    if h % 2 == 1:
                        em.op("vector", lambda e, hp=hp, pa=pa: e.tensor_copy(yT[:, pa, hp, :], bank[5][:, 0:128]),
                              reads=[("ps_av", 0), ("ps_av", 1)], writes=[("yT", pa, hp)], tag="ev_av")
                for c in range(2):
                    emit_mm_group(P, bank[6][:, :], BK(6),
                                  [(yT[:, pa, k, :], wo[:, k, c * 512:(c + 1) * 512]) for k in range(ND)],
                                  reads=[("yT", pa, k) for k in range(ND)] + [("wo", k) for k in range(ND)],
                                  tag="mm_pr")
                    em.op("vector", lambda e, c=c, pa=pa: e.tensor_tensor(
                        xob[:, pa, c * 512:(c + 1) * 512], bank[6][:, :], xat[:, pa, c * 512:(c + 1) * 512], ALU.add),
                        reads=[BK(6), ("xat", pa)], writes=[("xo", pa, c)], tag="resid")
                em.dma("sync", dst[t * 128:(t + 1) * 128, :], xob[:, pa, :],
                       reads=[("xo", pa, 0), ("xo", pa, 1)], writes=[("dst", t)], tag="stx")
        em.barrier()

    def emit_ffn_depth(i, src, dst):
        NCHF = 4
        CHF = 1024
        NTC = CHF // 128
        with ExitStack() as sf:
            A = lambda n, sh, dt=F32: sf.enter_context(nc.sbuf_tensor(P.uniq(n), list(sh), dt))
            w2r_sb = A("w2r", [128, 22, D], BF16)
            w1a_b = A("w1a", [128, 2, ND, 128], BF16)
            w1g_b = A("w1g", [128, 2, ND, 128], BF16)
            b2sb = A("b2sb", [1, D])
            lngs = A("lngs", [128, ND])
            lnbs = A("lnbs", [128, ND])
            b1as = A("b1as", [128, 22])
            b1gs = A("b1gs", [128, 22])
            xTf = A("xTf", [128, 2, ND, CHF + 1], BF16)
            sqk = A("sqk", [128, 1, CHF + 1], BF16)
            mu_f = A("mu_f", [1, 1, CHF])
            vr_f = A("vr_f", [1, 1, CHF])
            lb_f = A("lb_f", [1, 1, CHF])
            rs_f = A("rs_f", [1, 1, CHF])
            xnf = A("xnf", [128, 1, ND, CHF], BF16)
            asb = A("asb", [128, 1, 512])
            Pb = A("Pb", [128, 22, CHF], BF16)
            xw = A("xw", [128, 2, D])
            xobf = A("xobf", [128, 2, D])
            for kf in range(22):
                kk0 = kf * 128
                kk1 = min(DI2, kk0 + 128)
                em.dma_fn("gpsimd", lambda e, kf=kf, kk0=kk0, kk1=kk1: e.dma_start(
                    out=w2r_sb[0:kk1 - kk0, kf, :], in_=W2_in[i, kk0:kk1, :]),
                    writes=[("w2r",)], tag="ldw2")
            em.dma("sync", b2sb[:, :], b2_in[i:i + 1, :], writes=[("b2s",)], tag="ldb2")
            em.dma("sync", lngs[:, :], lng_in[:, i, :], writes=[("lngs",)], tag="ldg")
            em.dma("sync", lnbs[:, :], lnb_in[:, i, :], writes=[("lnbs",)], tag="ldb")
            em.dma("sync", b1as[:, :], b1a_in[:, i, :], writes=[("b1as",)], tag="ldb1a")
            em.dma("sync", b1gs[:, :], b1g_in[:, i, :], writes=[("b1gs",)], tag="ldb1g")
            for n in range(NCHF):
                pa = n % 2
                seq_start = (n % 2 == 0)
                for s_ in range(NTC):
                    t = n * NTC + s_
                    em.dma("sync", xw[:, pa, :], src[t * 128:(t + 1) * 128, :], writes=[("xw", pa)], tag="ldx")
                    for k in range(ND):
                        em.op("tensor", lambda e, k=k, pa=pa: e.transpose(
                            bank[6][:, 0:128], xw[:, pa, k * 128:(k + 1) * 128], ident[:, :]),
                            reads=[("xw", pa), ("ident",)], writes=[BK(6)], tag="trf")
                        em.op("vector", lambda e, k=k, pa=pa, s_=s_: e.tensor_copy(
                            xTf[:, pa, k, 1 + s_ * 128:1 + (s_ + 1) * 128], bank[6][:, 0:128]),
                            reads=[BK(6)], writes=[("xTf", pa, k)], tag="evf")
                for k in range(ND):
                    if seq_start:
                        em.op("gpsimd", lambda e, k=k, pa=pa: e.memset(xTf[:, pa, k, 0:1], 0.0),
                              writes=[("xTf", pa, k)], deps=(), tag="zb")
                    else:
                        em.op("scalar", lambda e, k=k, pa=pa: e.copy(xTf[:, pa, k, 0:1], xTf[:, 1 - pa, k, CHF:CHF + 1]),
                              reads=[("xTf", 1 - pa, k)], writes=[("xTf", pa, k)], tag="cb")
                xr_reads = [("xTf", pa, k) for k in range(ND)]
                def sl(k, hcol):
                    a = 1 if k < 4 else 0
                    return a + hcol * 512
                for hcol in range(2):
                    def fn_sum(e, pa=pa, hcol=hcol):
                        inst = None
                        for k in range(ND):
                            a = sl(k, hcol)
                            inst = e.matmul(bank[hcol][0:1, :], ones128b[:, :], xTf[:, pa, k, a:a + 512],
                                            start=(k == 0), stop=(k == ND - 1))
                        return inst
                    em.op("tensor", fn_sum, reads=xr_reads + [("o128b",)], writes=[BK(hcol)], tag="mm_s")
                    em.op("vector", lambda e, pa=pa, hcol=hcol: e.tensor_scalar(
                        mu_f[:, 0, hcol * 512:(hcol + 1) * 512], bank[hcol][0:1, :], 1.0 / D, None, ALU.mult),
                        reads=[BK(hcol)], writes=[("muk",)], tag="mu")
                sq_reads = []
                for k in range(ND):
                    em.op("scalar", lambda e, k=k, pa=pa: e.activation(sqk[:, 0, :], xTf[:, pa, k, :], AF.Square),
                          reads=[("xTf", pa, k)], writes=[("sqkk",)], tag="sq")
                    sq_reads.append(("sqkk",))
                    for hcol in range(2):
                        a = sl(k, hcol)
                        em.op("tensor", lambda e, k=k, pa=pa, a=a, hcol=hcol: e.matmul(
                            bank[2 + hcol][0:1, :], ones128b[:, :], sqk[:, 0, a:a + 512],
                            start=(k == 0), stop=(k == ND - 1)),
                            reads=[("sqkk",), ("o128b",)], writes=[BK(2 + hcol)], tag="mm_s2")
                for hcol in range(2):
                    hs = slice(hcol * 512, (hcol + 1) * 512)
                    em.op("vector", lambda e, hs=hs: e.tensor_mul(lb_f[:, 0, hs], mu_f[:, 0, hs], mu_f[:, 0, hs]),
                          reads=[("muk",)], writes=[("m2a", hcol)], tag="m2a")
                    em.op("vector", lambda e, hs=hs: e.tensor_scalar(vr_f[:, 0, hs], lb_f[:, 0, hs], -1e-5, None, ALU.add),
                          reads=[("m2a", hcol)], writes=[("m2k", hcol)], tag="m2b")
                    em.op("vector", lambda e, hs=hs, hcol=hcol: e.scalar_tensor_tensor(
                        lb_f[:, 0, hs], bank[2 + hcol][0:1, :], 1.0 / D, vr_f[:, 0, hs], ALU.mult, ALU.subtract),
                        reads=[BK(2 + hcol), ("m2k", hcol)], writes=[("vark", hcol)], tag="var")
                    em.op("scalar", lambda e, hs=hs: e.activation(vr_f[:, 0, hs], lb_f[:, 0, hs], AF.Ln),
                          reads=[("vark", hcol)], writes=[("lnvk", hcol)], tag="lnl")
                    em.op("scalar", lambda e, hs=hs: e.activation(rs_f[:, 0, hs], vr_f[:, 0, hs], AF.Exp, scale=-0.5),
                          reads=[("lnvk", hcol)], writes=[("rstdk", hcol)], tag="expl")
                emit_mm_group(P, bank[2][:, :], BK(2), [(ones1[:, :], mu_f[:, 0, 0:512])],
                              reads=[("o1",), ("muk",)], tag="mm_bmu")
                emit_mm_group(P, bank[3][:, :], BK(3), [(ones1[:, :], mu_f[:, 0, 512:1024])],
                              reads=[("o1",), ("muk",)], tag="mm_bmu2")
                emit_mm_group(P, bank[4][:, :], BK(4), [(ones1[:, :], rs_f[:, 0, 0:512])],
                              reads=[("o1",), ("rstdk", 0)], tag="mm_br")
                emit_mm_group(P, bank[5][:, :], BK(5), [(ones1[:, :], rs_f[:, 0, 512:1024])],
                              reads=[("o1",), ("rstdk", 1)], tag="mm_br2")
                for k in range(ND):
                    for hcol in range(2):
                        a = sl(k, hcol)
                        bmu = bank[2] if hcol == 0 else bank[3]
                        brs = bank[4] if hcol == 0 else bank[5]
                        em.op("vector", lambda e, k=k, pa=pa, a=a, hcol=hcol, bmu=bmu: e.tensor_tensor(
                            xnf[:, 0, k, hcol * 512:(hcol + 1) * 512],
                            xTf[:, pa, k, a:a + 512], bmu[:, :], ALU.subtract),
                            reads=[("xTf", pa, k), BK(2 if hcol == 0 else 3)], writes=[("xnfk", k)], tag="nsub")
                        em.op("vector", lambda e, k=k, pa=pa, hcol=hcol, brs=brs: e.tensor_tensor(
                            xnf[:, 0, k, hcol * 512:(hcol + 1) * 512],
                            xnf[:, 0, k, hcol * 512:(hcol + 1) * 512], brs[:, :], ALU.mult),
                            reads=[("xnfk", k), BK(4 if hcol == 0 else 5)], writes=[("xnfk", k)], tag="nmul")
                    em.op("vector", lambda e, k=k, pa=pa: e.tensor_scalar(
                        xnf[:, 0, k, :], xnf[:, 0, k, :], lngs[:, k:k + 1], lnbs[:, k:k + 1], ALU.mult, ALU.add),
                        reads=[("xnfk", k), ("lngs",), ("lnbs",)], writes=[("xnfk", k)], tag="ngb")
                xn_reads = [("xnfk", k) for k in range(ND)]
                if L1FFN_DEBUG and i == 0 and n == 0:
                    em.dma("gpsimd", dbg_mu, mu_f[:, 0, :], reads=[("muk",)], tag="dbgmu")
                    em.dma("gpsimd", dbg_rs, rs_f[:, 0, :], reads=[("rstdk", 0), ("rstdk", 1)], tag="dbgrs")
                    for k in range(ND):
                        em.dma("gpsimd", dbg_xnf[:, k, :], xnf[:, 0, k, :], reads=[("xnfk", k)], tag="dbgxn")
                for m in range(22):
                    wpa = m % 2
                    c0 = m * 128
                    c1 = min(DI2, c0 + 128)
                    w = c1 - c0
                    em.dma_fn("gpsimd", lambda e, c0=c0, c1=c1, wpa=wpa: e.dma_start(
                        out=w1a_b[:, wpa, :, 0:c1 - c0],
                        in_=W1_in[i].rearrange("(k p) c -> p k c", p=128)[:, :, c0:c1]),
                        writes=[("w1a", wpa, k) for k in range(ND)], tag="ldw1a")
                    em.dma_fn("gpsimd", lambda e, c0=c0, c1=c1, wpa=wpa: e.dma_start(
                        out=w1g_b[:, wpa, :, 0:c1 - c0],
                        in_=W1_in[i].rearrange("(k p) c -> p k c", p=128)[:, :, DI2 + c0:DI2 + c1]),
                        writes=[("w1g", wpa, k) for k in range(ND)], tag="ldw1g")
                    for half in range(2):
                        emit_mm_group(P, bank[4 + half][0:w, :], ("bank", 4 + half),
                                      [(w1a_b[:, wpa, k, 0:w], xnf[:, 0, k, half * 512:(half + 1) * 512]) for k in range(ND)],
                                      reads=xn_reads + [("w1a", wpa, k) for k in range(ND)], tag="mm_w1a")
                        emit_mm_group(P, bank[6 + half][0:w, :], ("bank", 6 + half),
                                      [(w1g_b[:, wpa, k, 0:w], xnf[:, 0, k, half * 512:(half + 1) * 512]) for k in range(ND)],
                                      reads=xn_reads + [("w1g", wpa, k) for k in range(ND)], tag="mm_w1g")
                    for half in range(2):
                        em.op("scalar", lambda e, m=m, w=w, half=half: e.activation(
                            asb[0:w, 0, :], bank[4 + half][0:w, :], AF.Identity,
                            bias=b1as[0:w, m:m + 1]),
                            reads=[("bank", 4 + half), ("b1as",)], writes=[("asbk",)], tag="ev_a")
                        em.op("scalar", lambda e, m=m, w=w, half=half: e.activation(
                            Pb[0:w, m, half * 512:(half + 1) * 512], bank[6 + half][0:w, :], AF.Gelu,
                            bias=b1gs[0:w, m:m + 1]),
                            reads=[("bank", 6 + half), ("b1gs",)], writes=[("Pb", m)], tag="ev_g")
                        em.op("vector", lambda e, m=m, w=w, half=half: e.tensor_tensor(
                            Pb[0:w, m, half * 512:(half + 1) * 512], asb[0:w, 0, :],
                            Pb[0:w, m, half * 512:(half + 1) * 512], ALU.mult),
                            reads=[("asbk",), ("Pb", m)], writes=[("Pb", m)], tag="pmul")
                if L1FFN_DEBUG and i == 0 and n == 0:
                    for kf in range(22):
                        em.dma("gpsimd", dbg_pb[:, kf, :], Pb[:, kf, :], reads=[("Pb", kf)], tag="dbgpb")
                for s_ in range(NTC):
                    t = n * NTC + s_
                    em.dma("sync", xw[:, pa, :], src[t * 128:(t + 1) * 128, :], writes=[("xw", pa)], tag="ldx2")
                    for dc in range(2):
                        bko = dc
                        pairs = []
                        for kf in range(22):
                            kk0 = kf * 128
                            w = min(DI2, kk0 + 128) - kk0
                            pairs.append((Pb[0:w, kf, s_ * 128:(s_ + 1) * 128], w2r_sb[0:w, kf, dc * 512:(dc + 1) * 512]))
                        pairs.append((ones1[:, :], b2sb[:, dc * 512:(dc + 1) * 512]))
                        emit_mm_group(P, bank[bko][:, :], BK(bko), pairs,
                                      reads=[("Pb", kf) for kf in range(22)] + [("w2r",), ("o1",), ("b2s",)],
                                      tag="mm_w2f")
                        em.op("vector", lambda e, dc=dc, pa=pa, bko=bko: e.tensor_tensor(
                            xobf[:, pa, dc * 512:(dc + 1) * 512], bank[bko][:, :],
                            xw[:, pa, dc * 512:(dc + 1) * 512], ALU.add),
                            reads=[BK(bko), ("xw", pa)], writes=[("xof", pa, dc)], tag="residf")
                    em.dma("sync", dst[t * 128:(t + 1) * 128, :], xobf[:, pa, :],
                           reads=[("xof", pa, 0), ("xof", pa, 1)], writes=[("dstf", t)], tag="stxf")
        em.barrier()

    cur = 0
    for i in range(L1FFN_DEPTHS):
        if "a" in L1FFN_PARTS:
            emit_attn_depth(i, xd[cur], xd[1 - cur])
            cur = 1 - cur
        if "f" in L1FFN_PARTS:
            emit_ffn_depth(i, xd[cur], xd[1 - cur])
            cur = 1 - cur

    with ExitStack() as s9:
        A = lambda n, sh, dt=F32: s9.enter_context(nc.sbuf_tensor(P.uniq(n), list(sh), dt))
        xcf = A("xcf", [128, 2, D])
        xsf = A("xsf", [128, 2, D])
        xof2 = A("xof2", [128, 2, D])
        for t in range(NT):
            pa = t % 2
            em.dma("sync", xcf[:, pa, :], x_in[t * 128:(t + 1) * 128, :], writes=[("xf", pa)], tag="ldA")
            em.dma("sync", xsf[:, pa, :], xd[cur][t * 128:(t + 1) * 128, :], writes=[("xs", pa)], tag="ldB")
            em.op("vector", lambda e, pa=pa: e.scalar_tensor_tensor(xof2[:, pa, :], xsf[:, pa, :], 0.5,
                                                                    xcf[:, pa, :], ALU.mult, ALU.add),
                  reads=[("xf", pa), ("xs", pa)], writes=[("xo2", pa)], tag="res")
            em.dma("sync", xo_out[t * 128:(t + 1) * 128, :], xof2[:, pa, :],
                   reads=[("xo2", pa)], writes=[("st", t)], tag="stf")
    inst_ctx.close()
    em.barrier()
    return P


def _l1ffn_tables():
    n, DH, WIN = 2048, 64, 256
    half = np.arange(0, DH, 2, dtype=np.float64)
    inv_freq = 1.0 / 10000.0 ** (half / DH)
    t = np.arange(n, dtype=np.float64)
    freqs = np.concatenate([t[:, None] * inv_freq[None, :]] * 2, -1)
    base_scale = (half + 0.4 * DH) / (1.4 * DH)
    power = (t - n // 2) / (WIN // 2)
    scale = np.concatenate([base_scale[None, :] ** power[:, None]] * 2, -1)
    cs = np.cos(freqs) * scale
    ss = np.sin(freqs) * scale
    ci = np.cos(freqs) / scale
    si = np.sin(freqs) / scale
    lay = lambda v: np.ascontiguousarray(v.reshape(16, 128, v.shape[-1]).transpose(1, 0, 2)).astype(np.float32)
    return {
        "tab_cs": lay(cs), "tab_sslo": lay(-ss[:, 0:32]), "tab_sshi": lay(ss[:, 32:64]),
        "tab_ci": lay(ci), "tab_silo": lay(-si[:, 0:32]), "tab_sihi": lay(si[:, 32:64]),
    }


def _l1ffn_masks():
    r = np.arange(128)[:, None]
    j = np.arange(384)[None, :]
    mf = np.where((j >= r) & (j <= 256 + r), 0.0, NEG).astype(np.float32)
    m1 = mf.copy(); m1[:, 0:128] = NEG
    m0 = mf.copy(); m0[:, 0:256] = NEG
    return mf, m1, m0


def run_l1ffn(x_full, Wqkv, q_scale, k_scale, Wo, ln_g, ln_b, W1, b1, W2, b2):
    P = build_l1ffn()
    nc = P.close_build()
    DI2 = 2730
    pmj6 = lambda v: np.ascontiguousarray(v.reshape(6, ND, 128).transpose(2, 0, 1))
    b1pad = np.zeros((6, 2, 2816), np.float32)
    b1pad[:, 0, 0:DI2] = b1[:, 0:DI2]
    b1pad[:, 1, 0:DI2] = b1[:, DI2:]
    b1a = np.ascontiguousarray(b1pad[:, 0].reshape(6, 22, 128).transpose(2, 0, 1))
    b1g = np.ascontiguousarray(b1pad[:, 1].reshape(6, 22, 128).transpose(2, 0, 1))
    mf, m1, m0 = _l1ffn_masks()
    base = dict(_l1ffn_tables())
    base.update({
        "Wqkv": Wqkv, "Wo": Wo, "lng": pmj6(ln_g), "lnb": pmj6(ln_b),
        "W1": W1, "b1a": b1a, "b1g": b1g, "W2": W2, "b2": b2,
        "qs": q_scale, "ks": k_scale,
        "identf": np.eye(128, dtype=np.float32), "identb": np.eye(128, dtype=BFNP),
        "m_full": mf, "m_1": m1, "m_0": m0,
        "ones128": np.ones((128, 1), np.float32), "ones1": np.ones((1, 128), np.float32),
    })
    xs = x_full.reshape(NCORES, T, D)
    in_maps = [dict(base, x=np.ascontiguousarray(xs[c])) for c in range(NCORES)]
    res = run_bass_kernel_spmd(nc, in_maps, core_ids=list(range(NCORES)))
    out = np.stack([res.results[c]["xo"] for c in range(NCORES)])
    return out.reshape(16, 2048, D)


_PROG_CACHE = {}


def _get_prog(name, builder):
    if name not in _PROG_CACHE:
        P = builder()
        _PROG_CACHE[name] = P.close_build()
    return _PROG_CACHE[name]


def kernel(**inp):
    if FUSED:
        return kernel_fused(**inp)
    x = np.asarray(inp["L0_attn_x"], np.float32)

    x = run_moe_cached(x, inp["L0_ffn_wg"], inp["L0_ffn_w1"], inp["L0_ffn_b1"],
                       inp["L0_ffn_w2"], inp["L0_ffn_b2"])
    x = run_l0attn_cached(x, inp["L0_attn_Wqkv"], inp["L0_attn_Wproj"])
    x = run_moe_cached(x, inp["L0_ffn_wg"], inp["L0_ffn_w1"], inp["L0_ffn_b1"],
                       inp["L0_ffn_w2"], inp["L0_ffn_b2"])
    x = run_l1ffn_cached(x, inp["L1_ffn_Wqkv"], inp["L1_ffn_q_scale"], inp["L1_ffn_k_scale"],
                         inp["L1_ffn_Wo"], inp["L1_ffn_ln_g"], inp["L1_ffn_ln_b"],
                         inp["L1_ffn_W1"], inp["L1_ffn_b1"], inp["L1_ffn_W2"], inp["L1_ffn_b2"])
    x = run_l1attn_cached(x, inp["L1_attn_w_rms_mix"], inp["L1_attn_Wg"], inp["L1_attn_bg"],
                          inp["L1_attn_Wv"], inp["L1_attn_bv"], inp["L1_attn_Wd"], inp["L1_attn_bd"],
                          inp["L1_attn_w_rms_attn"], inp["L1_attn_Wqkv"], inp["L1_attn_Wo"],
                          inp["L1_attn_w_rms_ffn"], inp["L1_attn_Wgate"], inp["L1_attn_Wup"],
                          inp["L1_attn_Wout"])
    x = run_l1ffn_cached(x, inp["L1_ffn_Wqkv"], inp["L1_ffn_q_scale"], inp["L1_ffn_k_scale"],
                         inp["L1_ffn_Wo"], inp["L1_ffn_ln_g"], inp["L1_ffn_ln_b"],
                         inp["L1_ffn_W1"], inp["L1_ffn_b1"], inp["L1_ffn_W2"], inp["L1_ffn_b2"])
    return np.ascontiguousarray(x, dtype=np.float32)


def run_moe_cached(x_full, wg, w1, b1, w2, b2):
    nc = _get_prog("moe", build_moe)
    tri, ones128, ones1, _ = _moe_consts(0)
    ident = np.eye(128, dtype=np.float32)
    b1t = np.ascontiguousarray(b1.reshape(8, 32, 128).transpose(2, 0, 1))
    xs = x_full.reshape(NCORES, T, D)
    in_maps = []
    for c in range(NCORES):
        in_maps.append({
            "x": np.ascontiguousarray(xs[c]), "wg": wg, "w1": w1, "b1t": b1t,
            "w2": w2, "b2": b2, "identf": ident, "tri": tri,
            "ones128": ones128, "ones1": ones1, "prefmask": _moe_consts(c)[3], "ebase": _moe_ebase(),
            "identb": np.eye(128, dtype=BFNP),
        })
    res = run_bass_kernel_spmd(nc, in_maps, core_ids=list(range(NCORES)))
    return np.stack([res.results[c]["xo"] for c in range(NCORES)]).reshape(16, 2048, D)


def run_l0attn_cached(x_full, Wqkv, Wproj):
    nc = _get_prog("l0attn", build_l0attn)
    mask_band, mask_first = _l0attn_masks()
    ident = np.eye(128, dtype=np.float32)
    xs = x_full.reshape(NCORES, T, D)
    in_maps = [{
        "x": np.ascontiguousarray(xs[c]), "Wqkv": Wqkv, "Wproj": Wproj,
        "identf": ident, "identb": np.eye(128, dtype=BFNP),
        "mask_band": mask_band, "mask_first": mask_first,
    } for c in range(NCORES)]
    res = run_bass_kernel_spmd(nc, in_maps, core_ids=list(range(NCORES)))
    return np.stack([res.results[c]["xo"] for c in range(NCORES)]).reshape(16, 2048, D)


def run_l1attn_cached(x_full, w_rms_mix, Wg, bg, Wv, bv, Wd, bd, w_rms_attn, Wqkv, Wo,
                      w_rms_ffn, Wgate, Wup, Wout):
    nc = _get_prog("l1attn", build_l1attn)
    pmj = lambda v: np.ascontiguousarray(v.reshape(ND, 128).T)
    r = np.arange(128)[:, None]
    j = np.arange(256)[None, :]
    mband = np.where((j >= r + 1) & (j <= 128 + r), 0.0, NEG).astype(np.float32)
    jj = np.arange(128)[None, :]
    mfirst = np.where(jj <= r, 0.0, NEG).astype(np.float32)
    xs = x_full.reshape(NCORES, T, D)
    base = {
        "wmixt": pmj(w_rms_mix), "WgT": np.ascontiguousarray(Wg.T), "bgt": pmj(bg),
        "WvT": np.ascontiguousarray(Wv.T), "bvt": pmj(bv),
        "WdT": np.ascontiguousarray(Wd.T), "bdt": pmj(bd),
        "wattnt": pmj(w_rms_attn), "WqkvT": np.ascontiguousarray(Wqkv.T),
        "WoT": np.ascontiguousarray(Wo.T), "wffnt": pmj(w_rms_ffn),
        "WgateT": np.ascontiguousarray(Wgate.T), "WupT": np.ascontiguousarray(Wup.T),
        "WoutT": np.ascontiguousarray(Wout.T),
        "identf": np.eye(128, dtype=np.float32), "identb": np.eye(128, dtype=BFNP),
        "mask_band": mband, "mask_first": mfirst,
    }
    in_maps = [dict(base, x=np.ascontiguousarray(xs[c])) for c in range(NCORES)]
    res = run_bass_kernel_spmd(nc, in_maps, core_ids=list(range(NCORES)))
    return np.stack([res.results[c]["xo"] for c in range(NCORES)]).reshape(16, 2048, D)


def run_l1ffn_cached(x_full, Wqkv, q_scale, k_scale, Wo, ln_g, ln_b, W1, b1, W2, b2):
    nc = _get_prog("l1ffn", build_l1ffn)
    DI2 = 2730
    pmj6 = lambda v: np.ascontiguousarray(v.reshape(6, ND, 128).transpose(2, 0, 1))
    b1pad = np.zeros((6, 2, 2816), np.float32)
    b1pad[:, 0, 0:DI2] = b1[:, 0:DI2]
    b1pad[:, 1, 0:DI2] = b1[:, DI2:]
    b1a = np.ascontiguousarray(b1pad[:, 0].reshape(6, 22, 128).transpose(2, 0, 1))
    b1g = np.ascontiguousarray(b1pad[:, 1].reshape(6, 22, 128).transpose(2, 0, 1))
    mf, m1, m0 = _l1ffn_masks()
    base = dict(_l1ffn_tables())
    base.update({
        "Wqkv": Wqkv, "Wo": Wo, "lng": pmj6(ln_g), "lnb": pmj6(ln_b),
        "W1": W1, "b1a": b1a, "b1g": b1g, "W2": W2, "b2": b2,
        "qs": q_scale, "ks": k_scale,
        "identf": np.eye(128, dtype=np.float32), "identb": np.eye(128, dtype=BFNP),
        "m_full": mf, "m_1": m1, "m_0": m0,
        "ones128": np.ones((128, 1), np.float32), "ones1": np.ones((1, 128), np.float32),
    })
    xs = x_full.reshape(NCORES, T, D)
    in_maps = [dict(base, x=np.ascontiguousarray(xs[c])) for c in range(NCORES)]
    res = run_bass_kernel_spmd(nc, in_maps, core_ids=list(range(NCORES)))
    return np.stack([res.results[c]["xo"] for c in range(NCORES)]).reshape(16, 2048, D)


def build_fused():
    P = Prog("fused")
    DEPTH6, DI2 = 6, 2730
    V = {
        "identf": P.inp("identf", [128, 128], F32),
        "identb": P.inp("identb", [128, 128], BF16),
        "ones128": P.inp("ones128", [128, 1], F32),
        "ones1": P.inp("ones1", [1, 128], F32),
        "tri": P.inp("tri", [128, 128], F32),
        "prefmask": P.inp("prefmask", [128, 1], F32),
        "ebase": P.inp("ebase", [1, 8], F32),
        "mask_band": P.inp("l0_mask_band", [128, 256], F32),
        "mask_first": P.inp("mask_first", [128, 128], F32),
        "mask_band_l1": P.inp("l1_mask_band", [128, 256], F32),
        "m_full": P.inp("m_full", [128, 384], F32),
        "m_1": P.inp("m_1", [128, 384], F32),
        "m_0": P.inp("m_0", [128, 384], F32),
        "tab_cs": P.inp("tab_cs", [128, 16, 64], F32),
        "tab_sslo": P.inp("tab_sslo", [128, 16, 32], F32),
        "tab_sshi": P.inp("tab_sshi", [128, 16, 32], F32),
        "tab_ci": P.inp("tab_ci", [128, 16, 64], F32),
        "tab_silo": P.inp("tab_silo", [128, 16, 32], F32),
        "tab_sihi": P.inp("tab_sihi", [128, 16, 32], F32),
        "wg": P.inp("m_wg", [D, 8]),
        "w1": P.inp("m_w1", [8, D, 4096]),
        "b1t": P.inp("m_b1t", [128, 8, 32]),
        "w2": P.inp("m_w2", [8, 4096, D]),
        "b2": P.inp("m_b2", [8, D]),
        "Wqkv": P.inp("a0_Wqkv", [D, 3 * D]),
        "Wproj": P.inp("a0_Wproj", [D, D]),
        "wmixt": P.inp("A_wmixt", [128, ND]),
        "WgT": P.inp("A_WgT", [D, D]),
        "bgt": P.inp("A_bgt", [128, ND]),
        "WvT": P.inp("A_WvT", [D, D]),
        "bvt": P.inp("A_bvt", [128, ND]),
        "WdT": P.inp("A_WdT", [D, D]),
        "bdt": P.inp("A_bdt", [128, ND]),
        "wattnt": P.inp("A_wattnt", [128, ND]),
        "WqkvT": P.inp("A_WqkvT", [D, 3 * D]),
        "WoT": P.inp("A_WoT", [D, D]),
        "wffnt": P.inp("A_wffnt", [128, ND]),
        "WgateT": P.inp("A_WgateT", [D, 4096]),
        "WupT": P.inp("A_WupT", [D, 4096]),
        "WoutT": P.inp("A_WoutT", [4096, D]),
        "Wqkv6": P.inp("F_Wqkv", [DEPTH6, D, 3 * 1024]),
        "Wo6": P.inp("F_Wo", [DEPTH6, 1024, D]),
        "lng": P.inp("F_lng", [128, DEPTH6, ND]),
        "lnb": P.inp("F_lnb", [128, DEPTH6, ND]),
        "W1": P.inp("F_W1", [DEPTH6, D, 2 * DI2]),
        "b1a": P.inp("F_b1a", [128, DEPTH6, 22]),
        "b1g": P.inp("F_b1g", [128, DEPTH6, 22]),
        "W2": P.inp("F_W2", [DEPTH6, DI2, D]),
        "b26": P.inp("F_b2", [DEPTH6, D]),
        "qs": P.inp("F_qs", [DEPTH6, 64]),
        "ks": P.inp("F_ks", [DEPTH6, 64]),
    }
    x_in = P.inp("x", [T, D])
    xo_out = P.out("xo", [T, D])
    xA = P.dram("xA", [T, D])
    xB = P.dram("xB", [T, D])
    xC = P.dram("xC", [T, D])
    xD = P.dram("xD", [T, D])
    xE = P.dram("xE", [T, D])
    bank = [P.psum(f"bank{i}", [128, 512], F32) for i in range(8)]

    build_moe(P, V, x_in, xA, bank)
    build_l0attn(P, V, xA, xB, bank)
    build_moe(P, V, xB, xC, bank)
    build_l1ffn(P, V, xC, xD, bank)
    build_l1attn(P, V, xD, xE, bank)
    build_l1ffn(P, V, xE, xo_out, bank)
    return P


FUSED = True


def _fused_in_maps(inp):
    DI2 = 2730
    pmj = lambda v: np.ascontiguousarray(v.reshape(ND, 128).T.astype(np.float32))
    pmj6 = lambda v: np.ascontiguousarray(v.reshape(6, ND, 128).transpose(2, 0, 1).astype(np.float32))
    tT = lambda v: np.ascontiguousarray(np.asarray(v, np.float32).T)
    b1 = np.asarray(inp["L1_ffn_b1"], np.float32)
    b1pad = np.zeros((6, 2, 2816), np.float32)
    b1pad[:, 0, 0:DI2] = b1[:, 0:DI2]
    b1pad[:, 1, 0:DI2] = b1[:, DI2:]
    mb0, mf0 = _l0attn_masks()
    r = np.arange(128)[:, None]
    j = np.arange(256)[None, :]
    mb1 = np.where((j >= r + 1) & (j <= 128 + r), 0.0, NEG).astype(np.float32)
    mff, mf1, mf2 = _l1ffn_masks()
    tri, ones128, ones1, _ = _moe_consts(0)
    base = dict(_l1ffn_tables())
    base.update({
        "identf": np.eye(128, dtype=np.float32), "identb": np.eye(128, dtype=BFNP),
        "ones128": ones128, "ones1": ones1, "tri": tri, "ebase": _moe_ebase(),
        "l0_mask_band": mb0, "mask_first": mf0, "l1_mask_band": mb1,
        "m_full": mff, "m_1": mf1, "m_0": mf2,
        "m_wg": np.asarray(inp["L0_ffn_wg"], np.float32),
        "m_w1": np.asarray(inp["L0_ffn_w1"], np.float32),
        "m_b1t": np.ascontiguousarray(np.asarray(inp["L0_ffn_b1"], np.float32).reshape(8, 32, 128).transpose(2, 0, 1)),
        "m_w2": np.asarray(inp["L0_ffn_w2"], np.float32),
        "m_b2": np.asarray(inp["L0_ffn_b2"], np.float32),
        "a0_Wqkv": np.asarray(inp["L0_attn_Wqkv"], np.float32),
        "a0_Wproj": np.asarray(inp["L0_attn_Wproj"], np.float32),
        "A_wmixt": pmj(np.asarray(inp["L1_attn_w_rms_mix"], np.float32)),
        "A_WgT": tT(inp["L1_attn_Wg"]), "A_bgt": pmj(np.asarray(inp["L1_attn_bg"], np.float32)),
        "A_WvT": tT(inp["L1_attn_Wv"]), "A_bvt": pmj(np.asarray(inp["L1_attn_bv"], np.float32)),
        "A_WdT": tT(inp["L1_attn_Wd"]), "A_bdt": pmj(np.asarray(inp["L1_attn_bd"], np.float32)),
        "A_wattnt": pmj(np.asarray(inp["L1_attn_w_rms_attn"], np.float32)),
        "A_WqkvT": tT(inp["L1_attn_Wqkv"]), "A_WoT": tT(inp["L1_attn_Wo"]),
        "A_wffnt": pmj(np.asarray(inp["L1_attn_w_rms_ffn"], np.float32)),
        "A_WgateT": tT(inp["L1_attn_Wgate"]), "A_WupT": tT(inp["L1_attn_Wup"]),
        "A_WoutT": tT(inp["L1_attn_Wout"]),
        "F_Wqkv": np.asarray(inp["L1_ffn_Wqkv"], np.float32),
        "F_Wo": np.asarray(inp["L1_ffn_Wo"], np.float32),
        "F_lng": pmj6(np.asarray(inp["L1_ffn_ln_g"], np.float32)),
        "F_lnb": pmj6(np.asarray(inp["L1_ffn_ln_b"], np.float32)),
        "F_W1": np.asarray(inp["L1_ffn_W1"], np.float32),
        "F_b1a": np.ascontiguousarray(b1pad[:, 0].reshape(6, 22, 128).transpose(2, 0, 1)),
        "F_b1g": np.ascontiguousarray(b1pad[:, 1].reshape(6, 22, 128).transpose(2, 0, 1)),
        "F_W2": np.asarray(inp["L1_ffn_W2"], np.float32),
        "F_b2": np.asarray(inp["L1_ffn_b2"], np.float32),
        "F_qs": np.asarray(inp["L1_ffn_q_scale"], np.float32),
        "F_ks": np.asarray(inp["L1_ffn_k_scale"], np.float32),
    })
    x = np.asarray(inp["L0_attn_x"], np.float32).reshape(NCORES, T, D)
    in_maps = []
    for c in range(NCORES):
        m = dict(base)
        m["x"] = np.ascontiguousarray(x[c])
        m["prefmask"] = _moe_consts(c)[3]
        in_maps.append(m)
    return in_maps


def kernel_fused(**inp):
    nc = _get_prog("fused", build_fused)
    in_maps = _fused_in_maps(inp)
    res = run_bass_kernel_spmd(nc, in_maps, core_ids=list(range(NCORES)))
    out = np.stack([res.results[c]["xo"] for c in range(NCORES)])
    return np.ascontiguousarray(out.reshape(16, 2048, D), dtype=np.float32)


# version 05
# speedup vs baseline: 1.0033x; 1.0033x over previous
import numpy as np
import ml_dtypes
from contextlib import ExitStack

import concourse.bass as bass
import concourse.mybir as mybir
from concourse.bass_utils import run_bass_kernel_spmd
from concourse.bass_core_rust import IndirectOffsetOnAxis

F32 = mybir.dt.float32
I32 = mybir.dt.int32
BF16 = mybir.dt.bfloat16
AF = mybir.ActivationFunctionType
ALU = mybir.AluOpType
AX = mybir.AxisListType.X
BFNP = ml_dtypes.bfloat16

NCORES = 8
S = 2048
D = 1024
T = 2 * S
NT = T // 128
ND = D // 128
NT_SEQ = S // 128
NEG = -1.0e30

COMPUTE_ENGINES = ("tensor", "vector", "scalar", "gpsimd")
DMA_QUEUES = ("sync", "scalar", "gpsimd")


class _Ev:
    __slots__ = ("idx", "engine", "fn", "deps", "is_dma", "mile", "semval", "tag", "inc", "semkey", "gate")

    def __init__(self, idx, engine, fn, deps, is_dma, tag):
        self.idx = idx
        self.engine = engine
        self.fn = fn
        self.deps = deps
        self.is_dma = is_dma
        self.inc = 16 if is_dma else 1
        self.mile = False
        self.semkey = None
        self.gate = None
        self.semval = None
        self.tag = tag


class Em:

    NSUB = 4

    def __init__(self, nc, ctx):
        self.nc = nc
        self.events = []
        self.res = {}
        self.sem = {}
        for e in COMPUTE_ENGINES:
            self.sem[e] = ctx.enter_context(nc.semaphore(f"sem_{e}"))
        for q in DMA_QUEUES:
            for r in range(self.NSUB):
                self.sem[f"dma_{q}_{r}"] = ctx.enter_context(nc.semaphore(f"sem_dma_{q}_{r}"))

    @staticmethod
    def _excl(k):
        return isinstance(k, tuple) and k and isinstance(k[0], str) and (
            k[0].startswith("bank") or k[0].startswith("ps"))

    def _track(self, idx, reads, writes, deps):
        reads = [k for k in reads]
        writes = [k for k in writes]
        excl = [k for k in reads if self._excl(k)]
        if excl:
            reads = [k for k in reads if not self._excl(k)]
            writes = writes + excl
        dep = set(deps)
        for k in reads:
            r = self.res.get(k)
            if r is not None and r[0] is not None:
                dep.add(r[0])
        for k in writes:
            r = self.res.get(k)
            if r is not None:
                if r[0] is not None:
                    dep.add(r[0])
                dep.update(r[1])
            self.res[k] = [idx, []]
        for k in reads:
            r = self.res.setdefault(k, [None, []])
            r[1].append(idx)
        dep.discard(idx)
        return sorted(dep)

    def op(self, engine, fn, reads=(), writes=(), deps=(), tag=""):
        idx = len(self.events)
        d = self._track(idx, reads, writes, deps)
        self.events.append(_Ev(idx, engine, fn, d, False, tag))
        return idx

    def dma(self, queue, out, in_, reads=(), writes=(), deps=(), tag="dma"):
        idx = len(self.events)
        d = self._track(idx, reads, writes, deps)
        fn = (lambda eng, out=out, in_=in_: eng.dma_start(out=out, in_=in_))
        self.events.append(_Ev(idx, queue, fn, d, True, tag))
        return idx

    def dma_fn(self, queue, fn, reads=(), writes=(), deps=(), tag="dma", inc=16):
        idx = len(self.events)
        d = self._track(idx, reads, writes, deps)
        ev = _Ev(idx, queue, fn, d, True, tag)
        ev.inc = inc
        self.events.append(ev)
        return idx

    def barrier(self):
        last = {}
        for ev in self.events:
            key = ("q:" + ev.engine) if ev.is_dma else ev.engine
            last[key] = ev.idx
        deps = sorted(set(last.values()))
        ids = []
        for e in COMPUTE_ENGINES:
            ids.append(self.op(e, lambda eng: eng.nop(), deps=deps, tag="barrier"))
        for k in self.res:
            self.res[k] = [ids[0], ids[:]]
        return ids

    def finalize(self, block):
        events = self.events
        for ev in events:
            for d in ev.deps:
                de = events[d]
                if de.is_dma or not (de.engine == ev.engine == "tensor" and not ev.is_dma):
                    de.mile = True
        cnt = {}
        qn = {}
        for ev in events:
            if ev.is_dma:
                q = ev.engine
                n = qn.get(q, 0)
                qn[q] = n + 1
                semkey = f"dma_{q}_{n % self.NSUB}"
                ev.semkey = semkey
                prev = cnt.get(semkey, 0)
                ev.gate = (semkey, prev) if prev else None
                cnt[semkey] = prev + ev.inc
                ev.semval = cnt[semkey]
                ev.mile = True
            elif ev.mile:
                semkey = ev.engine
                ev.semkey = semkey
                cnt[semkey] = cnt.get(semkey, 0) + 1
                ev.semval = cnt[semkey]
        self.final_counts = dict(cnt)

        per_engine = {e: [] for e in COMPUTE_ENGINES}
        sync_evs = []
        for ev in events:
            if ev.is_dma and ev.engine == "sync":
                sync_evs.append(ev)
            else:
                per_engine[ev.engine].append(ev)

        sems = self.sem

        def run(eng, engine_name, evs, final=False):
            watermark = {}
            for ev in evs:
                waits = []
                if ev.is_dma and ev.gate is not None:
                    waits.append(ev.gate)
                for d in ev.deps:
                    de = events[d]
                    if (not de.is_dma) and (not ev.is_dma) and de.engine == engine_name == "tensor":
                        continue
                    if de.semval is None:
                        raise RuntimeError(f"dep on non-milestone event {de.idx} {de.tag}")
                    waits.append((de.semkey, de.semval))
                for semkey, val in waits:
                    if watermark.get(semkey, 0) >= val:
                        continue
                    eng.wait_ge(sems[semkey], val)
                    watermark[semkey] = val
                inst = ev.fn(eng)
                if ev.mile:
                    if inst is None:
                        raise RuntimeError(f"milestone event {ev.idx} ({ev.tag}) returned None")
                    inst.then_inc(sems[ev.semkey], ev.inc if ev.is_dma else 1)
            if final:
                for k, v in self.final_counts.items():
                    if k.startswith("dma_") and watermark.get(k, 0) < v:
                        eng.wait_ge(sems[k], v)

        @block.tensor
        def _(tensor):
            run(tensor, "tensor", per_engine["tensor"])

        @block.vector
        def _(vector):
            run(vector, "vector", per_engine["vector"])

        @block.scalar
        def _(scalar):
            run(scalar, "scalar", per_engine["scalar"])

        @block.gpsimd
        def _(gpsimd):
            run(gpsimd, "gpsimd", per_engine["gpsimd"])

        @block.sync
        def _(sync):
            run(sync, "__sync__", sync_evs, final=True)


class Prog:
    def __init__(self, name="prog"):
        self.nc = bass.Bass("TRN2", target_bir_lowering=False, debug=False)
        self.ctx = ExitStack()
        self.em = Em(self.nc, self.ctx)
        self._uniq = 0

    def uniq(self, base):
        self._uniq += 1
        return f"{base}_{self._uniq}"

    def inp(self, name, shape, dtype=F32):
        return self.nc.dram_tensor(name, list(shape), dtype, kind="ExternalInput").ap()

    def out(self, name, shape, dtype=F32):
        return self.nc.dram_tensor(name, list(shape), dtype, kind="ExternalOutput").ap()

    def dram(self, name, shape, dtype=F32):
        return self.nc.dram_tensor(self.uniq(name), list(shape), dtype).ap()

    def sb(self, name, shape, dtype=F32):
        return self.ctx.enter_context(self.nc.sbuf_tensor(self.uniq(name), list(shape), dtype))

    def psum(self, name, shape, dtype=F32):
        return self.ctx.enter_context(self.nc.psum_tensor(self.uniq(name), list(shape), dtype))

    def close_build(self):
        block = self.ctx.enter_context(self.nc.Block())
        self.em.finalize(block)
        self.ctx.close()
        return self.nc


def alloc_ln_scratch(P, tag, npar=2):
    return {
        "sq": P.sb(tag + "_sq", [128, npar, D], F32),
        "s": P.sb(tag + "_s", [128, npar, 1], F32),
        "s2": P.sb(tag + "_s2", [128, npar, 1], F32),
        "mu": P.sb(tag + "_mu", [128, npar, 1], F32),
        "m2": P.sb(tag + "_m2", [128, npar, 1], F32),
        "var": P.sb(tag + "_var", [128, npar, 1], F32),
        "lnv": P.sb(tag + "_lnv", [128, npar, 1], F32),
        "rstd": P.sb(tag + "_rstd", [128, npar, 1], F32),
        "xn": P.sb(tag + "_xn", [128, npar, D], F32),
        "tag": tag,
    }


def emit_ln_tile(P, scr, pa, x_ap, x_key, rms=False, eps=1e-5):
    em = P.em
    kk = (scr["tag"], pa)
    sq = scr["sq"][:, pa, :]
    s_ = scr["s"][:, pa, :]
    s2 = scr["s2"][:, pa, :]
    mu = scr["mu"][:, pa, :]
    m2 = scr["m2"][:, pa, :]
    var = scr["var"][:, pa, :]
    lnv = scr["lnv"][:, pa, :]
    rstd = scr["rstd"][:, pa, :]
    xn = scr["xn"][:, pa, :]

    em.op("scalar", lambda e: e.activation(sq, x_ap, AF.Square, accum_out=s2),
          reads=[x_key], writes=[kk + ("sq",), kk + ("s2",)], tag="ln_sq")
    if not rms:
        em.op("vector", lambda e: e.reduce_sum(s_, x_ap, axis=AX),
              reads=[x_key], writes=[kk + ("s",)], tag="ln_s")
        em.op("scalar", lambda e: e.mul(mu, s_, 1.0 / D),
              reads=[kk + ("s",)], writes=[kk + ("mu",)], tag="ln_mu")
        em.op("vector", lambda e: e.tensor_scalar(m2, mu, mu, -float(eps), ALU.mult, ALU.add),
              reads=[kk + ("mu",)], writes=[kk + ("m2",)], tag="ln_m2")
        em.op("vector", lambda e: e.scalar_tensor_tensor(var, s2, 1.0 / D, m2, ALU.mult, ALU.subtract),
              reads=[kk + ("s2",), kk + ("m2",)], writes=[kk + ("var",)], tag="ln_var")
    else:
        em.op("vector", lambda e: e.tensor_scalar(var, s2, 1.0 / D, float(eps), ALU.mult, ALU.add),
              reads=[kk + ("s2",)], writes=[kk + ("var",)], tag="rms_var")
    em.op("scalar", lambda e: e.activation(lnv, var, AF.Ln),
          reads=[kk + ("var",)], writes=[kk + ("lnv",)], tag="ln_ln")
    em.op("scalar", lambda e: e.activation(rstd, lnv, AF.Exp, scale=-0.5),
          reads=[kk + ("lnv",)], writes=[kk + ("rstd",)], tag="ln_exp")
    if rms:
        em.op("vector", lambda e: e.tensor_scalar(xn, x_ap, rstd, None, ALU.mult),
              reads=[x_key, kk + ("rstd",)], writes=[kk + ("xn",)], tag="rms_apply")
    else:
        em.op("vector", lambda e: e.tensor_scalar(xn, x_ap, mu, rstd, ALU.subtract, ALU.mult),
              reads=[x_key, kk + ("mu",), kk + ("rstd",)], writes=[kk + ("xn",)], tag="ln_apply")
    return kk + ("xn",)


def emit_mm_group(P, psum_ap, psum_key, pairs, reads, n_parts=None, tag="mm",
                  tile_position=None):
    em = P.em

    def fn(e):
        n = len(pairs)
        inst = None
        for i, (lt, rh) in enumerate(pairs):
            inst = e.matmul(psum_ap, lt, rh, start=(i == 0), stop=(i == n - 1),
                            tile_position=tile_position)
        return inst

    return em.op("tensor", fn, reads=list(reads), writes=[psum_key], tag=tag)


def build_l0attn(P=None, V=None, x_in=None, xo_out=None, bank=None):
    standalone = P is None
    if standalone:
        P = Prog("l0attn")
        V = {"Wqkv": P.inp("Wqkv", [D, 3 * D]), "Wproj": P.inp("Wproj", [D, D]),
             "identf": P.inp("identf", [128, 128], F32), "identb": P.inp("identb", [128, 128], BF16),
             "mask_band": P.inp("mask_band", [128, 256], F32),
             "mask_first": P.inp("mask_first", [128, 128], F32)}
        x_in = P.inp("x", [T, D])
        xo_out = P.out("xo", [T, D])
    em = P.em
    nc = P.nc
    wqkv_in = V["Wqkv"]; wproj_in = V["Wproj"]
    identf = V["identf"]; identb_in = V["identb"]
    mask_band = V["mask_band"]; mask_first = V["mask_first"]

    inst_ctx = ExitStack()
    _sb = lambda n, sh, dt=F32: inst_ctx.enter_context(nc.sbuf_tensor(P.uniq(n), list(sh), dt))
    wqkv = _sb("wqkv", [128, ND, 3 * D], BF16)
    wproj = _sb("wproj", [128, ND, D], BF16)
    ident = _sb("ident", [128, 128], F32)
    identb = _sb("identb", [128, 128], BF16)
    mband = _sb("mband", [128, 256], F32)
    mfirst = _sb("mfirst", [128, 128], F32)
    em.dma_fn("gpsimd", lambda e: e.dma_start(
        out=wqkv[:, :, :], in_=wqkv_in.rearrange("(k p) c -> p k c", p=128)),
        writes=[("wqkv", k) for k in range(ND)], tag="ld_wqkv")
    em.dma_fn("gpsimd", lambda e: e.dma_start(
        out=wproj[:, :, :], in_=wproj_in.rearrange("(k p) c -> p k c", p=128)),
        writes=[("wproj", k) for k in range(ND)], tag="ld_wproj")
    em.dma("sync", ident[:, :], identf, writes=[("ident",)], tag="ld_ident")
    em.dma("sync", identb[:, :], identb_in, writes=[("identb",)], tag="ld_identb")
    em.dma("sync", mband[:, :], mask_band, writes=[("mband",)], tag="ld_mb")
    em.dma("sync", mfirst[:, :], mask_first, writes=[("mfirst",)], tag="ld_mf")

    x_cur = _sb("x_cur", [128, 2, D], F32)
    lns = {k: _sb("ln0" + k, [128, 2, D if k in ("sq", "xn") else 1]) for k in
           ("sq", "s", "s2", "mu", "m2", "var", "lnv", "rstd", "xn")}
    lns["tag"] = "ln0"
    xnT = _sb("xnT", [128, 2, ND, 128], BF16)
    qT = _sb("qT", [128, 2, ND, 128], BF16)
    kT = _sb("kT", [128, 2, ND, 128], BF16)
    v_roll = _sb("v_roll", [128, 2, D], BF16)
    kTg = _sb("kTg", [128, ND, 32], BF16)
    vg = _sb("vg", [32, D], BF16)
    probs = _sb("probs", [128, 2, 320], F32)
    probs_bf = _sb("probs_bf", [128, 2, 320], BF16)
    probs_bf2 = _sb("probs_bf2", [128, 2, 320], BF16)
    pT = _sb("pT", [128, 2, 3, 128], BF16)
    nm = _sb("nm", [128, 2, 1], F32)
    sume = _sb("sume", [128, 2, 1], F32)
    rec = _sb("rec", [128, 2, 1], F32)
    yT = _sb("yT", [128, 2, ND, 128], BF16)
    xo_buf = _sb("xo_buf", [128, 2, D], F32)

    if bank is None:
        bank = [P.psum(f"bank{i}", [128, 512], F32) for i in range(8)]
    ps_tr = bank[0][:, 0:128]
    ps_qk = [bank[1][:, 0:128], bank[2][:, 0:128]]
    ps_v = bank[3][:, :]
    ps_sc = bank[4][:, :]
    ps_pT = bank[5][:, :].bitcast(BF16)[:, 0:128]
    ps_av = bank[6][:, 0:128]
    ps_pr = bank[7][:, :]

    for t in range(NT):
        pa = t % 2
        t_loc = t % NT_SEQ

        em.dma("sync", x_cur[:, pa, :], x_in[t * 128:(t + 1) * 128, :],
               writes=[("x", pa)], tag="ld_x")
        xn_key = emit_ln_tile(P, lns, pa, x_cur[:, pa, :], ("x", pa))
        xn = lns["xn"][:, pa, :]

        for k in range(ND):
            em.op("tensor", lambda e, k=k, xn=xn: e.transpose(ps_tr[:, :], xn[:, k * 128:(k + 1) * 128], ident[:, :]),
                  reads=[xn_key, ("ident",)], writes=[("ps_tr",)], tag="tr_xn")
            em.op("vector", lambda e, k=k, pa=pa: e.tensor_copy(xnT[:, pa, k, :], ps_tr[:, :]),
                  reads=[("ps_tr",)], writes=[("xnT", pa, k)], tag="tr_evac")

        xnT_reads = [("xnT", pa, k) for k in range(ND)]
        for hp in range(ND):
            bank = ps_qk[hp % 2]
            bkey = ("ps_qk", hp % 2)
            emit_mm_group(P, bank[:, :], bkey,
                          [(wqkv[:, k, hp * 128:(hp + 1) * 128], xnT[:, pa, k, :]) for k in range(ND)],
                          reads=xnT_reads + [("wqkv", k) for k in range(ND)], tag="mm_q")
            em.op("scalar", lambda e, hp=hp, bank=bank, pa=pa: e.activation(qT[:, pa, hp, :], bank[:, :], AF.Copy, scale=0.125),
                  reads=[bkey], writes=[("qT", pa, hp)], tag="ev_q")
            emit_mm_group(P, bank[:, :], bkey,
                          [(wqkv[:, k, D + hp * 128:D + (hp + 1) * 128], xnT[:, pa, k, :]) for k in range(ND)],
                          reads=xnT_reads + [("wqkv", k) for k in range(ND)], tag="mm_k")
            em.op("scalar", lambda e, hp=hp, bank=bank, pa=pa: e.copy(kT[:, pa, hp, :], bank[:, :]),
                  reads=[bkey], writes=[("kT", pa, hp)], tag="ev_k")
        for c in range(2):
            emit_mm_group(P, ps_v[:, :], ("ps_v",),
                          [(xnT[:, pa, k, :], wqkv[:, k, 2 * D + c * 512:2 * D + (c + 1) * 512]) for k in range(ND)],
                          reads=xnT_reads + [("wqkv", k) for k in range(ND)], tag="mm_v")
            em.op("vector", lambda e, c=c, pa=pa: e.tensor_copy(v_roll[:, pa, c * 512:(c + 1) * 512], ps_v[:, :]),
                  reads=[("ps_v",)], writes=[("v", pa, c)], tag="ev_v")

        em.op("vector",
              lambda e, pa=pa, t_loc=t_loc: e.tensor_copy(kTg[:, :, 2 * t_loc:2 * t_loc + 2],
                                                          kT.ap()[:, pa, :, 0:128:64]),
              reads=[("kT", pa, k) for k in range(ND)], writes=[("kTg", 2 * t_loc), ("kTg", 2 * t_loc + 1)],
              tag="kTg")
        em.dma_fn("scalar",
                  lambda e, t_loc=t_loc, pa=pa: e.dma_start(out=vg[2 * t_loc:2 * t_loc + 2, :], in_=v_roll.ap()[0:128:64, pa, :]),
                  reads=[("v", pa, 0), ("v", pa, 1)],
                  writes=[("vg", 2 * t_loc), ("vg", 2 * t_loc + 1)], tag="vg")

        g = 2 * (t_loc - 1) if t_loc >= 1 else 0
        Wd = 128 if t_loc == 0 else 256 + g
        for h in range(16):
            hp = h // 2
            base = (h % 2) * 64
            q_ap = qT[base:base + 64, pa, hp, :]
            sc_reads = [("qT", pa, hp)]
            pairs = []
            if t_loc == 0:
                pairs.append((q_ap, kT[base:base + 64, pa, hp, :], 0))
                sc_reads.append(("kT", pa, hp))
            else:
                pairs.append((q_ap, kT[base:base + 64, 1 - pa, hp, :], 0))
                pairs.append((q_ap, kT[base:base + 64, pa, hp, :], 128))
                sc_reads += [("kT", 1 - pa, hp), ("kT", pa, hp)]
                if g > 0:
                    pairs.append((q_ap, kTg[base:base + 64, hp, 0:g], 256))
                    sc_reads += [("kTg", i) for i in range(g)]

            def fn_sc(e, pairs=pairs, g=g):
                inst = None
                for (lt, rh, off) in pairs:
                    n = rh.shape[-1]
                    inst = e.matmul(ps_sc[:, off:off + n], lt, rh, start=True, stop=True)
                return inst
            em.op("tensor", fn_sc, reads=sc_reads, writes=[("ps_sc",)], tag="mm_sc")

            pr = probs[:, pa, :]
            if t_loc == 0:
                em.op("vector", lambda e, pr=pr: e.tensor_tensor(pr[:, 0:128], ps_sc[:, 0:128], mfirst[:, :], ALU.add),
                      reads=[("ps_sc",), ("mfirst",)], writes=[("probs", pa)], tag="mask0")
            else:
                em.op("vector", lambda e, pr=pr: e.tensor_tensor(pr[:, 0:256], ps_sc[:, 0:256], mband[:, :], ALU.add),
                      reads=[("ps_sc",), ("mband",)], writes=[("probs", pa)], tag="mask")
                if g > 0:
                    em.op("scalar", lambda e, pr=pr, g=g: e.copy(pr[:, 256:256 + g], ps_sc[:, 256:256 + g]),
                          reads=[("ps_sc",)], writes=[("probs", pa)], tag="cpglob")
            em.op("vector", lambda e, pr=pr, Wd=Wd, pa=pa: e.reduce_max(nm[:, pa, :], pr[:, 0:Wd], axis=AX, negate=True),
                  reads=[("probs", pa)], writes=[("nm", pa)], tag="negmax")
            em.op("scalar", lambda e, pr=pr, Wd=Wd, pa=pa: e.activation(probs_bf[:, pa, 0:Wd], pr[:, 0:Wd], AF.Exp,
                                                                 bias=nm[:, pa, :], accum_out=sume[:, pa, :]),
                  reads=[("probs", pa), ("nm", pa)], writes=[("probs_bf", pa), ("sume", pa)], tag="exp")
            em.op("vector", lambda e, pa=pa: e.reciprocal(rec[:, pa, :], sume[:, pa, :]),
                  reads=[("sume", pa)], writes=[("rec", pa)], tag="recip")
            em.op("vector", lambda e, Wd=Wd, pa=pa: e.tensor_scalar(probs_bf2[:, pa, 0:Wd], probs_bf[:, pa, 0:Wd],
                                                                    rec[:, pa, :], None, ALU.mult),
                  reads=[("probs_bf", pa), ("rec", pa)], writes=[("probs_bf2", pa)], tag="pscale")

            nch = (Wd + 127) // 128
            for c in range(nch):
                w = min(128, Wd - c * 128)
                em.op("tensor", lambda e, c=c, w=w, pa=pa: e.transpose(ps_pT[0:w, :], probs_bf2[:, pa, c * 128:c * 128 + w], identb[:, :]),
                      reads=[("probs_bf2", pa), ("identb",)], writes=[("ps_pT",)], tag="tr_p")
                em.op("scalar", lambda e, c=c, w=w, pa=pa: e.copy(pT[0:w, pa, c, :], ps_pT[0:w, :]),
                      reads=[("ps_pT",)], writes=[("pT", pa, c)], tag="ev_p")

            av_pairs = []
            av_reads = []
            if t_loc == 0:
                av_pairs.append((v_roll[:, pa, h * 64:(h + 1) * 64], pT[:, pa, 0, :]))
                av_reads += [("v", pa, 0), ("v", pa, 1), ("pT", pa, 0)]
            else:
                av_pairs.append((v_roll[:, 1 - pa, h * 64:(h + 1) * 64], pT[:, pa, 0, :]))
                av_pairs.append((v_roll[:, pa, h * 64:(h + 1) * 64], pT[:, pa, 1, :]))
                av_reads += [("v", 1 - pa, 0), ("v", 1 - pa, 1), ("v", pa, 0), ("v", pa, 1),
                             ("pT", pa, 0), ("pT", pa, 1)]
                if g > 0:
                    av_pairs.append((vg[0:g, h * 64:(h + 1) * 64], pT[0:g, pa, 2, :]))
                    av_reads += [("vg", i) for i in range(g)] + [("pT", pa, 2)]
            tp = None if base == 0 else (0, 64)
            emit_mm_group(P, ps_av[base:base + 64, :], ("ps_av", h % 2), av_pairs,
                          reads=av_reads, tag="mm_av", tile_position=tp)
            if h % 2 == 1:
                em.op("vector", lambda e, hp=hp, pa=pa: e.tensor_copy(yT[:, pa, hp, :], ps_av[:, :]),
                      reads=[("ps_av", 0), ("ps_av", 1)], writes=[("yT", pa, hp)], tag="ev_av")

        for c in range(2):
            emit_mm_group(P, ps_pr[:, :], ("ps_pr",),
                          [(yT[:, pa, k, :], wproj[:, k, c * 512:(c + 1) * 512]) for k in range(ND)],
                          reads=[("yT", pa, k) for k in range(ND)] + [("wproj", k) for k in range(ND)],
                          tag="mm_pr")
            em.op("vector", lambda e, c=c, pa=pa: e.tensor_tensor(xo_buf[:, pa, c * 512:(c + 1) * 512],
                                                           ps_pr[:, :], x_cur[:, pa, c * 512:(c + 1) * 512], ALU.add),
                  reads=[("ps_pr",), ("x", pa)], writes=[("xo", pa, c)], tag="resid")
        em.dma("sync", xo_out[t * 128:(t + 1) * 128, :], xo_buf[:, pa, :],
               reads=[("xo", pa, 0), ("xo", pa, 1)], writes=[("xo_st", t)], tag="st_x")

    inst_ctx.close()
    em.barrier()
    return P


def _l0attn_masks():
    r = np.arange(128)[:, None]
    j = np.arange(256)[None, :]
    allowed = (j <= 128 + r) & ((j >= r + 1) | (j % 64 == 0))
    mask_band = np.where(allowed, 0.0, NEG).astype(np.float32)
    jj = np.arange(128)[None, :]
    mask_first = np.where(jj <= r, 0.0, NEG).astype(np.float32)
    return mask_band, mask_first


def run_l0attn(x_full, Wqkv, Wproj):
    P = build_l0attn()
    nc = P.close_build()
    mask_band, mask_first = _l0attn_masks()
    ident = np.eye(128, dtype=np.float32)
    in_maps = []
    xs = x_full.reshape(NCORES, T, D)
    for c in range(NCORES):
        in_maps.append({
            "x": np.ascontiguousarray(xs[c]),
            "Wqkv": Wqkv, "Wproj": Wproj,
            "identf": ident, "identb": np.eye(128, dtype=BFNP),
            "mask_band": mask_band, "mask_first": mask_first,
        })
    res = run_bass_kernel_spmd(nc, in_maps, core_ids=list(range(NCORES)))
    out = np.stack([res.results[c]["xo"] for c in range(NCORES)])
    return out.reshape(16, 2048, D)


CAP = 5120
MOE_PHASES = "ABC"
MOE_NT = NT
MOE_SKIP_CC = False


def build_moe(P=None, V=None, x_in=None, xo_out=None, bank=None):
    standalone = P is None
    if standalone:
        P = Prog("moe")
        V = {"wg": P.inp("wg", [D, 8]), "w1": P.inp("w1", [8, D, 4096]),
             "b1t": P.inp("b1t", [128, 8, 32]), "w2": P.inp("w2", [8, 4096, D]),
             "b2": P.inp("b2", [8, D]), "identf": P.inp("identf", [128, 128], F32),
             "tri": P.inp("tri", [128, 128], F32), "ones128": P.inp("ones128", [128, 1], F32),
             "ones1": P.inp("ones1", [1, 128], F32), "prefmask": P.inp("prefmask", [128, 1], F32),
             "ebase": P.inp("ebase", [1, 8], F32), "identb": P.inp("identb", [128, 128], BF16)}
        x_in = P.inp("x", [T, D])
        xo_out = P.out("xo", [T, D])
    em = P.em
    nc = P.nc
    wg_in = V["wg"]; w1_in = V["w1"]; b1t_in = V["b1t"]; w2_in = V["w2"]; b2_in = V["b2"]
    identf_in = V["identf"]; tri_in = V["tri"]; ones128_in = V["ones128"]
    ones1_in = V["ones1"]; prefmask_in = V["prefmask"]; ebase_in = V["ebase"]

    cc_in = P.dram("cc_in", [16, 128])
    cc_out = P.dram("cc_out", [128, 128])
    CAPL = 768
    NSLOT = 8 * CAPL
    xg_dram = P.dram("xg_moe", [NSLOT + T, D], BF16)
    yg_dram = P.dram("yg_moe", [NSLOT + T, D], BF16)

    inst_ctx = ExitStack()
    _sb = lambda n, sh, dt=F32: inst_ctx.enter_context(nc.sbuf_tensor(P.uniq(n), list(sh), dt))
    coef8 = _sb("coef8", [128, NT, 8], F32)
    coef1 = _sb("coef1", [128, NT, 1], F32)
    slot_i = _sb("slot_i", [128, NT, 1], I32)
    if bank is None:
        bank = [P.psum(f"bank{i}", [128, 512], F32) for i in range(8)]
    BK = lambda i: ("bank", i)


    with ExitStack() as sa:
        A = lambda name, shape, dt=F32: sa.enter_context(nc.sbuf_tensor(P.uniq(name), list(shape), dt))
        wg = A("wg", [128, ND, 8])
        ident = A("ident", [128, 128])
        tri = A("tri", [128, 128])
        ones128 = A("ones128", [128, 1])
        ones1 = A("ones1", [1, 128])
        prefmask = A("prefmask", [128, 1])
        x_cur = A("x_cur", [128, 2, D])
        lns = {
            "sq": A("sq", [128, 2, D]), "s": A("s", [128, 2, 1]), "s2": A("s2", [128, 2, 1]),
            "mu": A("mu", [128, 2, 1]), "m2": A("m2", [128, 2, 1]), "var": A("var", [128, 2, 1]),
            "lnv": A("lnv", [128, 2, 1]), "rstd": A("rstd", [128, 2, 1]),
            "xn": A("xn", [128, 2, D]), "tag": "lnm",
        }
        xg = A("xg", [128, 2, ND, 128])
        lg = A("lg", [128, 2, 8]); nmg = A("nmg", [128, 2, 1]); eg = A("eg", [128, 2, 8])
        sumg = A("sumg", [128, 2, 1]); eq = A("eq", [128, 2, 8]); cum = A("cum", [128, 2, 8])
        m1 = A("m1", [128, 2, 8]); zero8 = A("zero8", [128, 8])
        mask_sb = A("mask_sb", [128, NT, 8]); r1_sb = A("r1_sb", [128, NT, 8])
        g1_sb = A("g1_sb", [128, NT, 1])
        offs_sb = A("offs_sb", [1, NT, 8]); offs_run = A("offs_run", [1, 2, 8])
        tot_sb = A("tot_sb", [1, 2, 8])
        counts_st = A("counts_st", [16, 128]); allc = A("allc", [128, 128])
        base_sb = A("base_sb", [1, 8]); bvec = A("bvec", [1, 2, 8])
        posb = A("posb", [128, 2, 8]); keep = A("keep", [128, 2, 8])
        kc = A("kc", [128, 2, 8]); ks = A("ks", [128, 2, 1]); coefb = A("coefb", [128, 2, 1])
        ebase = A("ebase", [1, 8]); bvec2 = A("bvec2", [1, 2, 8])
        sl8 = A("sl8", [128, 2, 8]); slf = A("slf", [128, 2, 1])
        xntb = A("xntb", [128, D], BF16)

        for k in range(ND):
            em.dma("sync", wg[:, k, :], wg_in[k * 128:(k + 1) * 128, :], writes=[("wg",)], tag="ld_wg")
        em.dma("sync", ident[:, :], identf_in, writes=[("ident",)])
        em.dma("sync", tri[:, :], tri_in, writes=[("tri",)])
        em.dma("sync", ones128[:, :], ones128_in, writes=[("ones128",)])
        em.dma("sync", ones1[:, :], ones1_in, writes=[("ones1",)])
        em.dma("sync", prefmask[:, :], prefmask_in, writes=[("prefmask",)])
        em.dma("sync", ebase[:, :], ebase_in, writes=[("ebase",)], tag="ld_eb")
        zrow = A("zrow", [128, D], BF16)
        em.op("gpsimd", lambda e: e.memset(zrow[:, :], 0.0), writes=[("zrow",)], tag="zrow")
        for zi in range(48):
            em.dma_fn("scalar", lambda e, zi=zi: e.dma_start(
                out=xg_dram[zi * 128:(zi + 1) * 128, :], in_=zrow[:, :]),
                reads=[("zrow",)], writes=[("xgz",)], tag="zxg")
        em.op("gpsimd", lambda e: e.memset(zero8[:, :], 0.0), writes=[("zero8",)], tag="z8")
        em.op("gpsimd", lambda e: e.memset(counts_st[:, :], 0.0), writes=[("counts0",)], tag="z16")
        em.op("gpsimd", lambda e: e.memset(offs_run[:, 0, :], 0.0), writes=[("offs_run", 0)], tag="zoffs")

        for t in range(MOE_NT):
            pa = t % 2
            em.dma("sync", x_cur[:, pa, :], x_in[t * 128:(t + 1) * 128, :], writes=[("x", pa)], tag="ld_x")
            if MOE_PHASES == "A1a":
                continue
            xn_key = emit_ln_tile(P, lns, pa, x_cur[:, pa, :], ("x", pa))
            xn = lns["xn"][:, pa, :]
            em.op("scalar", lambda e, xn=xn: e.copy(xntb[:, :], xn),
                  reads=[xn_key], writes=[("xntb",)], tag="xntb")
            if MOE_PHASES == "A1b":
                continue
            for k in range(ND):
                em.op("tensor", lambda e, k=k, xn=xn: e.transpose(bank[0][:, 0:128], xn[:, k * 128:(k + 1) * 128], ident[:, :]),
                      reads=[xn_key, ("ident",)], writes=[BK(0)], tag="tr_xn")
                em.op("vector", lambda e, k=k, pa=pa: e.tensor_copy(xg[:, pa, k, :], bank[0][:, 0:128]),
                      reads=[BK(0)], writes=[("xg", pa, k)], tag="ev_xg")
            if MOE_PHASES == "A1":
                continue
            emit_mm_group(P, bank[1][:, 0:8], BK(1),
                          [(xg[:, pa, k, :], wg[:, k, :]) for k in range(ND)],
                          reads=[("xg", pa, k) for k in range(ND)] + [("wg",)], tag="mm_g")
            em.op("vector", lambda e, pa=pa: e.tensor_copy(lg[:, pa, :], bank[1][:, 0:8]),
                  reads=[BK(1)], writes=[("lg", pa)], tag="ev_g")
            em.op("vector", lambda e, pa=pa: e.reduce_max(nmg[:, pa, :], lg[:, pa, :], axis=AX, negate=True),
                  reads=[("lg", pa)], writes=[("nmg", pa)], tag="gmax")
            em.op("scalar", lambda e, pa=pa: e.activation(eg[:, pa, :], lg[:, pa, :], AF.Exp,
                                                          bias=nmg[:, pa, :], accum_out=sumg[:, pa, :]),
                  reads=[("lg", pa), ("nmg", pa)], writes=[("eg", pa), ("sumg", pa)], tag="gexp")
            em.op("vector", lambda e, pa=pa, t=t: e.reciprocal(g1_sb[:, t, :], sumg[:, pa, :]),
                  reads=[("sumg", pa)], writes=[("g1", t)], tag="g1")
            em.op("vector", lambda e, pa=pa: e.tensor_scalar(eq[:, pa, :], lg[:, pa, :], nmg[:, pa, :], 0.0,
                                                             ALU.add, ALU.is_equal),
                  reads=[("lg", pa), ("nmg", pa)], writes=[("eq", pa)], tag="eq")
            em.op("vector", lambda e, pa=pa: e.tensor_tensor_scan(cum[:, pa, :], eq[:, pa, :], zero8[:, :], 0.0,
                                                                  ALU.add, ALU.add),
                  reads=[("eq", pa), ("zero8",)], writes=[("cum", pa)], tag="cum")
            em.op("vector", lambda e, pa=pa: e.tensor_scalar(m1[:, pa, :], cum[:, pa, :], 1.0, None, ALU.is_equal),
                  reads=[("cum", pa)], writes=[("m1", pa)], tag="m1")
            em.op("vector", lambda e, pa=pa, t=t: e.tensor_tensor(mask_sb[:, t, :], eq[:, pa, :], m1[:, pa, :], ALU.mult),
                  reads=[("eq", pa), ("m1", pa)], writes=[("mask", t)], tag="mask")
            if MOE_PHASES == "A2":
                continue
            emit_mm_group(P, bank[2][:, 0:8], BK(2), [(tri[:, :], mask_sb[:, t, :])],
                          reads=[("tri",), ("mask", t)], tag="mm_r1")
            em.op("vector", lambda e, t=t: e.tensor_copy(r1_sb[:, t, :], bank[2][:, 0:8]),
                  reads=[BK(2)], writes=[("r1", t)], tag="ev_r1")
            emit_mm_group(P, bank[3][0:1, 0:8], BK(3), [(ones128[:, :], mask_sb[:, t, :])],
                          reads=[("ones128",), ("mask", t)], tag="mm_tot")
            em.op("vector", lambda e, t=t: e.tensor_copy(tot_sb[:, t % 2, :], bank[3][0:1, 0:8]),
                  reads=[BK(3)], writes=[("tot", t % 2)], tag="ev_tot")
            em.op("scalar", lambda e, t=t: e.copy(offs_sb[:, t, :], offs_run[:, t % 2, :]),
                  reads=[("offs_run", t % 2)], writes=[("offs", t)], tag="offs_cp")
            em.op("vector", lambda e, t=t: e.tensor_tensor(offs_run[:, (t + 1) % 2, :], offs_run[:, t % 2, :],
                                                           tot_sb[:, t % 2, :], ALU.add),
                  reads=[("offs_run", t % 2), ("tot", t % 2)], writes=[("offs_run", (t + 1) % 2)], tag="offs_add")
            em.op("vector", lambda e, t=t, pa=pa: e.tensor_tensor(bvec2[:, pa, :], offs_sb[:, t, :], ebase[:, :], ALU.add),
                  reads=[("offs", t), ("ebase",)], writes=[("bvec2", pa)], tag="bvec2")
            emit_mm_group(P, bank[6][:, 0:8], BK(6), [(ones1[:, :], bvec2[:, pa, :])],
                          reads=[("ones1",), ("bvec2", pa)], tag="mm_bc2")
            em.op("vector", lambda e, t=t, pa=pa: e.tensor_tensor(sl8[:, pa, :], r1_sb[:, t, :], bank[6][:, 0:8], ALU.add),
                  reads=[("r1", t), BK(6)], writes=[("sl8", pa)], tag="sl8a")
            em.op("vector", lambda e, t=t, pa=pa: e.tensor_tensor(sl8[:, pa, :], sl8[:, pa, :], mask_sb[:, t, :], ALU.mult),
                  reads=[("sl8", pa), ("mask", t)], writes=[("sl8", pa)], tag="sl8b")
            em.op("vector", lambda e, pa=pa: e.reduce_sum(slf[:, pa, :], sl8[:, pa, :], axis=AX),
                  reads=[("sl8", pa)], writes=[("slf", pa)], tag="slot_f")
            em.op("vector", lambda e, t=t, pa=pa: e.tensor_copy(slot_i[:, t, :], slf[:, pa, :]),
                  reads=[("slf", pa)], writes=[("slot", t)], tag="slot_i")
            em.dma_fn("gpsimd", lambda e, t=t: e.indirect_dma_start(
                xg_dram, IndirectOffsetOnAxis(slot_i[:, t, :], 0), xntb[:, :], None),
                reads=[("xntb",), ("slot", t), ("xgz",)], writes=[("xg", t)], tag="scat")

        if MOE_PHASES in ("A1", "A2", "A3"):
            for t in range(NT):
                em.dma("sync", xo_out[t * 128:(t + 1) * 128, :], x_in[t * 128:(t + 1) * 128, :],
                       writes=[("xo_st", t)], tag="st_dummy")
            return P
        em.op("scalar", lambda e: e.copy(counts_st[0:1, 0:8], offs_run[:, NT % 2, :]),
              reads=[("offs_run", NT % 2), ("counts0",)], writes=[("counts_st",)], tag="cnt_cp")
        em.dma("gpsimd", cc_in, counts_st[:, :],
               reads=[("counts_st",)], writes=[("cc_in",)], tag="cnt_st")
        if MOE_SKIP_CC:
            for r in range(NCORES):
                em.dma("gpsimd", cc_out[16 * r:16 * (r + 1), :], cc_in,
                       reads=[("cc_in",)], writes=[("cc_out",)], tag="fake_ag")
        else:
            em.dma_fn("gpsimd",
                      lambda e: e.collective_compute("AllGather", ALU.bypass,
                                                     replica_groups=[list(range(NCORES))],
                                                     ins=[cc_in.opt()],
                                                     outs=[cc_out.opt()]),
                      reads=[("cc_in",)], writes=[("cc_out",)], tag="allgather", inc=1)
        em.dma("gpsimd", allc[:, :], cc_out, reads=[("cc_out",)], writes=[("allc",)], tag="cnt_ld")
        emit_mm_group(P, bank[4][0:1, 0:8], BK(4), [(prefmask[:, :], allc[:, 0:8])],
                      reads=[("prefmask",), ("allc",)], tag="mm_base")
        em.op("vector", lambda e: e.tensor_copy(base_sb[:, :], bank[4][0:1, 0:8]),
              reads=[BK(4)], writes=[("base",)], tag="ev_base")

        for t in range(NT):
            pa = t % 2
            em.op("vector", lambda e, t=t, pa=pa: e.tensor_tensor(bvec[:, pa, :], offs_sb[:, t, :], base_sb[:, :], ALU.add),
                  reads=[("offs", t), ("base",)], writes=[("bvec", pa)], tag="bvec")
            emit_mm_group(P, bank[5][:, 0:8], BK(5), [(ones1[:, :], bvec[:, pa, :])],
                          reads=[("ones1",), ("bvec", pa)], tag="mm_bc")
            em.op("vector", lambda e, t=t, pa=pa: e.tensor_tensor(posb[:, pa, :], r1_sb[:, t, :], bank[5][:, 0:8], ALU.add),
                  reads=[("r1", t), BK(5)], writes=[("posb", pa)], tag="posb")
            em.op("vector", lambda e, pa=pa: e.tensor_scalar(keep[:, pa, :], posb[:, pa, :], float(CAP), None, ALU.is_le),
                  reads=[("posb", pa)], writes=[("keep", pa)], tag="keep")
            em.op("vector", lambda e, t=t, pa=pa: e.tensor_tensor(kc[:, pa, :], mask_sb[:, t, :], keep[:, pa, :], ALU.mult),
                  reads=[("mask", t), ("keep", pa)], writes=[("kc", pa)], tag="kc")
            em.op("vector", lambda e, pa=pa: e.reduce_sum(ks[:, pa, :], kc[:, pa, :], axis=AX),
                  reads=[("kc", pa)], writes=[("ks", pa)], tag="ks")
            em.op("vector", lambda e, t=t, pa=pa: e.tensor_scalar(coefb[:, pa, :], g1_sb[:, t, :], ks[:, pa, :], 0.5,
                                                                  ALU.mult, ALU.mult),
                  reads=[("g1", t), ("ks", pa)], writes=[("coef", pa)], tag="coef")
            em.op("vector", lambda e, t=t, pa=pa: e.tensor_scalar(coef8[:, t, :], mask_sb[:, t, :], coefb[:, pa, :], None,
                                                                  ALU.mult),
                  reads=[("mask", t), ("coef", pa)], writes=[("coef8", t)], tag="coef8")
            em.op("scalar", lambda e, t=t, pa=pa: e.copy(coef1[:, t, :], coefb[:, pa, :]),
                  reads=[("coef", pa)], writes=[("coef1", t)], tag="coef1")
    em.barrier()

    if "B" not in MOE_PHASES:
        for t in range(NT):
            em.dma("sync", xo_out[t * 128:(t + 1) * 128, :], x_in[t * 128:(t + 1) * 128, :],
                   writes=[("xo_st", t)], tag="st_dummy")
        return P

    with ExitStack() as sb_:
        A = lambda name, shape, dt=BF16: sb_.enter_context(nc.sbuf_tensor(P.uniq(name), list(shape), dt))
        w1e = A("w1e", [128, ND, 2048])
        w2e = A("w2e", [128, 32, D])
        xg_tok = A("xg_tok", [128, 6, D])
        XgT = A("XgT", [128, ND, 768])
        hn = A("hn", [128, 32, 768])
        ygb = A("ygb", [128, 2, D])
        b1t = A("b1t", [128, 8, 32], F32)
        identB = A("identB", [128, 128], BF16)
        em.dma("sync", b1t[:, :, :], b1t_in, writes=[("b1t",)], tag="ld_b1t")
        em.dma("sync", identB[:, :], V["identb"], writes=[("identB",)], tag="ld_idb")
        NCK = (768 + 511) // 512
        for e_i in range(8):
            for s_ in range(6):
                em.dma_fn("sync", lambda e, e_i=e_i, s_=s_: e.dma_start(
                    out=xg_tok[:, s_, :], in_=xg_dram[e_i * 768 + s_ * 128:e_i * 768 + (s_ + 1) * 128, :]),
                    reads=[("xg", t) for t in range(NT)], writes=[("xg_tok", s_)], tag="ld_xg")
                for k in range(ND):
                    em.op("tensor", lambda e, s_=s_, k=k: e.transpose(
                        bank[0][:, :].bitcast(BF16)[:, 0:128], xg_tok[:, s_, k * 128:(k + 1) * 128], identB[:, :]),
                        reads=[("xg_tok", s_), ("identB",)], writes=[BK(0)], tag="tr_xg")
                    em.op("vector", lambda e, s_=s_, k=k: e.tensor_copy(
                        XgT[:, k, s_ * 128:(s_ + 1) * 128], bank[0][:, :].bitcast(BF16)[:, 0:128]),
                        reads=[BK(0)], writes=[("XgT", k)], tag="ev_xg2")
            for fb in range(2):
                em.dma_fn("gpsimd", lambda e, e_i=e_i, fb=fb: e.dma_start(
                    out=w1e[:, :, :],
                    in_=w1_in[e_i].rearrange("(k p) c -> p k c", p=128)[:, :, fb * 2048:(fb + 1) * 2048]),
                    writes=[("w1e", k) for k in range(ND)], tag="ld_w1")
                w1r = [("w1e", k) for k in range(ND)]
                xgr = [("XgT", k) for k in range(ND)]
                for n in range(NCK):
                    c0 = n * 512
                    cw = min(768, c0 + 512) - c0
                    for m in range(16):
                        bk = m % 4
                        mg = fb * 16 + m
                        emit_mm_group(P, bank[bk][:, 0:cw], BK(bk),
                                      [(w1e[:, k, m * 128:(m + 1) * 128], XgT[:, k, c0:c0 + cw]) for k in range(ND)],
                                      reads=w1r + xgr, tag="mm_w1")
                        em.op("scalar", lambda e, bk=bk, e_i=e_i, mg=mg, c0=c0, cw=cw: e.activation(
                            hn[:, mg, c0:c0 + cw], bank[bk][:, 0:cw], AF.Relu, bias=b1t[:, e_i, mg:mg + 1]),
                            reads=[BK(bk), ("b1t",)], writes=[("hn", mg)], tag="ev_h")
            em.dma_fn("gpsimd", lambda e, e_i=e_i: e.dma_start(
                out=w2e[:, :, :], in_=w2_in[e_i].rearrange("(k p) c -> p k c", p=128)),
                writes=[("w2e", kf) for kf in range(32)], tag="ld_w2")
            w2r = [("w2e", kf) for kf in range(32)]
            for sub in range(6):
                for dc in range(2):
                    bk = 4 + (sub * 2 + dc) % 4
                    emit_mm_group(P, bank[bk][:, :], BK(bk),
                                  [(hn[:, kf, sub * 128:(sub + 1) * 128], w2e[:, kf, dc * 512:(dc + 1) * 512])
                                   for kf in range(32)],
                                  reads=[("hn", kf) for kf in range(32)] + w2r, tag="mm_w2")
                    em.op("vector", lambda e, bk=bk, dc=dc, sub=sub: e.tensor_copy(
                        ygb[:, sub % 2, dc * 512:(dc + 1) * 512], bank[bk][:, :]),
                        reads=[BK(bk)], writes=[("ygb", sub % 2, dc)], tag="ev_y")
                em.dma_fn("sync", lambda e, e_i=e_i, sub=sub: e.dma_start(
                    out=yg_dram[e_i * 768 + sub * 128:e_i * 768 + (sub + 1) * 128, :], in_=ygb[:, sub % 2, :]),
                    reads=[("ygb", sub % 2, 0), ("ygb", sub % 2, 1)], writes=[("yg", e_i, sub)], tag="st_yg")
    em.barrier()

    with ExitStack() as sc:
        A = lambda name, shape, dt=F32: sc.enter_context(nc.sbuf_tensor(P.uniq(name), list(shape), dt))
        x_cur = A("x_cur", [128, 2, D])
        xo_buf = A("xo_buf", [128, 2, D])
        c8T = A("c8T", [8, NT, 128], BF16)
        c8b = A("c8b", [128, 2, 8], BF16)
        b2s = A("b2s", [8, D], BF16)
        identc = A("identc", [128, 128], BF16)
        em.dma("gpsimd", b2s[:, :], b2_in, writes=[("b2s",)], tag="ld_b2")
        em.dma("gpsimd", identc[:, :], identf_in, writes=[("identc",)], tag="ld_idc")
        ytok = sc.enter_context(nc.sbuf_tensor(P.uniq("ytok"), [128, D], BF16))
        for t in range(NT):
            pa = t % 2
            em.dma("sync", x_cur[:, pa, :], x_in[t * 128:(t + 1) * 128, :], writes=[("x", pa)], tag="ld_x2")
            em.dma_fn("gpsimd", lambda e, t=t: e.indirect_dma_start(
                ytok[:, :], None, yg_dram, IndirectOffsetOnAxis(slot_i[:, t, :], 0)),
                reads=[("slot", t)] + [("yg", e2, sub) for e2 in range(8) for sub in range(6)],
                writes=[("ytok",)], tag="gath_y")
            em.op("vector", lambda e, t=t, pa=pa: e.tensor_copy(c8b[:, pa, :], coef8[:, t, :]),
                  reads=[("coef8", t)], writes=[("c8b", pa)], tag="c8cast")
            em.op("tensor", lambda e, t=t, pa=pa: e.transpose(bank[6][:, :].bitcast(BF16)[0:8, 0:128], c8b[:, pa, :], identc[:, :]),
                  reads=[("c8b", pa), ("identc",)], writes=[BK(6)], tag="tr_c8")
            em.op("scalar", lambda e, t=t: e.copy(c8T[:, t, :], bank[6][:, :].bitcast(BF16)[0:8, 0:128]),
                  reads=[BK(6)], writes=[("c8T", t)], tag="ev_c8")
            for dc in range(2):
                emit_mm_group(P, bank[dc][:, :], BK(dc),
                              [(c8T[:, t, :], b2s[:, dc * 512:(dc + 1) * 512])],
                              reads=[("c8T", t), ("b2s",)], tag="mm_b2")
                em.op("vector", lambda e, t=t, pa=pa, dc=dc: e.scalar_tensor_tensor(
                    xo_buf[:, pa, dc * 512:(dc + 1) * 512], ytok[:, dc * 512:(dc + 1) * 512],
                    coef1[:, t, :], x_cur[:, pa, dc * 512:(dc + 1) * 512], ALU.mult, ALU.add),
                    reads=[("x", pa), ("ytok",), ("coef1", t)], writes=[("xo", pa, dc)], tag="resid1")
                em.op("vector", lambda e, pa=pa, dc=dc: e.tensor_tensor(
                    xo_buf[:, pa, dc * 512:(dc + 1) * 512], xo_buf[:, pa, dc * 512:(dc + 1) * 512],
                    bank[dc][:, :], ALU.add),
                    reads=[("xo", pa, dc), BK(dc)], writes=[("xo", pa, dc)], tag="resid2")
            em.dma("sync", xo_out[t * 128:(t + 1) * 128, :], xo_buf[:, pa, :],
                   reads=[("xo", pa, 0), ("xo", pa, 1)], writes=[("xo_st", t)], tag="st_x")
    inst_ctx.close()
    em.barrier()
    return P


def _moe_consts(rank):
    tri = (np.arange(128)[:, None] <= np.arange(128)[None, :]).astype(np.float32)
    ones128 = np.ones((128, 1), np.float32)
    ones1 = np.ones((1, 128), np.float32)
    pm = np.zeros((128, 1), np.float32)
    for r in range(rank):
        pm[16 * r, 0] = 1.0
    return tri, ones128, ones1, pm


def _moe_ebase():
    return (np.arange(8, dtype=np.float32) * 768.0 - 1.0).reshape(1, 8)


def run_moe(x_full, wg, w1, b1, w2, b2):
    P = build_moe()
    nc = P.close_build()
    tri, ones128, ones1, _ = _moe_consts(0)
    ident = np.eye(128, dtype=np.float32)
    b1t = np.ascontiguousarray(b1.reshape(8, 32, 128).transpose(2, 0, 1))
    xs = x_full.reshape(NCORES, T, D)
    in_maps = []
    for c in range(NCORES):
        in_maps.append({
            "x": np.ascontiguousarray(xs[c]), "wg": wg, "w1": w1, "b1t": b1t,
            "w2": w2, "b2": b2, "identf": ident, "tri": tri,
            "ones128": ones128, "ones1": ones1, "prefmask": _moe_consts(c)[3], "ebase": _moe_ebase(),
            "identb": np.eye(128, dtype=BFNP),
        })
    res = run_bass_kernel_spmd(nc, in_maps, core_ids=list(range(NCORES)))
    out = np.stack([res.results[c]["xo"] for c in range(NCORES)])
    return out.reshape(16, 2048, D)


def build_l1attn(P=None, V=None, x_in=None, xo_out=None, bank=None):
    standalone = P is None
    if standalone:
        P = Prog("l1attn")
        V = {"wmixt": P.inp("wmixt", [128, ND]), "WgT": P.inp("WgT", [D, D]),
             "bgt": P.inp("bgt", [128, ND]), "WvT": P.inp("WvT", [D, D]),
             "bvt": P.inp("bvt", [128, ND]), "WdT": P.inp("WdT", [D, D]),
             "bdt": P.inp("bdt", [128, ND]), "wattnt": P.inp("wattnt", [128, ND]),
             "WqkvT": P.inp("WqkvT", [D, 3 * D]), "WoT": P.inp("WoT", [D, D]),
             "wffnt": P.inp("wffnt", [128, ND]), "WgateT": P.inp("WgateT", [D, 4096]),
             "WupT": P.inp("WupT", [D, 4096]), "WoutT": P.inp("WoutT", [4096, D]),
             "identf": P.inp("identf", [128, 128], F32), "identb": P.inp("identb", [128, 128], BF16),
             "mask_band_l1": P.inp("mask_band", [128, 256], F32),
             "mask_first": P.inp("mask_first", [128, 128], F32)}
        x_in = P.inp("x", [T, D])
        xo_out = P.out("xo", [T, D])
    em = P.em
    nc = P.nc
    wmix_in = V["wmixt"]; WgT_in = V["WgT"]; bg_in = V["bgt"]; WvT_in = V["WvT"]
    bv_in = V["bvt"]; WdT_in = V["WdT"]; bd_in = V["bdt"]; wattn_in = V["wattnt"]
    WqkvT_in = V["WqkvT"]; WoT_in = V["WoT"]; wffn_in = V["wffnt"]
    WgateT_in = V["WgateT"]; WupT_in = V["WupT"]; WoutT_in = V["WoutT"]
    identf_in = V["identf"]; identb_in = V["identb"]
    mask_band_in = V["mask_band_l1"]; mask_first_in = V["mask_first"]

    xd0 = P.dram("xd0", [T, D])
    xd1 = P.dram("xd1", [T, D])
    xd2 = P.dram("xd2", [T, D])
    xd3 = P.dram("xd3", [T, D])

    if bank is None:
        bank = [P.psum(f"bank{i}", [128, 512], F32) for i in range(8)]
    BK = lambda i: ("bank", i)
    NCH = 8
    CHT = 512

    with ExitStack() as s0:
        A = lambda n, sh, dt=F32: s0.enter_context(nc.sbuf_tensor(P.uniq(n), list(sh), dt))
        xc0 = A("xc0", [128, 2, D])
        lns0 = {k: A("l0" + k, [128, 2, D if k in ("sq", "xn") else 1]) for k in
               ("sq", "s", "s2", "mu", "m2", "var", "lnv", "rstd", "xn")}
        lns0["tag"] = "lnA"
        for t in range(NT):
            pa = t % 2
            em.dma("sync", xc0[:, pa, :], x_in[t * 128:(t + 1) * 128, :], writes=[("x", pa)], tag="ld_x")
            xk = emit_ln_tile(P, lns0, pa, xc0[:, pa, :], ("x", pa))
            em.dma("sync", xd0[t * 128:(t + 1) * 128, :], lns0["xn"][:, pa, :], reads=[xk],
                   writes=[("xd0", t)], tag="st0")
    em.barrier()

    with ExitStack() as s1:
        A = lambda n, sh, dt=F32: s1.enter_context(nc.sbuf_tensor(P.uniq(n), list(sh), dt))
        WgT = A("WgT", [128, ND, D], BF16)
        WvT = A("WvT", [128, ND, D], BF16)
        WdT = A("WdT", [128, ND, D], BF16)
        wmix = A("wmix", [128, ND])
        bgt = A("bgt", [128, ND])
        bvt = A("bvt", [128, ND])
        bdt = A("bdt", [128, ND])
        identa = A("identa", [128, 128])
        identba = A("identb", [128, 128], BF16)
        xra = A("xra", [128, 2, 4, D])
        rmsa = {k: A("r1" + k, [128, 2, D if k in ("sq", "xn") else 1]) for k in
               ("sq", "s", "s2", "mu", "m2", "var", "lnv", "rstd", "xn")}
        rmsa["tag"] = "rmsA"
        xmTa = A("xmTa", [128, 2, ND, CHT], BF16)
        sg = A("sg", [128, 2, CHT])
        tv = A("tv", [128, 2, CHT])
        sd = A("sd", [128, 2, CHT])
        aa = A("aa", [128, 2, CHT])
        xs = A("xs", [128, 2, CHT], BF16)
        hh = A("hh", [128, 2, CHT], BF16)
        carry = A("carry", [128, ND, 1])
        for (wt, win, wkey) in ((WgT, WgT_in, "WgT"), (WvT, WvT_in, "WvT"), (WdT, WdT_in, "WdT")):
            em.dma_fn("gpsimd", lambda e, wt=wt, win=win: e.dma_start(
                out=wt[:, :, :], in_=win.rearrange("(k p) c -> p k c", p=128)),
                writes=[(wkey,)], tag="ldw")
        em.dma("sync", wmix[:, :], wmix_in, writes=[("wmix",)])
        em.dma("sync", bgt[:, :], bg_in, writes=[("bgt",)])
        em.dma("sync", bvt[:, :], bv_in, writes=[("bvt",)])
        em.dma("sync", bdt[:, :], bd_in, writes=[("bdt",)])
        em.dma("sync", identa[:, :], identf_in, writes=[("identa",)])
        em.dma("sync", identba[:, :], identb_in, writes=[("identb",)])
        for n in range(NCH):
            pa = n % 2
            tok0 = n * CHT
            for s_ in range(4):
                t = n * 4 + s_
                em.dma("sync", xra[:, pa, s_, :], xd0[t * 128:(t + 1) * 128, :],
                       writes=[("xra", pa, s_)], tag="ld_c")
                xk = emit_ln_tile(P, rmsa, pa, xra[:, pa, s_, :], ("xra", pa, s_), rms=True, eps=1e-6)
                xn = rmsa["xn"][:, pa, :]
                for k in range(ND):
                    em.op("tensor", lambda e, k=k, xn=xn: e.transpose(bank[6][:, 0:128], xn[:, k * 128:(k + 1) * 128], identa[:, :]),
                          reads=[xk, ("identa",)], writes=[BK(6)], tag="trm")
                    em.op("vector", lambda e, k=k, pa=pa, s_=s_: e.tensor_scalar(
                        xmTa[:, pa, k, s_ * 128:(s_ + 1) * 128], bank[6][:, 0:128], wmix[:, k:k + 1], None, ALU.mult),
                        reads=[BK(6), ("wmix",)], writes=[("xmTa", pa, k)], tag="evm")
            xmr = [("xmTa", pa, k) for k in range(ND)]
            for m in range(ND):
                for (wt, wkey, bk) in ((WgT, "WgT", 0), (WvT, "WvT", 1), (WdT, "WdT", 2)):
                    emit_mm_group(P, bank[bk][:, :], BK(bk),
                                  [(wt[:, k, m * 128:(m + 1) * 128], xmTa[:, pa, k, :]) for k in range(ND)],
                                  reads=xmr + [(wkey,)], tag="mm_gvd")
                em.op("scalar", lambda e, m=m, pa=pa: e.activation(sg[:, pa, :], bank[0][:, :], AF.Sigmoid,
                                                                   bias=bgt[:, m:m + 1]),
                      reads=[BK(0), ("bgt",)], writes=[("sg", pa)], tag="ev_g")
                em.op("scalar", lambda e, m=m, pa=pa: e.activation(tv[:, pa, :], bank[1][:, :], AF.Tanh,
                                                                   bias=bvt[:, m:m + 1]),
                      reads=[BK(1), ("bvt",)], writes=[("tv", pa)], tag="ev_v")
                em.op("scalar", lambda e, m=m, pa=pa: e.activation(sd[:, pa, :], bank[2][:, :], AF.Sigmoid,
                                                                   bias=bdt[:, m:m + 1]),
                      reads=[BK(2), ("bdt",)], writes=[("sd", pa)], tag="ev_d")
                em.op("vector", lambda e, pa=pa: e.tensor_mul(xs[:, pa, :], sg[:, pa, :], tv[:, pa, :]),
                      reads=[("sg", pa), ("tv", pa)], writes=[("xs", pa)], tag="xs")
                em.op("vector", lambda e, pa=pa: e.tensor_scalar(aa[:, pa, :], sd[:, pa, :], 0.998, 0.001,
                                                                 ALU.mult, ALU.add),
                      reads=[("sd", pa)], writes=[("aa", pa)], tag="aa")
                if n % 4 == 0:
                    em.op("vector", lambda e, m=m, pa=pa: e.tensor_tensor_scan(
                        hh[:, pa, :], aa[:, pa, :], xs[:, pa, :], 0.0, ALU.mult, ALU.add),
                        reads=[("aa", pa), ("xs", pa)], writes=[("hh", pa)], tag="scan")
                else:
                    em.op("vector", lambda e, m=m, pa=pa: e.tensor_tensor_scan(
                        hh[:, pa, :], aa[:, pa, :], xs[:, pa, :], carry[:, m, :], ALU.mult, ALU.add),
                        reads=[("aa", pa), ("xs", pa), ("carry", m)], writes=[("hh", pa)], tag="scan")
                em.op("vector", lambda e, m=m, pa=pa: e.tensor_copy(carry[:, m, :], hh[:, pa, CHT - 1:CHT]),
                      reads=[("hh", pa)], writes=[("carry", m)], tag="carry")
                for s_ in range(4):
                    em.op("tensor", lambda e, m=m, pa=pa, s_=s_: e.transpose(
                        bank[7][:, :].bitcast(BF16)[:, 0:128], hh[:, pa, s_ * 128:(s_ + 1) * 128], identba[:, :]),
                        reads=[("hh", pa), ("identb",)], writes=[BK(7)], tag="trh")
                    em.op("vector", lambda e, m=m, pa=pa, s_=s_: e.tensor_tensor(
                        xra[:, pa, s_, m * 128:(m + 1) * 128], xra[:, pa, s_, m * 128:(m + 1) * 128],
                        bank[7][:, :].bitcast(BF16)[:, 0:128], ALU.add),
                        reads=[BK(7), ("xra", pa, s_)], writes=[("xra", pa, s_)], tag="addh")
            for s_ in range(4):
                t = n * 4 + s_
                em.dma("sync", xd1[t * 128:(t + 1) * 128, :], xra[:, pa, s_, :],
                       reads=[("xra", pa, s_)], writes=[("xd1", t)], tag="st1")
    em.barrier()

    with ExitStack() as s2:
        A = lambda n, sh, dt=F32: s2.enter_context(nc.sbuf_tensor(P.uniq(n), list(sh), dt))
        WqkvT = A("WqkvT", [128, ND, 3 * D], BF16)
        WoT = A("WoT", [128, ND, D], BF16)
        wattn = A("wattn", [128, ND])
        identb2 = A("identb2", [128, 128])
        identbb = A("identbb", [128, 128], BF16)
        mband = A("mband", [128, 256])
        mfirst = A("mfirst", [128, 128])
        xcb = A("xcb", [128, 2, D])
        rmsb = {k: A("r2" + k, [128, 2, D if k in ("sq", "xn") else 1]) for k in
               ("sq", "s", "s2", "mu", "m2", "var", "lnv", "rstd", "xn")}
        rmsb["tag"] = "rmsB"
        xmTb = A("xmTb", [128, 2, ND, 128], BF16)
        qTb = A("qTb", [128, 2, ND, 128], BF16)
        kTb = A("kTb", [128, 2, ND, 128], BF16)
        v_rollb = A("v_rollb", [128, 2, D], BF16)
        probsb = A("probsb", [128, 2, 256])
        probsb_bf = A("probsb_bf", [128, 2, 256], BF16)
        probsb_bf2 = A("probsb_bf2", [128, 2, 256], BF16)
        pTb2 = A("pTb2", [128, 2, 2, 128], BF16)
        nmb = A("nmb", [128, 2, 1])
        sumeb = A("sumeb", [128, 2, 1])
        recb = A("recb", [128, 2, 1])
        yTb = A("yTb", [128, 2, ND, 128], BF16)
        xob = A("xob", [128, 2, D])
        em.dma_fn("gpsimd", lambda e: e.dma_start(
            out=WqkvT[:, :, :], in_=WqkvT_in.rearrange("(k p) c -> p k c", p=128)),
            writes=[("WqkvT", k) for k in range(ND)], tag="ldw")
        em.dma_fn("gpsimd", lambda e: e.dma_start(
            out=WoT[:, :, :], in_=WoT_in.rearrange("(k p) c -> p k c", p=128)),
            writes=[("WoT", k) for k in range(ND)], tag="ldw")
        em.dma("sync", wattn[:, :], wattn_in, writes=[("wattn",)])
        em.dma("sync", identb2[:, :], identf_in, writes=[("identb2",)])
        em.dma("sync", identbb[:, :], identb_in, writes=[("identbb",)])
        em.dma("sync", mband[:, :], mask_band_in, writes=[("mband",)])
        em.dma("sync", mfirst[:, :], mask_first_in, writes=[("mfirst",)])
        wqr = [("WqkvT", k) for k in range(ND)]
        for t in range(NT):
            pa = t % 2
            t_loc = t % NT_SEQ
            em.dma("sync", xcb[:, pa, :], xd1[t * 128:(t + 1) * 128, :], writes=[("x", pa)],
                   deps=[], tag="ld_x")
            xk = emit_ln_tile(P, rmsb, pa, xcb[:, pa, :], ("x", pa), rms=True, eps=1e-6)
            xn = rmsb["xn"][:, pa, :]
            for k in range(ND):
                em.op("tensor", lambda e, k=k, xn=xn: e.transpose(bank[6][:, 0:128], xn[:, k * 128:(k + 1) * 128], identb2[:, :]),
                      reads=[xk, ("identb2",)], writes=[BK(6)], tag="trb")
                em.op("vector", lambda e, k=k, pa=pa: e.tensor_scalar(
                    xmTb[:, pa, k, :], bank[6][:, 0:128], wattn[:, k:k + 1], None, ALU.mult),
                    reads=[BK(6), ("wattn",)], writes=[("xmTb", pa, k)], tag="evb")
            xmr = [("xmTb", pa, k) for k in range(ND)]
            for hp in range(ND):
                bk = hp % 2
                emit_mm_group(P, bank[bk][:, 0:128], BK(bk),
                              [(WqkvT[:, k, hp * 128:(hp + 1) * 128], xmTb[:, pa, k, :]) for k in range(ND)],
                              reads=xmr + wqr, tag="mm_q")
                em.op("scalar", lambda e, hp=hp, bk=bk, pa=pa: e.activation(qTb[:, pa, hp, :], bank[bk][:, 0:128],
                                                                            AF.Copy, scale=0.125),
                      reads=[BK(bk)], writes=[("qTb", pa, hp)], tag="ev_q")
                emit_mm_group(P, bank[bk][:, 0:128], BK(bk),
                              [(WqkvT[:, k, D + hp * 128:D + (hp + 1) * 128], xmTb[:, pa, k, :]) for k in range(ND)],
                              reads=xmr + wqr, tag="mm_k")
                em.op("scalar", lambda e, hp=hp, bk=bk, pa=pa: e.copy(kTb[:, pa, hp, :], bank[bk][:, 0:128]),
                      reads=[BK(bk)], writes=[("kTb", pa, hp)], tag="ev_k")
            for c in range(2):
                emit_mm_group(P, bank[2][:, :], BK(2),
                              [(xmTb[:, pa, k, :], WqkvT[:, k, 2 * D + c * 512:2 * D + (c + 1) * 512]) for k in range(ND)],
                              reads=xmr + wqr, tag="mm_v")
                em.op("vector", lambda e, c=c, pa=pa: e.tensor_copy(v_rollb[:, pa, c * 512:(c + 1) * 512], bank[2][:, :]),
                      reads=[BK(2)], writes=[("v", pa, c)], tag="ev_v")
            Wd_ = 128 if t_loc == 0 else 256
            for h in range(16):
                hp = h // 2
                base = (h % 2) * 64
                q_ap = qTb[base:base + 64, pa, hp, :]

                def fn_sc(e, q_ap=q_ap, pa=pa, hp=hp, base=base, t_loc=t_loc):
                    if t_loc == 0:
                        return e.matmul(bank[3][:, 0:128], q_ap, kTb[base:base + 64, pa, hp, :], start=True, stop=True)
                    e.matmul(bank[3][:, 0:128], q_ap, kTb[base:base + 64, 1 - pa, hp, :], start=True, stop=True)
                    return e.matmul(bank[3][:, 128:256], q_ap, kTb[base:base + 64, pa, hp, :], start=True, stop=True)
                sc_reads = [("qTb", pa, hp), ("kTb", pa, hp)] + ([("kTb", 1 - pa, hp)] if t_loc > 0 else [])
                em.op("tensor", fn_sc, reads=sc_reads, writes=[BK(3)], tag="mm_sc")
                pr = probsb[:, pa, :]
                if t_loc == 0:
                    em.op("vector", lambda e, pr=pr: e.tensor_tensor(pr[:, 0:128], bank[3][:, 0:128], mfirst[:, :], ALU.add),
                          reads=[BK(3), ("mfirst",)], writes=[("probsb", pa)], tag="mask0")
                else:
                    em.op("vector", lambda e, pr=pr: e.tensor_tensor(pr[:, 0:256], bank[3][:, 0:256], mband[:, :], ALU.add),
                          reads=[BK(3), ("mband",)], writes=[("probsb", pa)], tag="mask")
                em.op("vector", lambda e, pr=pr, Wd_=Wd_, pa=pa: e.reduce_max(nmb[:, pa, :], pr[:, 0:Wd_], axis=AX, negate=True),
                      reads=[("probsb", pa)], writes=[("nmb", pa)], tag="negmax")
                em.op("scalar", lambda e, pr=pr, Wd_=Wd_, pa=pa: e.activation(probsb_bf[:, pa, 0:Wd_], pr[:, 0:Wd_], AF.Exp,
                                                                              bias=nmb[:, pa, :], accum_out=sumeb[:, pa, :]),
                      reads=[("probsb", pa), ("nmb", pa)], writes=[("probsb_bf", pa), ("sumeb", pa)], tag="exp")
                em.op("vector", lambda e, pa=pa: e.reciprocal(recb[:, pa, :], sumeb[:, pa, :]),
                      reads=[("sumeb", pa)], writes=[("recb", pa)], tag="recip")
                em.op("vector", lambda e, Wd_=Wd_, pa=pa: e.tensor_scalar(probsb_bf2[:, pa, 0:Wd_], probsb_bf[:, pa, 0:Wd_],
                                                                          recb[:, pa, :], None, ALU.mult),
                      reads=[("probsb_bf", pa), ("recb", pa)], writes=[("probsb_bf2", pa)], tag="pscale")
                nch = Wd_ // 128
                for c in range(nch):
                    em.op("tensor", lambda e, c=c, pa=pa: e.transpose(bank[4][:, :].bitcast(BF16)[:, 0:128],
                                                                      probsb_bf2[:, pa, c * 128:(c + 1) * 128], identbb[:, :]),
                          reads=[("probsb_bf2", pa), ("identbb",)], writes=[BK(4)], tag="tr_p")
                    em.op("scalar", lambda e, c=c, pa=pa: e.copy(pTb2[:, pa, c, :], bank[4][:, :].bitcast(BF16)[:, 0:128]),
                          reads=[BK(4)], writes=[("pTb2", pa, c)], tag="ev_p")
                av_pairs = []
                av_reads = []
                if t_loc == 0:
                    av_pairs.append((v_rollb[:, pa, h * 64:(h + 1) * 64], pTb2[:, pa, 0, :]))
                    av_reads += [("v", pa, 0), ("v", pa, 1), ("pTb2", pa, 0)]
                else:
                    av_pairs.append((v_rollb[:, 1 - pa, h * 64:(h + 1) * 64], pTb2[:, pa, 0, :]))
                    av_pairs.append((v_rollb[:, pa, h * 64:(h + 1) * 64], pTb2[:, pa, 1, :]))
                    av_reads += [("v", 1 - pa, 0), ("v", 1 - pa, 1), ("v", pa, 0), ("v", pa, 1),
                                 ("pTb2", pa, 0), ("pTb2", pa, 1)]
                tp = None if base == 0 else (0, 64)
                emit_mm_group(P, bank[5][base:base + 64, 0:128], ("ps_av", h % 2), av_pairs,
                              reads=av_reads, tag="mm_av", tile_position=tp)
                if h % 2 == 1:
                    em.op("vector", lambda e, hp=hp, pa=pa: e.tensor_copy(yTb[:, pa, hp, :], bank[5][:, 0:128]),
                          reads=[("ps_av", 0), ("ps_av", 1)], writes=[("yTb", pa, hp)], tag="ev_av")
            for c in range(2):
                emit_mm_group(P, bank[6][:, :], BK(6),
                              [(yTb[:, pa, k, :], WoT[:, k, c * 512:(c + 1) * 512]) for k in range(ND)],
                              reads=[("yTb", pa, k) for k in range(ND)] + [("WoT", k) for k in range(ND)],
                              tag="mm_pr")
                em.op("vector", lambda e, c=c, pa=pa: e.tensor_tensor(xob[:, pa, c * 512:(c + 1) * 512],
                                                                      bank[6][:, :], xcb[:, pa, c * 512:(c + 1) * 512], ALU.add),
                      reads=[BK(6), ("x", pa)], writes=[("xo", pa, c)], tag="resid")
            em.dma("sync", xd2[t * 128:(t + 1) * 128, :], xob[:, pa, :],
                   reads=[("xo", pa, 0), ("xo", pa, 1)], writes=[("xd2", t)], tag="st2")
    em.barrier()

    with ExitStack() as s3:
        A = lambda n, sh, dt=F32: s3.enter_context(nc.sbuf_tensor(P.uniq(n), list(sh), dt))
        WoutT = A("WoutT", [128, 32, D], BF16)
        wgate_b = A("wgate_b", [128, 2, ND, 128], BF16)
        wup_b = A("wup_b", [128, 2, ND, 128], BF16)
        wffn = A("wffn", [128, ND])
        identc2 = A("identc2", [128, 128])
        xrc = A("xrc", [128, 2, 4, D])
        rmsc = {k: A("r3" + k, [128, 2, D if k in ("sq", "xn") else 1]) for k in
               ("sq", "s", "s2", "mu", "m2", "var", "lnv", "rstd", "xn")}
        rmsc["tag"] = "rmsC"
        xmTc = A("xmTc", [128, 2, ND, 512], BF16)
        gbufc = A("gbufc", [128, 2, 512])
        pTb = A("pTb", [128, 32, 512], BF16)
        for kf in range(32):
            em.dma_fn("gpsimd", lambda e, kf=kf: e.dma_start(out=WoutT[:, kf, :], in_=WoutT_in[kf * 128:(kf + 1) * 128, :]),
                      writes=[("WoutT",)], tag="ld_wo")
        em.dma("sync", wffn[:, :], wffn_in, writes=[("wffn",)])
        em.dma("sync", identc2[:, :], identf_in, writes=[("identc2",)], tag="ldi")
        for n in range(NCH):
            pa = n % 2
            for s_ in range(4):
                t = n * 4 + s_
                em.dma("sync", xrc[:, pa, s_, :], xd2[t * 128:(t + 1) * 128, :],
                       writes=[("xrc", pa, s_)], tag="ld_c")
                xk = emit_ln_tile(P, rmsc, pa, xrc[:, pa, s_, :], ("xrc", pa, s_), rms=True, eps=1e-6)
                xn = rmsc["xn"][:, pa, :]
                for k in range(ND):
                    em.op("tensor", lambda e, k=k, xn=xn: e.transpose(bank[6][:, 0:128], xn[:, k * 128:(k + 1) * 128], identc2[:, :]),
                          reads=[xk, ("identc2",)], writes=[BK(6)], tag="trc")
                    em.op("vector", lambda e, k=k, pa=pa, s_=s_: e.tensor_scalar(
                        xmTc[:, pa, k, s_ * 128:(s_ + 1) * 128], bank[6][:, 0:128], wffn[:, k:k + 1], None, ALU.mult),
                        reads=[BK(6), ("wffn",)], writes=[("xmTc", pa, k)], tag="evc")
            xmr = [("xmTc", pa, k) for k in range(ND)]
            for m in range(32):
                wpa = m % 2
                em.dma_fn("gpsimd", lambda e, m=m, wpa=wpa: e.dma_start(
                    out=wgate_b[:, wpa, :, :],
                    in_=WgateT_in.rearrange("(k p) c -> p k c", p=128)[:, :, m * 128:(m + 1) * 128]),
                    writes=[("wgate_b", wpa, k) for k in range(ND)], tag="ld_wg")
                em.dma_fn("gpsimd", lambda e, m=m, wpa=wpa: e.dma_start(
                    out=wup_b[:, wpa, :, :],
                    in_=WupT_in.rearrange("(k p) c -> p k c", p=128)[:, :, m * 128:(m + 1) * 128]),
                    writes=[("wup_b", wpa, k) for k in range(ND)], tag="ld_wu")
                bk = m % 2
                emit_mm_group(P, bank[bk][:, :], BK(bk),
                              [(wgate_b[:, wpa, k, :], xmTc[:, pa, k, :]) for k in range(ND)],
                              reads=xmr + [("wgate_b", wpa, k) for k in range(ND)], tag="mm_gate")
                em.op("scalar", lambda e, bk=bk, pa=pa: e.activation(gbufc[:, pa, :], bank[bk][:, :], AF.Silu),
                      reads=[BK(bk)], writes=[("gbufc", pa)], tag="ev_gate")
                emit_mm_group(P, bank[2 + bk][:, :], ("bank", 2 + bk),
                              [(wup_b[:, wpa, k, :], xmTc[:, pa, k, :]) for k in range(ND)],
                              reads=xmr + [("wup_b", wpa, k) for k in range(ND)], tag="mm_up")
                em.op("vector", lambda e, m=m, bk=bk, pa=pa: e.tensor_tensor(pTb[:, m, :], gbufc[:, pa, :],
                                                                             bank[2 + bk][:, :], ALU.mult),
                      reads=[("gbufc", pa), ("bank", 2 + bk)], writes=[("pTb", m)], tag="pmul")
            for sub in range(4):
                for dc in range(2):
                    bko = 4 + (sub % 2) * 2 + dc
                    emit_mm_group(P, bank[bko][:, :], ("bank", bko),
                                  [(pTb[:, kf, sub * 128:(sub + 1) * 128], WoutT[:, kf, dc * 512:(dc + 1) * 512])
                                   for kf in range(32)],
                                  reads=[("pTb", kf) for kf in range(32)] + [("WoutT",)], tag="mm_wout")
                    em.op("vector", lambda e, sub=sub, dc=dc, bko=bko, pa=pa: e.tensor_tensor(
                        xrc[:, pa, sub, dc * 512:(dc + 1) * 512], xrc[:, pa, sub, dc * 512:(dc + 1) * 512],
                        bank[bko][:, :], ALU.add),
                        reads=[("bank", bko), ("xrc", pa, sub)], writes=[("xrc", pa, sub)], tag="residc")
            for s_ in range(4):
                t = n * 4 + s_
                em.dma("sync", xd3[t * 128:(t + 1) * 128, :], xrc[:, pa, s_, :],
                       reads=[("xrc", pa, s_)], writes=[("xd3", t)], tag="st3")
    em.barrier()

    with ExitStack() as s4:
        A = lambda n, sh, dt=F32: s4.enter_context(nc.sbuf_tensor(P.uniq(n), list(sh), dt))
        xcd = A("xcd", [128, 2, D])
        x3cd = A("x3cd", [128, 2, D])
        xod = A("xod", [128, 2, D])
        for t in range(NT):
            pa = t % 2
            em.dma("sync", xcd[:, pa, :], x_in[t * 128:(t + 1) * 128, :], writes=[("x", pa)], tag="ld_x")
            em.dma("sync", x3cd[:, pa, :], xd3[t * 128:(t + 1) * 128, :], writes=[("x3", pa)], tag="ld_x3")
            em.op("vector", lambda e, pa=pa: e.tensor_tensor(xod[:, pa, :], xcd[:, pa, :], x3cd[:, pa, :], ALU.add),
                  reads=[("x", pa), ("x3", pa)], writes=[("xo", pa)], tag="resid")
            em.dma("sync", xo_out[t * 128:(t + 1) * 128, :], xod[:, pa, :],
                   reads=[("xo", pa)], writes=[("xo_st", t)], tag="st_x")
    em.barrier()
    return P


def run_l1attn(x_full, w_rms_mix, Wg, bg, Wv, bv, Wd, bd, w_rms_attn, Wqkv, Wo,
               w_rms_ffn, Wgate, Wup, Wout):
    P = build_l1attn()
    nc = P.close_build()
    pmj = lambda v: np.ascontiguousarray(v.reshape(ND, 128).T)
    r = np.arange(128)[:, None]
    j = np.arange(256)[None, :]
    mband = np.where((j >= r + 1) & (j <= 128 + r), 0.0, NEG).astype(np.float32)
    jj = np.arange(128)[None, :]
    mfirst = np.where(jj <= r, 0.0, NEG).astype(np.float32)
    xs = x_full.reshape(NCORES, T, D)
    base = {
        "wmixt": pmj(w_rms_mix), "WgT": np.ascontiguousarray(Wg.T), "bgt": pmj(bg),
        "WvT": np.ascontiguousarray(Wv.T), "bvt": pmj(bv),
        "WdT": np.ascontiguousarray(Wd.T), "bdt": pmj(bd),
        "wattnt": pmj(w_rms_attn), "WqkvT": np.ascontiguousarray(Wqkv.T),
        "WoT": np.ascontiguousarray(Wo.T), "wffnt": pmj(w_rms_ffn),
        "WgateT": np.ascontiguousarray(Wgate.T), "WupT": np.ascontiguousarray(Wup.T),
        "WoutT": np.ascontiguousarray(Wout.T),
        "identf": np.eye(128, dtype=np.float32), "identb": np.eye(128, dtype=BFNP),
        "mask_band": mband, "mask_first": mfirst,
    }
    in_maps = [dict(base, x=np.ascontiguousarray(xs[c])) for c in range(NCORES)]
    res = run_bass_kernel_spmd(nc, in_maps, core_ids=list(range(NCORES)))
    out = np.stack([res.results[c]["xo"] for c in range(NCORES)])
    return out.reshape(16, 2048, D)


L1FFN_DEPTHS = 6
L1FFN_PARTS = "af"
L1FFN_DEBUG = False


def build_l1ffn(P=None, V=None, x_in=None, xo_out=None, bank=None):
    DEPTH6 = 6
    INNER = 1024
    DI2 = 2730
    standalone = P is None
    if standalone:
        P = Prog("l1ffn")
        V = {"Wqkv6": P.inp("Wqkv", [DEPTH6, D, 3 * INNER]), "Wo6": P.inp("Wo", [DEPTH6, INNER, D]),
             "lng": P.inp("lng", [128, DEPTH6, ND]), "lnb": P.inp("lnb", [128, DEPTH6, ND]),
             "W1": P.inp("W1", [DEPTH6, D, 2 * DI2]), "b1a": P.inp("b1a", [128, DEPTH6, 22]),
             "b1g": P.inp("b1g", [128, DEPTH6, 22]), "W2": P.inp("W2", [DEPTH6, DI2, D]),
             "b26": P.inp("b2", [DEPTH6, D]), "qs": P.inp("qs", [DEPTH6, 64]),
             "ks": P.inp("ks", [DEPTH6, 64]),
             "tab_cs": P.inp("tab_cs", [128, 16, 64], F32),
             "tab_sslo": P.inp("tab_sslo", [128, 16, 32], F32),
             "tab_sshi": P.inp("tab_sshi", [128, 16, 32], F32),
             "tab_ci": P.inp("tab_ci", [128, 16, 64], F32),
             "tab_silo": P.inp("tab_silo", [128, 16, 32], F32),
             "tab_sihi": P.inp("tab_sihi", [128, 16, 32], F32),
             "identf": P.inp("identf", [128, 128], F32), "identb": P.inp("identb", [128, 128], BF16),
             "m_full": P.inp("m_full", [128, 384], F32), "m_1": P.inp("m_1", [128, 384], F32),
             "m_0": P.inp("m_0", [128, 384], F32), "ones128": P.inp("ones128", [128, 1], F32),
             "ones1": P.inp("ones1", [1, 128], F32)}
        x_in = P.inp("x", [T, D])
        xo_out = P.out("xo", [T, D])
    em = P.em
    nc = P.nc
    Wqkv_in = V["Wqkv6"]; Wo_in = V["Wo6"]; lng_in = V["lng"]; lnb_in = V["lnb"]
    W1_in = V["W1"]; b1a_in = V["b1a"]; b1g_in = V["b1g"]; W2_in = V["W2"]; b2_in = V["b26"]
    qs_in = V["qs"]; ks_in = V["ks"]
    tab_cs_in = V["tab_cs"]; tab_sslo_in = V["tab_sslo"]; tab_sshi_in = V["tab_sshi"]
    tab_ci_in = V["tab_ci"]; tab_silo_in = V["tab_silo"]; tab_sihi_in = V["tab_sihi"]
    identf_in = V["identf"]; identb_in = V["identb"]
    m_full_in = V["m_full"]; m_1_in = V["m_1"]; m_0_in = V["m_0"]
    ones128_in = V["ones128"]; ones1_in = V["ones1"]

    xd = [P.dram(f"xd{i}", [T, D]) for i in range(2)]
    dbg_mu = P.out("dbg_mu", [1, 1024]) if L1FFN_DEBUG else None
    dbg_rs = P.out("dbg_rs", [1, 1024]) if L1FFN_DEBUG else None
    dbg_xnf = P.out("dbg_xnf", [128, ND, 1024], BF16) if L1FFN_DEBUG else None
    dbg_pb = P.out("dbg_pb", [128, 22, 1024], BF16) if L1FFN_DEBUG else None

    if bank is None:
        bank = [P.psum(f"bank{i}", [128, 512], F32) for i in range(8)]
    BK = lambda i: ("bank", i)

    inst_ctx = ExitStack()
    _sb = lambda n, sh, dt=F32: inst_ctx.enter_context(nc.sbuf_tensor(P.uniq(n), list(sh), dt))
    ident = _sb("ident", [128, 128], F32)
    identb = _sb("identb", [128, 128], BF16)
    tab_cs = _sb("tab_cs", [128, 16, 64], F32)
    tab_sslo = _sb("tab_sslo", [128, 16, 32], F32)
    tab_sshi = _sb("tab_sshi", [128, 16, 32], F32)
    tab_ci = _sb("tab_ci", [128, 16, 64], F32)
    tab_silo = _sb("tab_silo", [128, 16, 32], F32)
    tab_sihi = _sb("tab_sihi", [128, 16, 32], F32)
    m_full = _sb("m_full", [128, 384], F32)
    m_1 = _sb("m_1", [128, 384], F32)
    m_0 = _sb("m_0", [128, 384], F32)
    ones128 = _sb("ones128", [128, 1], F32)
    ones128b = _sb("ones128b", [128, 1], BF16)
    ones1 = _sb("ones1", [1, 128], F32)
    em.dma_fn("gpsimd", lambda e: e.dma_start(out=ones128b[:, :], in_=ones128_in),
              writes=[("o128b",)], tag="ldcb")
    for ap_s, ap_i, key in ((ident, identf_in, "ident"), (identb, identb_in, "identb"),
                            (tab_cs, tab_cs_in, "tcs"), (tab_sslo, tab_sslo_in, "tsl"),
                            (tab_sshi, tab_sshi_in, "tsh"), (tab_ci, tab_ci_in, "tci"),
                            (tab_silo, tab_silo_in, "til"), (tab_sihi, tab_sihi_in, "tih"),
                            (m_full, m_full_in, "mf"), (m_1, m_1_in, "m1m"), (m_0, m_0_in, "m0m"),
                            (ones128, ones128_in, "o128"), (ones1, ones1_in, "o1")):
        em.dma("sync", ap_s[:, :] if ap_s.shape.__len__() == 2 else ap_s[:, :, :], ap_i,
               writes=[(key,)], tag="ldc")

    with ExitStack() as s0:
        A = lambda n, sh, dt=F32: s0.enter_context(nc.sbuf_tensor(P.uniq(n), list(sh), dt))
        xc0 = A("xc0", [128, 2, D])
        lns0 = {k: A("f0" + k, [128, 2, D if k in ("sq", "xn") else 1]) for k in
                ("sq", "s", "s2", "mu", "m2", "var", "lnv", "rstd", "xn")}
        lns0["tag"] = "lnF"
        for t in range(NT):
            pa = t % 2
            em.dma("sync", xc0[:, pa, :], x_in[t * 128:(t + 1) * 128, :], writes=[("x0", pa)], tag="ld")
            xk = emit_ln_tile(P, lns0, pa, xc0[:, pa, :], ("x0", pa))
            em.dma("sync", xd[0][t * 128:(t + 1) * 128, :], lns0["xn"][:, pa, :], reads=[xk],
                   writes=[("xd0", t)], tag="st")
    em.barrier()

    def emit_attn_depth(i, src, dst):
        with ExitStack() as sa:
            A = lambda n, sh, dt=F32: sa.enter_context(nc.sbuf_tensor(P.uniq(n), list(sh), dt))
            wq = A("wq", [128, ND, 3 * INNER], BF16)
            wo = A("wo", [128, ND, D], BF16)
            qsb = A("qsb", [1, 64])
            ksb = A("ksb", [1, 64])
            qkb2 = A("qkb2", [128, 2, 64])
            Aq = A("Aq", [128, 16, 64])
            Bql = A("Bql", [128, 16, 32])
            Bqh = A("Bqh", [128, 16, 32])
            Ak = A("Ak", [128, 16, 64])
            Bkl = A("Bkl", [128, 16, 32])
            Bkh = A("Bkh", [128, 16, 32])
            xat = A("xat", [128, 2, D])
            xT = A("xT", [128, 2, ND, 128], BF16)
            xsT = A("xsT", [128, 2, 4, 128], BF16)
            qb = A("qb", [128, 2, INNER])
            kb = A("kb", [128, 2, INNER])
            sqq = A("sqq", [128, 2, INNER], BF16)
            s2q = A("s2q", [128, 2, 16])
            s2k = A("s2k", [128, 2, 16])
            rsq = A("rsq", [128, 2, 16])
            rsk = A("rsk", [128, 2, 16])
            tmpq = A("tmpq", [128, 2, INNER])
            tmpv = A("tmpv", [128, 2, INNER])
            q2 = A("q2", [128, 2, INNER], BF16)
            k2 = A("k2", [128, 2, INNER], BF16)
            qT = A("qT", [128, 2, ND, 128], BF16)
            kT = A("kT", [128, ND, 6 * 128], BF16)
            v_roll = A("v_roll", [128, 3, INNER], BF16)
            maskb = A("maskb", [128, 3, 384], BF16)
            probs_bf = A("probs_bf", [128, 4, 384], BF16)
            probs_bf2 = A("probs_bf2", [128, 4, 384], BF16)
            pT = A("pT", [128, 4, 3, 128], BF16)
            nm = A("nm", [128, 4, 1])
            sume = A("sume", [128, 4, 1])
            rec = A("rec", [128, 4, 1])
            yT = A("yT", [128, 2, ND, 128], BF16)
            xob = A("xob", [128, 2, D])
            em.dma_fn("gpsimd", lambda e: e.dma_start(
                out=wq[:, :, :], in_=Wqkv_in[i].rearrange("(k p) c -> p k c", p=128)),
                writes=[("wq", k) for k in range(ND)], tag="ldw")
            em.dma_fn("gpsimd", lambda e: e.dma_start(
                out=wo[:, :, :], in_=Wo_in[i].rearrange("(k p) c -> p k c", p=128)),
                writes=[("wo", k) for k in range(ND)], tag="ldw")
            for mi, m_in in enumerate((m_0_in, m_1_in, m_full_in)):
                em.dma_fn("gpsimd", lambda e, mi=mi, m_in=m_in: e.dma_start(out=maskb[:, mi, :], in_=m_in),
                          writes=[("maskb",)], tag="ldmb")
            em.dma("sync", qsb[:, :], qs_in[i:i + 1, :], writes=[("qsb",)], tag="ldq")
            em.dma("sync", ksb[:, :], ks_in[i:i + 1, :], writes=[("ksb",)], tag="ldk")
            emit_mm_group(P, bank[0][:, 0:64], BK(0), [(ones1[:, :], qsb[:, :])],
                          reads=[("o1",), ("qsb",)], tag="mm_bq")
            em.op("vector", lambda e: e.tensor_copy(qkb2[:, 0, :], bank[0][:, 0:64]),
                  reads=[BK(0)], writes=[("qs_b",)], tag="ev_bq")
            emit_mm_group(P, bank[0][:, 0:64], BK(0), [(ones1[:, :], ksb[:, :])],
                          reads=[("o1",), ("ksb",)], tag="mm_bk")
            em.op("vector", lambda e: e.tensor_copy(qkb2[:, 1, :], bank[0][:, 0:64]),
                  reads=[BK(0)], writes=[("ks_b",)], tag="ev_bk")
            qs_b = qkb2[:, 0, :]
            ks_b = qkb2[:, 1, :]
            em.op("vector", lambda e: e.scalar_tensor_tensor(
                Aq[:, :, :], tab_cs[:, :, :], 8.0,
                qs_b.unsqueeze(1).broadcast_to([128, 16, 64]), ALU.mult, ALU.mult),
                reads=[("tcs",), ("qs_b",)], writes=[("Aq",)], tag="tAq")
            em.op("vector", lambda e: e.scalar_tensor_tensor(
                Bql[:, :, :], tab_sslo[:, :, :], 8.0,
                qs_b[:, 32:64].unsqueeze(1).broadcast_to([128, 16, 32]), ALU.mult, ALU.mult),
                reads=[("tsl",), ("qs_b",)], writes=[("Bql",)], tag="tBql")
            em.op("vector", lambda e: e.scalar_tensor_tensor(
                Bqh[:, :, :], tab_sshi[:, :, :], 8.0,
                qs_b[:, 0:32].unsqueeze(1).broadcast_to([128, 16, 32]), ALU.mult, ALU.mult),
                reads=[("tsh",), ("qs_b",)], writes=[("Bqh",)], tag="tBqh")
            em.op("vector", lambda e: e.tensor_tensor(
                Ak[:, :, :], tab_ci[:, :, :], ks_b.unsqueeze(1).broadcast_to([128, 16, 64]), ALU.mult),
                reads=[("tci",), ("ks_b",)], writes=[("Ak",)], tag="tAk")
            em.op("vector", lambda e: e.tensor_tensor(
                Bkl[:, :, :], tab_silo[:, :, :], ks_b[:, 32:64].unsqueeze(1).broadcast_to([128, 16, 32]), ALU.mult),
                reads=[("til",), ("ks_b",)], writes=[("Bkl",)], tag="tBkl")
            em.op("vector", lambda e: e.tensor_tensor(
                Bkh[:, :, :], tab_sihi[:, :, :], ks_b[:, 0:32].unsqueeze(1).broadcast_to([128, 16, 32]), ALU.mult),
                reads=[("tih",), ("ks_b",)], writes=[("Bkh",)], tag="tBkh")

            wqr = [("wq", k) for k in range(ND)]
            for t in range(NT):
                pa = t % 2
                t_loc = t % NT_SEQ
                r3 = t % 3
                em.dma("sync", xat[:, pa, :], src[t * 128:(t + 1) * 128, :], writes=[("xat", pa)], tag="ldx")
                for k in range(ND):
                    em.op("tensor", lambda e, k=k, pa=pa: e.transpose(
                        bank[6][:, 0:128], xat[:, pa, k * 128:(k + 1) * 128], ident[:, :]),
                        reads=[("xat", pa), ("ident",)], writes=[BK(6)], tag="trx")
                    em.op("vector", lambda e, k=k, pa=pa: e.tensor_copy(xT[:, pa, k, :], bank[6][:, 0:128]),
                          reads=[BK(6)], writes=[("xT", pa, k)], tag="evx")
                for k in range(4, ND):
                    if t_loc == 0:
                        em.op("gpsimd", lambda e, k=k, pa=pa: e.memset(xsT[:, pa, k - 4, 0:1], 0.0),
                              reads=[], writes=[("xsT", pa, k)], tag="z_sft")
                    else:
                        em.op("scalar", lambda e, k=k, pa=pa: e.copy(xsT[:, pa, k - 4, 0:1], xT[:, 1 - pa, k, 127:128]),
                              reads=[("xT", 1 - pa, k)], writes=[("xsT", pa, k)], tag="c_sft0")
                    em.op("scalar", lambda e, k=k, pa=pa: e.copy(xsT[:, pa, k - 4, 1:128], xT[:, pa, k, 0:127]),
                          reads=[("xT", pa, k)], writes=[("xsT", pa, k)], deps=(), tag="c_sft")

                def rhs_k(k, pa=pa):
                    return xT[:, pa, k, :] if k < 4 else xsT[:, pa, k - 4, :]
                xs_reads = [("xT", pa, k) for k in range(4)] + [("xsT", pa, k) for k in range(4, ND)]
                for (dst_b, key, col0) in ((qb, "qb", 0), (kb, "kb", INNER)):
                    for c in range(2):
                        emit_mm_group(P, bank[c][:, :], BK(c),
                                      [(rhs_k(k), wq[:, k, col0 + c * 512:col0 + (c + 1) * 512]) for k in range(ND)],
                                      reads=xs_reads + wqr, tag="mm_qk")
                    em.op("vector", lambda e, dst_b=dst_b, pa=pa: e.tensor_copy(dst_b[:, pa, 0:512], bank[0][:, :]),
                          reads=[BK(0)], writes=[(key, pa)], tag="ev_qk")
                    em.op("vector", lambda e, dst_b=dst_b, pa=pa: e.tensor_copy(dst_b[:, pa, 512:1024], bank[1][:, :]),
                          reads=[BK(1)], writes=[(key, pa)], deps=(), tag="ev_qk2")
                for c in range(2):
                    emit_mm_group(P, bank[2][:, :], BK(2),
                                  [(rhs_k(k), wq[:, k, 2 * INNER + c * 512:2 * INNER + (c + 1) * 512]) for k in range(ND)],
                                  reads=xs_reads + wqr, tag="mm_v")
                    em.op("vector", lambda e, c=c, r3=r3: e.tensor_copy(v_roll[:, r3, c * 512:(c + 1) * 512], bank[2][:, :]),
                          reads=[BK(2)], writes=[("v", r3, c)], tag="ev_v")
                for (buf, key, s2b, rsb, Atab, Bltab, Bhtab, outb, okey) in (
                        (qb, "qb", s2q, rsq, Aq, Bql, Bqh, q2, "q2"),
                        (kb, "kb", s2k, rsk, Ak, Bkl, Bkh, k2, "k2")):
                    em.op("scalar", lambda e, buf=buf, pa=pa, s2b=s2b: e.activation(
                        sqq[:, pa, :], buf[:, pa, :], AF.Square),
                        reads=[(key, pa)], writes=[("sqq", pa)], tag="rsq")
                    em.op("vector", lambda e, pa=pa, s2b=s2b: e.reduce_sum(
                        s2b[:, pa, :], sqq.ap()[:, pa, :].rearrange("p (h d) -> p h d", d=64), axis=AX),
                        reads=[("sqq", pa)], writes=[("s2", key, pa)], tag="rs2")
                    em.op("scalar", lambda e, pa=pa, s2b=s2b, rsb=rsb: e.activation(rsb[:, pa, :], s2b[:, pa, :], AF.Ln),
                          reads=[("s2", key, pa)], writes=[("lnv", key, pa)], tag="rln")
                    em.op("scalar", lambda e, pa=pa, rsb=rsb: e.activation(rsb[:, pa, :], rsb[:, pa, :], AF.Exp, scale=0.5),
                          reads=[("lnv", key, pa)], writes=[("st", key, pa)], tag="rexp")
                    em.op("vector", lambda e, pa=pa, rsb=rsb: e.tensor_scalar(rsb[:, pa, :], rsb[:, pa, :], 1e-12, None, ALU.max),
                          reads=[("st", key, pa)], writes=[("mx", key, pa)], tag="rmax")
                    em.op("vector", lambda e, pa=pa, rsb=rsb: e.reciprocal(rsb[:, pa, :], rsb[:, pa, :]),
                          reads=[("mx", key, pa)], writes=[("rs", key, pa)], tag="rrec")
                    b3 = lambda ap: ap
                    em.op("vector", lambda e, buf=buf, pa=pa, Atab=Atab, t_loc=t_loc: e.tensor_tensor(
                        tmpq[:, pa, :].rearrange("p (h d) -> p h d", d=64),
                        buf[:, pa, :].rearrange("p (h d) -> p h d", d=64),
                        Atab[:, t_loc:t_loc + 1, :].broadcast_to([128, 16, 64]), ALU.mult),
                        reads=[(key, pa), ("Aq",), ("Ak",)], writes=[("tmpq", pa)], tag="rotA")
                    em.op("vector", lambda e, buf=buf, pa=pa, Bltab=Bltab, t_loc=t_loc: e.tensor_tensor(
                        tmpv[:, pa, :].rearrange("p (h d) -> p h d", d=64)[:, :, 0:32],
                        buf[:, pa, :].rearrange("p (h d) -> p h d", d=64)[:, :, 32:64],
                        Bltab[:, t_loc:t_loc + 1, :].broadcast_to([128, 16, 32]), ALU.mult),
                        reads=[(key, pa), ("Bql",), ("Bkl",)], writes=[("tmpv", pa)], tag="rotBl")
                    em.op("vector", lambda e, buf=buf, pa=pa, Bhtab=Bhtab, t_loc=t_loc: e.tensor_tensor(
                        tmpv[:, pa, :].rearrange("p (h d) -> p h d", d=64)[:, :, 32:64],
                        buf[:, pa, :].rearrange("p (h d) -> p h d", d=64)[:, :, 0:32],
                        Bhtab[:, t_loc:t_loc + 1, :].broadcast_to([128, 16, 32]), ALU.mult),
                        reads=[(key, pa), ("Bqh",), ("Bkh",)], writes=[("tmpv", pa)], deps=(), tag="rotBh")
                    em.op("vector", lambda e, pa=pa: e.tensor_tensor(tmpq[:, pa, :], tmpq[:, pa, :], tmpv[:, pa, :], ALU.add),
                          reads=[("tmpq", pa), ("tmpv", pa)], writes=[("tmpq", pa)], tag="rotsum")
                    em.op("vector", lambda e, pa=pa, rsb=rsb, outb=outb: e.tensor_tensor(
                        outb[:, pa, :].rearrange("p (h d) -> p h d", d=64),
                        tmpq[:, pa, :].rearrange("p (h d) -> p h d", d=64),
                        rsb[:, pa, :].unsqueeze(2).broadcast_to([128, 16, 64]), ALU.mult),
                        reads=[("tmpq", pa), ("rs", key, pa)], writes=[(okey, pa)], tag="rotout")
                for k in range(ND):
                    em.op("tensor", lambda e, k=k, pa=pa: e.transpose(
                        bank[6][:, :].bitcast(BF16)[:, 0:128], q2[:, pa, k * 128:(k + 1) * 128], identb[:, :]),
                        reads=[("q2", pa), ("identb",)], writes=[BK(6)], tag="trq")
                    em.op("scalar", lambda e, k=k, pa=pa: e.copy(qT[:, pa, k, :], bank[6][:, :].bitcast(BF16)[:, 0:128]),
                          reads=[BK(6)], writes=[("qT", pa, k)], tag="evq")
                    em.op("tensor", lambda e, k=k, pa=pa: e.transpose(
                        bank[6][:, :].bitcast(BF16)[:, 0:128], k2[:, pa, k * 128:(k + 1) * 128], identb[:, :]),
                        reads=[("k2", pa), ("identb",)], writes=[BK(6)], tag="trk")
                    em.op("scalar", lambda e, k=k, kr=t % 6: e.copy(kT[:, k, kr * 128:(kr + 1) * 128],
                                                                    bank[6][:, :].bitcast(BF16)[:, 0:128]),
                          reads=[BK(6)], writes=[("kT", t % 6, k)], tag="evk")
                kr8 = t % 6
                mi = min(t_loc, 2)
                for h in range(16):
                    hp = h // 2
                    base = (h % 2) * 64
                    hb = h % 4
                    scb = 3 if h % 2 == 0 else 6
                    q_ap = qT[base:base + 64, pa, hp, :]

                    def fn_sc(e, q_ap=q_ap, hp=hp, base=base, t_loc=t_loc, kr8=kr8, mi=mi, scb=scb):
                        bb = bank[scb][:, :].bitcast(BF16)
                        e.matmul(bank[scb][:, 0:384], identb[:, :], maskb[:, mi, :], start=True, stop=False)
                        if t_loc >= 2 and kr8 >= 2:
                            return e.matmul(bank[scb][:, 0:384], q_ap,
                                            kT[base:base + 64, hp, (kr8 - 2) * 128:(kr8 + 1) * 128],
                                            start=False, stop=True)
                        inst = None
                        for bi in range(3):
                            tt = t_loc - 2 + bi
                            if tt < 0:
                                continue
                            kk = (kr8 - 2 + bi) % 6
                            inst = e.matmul(bank[scb][:, bi * 128:(bi + 1) * 128], q_ap,
                                            kT[base:base + 64, hp, kk * 128:(kk + 1) * 128], start=False, stop=True)
                        return inst
                    sc_reads = [("qT", pa, hp), ("maskb",), ("identb",)] +                                [("kT", (kr8 - 2 + bi) % 6, hp) for bi in range(3)]
                    em.op("tensor", fn_sc, reads=sc_reads, writes=[BK(scb)], tag="mm_sc")
                    em.op("vector", lambda e, hb=hb, scb=scb: e.reduce_max(nm[:, hb, :], bank[scb][:, 0:384], axis=AX, negate=True),
                          reads=[BK(scb)], writes=[("nm", hb)], tag="negmax")
                    em.op("scalar", lambda e, hb=hb, scb=scb: e.activation(probs_bf[:, hb, 0:384], bank[scb][:, 0:384], AF.Exp,
                                                                           bias=nm[:, hb, :], accum_out=sume[:, hb, :]),
                          reads=[BK(scb), ("nm", hb)], writes=[("probs_bf", hb), ("sume", hb)], tag="exp")
                    em.op("vector", lambda e, hb=hb: e.reciprocal(rec[:, hb, :], sume[:, hb, :]),
                          reads=[("sume", hb)], writes=[("rec", hb)], tag="recip")
                    em.op("vector", lambda e, hb=hb: e.tensor_scalar(probs_bf2[:, hb, 0:384], probs_bf[:, hb, 0:384],
                                                                     rec[:, hb, :], None, ALU.mult),
                          reads=[("probs_bf", hb), ("rec", hb)], writes=[("probs_bf2", hb)], tag="pscale")

                    def fn_trp(e, hb=hb, h2=h % 2):
                        b4 = bank[4][:, :].bitcast(BF16)
                        inst = None
                        for c in range(3):
                            inst = e.transpose(b4[:, (h2 * 3 + c) * 128:(h2 * 3 + c + 1) * 128],
                                               probs_bf2[:, hb, c * 128:(c + 1) * 128], identb[:, :])
                        return inst
                    em.op("tensor", fn_trp, reads=[("probs_bf2", hb), ("identb",)], writes=[BK(4)], tag="tr_p")
                    em.op("scalar", lambda e, hb=hb, h2=h % 2: e.copy(pT.ap()[:, hb, :, :].opt(),
                                                            bank[4][:, :].bitcast(BF16)[:, h2 * 384:h2 * 384 + 384]),
                          reads=[BK(4)], writes=[("pT", hb)], tag="ev_p")
                    av_pairs = []
                    av_reads = [("pT", hb)]
                    for bi in range(3):
                        tt = t_loc - 2 + bi
                        if tt < 0:
                            continue
                        rr = (r3 + 1 + bi) % 3
                        av_pairs.append((v_roll[:, rr, h * 64:(h + 1) * 64], pT[:, hb, bi, :]))
                        av_reads += [("v", rr, 0), ("v", rr, 1)]
                    tp = None if base == 0 else (0, 64)
                    emit_mm_group(P, bank[5][base:base + 64, 0:128], ("ps_av", h % 2), av_pairs,
                                  reads=av_reads, tag="mm_av", tile_position=tp)
                    if h % 2 == 1:
                        em.op("vector", lambda e, hp=hp, pa=pa: e.tensor_copy(yT[:, pa, hp, :], bank[5][:, 0:128]),
                              reads=[("ps_av", 0), ("ps_av", 1)], writes=[("yT", pa, hp)], tag="ev_av")
                for c in range(2):
                    emit_mm_group(P, bank[7][:, :], BK(7),
                                  [(yT[:, pa, k, :], wo[:, k, c * 512:(c + 1) * 512]) for k in range(ND)],
                                  reads=[("yT", pa, k) for k in range(ND)] + [("wo", k) for k in range(ND)],
                                  tag="mm_pr")
                    em.op("vector", lambda e, c=c, pa=pa: e.tensor_tensor(
                        xob[:, pa, c * 512:(c + 1) * 512], bank[7][:, :], xat[:, pa, c * 512:(c + 1) * 512], ALU.add),
                        reads=[BK(7), ("xat", pa)], writes=[("xo", pa, c)], tag="resid")
                em.dma("sync", dst[t * 128:(t + 1) * 128, :], xob[:, pa, :],
                       reads=[("xo", pa, 0), ("xo", pa, 1)], writes=[("dst", t)], tag="stx")
        em.barrier()

    def emit_ffn_depth(i, src, dst):
        NCHF = 4
        CHF = 1024
        NTC = CHF // 128
        with ExitStack() as sf:
            A = lambda n, sh, dt=F32: sf.enter_context(nc.sbuf_tensor(P.uniq(n), list(sh), dt))
            w2r_sb = A("w2r", [128, 22, D], BF16)
            w1a_b = A("w1a", [128, 2, ND, 128], BF16)
            w1g_b = A("w1g", [128, 2, ND, 128], BF16)
            b2sb = A("b2sb", [1, D])
            lngs = A("lngs", [128, ND])
            lnbs = A("lnbs", [128, ND])
            b1as = A("b1as", [128, 22])
            b1gs = A("b1gs", [128, 22])
            xTf = A("xTf", [128, 2, ND, CHF + 1], BF16)
            sqk = A("sqk", [128, 1, CHF + 1], BF16)
            mu_f = A("mu_f", [1, 1, CHF])
            vr_f = A("vr_f", [1, 1, CHF])
            lb_f = A("lb_f", [1, 1, CHF])
            rs_f = A("rs_f", [1, 1, CHF])
            xnf = A("xnf", [128, 1, ND, CHF], BF16)
            asb = A("asb", [128, 1, 512])
            Pb = A("Pb", [128, 22, CHF], BF16)
            xw = A("xw", [128, 2, D])
            xobf = A("xobf", [128, 2, D])
            for kf in range(22):
                kk0 = kf * 128
                kk1 = min(DI2, kk0 + 128)
                em.dma_fn("gpsimd", lambda e, kf=kf, kk0=kk0, kk1=kk1: e.dma_start(
                    out=w2r_sb[0:kk1 - kk0, kf, :], in_=W2_in[i, kk0:kk1, :]),
                    writes=[("w2r",)], tag="ldw2")
            em.dma("sync", b2sb[:, :], b2_in[i:i + 1, :], writes=[("b2s",)], tag="ldb2")
            em.dma("sync", lngs[:, :], lng_in[:, i, :], writes=[("lngs",)], tag="ldg")
            em.dma("sync", lnbs[:, :], lnb_in[:, i, :], writes=[("lnbs",)], tag="ldb")
            em.dma("sync", b1as[:, :], b1a_in[:, i, :], writes=[("b1as",)], tag="ldb1a")
            em.dma("sync", b1gs[:, :], b1g_in[:, i, :], writes=[("b1gs",)], tag="ldb1g")
            for n in range(NCHF):
                pa = n % 2
                seq_start = (n % 2 == 0)
                for s_ in range(NTC):
                    t = n * NTC + s_
                    em.dma("sync", xw[:, pa, :], src[t * 128:(t + 1) * 128, :], writes=[("xw", pa)], tag="ldx")
                    for k in range(ND):
                        em.op("tensor", lambda e, k=k, pa=pa: e.transpose(
                            bank[6][:, 0:128], xw[:, pa, k * 128:(k + 1) * 128], ident[:, :]),
                            reads=[("xw", pa), ("ident",)], writes=[BK(6)], tag="trf")
                        em.op("vector", lambda e, k=k, pa=pa, s_=s_: e.tensor_copy(
                            xTf[:, pa, k, 1 + s_ * 128:1 + (s_ + 1) * 128], bank[6][:, 0:128]),
                            reads=[BK(6)], writes=[("xTf", pa, k)], tag="evf")
                for k in range(ND):
                    if seq_start:
                        em.op("gpsimd", lambda e, k=k, pa=pa: e.memset(xTf[:, pa, k, 0:1], 0.0),
                              writes=[("xTf", pa, k)], deps=(), tag="zb")
                    else:
                        em.op("scalar", lambda e, k=k, pa=pa: e.copy(xTf[:, pa, k, 0:1], xTf[:, 1 - pa, k, CHF:CHF + 1]),
                              reads=[("xTf", 1 - pa, k)], writes=[("xTf", pa, k)], tag="cb")
                xr_reads = [("xTf", pa, k) for k in range(ND)]
                def sl(k, hcol):
                    a = 1 if k < 4 else 0
                    return a + hcol * 512
                for hcol in range(2):
                    def fn_sum(e, pa=pa, hcol=hcol):
                        inst = None
                        for k in range(ND):
                            a = sl(k, hcol)
                            inst = e.matmul(bank[hcol][0:1, :], ones128b[:, :], xTf[:, pa, k, a:a + 512],
                                            start=(k == 0), stop=(k == ND - 1))
                        return inst
                    em.op("tensor", fn_sum, reads=xr_reads + [("o128b",)], writes=[BK(hcol)], tag="mm_s")
                    em.op("vector", lambda e, pa=pa, hcol=hcol: e.tensor_scalar(
                        mu_f[:, 0, hcol * 512:(hcol + 1) * 512], bank[hcol][0:1, :], 1.0 / D, None, ALU.mult),
                        reads=[BK(hcol)], writes=[("muk",)], tag="mu")
                sq_reads = []
                for k in range(ND):
                    em.op("scalar", lambda e, k=k, pa=pa: e.activation(sqk[:, 0, :], xTf[:, pa, k, :], AF.Square),
                          reads=[("xTf", pa, k)], writes=[("sqkk",)], tag="sq")
                    sq_reads.append(("sqkk",))
                    for hcol in range(2):
                        a = sl(k, hcol)
                        em.op("tensor", lambda e, k=k, pa=pa, a=a, hcol=hcol: e.matmul(
                            bank[2 + hcol][0:1, :], ones128b[:, :], sqk[:, 0, a:a + 512],
                            start=(k == 0), stop=(k == ND - 1)),
                            reads=[("sqkk",), ("o128b",)], writes=[BK(2 + hcol)], tag="mm_s2")
                for hcol in range(2):
                    hs = slice(hcol * 512, (hcol + 1) * 512)
                    em.op("vector", lambda e, hs=hs: e.tensor_mul(lb_f[:, 0, hs], mu_f[:, 0, hs], mu_f[:, 0, hs]),
                          reads=[("muk",)], writes=[("m2a", hcol)], tag="m2a")
                    em.op("vector", lambda e, hs=hs: e.tensor_scalar(vr_f[:, 0, hs], lb_f[:, 0, hs], -1e-5, None, ALU.add),
                          reads=[("m2a", hcol)], writes=[("m2k", hcol)], tag="m2b")
                    em.op("vector", lambda e, hs=hs, hcol=hcol: e.scalar_tensor_tensor(
                        lb_f[:, 0, hs], bank[2 + hcol][0:1, :], 1.0 / D, vr_f[:, 0, hs], ALU.mult, ALU.subtract),
                        reads=[BK(2 + hcol), ("m2k", hcol)], writes=[("vark", hcol)], tag="var")
                    em.op("scalar", lambda e, hs=hs: e.activation(vr_f[:, 0, hs], lb_f[:, 0, hs], AF.Ln),
                          reads=[("vark", hcol)], writes=[("lnvk", hcol)], tag="lnl")
                    em.op("scalar", lambda e, hs=hs: e.activation(rs_f[:, 0, hs], vr_f[:, 0, hs], AF.Exp, scale=-0.5),
                          reads=[("lnvk", hcol)], writes=[("rstdk", hcol)], tag="expl")
                emit_mm_group(P, bank[2][:, :], BK(2), [(ones1[:, :], mu_f[:, 0, 0:512])],
                              reads=[("o1",), ("muk",)], tag="mm_bmu")
                emit_mm_group(P, bank[3][:, :], BK(3), [(ones1[:, :], mu_f[:, 0, 512:1024])],
                              reads=[("o1",), ("muk",)], tag="mm_bmu2")
                emit_mm_group(P, bank[4][:, :], BK(4), [(ones1[:, :], rs_f[:, 0, 0:512])],
                              reads=[("o1",), ("rstdk", 0)], tag="mm_br")
                emit_mm_group(P, bank[5][:, :], BK(5), [(ones1[:, :], rs_f[:, 0, 512:1024])],
                              reads=[("o1",), ("rstdk", 1)], tag="mm_br2")
                for k in range(ND):
                    for hcol in range(2):
                        a = sl(k, hcol)
                        bmu = bank[2] if hcol == 0 else bank[3]
                        brs = bank[4] if hcol == 0 else bank[5]
                        em.op("vector", lambda e, k=k, pa=pa, a=a, hcol=hcol, bmu=bmu: e.tensor_tensor(
                            xnf[:, 0, k, hcol * 512:(hcol + 1) * 512],
                            xTf[:, pa, k, a:a + 512], bmu[:, :], ALU.subtract),
                            reads=[("xTf", pa, k), BK(2 if hcol == 0 else 3)], writes=[("xnfk", k)], tag="nsub")
                        em.op("vector", lambda e, k=k, pa=pa, hcol=hcol, brs=brs: e.tensor_tensor(
                            xnf[:, 0, k, hcol * 512:(hcol + 1) * 512],
                            xnf[:, 0, k, hcol * 512:(hcol + 1) * 512], brs[:, :], ALU.mult),
                            reads=[("xnfk", k), BK(4 if hcol == 0 else 5)], writes=[("xnfk", k)], tag="nmul")
                    em.op("vector", lambda e, k=k, pa=pa: e.tensor_scalar(
                        xnf[:, 0, k, :], xnf[:, 0, k, :], lngs[:, k:k + 1], lnbs[:, k:k + 1], ALU.mult, ALU.add),
                        reads=[("xnfk", k), ("lngs",), ("lnbs",)], writes=[("xnfk", k)], tag="ngb")
                xn_reads = [("xnfk", k) for k in range(ND)]
                if L1FFN_DEBUG and i == 0 and n == 0:
                    em.dma("gpsimd", dbg_mu, mu_f[:, 0, :], reads=[("muk",)], tag="dbgmu")
                    em.dma("gpsimd", dbg_rs, rs_f[:, 0, :], reads=[("rstdk", 0), ("rstdk", 1)], tag="dbgrs")
                    for k in range(ND):
                        em.dma("gpsimd", dbg_xnf[:, k, :], xnf[:, 0, k, :], reads=[("xnfk", k)], tag="dbgxn")
                for m in range(22):
                    wpa = m % 2
                    c0 = m * 128
                    c1 = min(DI2, c0 + 128)
                    w = c1 - c0
                    em.dma_fn("gpsimd", lambda e, c0=c0, c1=c1, wpa=wpa: e.dma_start(
                        out=w1a_b[:, wpa, :, 0:c1 - c0],
                        in_=W1_in[i].rearrange("(k p) c -> p k c", p=128)[:, :, c0:c1]),
                        writes=[("w1a", wpa, k) for k in range(ND)], tag="ldw1a")
                    em.dma_fn("gpsimd", lambda e, c0=c0, c1=c1, wpa=wpa: e.dma_start(
                        out=w1g_b[:, wpa, :, 0:c1 - c0],
                        in_=W1_in[i].rearrange("(k p) c -> p k c", p=128)[:, :, DI2 + c0:DI2 + c1]),
                        writes=[("w1g", wpa, k) for k in range(ND)], tag="ldw1g")
                    for half in range(2):
                        emit_mm_group(P, bank[4 + half][0:w, :], ("bank", 4 + half),
                                      [(w1a_b[:, wpa, k, 0:w], xnf[:, 0, k, half * 512:(half + 1) * 512]) for k in range(ND)],
                                      reads=xn_reads + [("w1a", wpa, k) for k in range(ND)], tag="mm_w1a")
                        emit_mm_group(P, bank[6 + half][0:w, :], ("bank", 6 + half),
                                      [(w1g_b[:, wpa, k, 0:w], xnf[:, 0, k, half * 512:(half + 1) * 512]) for k in range(ND)],
                                      reads=xn_reads + [("w1g", wpa, k) for k in range(ND)], tag="mm_w1g")
                    for half in range(2):
                        em.op("scalar", lambda e, m=m, w=w, half=half: e.activation(
                            asb[0:w, 0, :], bank[4 + half][0:w, :], AF.Identity,
                            bias=b1as[0:w, m:m + 1]),
                            reads=[("bank", 4 + half), ("b1as",)], writes=[("asbk",)], tag="ev_a")
                        em.op("scalar", lambda e, m=m, w=w, half=half: e.activation(
                            Pb[0:w, m, half * 512:(half + 1) * 512], bank[6 + half][0:w, :], AF.Gelu,
                            bias=b1gs[0:w, m:m + 1]),
                            reads=[("bank", 6 + half), ("b1gs",)], writes=[("Pb", m)], tag="ev_g")
                        em.op("vector", lambda e, m=m, w=w, half=half: e.tensor_tensor(
                            Pb[0:w, m, half * 512:(half + 1) * 512], asb[0:w, 0, :],
                            Pb[0:w, m, half * 512:(half + 1) * 512], ALU.mult),
                            reads=[("asbk",), ("Pb", m)], writes=[("Pb", m)], tag="pmul")
                if L1FFN_DEBUG and i == 0 and n == 0:
                    for kf in range(22):
                        em.dma("gpsimd", dbg_pb[:, kf, :], Pb[:, kf, :], reads=[("Pb", kf)], tag="dbgpb")
                for s_ in range(NTC):
                    t = n * NTC + s_
                    em.dma("sync", xw[:, pa, :], src[t * 128:(t + 1) * 128, :], writes=[("xw", pa)], tag="ldx2")
                    for dc in range(2):
                        bko = dc
                        pairs = []
                        for kf in range(22):
                            kk0 = kf * 128
                            w = min(DI2, kk0 + 128) - kk0
                            pairs.append((Pb[0:w, kf, s_ * 128:(s_ + 1) * 128], w2r_sb[0:w, kf, dc * 512:(dc + 1) * 512]))
                        pairs.append((ones1[:, :], b2sb[:, dc * 512:(dc + 1) * 512]))
                        emit_mm_group(P, bank[bko][:, :], BK(bko), pairs,
                                      reads=[("Pb", kf) for kf in range(22)] + [("w2r",), ("o1",), ("b2s",)],
                                      tag="mm_w2f")
                        em.op("vector", lambda e, dc=dc, pa=pa, bko=bko: e.tensor_tensor(
                            xobf[:, pa, dc * 512:(dc + 1) * 512], bank[bko][:, :],
                            xw[:, pa, dc * 512:(dc + 1) * 512], ALU.add),
                            reads=[BK(bko), ("xw", pa)], writes=[("xof", pa, dc)], tag="residf")
                    em.dma("sync", dst[t * 128:(t + 1) * 128, :], xobf[:, pa, :],
                           reads=[("xof", pa, 0), ("xof", pa, 1)], writes=[("dstf", t)], tag="stxf")
        em.barrier()

    cur = 0
    for i in range(L1FFN_DEPTHS):
        if "a" in L1FFN_PARTS:
            emit_attn_depth(i, xd[cur], xd[1 - cur])
            cur = 1 - cur
        if "f" in L1FFN_PARTS:
            emit_ffn_depth(i, xd[cur], xd[1 - cur])
            cur = 1 - cur

    with ExitStack() as s9:
        A = lambda n, sh, dt=F32: s9.enter_context(nc.sbuf_tensor(P.uniq(n), list(sh), dt))
        xcf = A("xcf", [128, 2, D])
        xsf = A("xsf", [128, 2, D])
        xof2 = A("xof2", [128, 2, D])
        for t in range(NT):
            pa = t % 2
            em.dma("sync", xcf[:, pa, :], x_in[t * 128:(t + 1) * 128, :], writes=[("xf", pa)], tag="ldA")
            em.dma("sync", xsf[:, pa, :], xd[cur][t * 128:(t + 1) * 128, :], writes=[("xs", pa)], tag="ldB")
            em.op("vector", lambda e, pa=pa: e.scalar_tensor_tensor(xof2[:, pa, :], xsf[:, pa, :], 0.5,
                                                                    xcf[:, pa, :], ALU.mult, ALU.add),
                  reads=[("xf", pa), ("xs", pa)], writes=[("xo2", pa)], tag="res")
            em.dma("sync", xo_out[t * 128:(t + 1) * 128, :], xof2[:, pa, :],
                   reads=[("xo2", pa)], writes=[("st", t)], tag="stf")
    inst_ctx.close()
    em.barrier()
    return P


def _l1ffn_tables():
    n, DH, WIN = 2048, 64, 256
    half = np.arange(0, DH, 2, dtype=np.float64)
    inv_freq = 1.0 / 10000.0 ** (half / DH)
    t = np.arange(n, dtype=np.float64)
    freqs = np.concatenate([t[:, None] * inv_freq[None, :]] * 2, -1)
    base_scale = (half + 0.4 * DH) / (1.4 * DH)
    power = (t - n // 2) / (WIN // 2)
    scale = np.concatenate([base_scale[None, :] ** power[:, None]] * 2, -1)
    cs = np.cos(freqs) * scale
    ss = np.sin(freqs) * scale
    ci = np.cos(freqs) / scale
    si = np.sin(freqs) / scale
    lay = lambda v: np.ascontiguousarray(v.reshape(16, 128, v.shape[-1]).transpose(1, 0, 2)).astype(np.float32)
    return {
        "tab_cs": lay(cs), "tab_sslo": lay(-ss[:, 0:32]), "tab_sshi": lay(ss[:, 32:64]),
        "tab_ci": lay(ci), "tab_silo": lay(-si[:, 0:32]), "tab_sihi": lay(si[:, 32:64]),
    }


def _l1ffn_masks():
    r = np.arange(128)[:, None]
    j = np.arange(384)[None, :]
    mf = np.where((j >= r) & (j <= 256 + r), 0.0, NEG).astype(np.float32)
    m1 = mf.copy(); m1[:, 0:128] = NEG
    m0 = mf.copy(); m0[:, 0:256] = NEG
    return mf, m1, m0


def run_l1ffn(x_full, Wqkv, q_scale, k_scale, Wo, ln_g, ln_b, W1, b1, W2, b2):
    P = build_l1ffn()
    nc = P.close_build()
    DI2 = 2730
    pmj6 = lambda v: np.ascontiguousarray(v.reshape(6, ND, 128).transpose(2, 0, 1))
    b1pad = np.zeros((6, 2, 2816), np.float32)
    b1pad[:, 0, 0:DI2] = b1[:, 0:DI2]
    b1pad[:, 1, 0:DI2] = b1[:, DI2:]
    b1a = np.ascontiguousarray(b1pad[:, 0].reshape(6, 22, 128).transpose(2, 0, 1))
    b1g = np.ascontiguousarray(b1pad[:, 1].reshape(6, 22, 128).transpose(2, 0, 1))
    mf, m1, m0 = _l1ffn_masks()
    base = dict(_l1ffn_tables())
    base.update({
        "Wqkv": Wqkv, "Wo": Wo, "lng": pmj6(ln_g), "lnb": pmj6(ln_b),
        "W1": W1, "b1a": b1a, "b1g": b1g, "W2": W2, "b2": b2,
        "qs": q_scale, "ks": k_scale,
        "identf": np.eye(128, dtype=np.float32), "identb": np.eye(128, dtype=BFNP),
        "m_full": mf, "m_1": m1, "m_0": m0,
        "ones128": np.ones((128, 1), np.float32), "ones1": np.ones((1, 128), np.float32),
    })
    xs = x_full.reshape(NCORES, T, D)
    in_maps = [dict(base, x=np.ascontiguousarray(xs[c])) for c in range(NCORES)]
    res = run_bass_kernel_spmd(nc, in_maps, core_ids=list(range(NCORES)))
    out = np.stack([res.results[c]["xo"] for c in range(NCORES)])
    return out.reshape(16, 2048, D)


_PROG_CACHE = {}


def _get_prog(name, builder):
    if name not in _PROG_CACHE:
        P = builder()
        _PROG_CACHE[name] = P.close_build()
    return _PROG_CACHE[name]


def kernel(**inp):
    if FUSED:
        return kernel_fused(**inp)
    x = np.asarray(inp["L0_attn_x"], np.float32)

    x = run_moe_cached(x, inp["L0_ffn_wg"], inp["L0_ffn_w1"], inp["L0_ffn_b1"],
                       inp["L0_ffn_w2"], inp["L0_ffn_b2"])
    x = run_l0attn_cached(x, inp["L0_attn_Wqkv"], inp["L0_attn_Wproj"])
    x = run_moe_cached(x, inp["L0_ffn_wg"], inp["L0_ffn_w1"], inp["L0_ffn_b1"],
                       inp["L0_ffn_w2"], inp["L0_ffn_b2"])
    x = run_l1ffn_cached(x, inp["L1_ffn_Wqkv"], inp["L1_ffn_q_scale"], inp["L1_ffn_k_scale"],
                         inp["L1_ffn_Wo"], inp["L1_ffn_ln_g"], inp["L1_ffn_ln_b"],
                         inp["L1_ffn_W1"], inp["L1_ffn_b1"], inp["L1_ffn_W2"], inp["L1_ffn_b2"])
    x = run_l1attn_cached(x, inp["L1_attn_w_rms_mix"], inp["L1_attn_Wg"], inp["L1_attn_bg"],
                          inp["L1_attn_Wv"], inp["L1_attn_bv"], inp["L1_attn_Wd"], inp["L1_attn_bd"],
                          inp["L1_attn_w_rms_attn"], inp["L1_attn_Wqkv"], inp["L1_attn_Wo"],
                          inp["L1_attn_w_rms_ffn"], inp["L1_attn_Wgate"], inp["L1_attn_Wup"],
                          inp["L1_attn_Wout"])
    x = run_l1ffn_cached(x, inp["L1_ffn_Wqkv"], inp["L1_ffn_q_scale"], inp["L1_ffn_k_scale"],
                         inp["L1_ffn_Wo"], inp["L1_ffn_ln_g"], inp["L1_ffn_ln_b"],
                         inp["L1_ffn_W1"], inp["L1_ffn_b1"], inp["L1_ffn_W2"], inp["L1_ffn_b2"])
    return np.ascontiguousarray(x, dtype=np.float32)


def run_moe_cached(x_full, wg, w1, b1, w2, b2):
    nc = _get_prog("moe", build_moe)
    tri, ones128, ones1, _ = _moe_consts(0)
    ident = np.eye(128, dtype=np.float32)
    b1t = np.ascontiguousarray(b1.reshape(8, 32, 128).transpose(2, 0, 1))
    xs = x_full.reshape(NCORES, T, D)
    in_maps = []
    for c in range(NCORES):
        in_maps.append({
            "x": np.ascontiguousarray(xs[c]), "wg": wg, "w1": w1, "b1t": b1t,
            "w2": w2, "b2": b2, "identf": ident, "tri": tri,
            "ones128": ones128, "ones1": ones1, "prefmask": _moe_consts(c)[3], "ebase": _moe_ebase(),
            "identb": np.eye(128, dtype=BFNP),
        })
    res = run_bass_kernel_spmd(nc, in_maps, core_ids=list(range(NCORES)))
    return np.stack([res.results[c]["xo"] for c in range(NCORES)]).reshape(16, 2048, D)


def run_l0attn_cached(x_full, Wqkv, Wproj):
    nc = _get_prog("l0attn", build_l0attn)
    mask_band, mask_first = _l0attn_masks()
    ident = np.eye(128, dtype=np.float32)
    xs = x_full.reshape(NCORES, T, D)
    in_maps = [{
        "x": np.ascontiguousarray(xs[c]), "Wqkv": Wqkv, "Wproj": Wproj,
        "identf": ident, "identb": np.eye(128, dtype=BFNP),
        "mask_band": mask_band, "mask_first": mask_first,
    } for c in range(NCORES)]
    res = run_bass_kernel_spmd(nc, in_maps, core_ids=list(range(NCORES)))
    return np.stack([res.results[c]["xo"] for c in range(NCORES)]).reshape(16, 2048, D)


def run_l1attn_cached(x_full, w_rms_mix, Wg, bg, Wv, bv, Wd, bd, w_rms_attn, Wqkv, Wo,
                      w_rms_ffn, Wgate, Wup, Wout):
    nc = _get_prog("l1attn", build_l1attn)
    pmj = lambda v: np.ascontiguousarray(v.reshape(ND, 128).T)
    r = np.arange(128)[:, None]
    j = np.arange(256)[None, :]
    mband = np.where((j >= r + 1) & (j <= 128 + r), 0.0, NEG).astype(np.float32)
    jj = np.arange(128)[None, :]
    mfirst = np.where(jj <= r, 0.0, NEG).astype(np.float32)
    xs = x_full.reshape(NCORES, T, D)
    base = {
        "wmixt": pmj(w_rms_mix), "WgT": np.ascontiguousarray(Wg.T), "bgt": pmj(bg),
        "WvT": np.ascontiguousarray(Wv.T), "bvt": pmj(bv),
        "WdT": np.ascontiguousarray(Wd.T), "bdt": pmj(bd),
        "wattnt": pmj(w_rms_attn), "WqkvT": np.ascontiguousarray(Wqkv.T),
        "WoT": np.ascontiguousarray(Wo.T), "wffnt": pmj(w_rms_ffn),
        "WgateT": np.ascontiguousarray(Wgate.T), "WupT": np.ascontiguousarray(Wup.T),
        "WoutT": np.ascontiguousarray(Wout.T),
        "identf": np.eye(128, dtype=np.float32), "identb": np.eye(128, dtype=BFNP),
        "mask_band": mband, "mask_first": mfirst,
    }
    in_maps = [dict(base, x=np.ascontiguousarray(xs[c])) for c in range(NCORES)]
    res = run_bass_kernel_spmd(nc, in_maps, core_ids=list(range(NCORES)))
    return np.stack([res.results[c]["xo"] for c in range(NCORES)]).reshape(16, 2048, D)


def run_l1ffn_cached(x_full, Wqkv, q_scale, k_scale, Wo, ln_g, ln_b, W1, b1, W2, b2):
    nc = _get_prog("l1ffn", build_l1ffn)
    DI2 = 2730
    pmj6 = lambda v: np.ascontiguousarray(v.reshape(6, ND, 128).transpose(2, 0, 1))
    b1pad = np.zeros((6, 2, 2816), np.float32)
    b1pad[:, 0, 0:DI2] = b1[:, 0:DI2]
    b1pad[:, 1, 0:DI2] = b1[:, DI2:]
    b1a = np.ascontiguousarray(b1pad[:, 0].reshape(6, 22, 128).transpose(2, 0, 1))
    b1g = np.ascontiguousarray(b1pad[:, 1].reshape(6, 22, 128).transpose(2, 0, 1))
    mf, m1, m0 = _l1ffn_masks()
    base = dict(_l1ffn_tables())
    base.update({
        "Wqkv": Wqkv, "Wo": Wo, "lng": pmj6(ln_g), "lnb": pmj6(ln_b),
        "W1": W1, "b1a": b1a, "b1g": b1g, "W2": W2, "b2": b2,
        "qs": q_scale, "ks": k_scale,
        "identf": np.eye(128, dtype=np.float32), "identb": np.eye(128, dtype=BFNP),
        "m_full": mf, "m_1": m1, "m_0": m0,
        "ones128": np.ones((128, 1), np.float32), "ones1": np.ones((1, 128), np.float32),
    })
    xs = x_full.reshape(NCORES, T, D)
    in_maps = [dict(base, x=np.ascontiguousarray(xs[c])) for c in range(NCORES)]
    res = run_bass_kernel_spmd(nc, in_maps, core_ids=list(range(NCORES)))
    return np.stack([res.results[c]["xo"] for c in range(NCORES)]).reshape(16, 2048, D)


def build_fused():
    P = Prog("fused")
    DEPTH6, DI2 = 6, 2730
    V = {
        "identf": P.inp("identf", [128, 128], F32),
        "identb": P.inp("identb", [128, 128], BF16),
        "ones128": P.inp("ones128", [128, 1], F32),
        "ones1": P.inp("ones1", [1, 128], F32),
        "tri": P.inp("tri", [128, 128], F32),
        "prefmask": P.inp("prefmask", [128, 1], F32),
        "ebase": P.inp("ebase", [1, 8], F32),
        "mask_band": P.inp("l0_mask_band", [128, 256], F32),
        "mask_first": P.inp("mask_first", [128, 128], F32),
        "mask_band_l1": P.inp("l1_mask_band", [128, 256], F32),
        "m_full": P.inp("m_full", [128, 384], F32),
        "m_1": P.inp("m_1", [128, 384], F32),
        "m_0": P.inp("m_0", [128, 384], F32),
        "tab_cs": P.inp("tab_cs", [128, 16, 64], F32),
        "tab_sslo": P.inp("tab_sslo", [128, 16, 32], F32),
        "tab_sshi": P.inp("tab_sshi", [128, 16, 32], F32),
        "tab_ci": P.inp("tab_ci", [128, 16, 64], F32),
        "tab_silo": P.inp("tab_silo", [128, 16, 32], F32),
        "tab_sihi": P.inp("tab_sihi", [128, 16, 32], F32),
        "wg": P.inp("m_wg", [D, 8]),
        "w1": P.inp("m_w1", [8, D, 4096]),
        "b1t": P.inp("m_b1t", [128, 8, 32]),
        "w2": P.inp("m_w2", [8, 4096, D]),
        "b2": P.inp("m_b2", [8, D]),
        "Wqkv": P.inp("a0_Wqkv", [D, 3 * D]),
        "Wproj": P.inp("a0_Wproj", [D, D]),
        "wmixt": P.inp("A_wmixt", [128, ND]),
        "WgT": P.inp("A_WgT", [D, D]),
        "bgt": P.inp("A_bgt", [128, ND]),
        "WvT": P.inp("A_WvT", [D, D]),
        "bvt": P.inp("A_bvt", [128, ND]),
        "WdT": P.inp("A_WdT", [D, D]),
        "bdt": P.inp("A_bdt", [128, ND]),
        "wattnt": P.inp("A_wattnt", [128, ND]),
        "WqkvT": P.inp("A_WqkvT", [D, 3 * D]),
        "WoT": P.inp("A_WoT", [D, D]),
        "wffnt": P.inp("A_wffnt", [128, ND]),
        "WgateT": P.inp("A_WgateT", [D, 4096]),
        "WupT": P.inp("A_WupT", [D, 4096]),
        "WoutT": P.inp("A_WoutT", [4096, D]),
        "Wqkv6": P.inp("F_Wqkv", [DEPTH6, D, 3 * 1024]),
        "Wo6": P.inp("F_Wo", [DEPTH6, 1024, D]),
        "lng": P.inp("F_lng", [128, DEPTH6, ND]),
        "lnb": P.inp("F_lnb", [128, DEPTH6, ND]),
        "W1": P.inp("F_W1", [DEPTH6, D, 2 * DI2]),
        "b1a": P.inp("F_b1a", [128, DEPTH6, 22]),
        "b1g": P.inp("F_b1g", [128, DEPTH6, 22]),
        "W2": P.inp("F_W2", [DEPTH6, DI2, D]),
        "b26": P.inp("F_b2", [DEPTH6, D]),
        "qs": P.inp("F_qs", [DEPTH6, 64]),
        "ks": P.inp("F_ks", [DEPTH6, 64]),
    }
    x_in = P.inp("x", [T, D])
    xo_out = P.out("xo", [T, D])
    xA = P.dram("xA", [T, D])
    xB = P.dram("xB", [T, D])
    xC = P.dram("xC", [T, D])
    xD = P.dram("xD", [T, D])
    xE = P.dram("xE", [T, D])
    bank = [P.psum(f"bank{i}", [128, 512], F32) for i in range(8)]

    build_moe(P, V, x_in, xA, bank)
    build_l0attn(P, V, xA, xB, bank)
    build_moe(P, V, xB, xC, bank)
    build_l1ffn(P, V, xC, xD, bank)
    build_l1attn(P, V, xD, xE, bank)
    build_l1ffn(P, V, xE, xo_out, bank)
    return P


FUSED = True


def _fused_in_maps(inp):
    DI2 = 2730
    pmj = lambda v: np.ascontiguousarray(v.reshape(ND, 128).T.astype(np.float32))
    pmj6 = lambda v: np.ascontiguousarray(v.reshape(6, ND, 128).transpose(2, 0, 1).astype(np.float32))
    tT = lambda v: np.ascontiguousarray(np.asarray(v, np.float32).T)
    b1 = np.asarray(inp["L1_ffn_b1"], np.float32)
    b1pad = np.zeros((6, 2, 2816), np.float32)
    b1pad[:, 0, 0:DI2] = b1[:, 0:DI2]
    b1pad[:, 1, 0:DI2] = b1[:, DI2:]
    mb0, mf0 = _l0attn_masks()
    r = np.arange(128)[:, None]
    j = np.arange(256)[None, :]
    mb1 = np.where((j >= r + 1) & (j <= 128 + r), 0.0, NEG).astype(np.float32)
    mff, mf1, mf2 = _l1ffn_masks()
    tri, ones128, ones1, _ = _moe_consts(0)
    base = dict(_l1ffn_tables())
    base.update({
        "identf": np.eye(128, dtype=np.float32), "identb": np.eye(128, dtype=BFNP),
        "ones128": ones128, "ones1": ones1, "tri": tri, "ebase": _moe_ebase(),
        "l0_mask_band": mb0, "mask_first": mf0, "l1_mask_band": mb1,
        "m_full": mff, "m_1": mf1, "m_0": mf2,
        "m_wg": np.asarray(inp["L0_ffn_wg"], np.float32),
        "m_w1": np.asarray(inp["L0_ffn_w1"], np.float32),
        "m_b1t": np.ascontiguousarray(np.asarray(inp["L0_ffn_b1"], np.float32).reshape(8, 32, 128).transpose(2, 0, 1)),
        "m_w2": np.asarray(inp["L0_ffn_w2"], np.float32),
        "m_b2": np.asarray(inp["L0_ffn_b2"], np.float32),
        "a0_Wqkv": np.asarray(inp["L0_attn_Wqkv"], np.float32),
        "a0_Wproj": np.asarray(inp["L0_attn_Wproj"], np.float32),
        "A_wmixt": pmj(np.asarray(inp["L1_attn_w_rms_mix"], np.float32)),
        "A_WgT": tT(inp["L1_attn_Wg"]), "A_bgt": pmj(np.asarray(inp["L1_attn_bg"], np.float32)),
        "A_WvT": tT(inp["L1_attn_Wv"]), "A_bvt": pmj(np.asarray(inp["L1_attn_bv"], np.float32)),
        "A_WdT": tT(inp["L1_attn_Wd"]), "A_bdt": pmj(np.asarray(inp["L1_attn_bd"], np.float32)),
        "A_wattnt": pmj(np.asarray(inp["L1_attn_w_rms_attn"], np.float32)),
        "A_WqkvT": tT(inp["L1_attn_Wqkv"]), "A_WoT": tT(inp["L1_attn_Wo"]),
        "A_wffnt": pmj(np.asarray(inp["L1_attn_w_rms_ffn"], np.float32)),
        "A_WgateT": tT(inp["L1_attn_Wgate"]), "A_WupT": tT(inp["L1_attn_Wup"]),
        "A_WoutT": tT(inp["L1_attn_Wout"]),
        "F_Wqkv": np.asarray(inp["L1_ffn_Wqkv"], np.float32),
        "F_Wo": np.asarray(inp["L1_ffn_Wo"], np.float32),
        "F_lng": pmj6(np.asarray(inp["L1_ffn_ln_g"], np.float32)),
        "F_lnb": pmj6(np.asarray(inp["L1_ffn_ln_b"], np.float32)),
        "F_W1": np.asarray(inp["L1_ffn_W1"], np.float32),
        "F_b1a": np.ascontiguousarray(b1pad[:, 0].reshape(6, 22, 128).transpose(2, 0, 1)),
        "F_b1g": np.ascontiguousarray(b1pad[:, 1].reshape(6, 22, 128).transpose(2, 0, 1)),
        "F_W2": np.asarray(inp["L1_ffn_W2"], np.float32),
        "F_b2": np.asarray(inp["L1_ffn_b2"], np.float32),
        "F_qs": np.asarray(inp["L1_ffn_q_scale"], np.float32),
        "F_ks": np.asarray(inp["L1_ffn_k_scale"], np.float32),
    })
    x = np.asarray(inp["L0_attn_x"], np.float32).reshape(NCORES, T, D)
    in_maps = []
    for c in range(NCORES):
        m = dict(base)
        m["x"] = np.ascontiguousarray(x[c])
        m["prefmask"] = _moe_consts(c)[3]
        in_maps.append(m)
    return in_maps


def kernel_fused(**inp):
    nc = _get_prog("fused", build_fused)
    in_maps = _fused_in_maps(inp)
    res = run_bass_kernel_spmd(nc, in_maps, core_ids=list(range(NCORES)))
    out = np.stack([res.results[c]["xo"] for c in range(NCORES)])
    return np.ascontiguousarray(out.reshape(16, 2048, D), dtype=np.float32)


# version 06
# speedup vs baseline: 1.0614x; 1.0579x over previous
import numpy as np
import ml_dtypes
from contextlib import ExitStack

import concourse.bass as bass
import concourse.mybir as mybir
from concourse.bass_utils import run_bass_kernel_spmd
from concourse.bass_core_rust import IndirectOffsetOnAxis

F32 = mybir.dt.float32
I32 = mybir.dt.int32
BF16 = mybir.dt.bfloat16
AF = mybir.ActivationFunctionType
ALU = mybir.AluOpType
AX = mybir.AxisListType.X
BFNP = ml_dtypes.bfloat16

NCORES = 8
S = 2048
D = 1024
T = 2 * S
NT = T // 128
ND = D // 128
NT_SEQ = S // 128
NEG = -1.0e30

COMPUTE_ENGINES = ("tensor", "vector", "scalar", "gpsimd")
DMA_QUEUES = ("sync", "scalar", "gpsimd")


class _Ev:
    __slots__ = ("idx", "engine", "fn", "deps", "is_dma", "mile", "semval", "tag", "inc", "semkey", "gate")

    def __init__(self, idx, engine, fn, deps, is_dma, tag):
        self.idx = idx
        self.engine = engine
        self.fn = fn
        self.deps = deps
        self.is_dma = is_dma
        self.inc = 16 if is_dma else 1
        self.mile = False
        self.semkey = None
        self.gate = None
        self.semval = None
        self.tag = tag


class Em:

    NSUB = 4

    def __init__(self, nc, ctx):
        self.nc = nc
        self.events = []
        self.res = {}
        self.sem = {}
        for e in COMPUTE_ENGINES:
            self.sem[e] = ctx.enter_context(nc.semaphore(f"sem_{e}"))
        for q in DMA_QUEUES:
            for r in range(self.NSUB):
                self.sem[f"dma_{q}_{r}"] = ctx.enter_context(nc.semaphore(f"sem_dma_{q}_{r}"))

    @staticmethod
    def _excl(k):
        return isinstance(k, tuple) and k and isinstance(k[0], str) and (
            k[0].startswith("bank") or k[0].startswith("ps"))

    def _track(self, idx, reads, writes, deps):
        reads = [k for k in reads]
        writes = [k for k in writes]
        excl = [k for k in reads if self._excl(k)]
        if excl:
            reads = [k for k in reads if not self._excl(k)]
            writes = writes + excl
        dep = set(deps)
        for k in reads:
            r = self.res.get(k)
            if r is not None and r[0] is not None:
                dep.add(r[0])
        for k in writes:
            r = self.res.get(k)
            if r is not None:
                if r[0] is not None:
                    dep.add(r[0])
                dep.update(r[1])
            self.res[k] = [idx, []]
        for k in reads:
            r = self.res.setdefault(k, [None, []])
            r[1].append(idx)
        dep.discard(idx)
        return sorted(dep)

    def op(self, engine, fn, reads=(), writes=(), deps=(), tag=""):
        idx = len(self.events)
        d = self._track(idx, reads, writes, deps)
        self.events.append(_Ev(idx, engine, fn, d, False, tag))
        return idx

    def dma(self, queue, out, in_, reads=(), writes=(), deps=(), tag="dma"):
        idx = len(self.events)
        d = self._track(idx, reads, writes, deps)
        fn = (lambda eng, out=out, in_=in_: eng.dma_start(out=out, in_=in_))
        self.events.append(_Ev(idx, queue, fn, d, True, tag))
        return idx

    def dma_fn(self, queue, fn, reads=(), writes=(), deps=(), tag="dma", inc=16):
        idx = len(self.events)
        d = self._track(idx, reads, writes, deps)
        ev = _Ev(idx, queue, fn, d, True, tag)
        ev.inc = inc
        self.events.append(ev)
        return idx

    def barrier(self):
        last = {}
        for ev in self.events:
            key = ("q:" + ev.engine) if ev.is_dma else ev.engine
            last[key] = ev.idx
        deps = sorted(set(last.values()))
        ids = []
        for e in COMPUTE_ENGINES:
            ids.append(self.op(e, lambda eng: eng.nop(), deps=deps, tag="barrier"))
        for k in self.res:
            self.res[k] = [ids[0], ids[:]]
        return ids

    def finalize(self, block):
        events = self.events
        for ev in events:
            for d in ev.deps:
                de = events[d]
                if de.is_dma or not (de.engine == ev.engine == "tensor" and not ev.is_dma):
                    de.mile = True
        cnt = {}
        qn = {}
        for ev in events:
            if ev.is_dma:
                q = ev.engine
                n = qn.get(q, 0)
                qn[q] = n + 1
                semkey = f"dma_{q}_{n % self.NSUB}"
                ev.semkey = semkey
                prev = cnt.get(semkey, 0)
                ev.gate = (semkey, prev) if prev else None
                cnt[semkey] = prev + ev.inc
                ev.semval = cnt[semkey]
                ev.mile = True
            elif ev.mile:
                semkey = ev.engine
                ev.semkey = semkey
                cnt[semkey] = cnt.get(semkey, 0) + 1
                ev.semval = cnt[semkey]
        self.final_counts = dict(cnt)

        per_engine = {e: [] for e in COMPUTE_ENGINES}
        sync_evs = []
        for ev in events:
            if ev.is_dma and ev.engine == "sync":
                sync_evs.append(ev)
            else:
                per_engine[ev.engine].append(ev)

        sems = self.sem

        def run(eng, engine_name, evs, final=False):
            watermark = {}
            for ev in evs:
                waits = []
                if ev.is_dma and ev.gate is not None:
                    waits.append(ev.gate)
                for d in ev.deps:
                    de = events[d]
                    if (not de.is_dma) and (not ev.is_dma) and de.engine == engine_name == "tensor":
                        continue
                    if de.semval is None:
                        raise RuntimeError(f"dep on non-milestone event {de.idx} {de.tag}")
                    waits.append((de.semkey, de.semval))
                for semkey, val in waits:
                    if watermark.get(semkey, 0) >= val:
                        continue
                    eng.wait_ge(sems[semkey], val)
                    watermark[semkey] = val
                inst = ev.fn(eng)
                if ev.mile:
                    if inst is None:
                        raise RuntimeError(f"milestone event {ev.idx} ({ev.tag}) returned None")
                    inst.then_inc(sems[ev.semkey], ev.inc if ev.is_dma else 1)
            if final:
                for k, v in self.final_counts.items():
                    if k.startswith("dma_") and watermark.get(k, 0) < v:
                        eng.wait_ge(sems[k], v)

        @block.tensor
        def _(tensor):
            run(tensor, "tensor", per_engine["tensor"])

        @block.vector
        def _(vector):
            run(vector, "vector", per_engine["vector"])

        @block.scalar
        def _(scalar):
            run(scalar, "scalar", per_engine["scalar"])

        @block.gpsimd
        def _(gpsimd):
            run(gpsimd, "gpsimd", per_engine["gpsimd"])

        @block.sync
        def _(sync):
            run(sync, "__sync__", sync_evs, final=True)


class Prog:
    def __init__(self, name="prog"):
        self.nc = bass.Bass("TRN2", target_bir_lowering=False, debug=False)
        self.ctx = ExitStack()
        self.em = Em(self.nc, self.ctx)
        self._uniq = 0

    def uniq(self, base):
        self._uniq += 1
        return f"{base}_{self._uniq}"

    def inp(self, name, shape, dtype=F32):
        return self.nc.dram_tensor(name, list(shape), dtype, kind="ExternalInput").ap()

    def out(self, name, shape, dtype=F32):
        return self.nc.dram_tensor(name, list(shape), dtype, kind="ExternalOutput").ap()

    def dram(self, name, shape, dtype=F32):
        return self.nc.dram_tensor(self.uniq(name), list(shape), dtype).ap()

    def sb(self, name, shape, dtype=F32):
        return self.ctx.enter_context(self.nc.sbuf_tensor(self.uniq(name), list(shape), dtype))

    def psum(self, name, shape, dtype=F32):
        return self.ctx.enter_context(self.nc.psum_tensor(self.uniq(name), list(shape), dtype))

    def close_build(self):
        block = self.ctx.enter_context(self.nc.Block())
        self.em.finalize(block)
        self.ctx.close()
        return self.nc


def alloc_ln_scratch(P, tag, npar=2):
    return {
        "sq": P.sb(tag + "_sq", [128, npar, D], F32),
        "s": P.sb(tag + "_s", [128, npar, 1], F32),
        "s2": P.sb(tag + "_s2", [128, npar, 1], F32),
        "mu": P.sb(tag + "_mu", [128, npar, 1], F32),
        "m2": P.sb(tag + "_m2", [128, npar, 1], F32),
        "var": P.sb(tag + "_var", [128, npar, 1], F32),
        "lnv": P.sb(tag + "_lnv", [128, npar, 1], F32),
        "rstd": P.sb(tag + "_rstd", [128, npar, 1], F32),
        "xn": P.sb(tag + "_xn", [128, npar, D], F32),
        "tag": tag,
    }


def emit_ln_tile(P, scr, pa, x_ap, x_key, rms=False, eps=1e-5):
    em = P.em
    kk = (scr["tag"], pa)
    sq = scr["sq"][:, pa, :]
    s_ = scr["s"][:, pa, :]
    s2 = scr["s2"][:, pa, :]
    mu = scr["mu"][:, pa, :]
    m2 = scr["m2"][:, pa, :]
    var = scr["var"][:, pa, :]
    lnv = scr["lnv"][:, pa, :]
    rstd = scr["rstd"][:, pa, :]
    xn = scr["xn"][:, pa, :]

    em.op("scalar", lambda e: e.activation(sq, x_ap, AF.Square, accum_out=s2),
          reads=[x_key], writes=[kk + ("sq",), kk + ("s2",)], tag="ln_sq")
    if not rms:
        em.op("vector", lambda e: e.reduce_sum(s_, x_ap, axis=AX),
              reads=[x_key], writes=[kk + ("s",)], tag="ln_s")
        em.op("scalar", lambda e: e.mul(mu, s_, 1.0 / D),
              reads=[kk + ("s",)], writes=[kk + ("mu",)], tag="ln_mu")
        em.op("vector", lambda e: e.tensor_scalar(m2, mu, mu, -float(eps), ALU.mult, ALU.add),
              reads=[kk + ("mu",)], writes=[kk + ("m2",)], tag="ln_m2")
        em.op("vector", lambda e: e.scalar_tensor_tensor(var, s2, 1.0 / D, m2, ALU.mult, ALU.subtract),
              reads=[kk + ("s2",), kk + ("m2",)], writes=[kk + ("var",)], tag="ln_var")
    else:
        em.op("vector", lambda e: e.tensor_scalar(var, s2, 1.0 / D, float(eps), ALU.mult, ALU.add),
              reads=[kk + ("s2",)], writes=[kk + ("var",)], tag="rms_var")
    em.op("scalar", lambda e: e.activation(lnv, var, AF.Ln),
          reads=[kk + ("var",)], writes=[kk + ("lnv",)], tag="ln_ln")
    em.op("scalar", lambda e: e.activation(rstd, lnv, AF.Exp, scale=-0.5),
          reads=[kk + ("lnv",)], writes=[kk + ("rstd",)], tag="ln_exp")
    if rms:
        em.op("vector", lambda e: e.tensor_scalar(xn, x_ap, rstd, None, ALU.mult),
              reads=[x_key, kk + ("rstd",)], writes=[kk + ("xn",)], tag="rms_apply")
    else:
        em.op("vector", lambda e: e.tensor_scalar(xn, x_ap, mu, rstd, ALU.subtract, ALU.mult),
              reads=[x_key, kk + ("mu",), kk + ("rstd",)], writes=[kk + ("xn",)], tag="ln_apply")
    return kk + ("xn",)


def emit_mm_group(P, psum_ap, psum_key, pairs, reads, n_parts=None, tag="mm",
                  tile_position=None):
    em = P.em

    def fn(e):
        n = len(pairs)
        inst = None
        for i, (lt, rh) in enumerate(pairs):
            inst = e.matmul(psum_ap, lt, rh, start=(i == 0), stop=(i == n - 1),
                            tile_position=tile_position)
        return inst

    return em.op("tensor", fn, reads=list(reads), writes=[psum_key], tag=tag)


def build_l0attn(P=None, V=None, x_in=None, xo_out=None, bank=None):
    standalone = P is None
    if standalone:
        P = Prog("l0attn")
        V = {"Wqkv": P.inp("Wqkv", [D, 3 * D]), "Wproj": P.inp("Wproj", [D, D]),
             "identf": P.inp("identf", [128, 128], F32), "identb": P.inp("identb", [128, 128], BF16),
             "mask_band": P.inp("mask_band", [128, 256], F32),
             "mask_first": P.inp("mask_first", [128, 128], F32)}
        x_in = P.inp("x", [T, D])
        xo_out = P.out("xo", [T, D])
    em = P.em
    nc = P.nc
    wqkv_in = V["Wqkv"]; wproj_in = V["Wproj"]
    identf = V["identf"]; identb_in = V["identb"]
    mask_band = V["mask_band"]; mask_first = V["mask_first"]

    inst_ctx = ExitStack()
    _sb = lambda n, sh, dt=F32: inst_ctx.enter_context(nc.sbuf_tensor(P.uniq(n), list(sh), dt))
    wqkv = _sb("wqkv", [128, ND, 3 * D], BF16)
    wproj = _sb("wproj", [128, ND, D], BF16)
    ident = _sb("ident", [128, 128], F32)
    identb = _sb("identb", [128, 128], BF16)
    mband = _sb("mband", [128, 256], F32)
    mfirst = _sb("mfirst", [128, 128], F32)
    em.dma_fn("gpsimd", lambda e: e.dma_start(
        out=wqkv[:, :, :], in_=wqkv_in.rearrange("(k p) c -> p k c", p=128)),
        writes=[("wqkv", k) for k in range(ND)], tag="ld_wqkv")
    em.dma_fn("gpsimd", lambda e: e.dma_start(
        out=wproj[:, :, :], in_=wproj_in.rearrange("(k p) c -> p k c", p=128)),
        writes=[("wproj", k) for k in range(ND)], tag="ld_wproj")
    em.dma("sync", ident[:, :], identf, writes=[("ident",)], tag="ld_ident")
    em.dma("sync", identb[:, :], identb_in, writes=[("identb",)], tag="ld_identb")
    em.dma("sync", mband[:, :], mask_band, writes=[("mband",)], tag="ld_mb")
    em.dma("sync", mfirst[:, :], mask_first, writes=[("mfirst",)], tag="ld_mf")

    x_cur = _sb("x_cur", [128, 2, D], F32)
    lns = {k: _sb("ln0" + k, [128, 2, D if k in ("sq", "xn") else 1]) for k in
           ("sq", "s", "s2", "mu", "m2", "var", "lnv", "rstd", "xn")}
    lns["tag"] = "ln0"
    xnT = _sb("xnT", [128, 2, ND, 128], BF16)
    qT = _sb("qT", [128, 2, ND, 128], BF16)
    kT = _sb("kT", [128, 2, ND, 128], BF16)
    v_roll = _sb("v_roll", [128, 2, D], BF16)
    kTg = _sb("kTg", [128, ND, 32], BF16)
    vg = _sb("vg", [32, D], BF16)
    probs = _sb("probs", [128, 2, 320], F32)
    probs_bf = _sb("probs_bf", [128, 2, 320], BF16)
    probs_bf2 = _sb("probs_bf2", [128, 2, 320], BF16)
    pT = _sb("pT", [128, 2, 3, 128], BF16)
    nm = _sb("nm", [128, 2, 1], F32)
    sume = _sb("sume", [128, 2, 1], F32)
    rec = _sb("rec", [128, 2, 1], F32)
    yT = _sb("yT", [128, 2, ND, 128], BF16)
    xo_buf = _sb("xo_buf", [128, 2, D], F32)

    if bank is None:
        bank = [P.psum(f"bank{i}", [128, 512], F32) for i in range(8)]
    ps_tr = bank[0][:, 0:128]
    ps_qk = [bank[1][:, 0:128], bank[2][:, 0:128]]
    ps_v = bank[3][:, :]
    ps_sc = bank[4][:, :]
    ps_pT = bank[5][:, :].bitcast(BF16)[:, 0:128]
    ps_av = bank[6][:, 0:128]
    ps_pr = bank[7][:, :]

    for t in range(NT):
        pa = t % 2
        t_loc = t % NT_SEQ

        em.dma("sync", x_cur[:, pa, :], x_in[t * 128:(t + 1) * 128, :],
               writes=[("x", pa)], tag="ld_x")
        xn_key = emit_ln_tile(P, lns, pa, x_cur[:, pa, :], ("x", pa))
        xn = lns["xn"][:, pa, :]

        for k in range(ND):
            em.op("tensor", lambda e, k=k, xn=xn: e.transpose(ps_tr[:, :], xn[:, k * 128:(k + 1) * 128], ident[:, :]),
                  reads=[xn_key, ("ident",)], writes=[("ps_tr",)], tag="tr_xn")
            em.op("vector", lambda e, k=k, pa=pa: e.tensor_copy(xnT[:, pa, k, :], ps_tr[:, :]),
                  reads=[("ps_tr",)], writes=[("xnT", pa, k)], tag="tr_evac")

        xnT_reads = [("xnT", pa, k) for k in range(ND)]
        for hp in range(ND):
            bank = ps_qk[hp % 2]
            bkey = ("ps_qk", hp % 2)
            emit_mm_group(P, bank[:, :], bkey,
                          [(wqkv[:, k, hp * 128:(hp + 1) * 128], xnT[:, pa, k, :]) for k in range(ND)],
                          reads=xnT_reads + [("wqkv", k) for k in range(ND)], tag="mm_q")
            em.op("scalar", lambda e, hp=hp, bank=bank, pa=pa: e.activation(qT[:, pa, hp, :], bank[:, :], AF.Copy, scale=0.125),
                  reads=[bkey], writes=[("qT", pa, hp)], tag="ev_q")
            emit_mm_group(P, bank[:, :], bkey,
                          [(wqkv[:, k, D + hp * 128:D + (hp + 1) * 128], xnT[:, pa, k, :]) for k in range(ND)],
                          reads=xnT_reads + [("wqkv", k) for k in range(ND)], tag="mm_k")
            em.op("scalar", lambda e, hp=hp, bank=bank, pa=pa: e.copy(kT[:, pa, hp, :], bank[:, :]),
                  reads=[bkey], writes=[("kT", pa, hp)], tag="ev_k")
        for c in range(2):
            emit_mm_group(P, ps_v[:, :], ("ps_v",),
                          [(xnT[:, pa, k, :], wqkv[:, k, 2 * D + c * 512:2 * D + (c + 1) * 512]) for k in range(ND)],
                          reads=xnT_reads + [("wqkv", k) for k in range(ND)], tag="mm_v")
            em.op("vector", lambda e, c=c, pa=pa: e.tensor_copy(v_roll[:, pa, c * 512:(c + 1) * 512], ps_v[:, :]),
                  reads=[("ps_v",)], writes=[("v", pa, c)], tag="ev_v")

        em.op("vector",
              lambda e, pa=pa, t_loc=t_loc: e.tensor_copy(kTg[:, :, 2 * t_loc:2 * t_loc + 2],
                                                          kT.ap()[:, pa, :, 0:128:64]),
              reads=[("kT", pa, k) for k in range(ND)], writes=[("kTg", 2 * t_loc), ("kTg", 2 * t_loc + 1)],
              tag="kTg")
        em.dma_fn("scalar",
                  lambda e, t_loc=t_loc, pa=pa: e.dma_start(out=vg[2 * t_loc:2 * t_loc + 2, :], in_=v_roll.ap()[0:128:64, pa, :]),
                  reads=[("v", pa, 0), ("v", pa, 1)],
                  writes=[("vg", 2 * t_loc), ("vg", 2 * t_loc + 1)], tag="vg")

        g = 2 * (t_loc - 1) if t_loc >= 1 else 0
        Wd = 128 if t_loc == 0 else 256 + g
        for h in range(16):
            hp = h // 2
            base = (h % 2) * 64
            q_ap = qT[base:base + 64, pa, hp, :]
            sc_reads = [("qT", pa, hp)]
            pairs = []
            if t_loc == 0:
                pairs.append((q_ap, kT[base:base + 64, pa, hp, :], 0))
                sc_reads.append(("kT", pa, hp))
            else:
                pairs.append((q_ap, kT[base:base + 64, 1 - pa, hp, :], 0))
                pairs.append((q_ap, kT[base:base + 64, pa, hp, :], 128))
                sc_reads += [("kT", 1 - pa, hp), ("kT", pa, hp)]
                if g > 0:
                    pairs.append((q_ap, kTg[base:base + 64, hp, 0:g], 256))
                    sc_reads += [("kTg", i) for i in range(g)]

            def fn_sc(e, pairs=pairs, g=g):
                inst = None
                for (lt, rh, off) in pairs:
                    n = rh.shape[-1]
                    inst = e.matmul(ps_sc[:, off:off + n], lt, rh, start=True, stop=True)
                return inst
            em.op("tensor", fn_sc, reads=sc_reads, writes=[("ps_sc",)], tag="mm_sc")

            pr = probs[:, pa, :]
            if t_loc == 0:
                em.op("vector", lambda e, pr=pr: e.tensor_tensor(pr[:, 0:128], ps_sc[:, 0:128], mfirst[:, :], ALU.add),
                      reads=[("ps_sc",), ("mfirst",)], writes=[("probs", pa)], tag="mask0")
            else:
                em.op("vector", lambda e, pr=pr: e.tensor_tensor(pr[:, 0:256], ps_sc[:, 0:256], mband[:, :], ALU.add),
                      reads=[("ps_sc",), ("mband",)], writes=[("probs", pa)], tag="mask")
                if g > 0:
                    em.op("scalar", lambda e, pr=pr, g=g: e.copy(pr[:, 256:256 + g], ps_sc[:, 256:256 + g]),
                          reads=[("ps_sc",)], writes=[("probs", pa)], tag="cpglob")
            em.op("vector", lambda e, pr=pr, Wd=Wd, pa=pa: e.reduce_max(nm[:, pa, :], pr[:, 0:Wd], axis=AX, negate=True),
                  reads=[("probs", pa)], writes=[("nm", pa)], tag="negmax")
            em.op("scalar", lambda e, pr=pr, Wd=Wd, pa=pa: e.activation(probs_bf[:, pa, 0:Wd], pr[:, 0:Wd], AF.Exp,
                                                                 bias=nm[:, pa, :], accum_out=sume[:, pa, :]),
                  reads=[("probs", pa), ("nm", pa)], writes=[("probs_bf", pa), ("sume", pa)], tag="exp")
            em.op("vector", lambda e, pa=pa: e.reciprocal(rec[:, pa, :], sume[:, pa, :]),
                  reads=[("sume", pa)], writes=[("rec", pa)], tag="recip")
            em.op("vector", lambda e, Wd=Wd, pa=pa: e.tensor_scalar(probs_bf2[:, pa, 0:Wd], probs_bf[:, pa, 0:Wd],
                                                                    rec[:, pa, :], None, ALU.mult),
                  reads=[("probs_bf", pa), ("rec", pa)], writes=[("probs_bf2", pa)], tag="pscale")

            nch = (Wd + 127) // 128
            for c in range(nch):
                w = min(128, Wd - c * 128)
                em.op("tensor", lambda e, c=c, w=w, pa=pa: e.transpose(ps_pT[0:w, :], probs_bf2[:, pa, c * 128:c * 128 + w], identb[:, :]),
                      reads=[("probs_bf2", pa), ("identb",)], writes=[("ps_pT",)], tag="tr_p")
                em.op("scalar", lambda e, c=c, w=w, pa=pa: e.copy(pT[0:w, pa, c, :], ps_pT[0:w, :]),
                      reads=[("ps_pT",)], writes=[("pT", pa, c)], tag="ev_p")

            av_pairs = []
            av_reads = []
            if t_loc == 0:
                av_pairs.append((v_roll[:, pa, h * 64:(h + 1) * 64], pT[:, pa, 0, :]))
                av_reads += [("v", pa, 0), ("v", pa, 1), ("pT", pa, 0)]
            else:
                av_pairs.append((v_roll[:, 1 - pa, h * 64:(h + 1) * 64], pT[:, pa, 0, :]))
                av_pairs.append((v_roll[:, pa, h * 64:(h + 1) * 64], pT[:, pa, 1, :]))
                av_reads += [("v", 1 - pa, 0), ("v", 1 - pa, 1), ("v", pa, 0), ("v", pa, 1),
                             ("pT", pa, 0), ("pT", pa, 1)]
                if g > 0:
                    av_pairs.append((vg[0:g, h * 64:(h + 1) * 64], pT[0:g, pa, 2, :]))
                    av_reads += [("vg", i) for i in range(g)] + [("pT", pa, 2)]
            tp = None if base == 0 else (0, 64)
            emit_mm_group(P, ps_av[base:base + 64, :], ("ps_av", h % 2), av_pairs,
                          reads=av_reads, tag="mm_av", tile_position=tp)
            if h % 2 == 1:
                em.op("vector", lambda e, hp=hp, pa=pa: e.tensor_copy(yT[:, pa, hp, :], ps_av[:, :]),
                      reads=[("ps_av", 0), ("ps_av", 1)], writes=[("yT", pa, hp)], tag="ev_av")

        for c in range(2):
            emit_mm_group(P, ps_pr[:, :], ("ps_pr",),
                          [(yT[:, pa, k, :], wproj[:, k, c * 512:(c + 1) * 512]) for k in range(ND)],
                          reads=[("yT", pa, k) for k in range(ND)] + [("wproj", k) for k in range(ND)],
                          tag="mm_pr")
            em.op("vector", lambda e, c=c, pa=pa: e.tensor_tensor(xo_buf[:, pa, c * 512:(c + 1) * 512],
                                                           ps_pr[:, :], x_cur[:, pa, c * 512:(c + 1) * 512], ALU.add),
                  reads=[("ps_pr",), ("x", pa)], writes=[("xo", pa, c)], tag="resid")
        em.dma("sync", xo_out[t * 128:(t + 1) * 128, :], xo_buf[:, pa, :],
               reads=[("xo", pa, 0), ("xo", pa, 1)], writes=[("xo_st", t)], tag="st_x")

    inst_ctx.close()
    em.barrier()
    return P


def _l0attn_masks():
    r = np.arange(128)[:, None]
    j = np.arange(256)[None, :]
    allowed = (j <= 128 + r) & ((j >= r + 1) | (j % 64 == 0))
    mask_band = np.where(allowed, 0.0, NEG).astype(np.float32)
    jj = np.arange(128)[None, :]
    mask_first = np.where(jj <= r, 0.0, NEG).astype(np.float32)
    return mask_band, mask_first


def run_l0attn(x_full, Wqkv, Wproj):
    P = build_l0attn()
    nc = P.close_build()
    mask_band, mask_first = _l0attn_masks()
    ident = np.eye(128, dtype=np.float32)
    in_maps = []
    xs = x_full.reshape(NCORES, T, D)
    for c in range(NCORES):
        in_maps.append({
            "x": np.ascontiguousarray(xs[c]),
            "Wqkv": Wqkv, "Wproj": Wproj,
            "identf": ident, "identb": np.eye(128, dtype=BFNP),
            "mask_band": mask_band, "mask_first": mask_first,
        })
    res = run_bass_kernel_spmd(nc, in_maps, core_ids=list(range(NCORES)))
    out = np.stack([res.results[c]["xo"] for c in range(NCORES)])
    return out.reshape(16, 2048, D)


CAP = 5120
MOE_PHASES = "ABC"
MOE_NT = NT
MOE_SKIP_CC = False


def build_moe(P=None, V=None, x_in=None, xo_out=None, bank=None):
    standalone = P is None
    if standalone:
        P = Prog("moe")
        V = {"wg": P.inp("wg", [D, 8]), "w1": P.inp("w1", [8, D, 4096]),
             "b1t": P.inp("b1t", [128, 8, 32]), "w2": P.inp("w2", [8, 4096, D]),
             "b2": P.inp("b2", [8, D]), "identf": P.inp("identf", [128, 128], F32),
             "tri": P.inp("tri", [128, 128], F32), "ones128": P.inp("ones128", [128, 1], F32),
             "ones1": P.inp("ones1", [1, 128], F32), "prefmask": P.inp("prefmask", [128, 1], F32),
             "ebase": P.inp("ebase", [1, 8], F32), "identb": P.inp("identb", [128, 128], BF16)}
        x_in = P.inp("x", [T, D])
        xo_out = P.out("xo", [T, D])
    em = P.em
    nc = P.nc
    wg_in = V["wg"]; w1_in = V["w1"]; b1t_in = V["b1t"]; w2_in = V["w2"]; b2_in = V["b2"]
    identf_in = V["identf"]; tri_in = V["tri"]; ones128_in = V["ones128"]
    ones1_in = V["ones1"]; prefmask_in = V["prefmask"]; ebase_in = V["ebase"]

    cc_in = P.dram("cc_in", [16, 128])
    cc_out = P.dram("cc_out", [128, 128])
    CAPL = 768
    NSLOT = 8 * CAPL
    xg_dram = P.dram("xg_moe", [NSLOT + T, D], BF16)
    yg_dram = P.dram("yg_moe", [NSLOT + T, D], BF16)

    inst_ctx = ExitStack()
    _sb = lambda n, sh, dt=F32: inst_ctx.enter_context(nc.sbuf_tensor(P.uniq(n), list(sh), dt))
    coef8 = _sb("coef8", [128, NT, 8], F32)
    coef1 = _sb("coef1", [128, NT, 1], F32)
    slot_i = _sb("slot_i", [128, NT, 1], I32)
    if bank is None:
        bank = [P.psum(f"bank{i}", [128, 512], F32) for i in range(8)]
    BK = lambda i: ("bank", i)


    with ExitStack() as sa:
        A = lambda name, shape, dt=F32: sa.enter_context(nc.sbuf_tensor(P.uniq(name), list(shape), dt))
        wg = A("wg", [128, ND, 8])
        ident = A("ident", [128, 128])
        tri = A("tri", [128, 128])
        ones128 = A("ones128", [128, 1])
        ones1 = A("ones1", [1, 128])
        prefmask = A("prefmask", [128, 1])
        x_cur = A("x_cur", [128, 2, D])
        lns = {
            "sq": A("sq", [128, 2, D]), "s": A("s", [128, 2, 1]), "s2": A("s2", [128, 2, 1]),
            "mu": A("mu", [128, 2, 1]), "m2": A("m2", [128, 2, 1]), "var": A("var", [128, 2, 1]),
            "lnv": A("lnv", [128, 2, 1]), "rstd": A("rstd", [128, 2, 1]),
            "xn": A("xn", [128, 2, D]), "tag": "lnm",
        }
        xg = A("xg", [128, 2, ND, 128])
        lg = A("lg", [128, 2, 8]); nmg = A("nmg", [128, 2, 1]); eg = A("eg", [128, 2, 8])
        sumg = A("sumg", [128, 2, 1]); eq = A("eq", [128, 2, 8]); cum = A("cum", [128, 2, 8])
        m1 = A("m1", [128, 2, 8]); zero8 = A("zero8", [128, 8])
        mask_sb = A("mask_sb", [128, NT, 8]); r1_sb = A("r1_sb", [128, NT, 8])
        g1_sb = A("g1_sb", [128, NT, 1])
        offs_sb = A("offs_sb", [1, NT, 8]); offs_run = A("offs_run", [1, 2, 8])
        tot_sb = A("tot_sb", [1, 2, 8])
        counts_st = A("counts_st", [16, 128]); allc = A("allc", [128, 128])
        base_sb = A("base_sb", [1, 8]); bvec = A("bvec", [1, 2, 8])
        posb = A("posb", [128, 2, 8]); keep = A("keep", [128, 2, 8])
        kc = A("kc", [128, 2, 8]); ks = A("ks", [128, 2, 1]); coefb = A("coefb", [128, 2, 1])
        ebase = A("ebase", [1, 8]); bvec2 = A("bvec2", [1, 2, 8])
        sl8 = A("sl8", [128, 2, 8]); slf = A("slf", [128, 2, 1])
        xntb = A("xntb", [128, D], BF16)

        for k in range(ND):
            em.dma("sync", wg[:, k, :], wg_in[k * 128:(k + 1) * 128, :], writes=[("wg",)], tag="ld_wg")
        em.dma("sync", ident[:, :], identf_in, writes=[("ident",)])
        em.dma("sync", tri[:, :], tri_in, writes=[("tri",)])
        em.dma("sync", ones128[:, :], ones128_in, writes=[("ones128",)])
        em.dma("sync", ones1[:, :], ones1_in, writes=[("ones1",)])
        em.dma("sync", prefmask[:, :], prefmask_in, writes=[("prefmask",)])
        em.dma("sync", ebase[:, :], ebase_in, writes=[("ebase",)], tag="ld_eb")
        zrow = A("zrow", [128, D], BF16)
        em.op("gpsimd", lambda e: e.memset(zrow[:, :], 0.0), writes=[("zrow",)], tag="zrow")
        for zi in range(48):
            em.dma_fn("scalar", lambda e, zi=zi: e.dma_start(
                out=xg_dram[zi * 128:(zi + 1) * 128, :], in_=zrow[:, :]),
                reads=[("zrow",)], writes=[("xgz",)], tag="zxg")
        em.op("gpsimd", lambda e: e.memset(zero8[:, :], 0.0), writes=[("zero8",)], tag="z8")
        em.op("gpsimd", lambda e: e.memset(counts_st[:, :], 0.0), writes=[("counts0",)], tag="z16")
        em.op("gpsimd", lambda e: e.memset(offs_run[:, 0, :], 0.0), writes=[("offs_run", 0)], tag="zoffs")

        for t in range(MOE_NT):
            pa = t % 2
            em.dma("sync", x_cur[:, pa, :], x_in[t * 128:(t + 1) * 128, :], writes=[("x", pa)], tag="ld_x")
            if MOE_PHASES == "A1a":
                continue
            xn_key = emit_ln_tile(P, lns, pa, x_cur[:, pa, :], ("x", pa))
            xn = lns["xn"][:, pa, :]
            em.op("scalar", lambda e, xn=xn: e.copy(xntb[:, :], xn),
                  reads=[xn_key], writes=[("xntb",)], tag="xntb")
            if MOE_PHASES == "A1b":
                continue
            for k in range(ND):
                em.op("tensor", lambda e, k=k, xn=xn: e.transpose(bank[0][:, 0:128], xn[:, k * 128:(k + 1) * 128], ident[:, :]),
                      reads=[xn_key, ("ident",)], writes=[BK(0)], tag="tr_xn")
                em.op("vector", lambda e, k=k, pa=pa: e.tensor_copy(xg[:, pa, k, :], bank[0][:, 0:128]),
                      reads=[BK(0)], writes=[("xg", pa, k)], tag="ev_xg")
            if MOE_PHASES == "A1":
                continue
            emit_mm_group(P, bank[1][:, 0:8], BK(1),
                          [(xg[:, pa, k, :], wg[:, k, :]) for k in range(ND)],
                          reads=[("xg", pa, k) for k in range(ND)] + [("wg",)], tag="mm_g")
            em.op("vector", lambda e, pa=pa: e.tensor_copy(lg[:, pa, :], bank[1][:, 0:8]),
                  reads=[BK(1)], writes=[("lg", pa)], tag="ev_g")
            em.op("vector", lambda e, pa=pa: e.reduce_max(nmg[:, pa, :], lg[:, pa, :], axis=AX, negate=True),
                  reads=[("lg", pa)], writes=[("nmg", pa)], tag="gmax")
            em.op("scalar", lambda e, pa=pa: e.activation(eg[:, pa, :], lg[:, pa, :], AF.Exp,
                                                          bias=nmg[:, pa, :], accum_out=sumg[:, pa, :]),
                  reads=[("lg", pa), ("nmg", pa)], writes=[("eg", pa), ("sumg", pa)], tag="gexp")
            em.op("vector", lambda e, pa=pa, t=t: e.reciprocal(g1_sb[:, t, :], sumg[:, pa, :]),
                  reads=[("sumg", pa)], writes=[("g1", t)], tag="g1")
            em.op("vector", lambda e, pa=pa: e.tensor_scalar(eq[:, pa, :], lg[:, pa, :], nmg[:, pa, :], 0.0,
                                                             ALU.add, ALU.is_equal),
                  reads=[("lg", pa), ("nmg", pa)], writes=[("eq", pa)], tag="eq")
            em.op("vector", lambda e, pa=pa: e.tensor_tensor_scan(cum[:, pa, :], eq[:, pa, :], zero8[:, :], 0.0,
                                                                  ALU.add, ALU.add),
                  reads=[("eq", pa), ("zero8",)], writes=[("cum", pa)], tag="cum")
            em.op("vector", lambda e, pa=pa: e.tensor_scalar(m1[:, pa, :], cum[:, pa, :], 1.0, None, ALU.is_equal),
                  reads=[("cum", pa)], writes=[("m1", pa)], tag="m1")
            em.op("vector", lambda e, pa=pa, t=t: e.tensor_tensor(mask_sb[:, t, :], eq[:, pa, :], m1[:, pa, :], ALU.mult),
                  reads=[("eq", pa), ("m1", pa)], writes=[("mask", t)], tag="mask")
            if MOE_PHASES == "A2":
                continue
            emit_mm_group(P, bank[2][:, 0:8], BK(2), [(tri[:, :], mask_sb[:, t, :])],
                          reads=[("tri",), ("mask", t)], tag="mm_r1")
            em.op("vector", lambda e, t=t: e.tensor_copy(r1_sb[:, t, :], bank[2][:, 0:8]),
                  reads=[BK(2)], writes=[("r1", t)], tag="ev_r1")
            emit_mm_group(P, bank[3][0:1, 0:8], BK(3), [(ones128[:, :], mask_sb[:, t, :])],
                          reads=[("ones128",), ("mask", t)], tag="mm_tot")
            em.op("vector", lambda e, t=t: e.tensor_copy(tot_sb[:, t % 2, :], bank[3][0:1, 0:8]),
                  reads=[BK(3)], writes=[("tot", t % 2)], tag="ev_tot")
            em.op("scalar", lambda e, t=t: e.copy(offs_sb[:, t, :], offs_run[:, t % 2, :]),
                  reads=[("offs_run", t % 2)], writes=[("offs", t)], tag="offs_cp")
            em.op("vector", lambda e, t=t: e.tensor_tensor(offs_run[:, (t + 1) % 2, :], offs_run[:, t % 2, :],
                                                           tot_sb[:, t % 2, :], ALU.add),
                  reads=[("offs_run", t % 2), ("tot", t % 2)], writes=[("offs_run", (t + 1) % 2)], tag="offs_add")
            em.op("vector", lambda e, t=t, pa=pa: e.tensor_tensor(bvec2[:, pa, :], offs_sb[:, t, :], ebase[:, :], ALU.add),
                  reads=[("offs", t), ("ebase",)], writes=[("bvec2", pa)], tag="bvec2")
            emit_mm_group(P, bank[6][:, 0:8], BK(6), [(ones1[:, :], bvec2[:, pa, :])],
                          reads=[("ones1",), ("bvec2", pa)], tag="mm_bc2")
            em.op("vector", lambda e, t=t, pa=pa: e.tensor_tensor(sl8[:, pa, :], r1_sb[:, t, :], bank[6][:, 0:8], ALU.add),
                  reads=[("r1", t), BK(6)], writes=[("sl8", pa)], tag="sl8a")
            em.op("vector", lambda e, t=t, pa=pa: e.tensor_tensor(sl8[:, pa, :], sl8[:, pa, :], mask_sb[:, t, :], ALU.mult),
                  reads=[("sl8", pa), ("mask", t)], writes=[("sl8", pa)], tag="sl8b")
            em.op("vector", lambda e, pa=pa: e.reduce_sum(slf[:, pa, :], sl8[:, pa, :], axis=AX),
                  reads=[("sl8", pa)], writes=[("slf", pa)], tag="slot_f")
            em.op("vector", lambda e, t=t, pa=pa: e.tensor_copy(slot_i[:, t, :], slf[:, pa, :]),
                  reads=[("slf", pa)], writes=[("slot", t)], tag="slot_i")
            em.dma_fn("gpsimd", lambda e, t=t: e.indirect_dma_start(
                xg_dram, IndirectOffsetOnAxis(slot_i[:, t, :], 0), xntb[:, :], None),
                reads=[("xntb",), ("slot", t), ("xgz",)], writes=[("xg", t)], tag="scat")

        if MOE_PHASES in ("A1", "A2", "A3"):
            for t in range(NT):
                em.dma("sync", xo_out[t * 128:(t + 1) * 128, :], x_in[t * 128:(t + 1) * 128, :],
                       writes=[("xo_st", t)], tag="st_dummy")
            return P
        em.op("scalar", lambda e: e.copy(counts_st[0:1, 0:8], offs_run[:, NT % 2, :]),
              reads=[("offs_run", NT % 2), ("counts0",)], writes=[("counts_st",)], tag="cnt_cp")
        em.dma("gpsimd", cc_in, counts_st[:, :],
               reads=[("counts_st",)], writes=[("cc_in",)], tag="cnt_st")
        if MOE_SKIP_CC:
            for r in range(NCORES):
                em.dma("gpsimd", cc_out[16 * r:16 * (r + 1), :], cc_in,
                       reads=[("cc_in",)], writes=[("cc_out",)], tag="fake_ag")
        else:
            em.dma_fn("gpsimd",
                      lambda e: e.collective_compute("AllGather", ALU.bypass,
                                                     replica_groups=[list(range(NCORES))],
                                                     ins=[cc_in.opt()],
                                                     outs=[cc_out.opt()]),
                      reads=[("cc_in",)], writes=[("cc_out",)], tag="allgather", inc=1)
        em.dma("gpsimd", allc[:, :], cc_out, reads=[("cc_out",)], writes=[("allc",)], tag="cnt_ld")
        emit_mm_group(P, bank[4][0:1, 0:8], BK(4), [(prefmask[:, :], allc[:, 0:8])],
                      reads=[("prefmask",), ("allc",)], tag="mm_base")
        em.op("vector", lambda e: e.tensor_copy(base_sb[:, :], bank[4][0:1, 0:8]),
              reads=[BK(4)], writes=[("base",)], tag="ev_base")

        for t in range(NT):
            pa = t % 2
            em.op("vector", lambda e, t=t, pa=pa: e.tensor_tensor(bvec[:, pa, :], offs_sb[:, t, :], base_sb[:, :], ALU.add),
                  reads=[("offs", t), ("base",)], writes=[("bvec", pa)], tag="bvec")
            emit_mm_group(P, bank[5][:, 0:8], BK(5), [(ones1[:, :], bvec[:, pa, :])],
                          reads=[("ones1",), ("bvec", pa)], tag="mm_bc")
            em.op("vector", lambda e, t=t, pa=pa: e.tensor_tensor(posb[:, pa, :], r1_sb[:, t, :], bank[5][:, 0:8], ALU.add),
                  reads=[("r1", t), BK(5)], writes=[("posb", pa)], tag="posb")
            em.op("vector", lambda e, pa=pa: e.tensor_scalar(keep[:, pa, :], posb[:, pa, :], float(CAP), None, ALU.is_le),
                  reads=[("posb", pa)], writes=[("keep", pa)], tag="keep")
            em.op("vector", lambda e, t=t, pa=pa: e.tensor_tensor(kc[:, pa, :], mask_sb[:, t, :], keep[:, pa, :], ALU.mult),
                  reads=[("mask", t), ("keep", pa)], writes=[("kc", pa)], tag="kc")
            em.op("vector", lambda e, pa=pa: e.reduce_sum(ks[:, pa, :], kc[:, pa, :], axis=AX),
                  reads=[("kc", pa)], writes=[("ks", pa)], tag="ks")
            em.op("vector", lambda e, t=t, pa=pa: e.tensor_scalar(coefb[:, pa, :], g1_sb[:, t, :], ks[:, pa, :], 0.5,
                                                                  ALU.mult, ALU.mult),
                  reads=[("g1", t), ("ks", pa)], writes=[("coef", pa)], tag="coef")
            em.op("vector", lambda e, t=t, pa=pa: e.tensor_scalar(coef8[:, t, :], mask_sb[:, t, :], coefb[:, pa, :], None,
                                                                  ALU.mult),
                  reads=[("mask", t), ("coef", pa)], writes=[("coef8", t)], tag="coef8")
            em.op("scalar", lambda e, t=t, pa=pa: e.copy(coef1[:, t, :], coefb[:, pa, :]),
                  reads=[("coef", pa)], writes=[("coef1", t)], tag="coef1")
    em.barrier()

    if "B" not in MOE_PHASES:
        for t in range(NT):
            em.dma("sync", xo_out[t * 128:(t + 1) * 128, :], x_in[t * 128:(t + 1) * 128, :],
                   writes=[("xo_st", t)], tag="st_dummy")
        return P

    with ExitStack() as sb_:
        A = lambda name, shape, dt=BF16: sb_.enter_context(nc.sbuf_tensor(P.uniq(name), list(shape), dt))
        w1e = A("w1e", [128, ND, 2048])
        w2e = A("w2e", [128, 32, D])
        xg_tok = A("xg_tok", [128, 6, D])
        XgT = A("XgT", [128, ND, 768])
        hn = A("hn", [128, 32, 768])
        ygb = A("ygb", [128, 2, D])
        b1t = A("b1t", [128, 8, 32], F32)
        identB = A("identB", [128, 128], BF16)
        em.dma("sync", b1t[:, :, :], b1t_in, writes=[("b1t",)], tag="ld_b1t")
        em.dma("sync", identB[:, :], V["identb"], writes=[("identB",)], tag="ld_idb")
        NCK = (768 + 511) // 512
        for e_i in range(8):
            for s_ in range(6):
                em.dma_fn("sync", lambda e, e_i=e_i, s_=s_: e.dma_start(
                    out=xg_tok[:, s_, :], in_=xg_dram[e_i * 768 + s_ * 128:e_i * 768 + (s_ + 1) * 128, :]),
                    reads=[("xg", t) for t in range(NT)], writes=[("xg_tok", s_)], tag="ld_xg")
                for k in range(ND):
                    em.op("tensor", lambda e, s_=s_, k=k: e.transpose(
                        bank[0][:, :].bitcast(BF16)[:, 0:128], xg_tok[:, s_, k * 128:(k + 1) * 128], identB[:, :]),
                        reads=[("xg_tok", s_), ("identB",)], writes=[BK(0)], tag="tr_xg")
                    em.op("vector", lambda e, s_=s_, k=k: e.tensor_copy(
                        XgT[:, k, s_ * 128:(s_ + 1) * 128], bank[0][:, :].bitcast(BF16)[:, 0:128]),
                        reads=[BK(0)], writes=[("XgT", k)], tag="ev_xg2")
            for fb in range(2):
                em.dma_fn("gpsimd", lambda e, e_i=e_i, fb=fb: e.dma_start(
                    out=w1e[:, :, :],
                    in_=w1_in[e_i].rearrange("(k p) c -> p k c", p=128)[:, :, fb * 2048:(fb + 1) * 2048]),
                    writes=[("w1e", k) for k in range(ND)], tag="ld_w1")
                w1r = [("w1e", k) for k in range(ND)]
                xgr = [("XgT", k) for k in range(ND)]
                for n in range(NCK):
                    c0 = n * 512
                    cw = min(768, c0 + 512) - c0
                    for m in range(16):
                        bk = m % 4
                        mg = fb * 16 + m
                        emit_mm_group(P, bank[bk][:, 0:cw], BK(bk),
                                      [(w1e[:, k, m * 128:(m + 1) * 128], XgT[:, k, c0:c0 + cw]) for k in range(ND)],
                                      reads=w1r + xgr, tag="mm_w1")
                        em.op("scalar", lambda e, bk=bk, e_i=e_i, mg=mg, c0=c0, cw=cw: e.activation(
                            hn[:, mg, c0:c0 + cw], bank[bk][:, 0:cw], AF.Relu, bias=b1t[:, e_i, mg:mg + 1]),
                            reads=[BK(bk), ("b1t",)], writes=[("hn", mg)], tag="ev_h")
            em.dma_fn("gpsimd", lambda e, e_i=e_i: e.dma_start(
                out=w2e[:, :, :], in_=w2_in[e_i].rearrange("(k p) c -> p k c", p=128)),
                writes=[("w2e", kf) for kf in range(32)], tag="ld_w2")
            w2r = [("w2e", kf) for kf in range(32)]
            for sub in range(6):
                for dc in range(2):
                    bk = 4 + (sub * 2 + dc) % 4
                    emit_mm_group(P, bank[bk][:, :], BK(bk),
                                  [(hn[:, kf, sub * 128:(sub + 1) * 128], w2e[:, kf, dc * 512:(dc + 1) * 512])
                                   for kf in range(32)],
                                  reads=[("hn", kf) for kf in range(32)] + w2r, tag="mm_w2")
                    em.op("vector", lambda e, bk=bk, dc=dc, sub=sub: e.tensor_copy(
                        ygb[:, sub % 2, dc * 512:(dc + 1) * 512], bank[bk][:, :]),
                        reads=[BK(bk)], writes=[("ygb", sub % 2, dc)], tag="ev_y")
                em.dma_fn("sync", lambda e, e_i=e_i, sub=sub: e.dma_start(
                    out=yg_dram[e_i * 768 + sub * 128:e_i * 768 + (sub + 1) * 128, :], in_=ygb[:, sub % 2, :]),
                    reads=[("ygb", sub % 2, 0), ("ygb", sub % 2, 1)], writes=[("yg", e_i, sub)], tag="st_yg")
    em.barrier()

    with ExitStack() as sc:
        A = lambda name, shape, dt=F32: sc.enter_context(nc.sbuf_tensor(P.uniq(name), list(shape), dt))
        x_cur = A("x_cur", [128, 2, D])
        xo_buf = A("xo_buf", [128, 2, D])
        c8T = A("c8T", [8, NT, 128], BF16)
        c8b = A("c8b", [128, 2, 8], BF16)
        b2s = A("b2s", [8, D], BF16)
        identc = A("identc", [128, 128], BF16)
        em.dma("gpsimd", b2s[:, :], b2_in, writes=[("b2s",)], tag="ld_b2")
        em.dma("gpsimd", identc[:, :], identf_in, writes=[("identc",)], tag="ld_idc")
        ytok = sc.enter_context(nc.sbuf_tensor(P.uniq("ytok"), [128, D], BF16))
        for t in range(NT):
            pa = t % 2
            em.dma("sync", x_cur[:, pa, :], x_in[t * 128:(t + 1) * 128, :], writes=[("x", pa)], tag="ld_x2")
            em.dma_fn("gpsimd", lambda e, t=t: e.indirect_dma_start(
                ytok[:, :], None, yg_dram, IndirectOffsetOnAxis(slot_i[:, t, :], 0)),
                reads=[("slot", t)] + [("yg", e2, sub) for e2 in range(8) for sub in range(6)],
                writes=[("ytok",)], tag="gath_y")
            em.op("vector", lambda e, t=t, pa=pa: e.tensor_copy(c8b[:, pa, :], coef8[:, t, :]),
                  reads=[("coef8", t)], writes=[("c8b", pa)], tag="c8cast")
            em.op("tensor", lambda e, t=t, pa=pa: e.transpose(bank[6][:, :].bitcast(BF16)[0:8, 0:128], c8b[:, pa, :], identc[:, :]),
                  reads=[("c8b", pa), ("identc",)], writes=[BK(6)], tag="tr_c8")
            em.op("scalar", lambda e, t=t: e.copy(c8T[:, t, :], bank[6][:, :].bitcast(BF16)[0:8, 0:128]),
                  reads=[BK(6)], writes=[("c8T", t)], tag="ev_c8")
            for dc in range(2):
                emit_mm_group(P, bank[dc][:, :], BK(dc),
                              [(c8T[:, t, :], b2s[:, dc * 512:(dc + 1) * 512])],
                              reads=[("c8T", t), ("b2s",)], tag="mm_b2")
                em.op("vector", lambda e, t=t, pa=pa, dc=dc: e.scalar_tensor_tensor(
                    xo_buf[:, pa, dc * 512:(dc + 1) * 512], ytok[:, dc * 512:(dc + 1) * 512],
                    coef1[:, t, :], x_cur[:, pa, dc * 512:(dc + 1) * 512], ALU.mult, ALU.add),
                    reads=[("x", pa), ("ytok",), ("coef1", t)], writes=[("xo", pa, dc)], tag="resid1")
                em.op("vector", lambda e, pa=pa, dc=dc: e.tensor_tensor(
                    xo_buf[:, pa, dc * 512:(dc + 1) * 512], xo_buf[:, pa, dc * 512:(dc + 1) * 512],
                    bank[dc][:, :], ALU.add),
                    reads=[("xo", pa, dc), BK(dc)], writes=[("xo", pa, dc)], tag="resid2")
            em.dma("sync", xo_out[t * 128:(t + 1) * 128, :], xo_buf[:, pa, :],
                   reads=[("xo", pa, 0), ("xo", pa, 1)], writes=[("xo_st", t)], tag="st_x")
    inst_ctx.close()
    em.barrier()
    return P


def _moe_consts(rank):
    tri = (np.arange(128)[:, None] <= np.arange(128)[None, :]).astype(np.float32)
    ones128 = np.ones((128, 1), np.float32)
    ones1 = np.ones((1, 128), np.float32)
    pm = np.zeros((128, 1), np.float32)
    for r in range(rank):
        pm[16 * r, 0] = 1.0
    return tri, ones128, ones1, pm


def _moe_ebase():
    return (np.arange(8, dtype=np.float32) * 768.0 - 1.0).reshape(1, 8)


def run_moe(x_full, wg, w1, b1, w2, b2):
    P = build_moe()
    nc = P.close_build()
    tri, ones128, ones1, _ = _moe_consts(0)
    ident = np.eye(128, dtype=np.float32)
    b1t = np.ascontiguousarray(b1.reshape(8, 32, 128).transpose(2, 0, 1))
    xs = x_full.reshape(NCORES, T, D)
    in_maps = []
    for c in range(NCORES):
        in_maps.append({
            "x": np.ascontiguousarray(xs[c]), "wg": wg, "w1": w1, "b1t": b1t,
            "w2": w2, "b2": b2, "identf": ident, "tri": tri,
            "ones128": ones128, "ones1": ones1, "prefmask": _moe_consts(c)[3], "ebase": _moe_ebase(),
            "identb": np.eye(128, dtype=BFNP),
        })
    res = run_bass_kernel_spmd(nc, in_maps, core_ids=list(range(NCORES)))
    out = np.stack([res.results[c]["xo"] for c in range(NCORES)])
    return out.reshape(16, 2048, D)


def build_l1attn(P=None, V=None, x_in=None, xo_out=None, bank=None):
    standalone = P is None
    if standalone:
        P = Prog("l1attn")
        V = {"wmixt": P.inp("wmixt", [128, ND]), "WgT": P.inp("WgT", [D, D]),
             "bgt": P.inp("bgt", [128, ND]), "WvT": P.inp("WvT", [D, D]),
             "bvt": P.inp("bvt", [128, ND]), "WdT": P.inp("WdT", [D, D]),
             "bdt": P.inp("bdt", [128, ND]), "wattnt": P.inp("wattnt", [128, ND]),
             "WqkvT": P.inp("WqkvT", [D, 3 * D]), "WoT": P.inp("WoT", [D, D]),
             "wffnt": P.inp("wffnt", [128, ND]), "WgateT": P.inp("WgateT", [D, 4096]),
             "WupT": P.inp("WupT", [D, 4096]), "WoutT": P.inp("WoutT", [4096, D]),
             "identf": P.inp("identf", [128, 128], F32), "identb": P.inp("identb", [128, 128], BF16),
             "mask_band_l1": P.inp("mask_band", [128, 256], F32),
             "mask_first": P.inp("mask_first", [128, 128], F32)}
        x_in = P.inp("x", [T, D])
        xo_out = P.out("xo", [T, D])
    em = P.em
    nc = P.nc
    wmix_in = V["wmixt"]; WgT_in = V["WgT"]; bg_in = V["bgt"]; WvT_in = V["WvT"]
    bv_in = V["bvt"]; WdT_in = V["WdT"]; bd_in = V["bdt"]; wattn_in = V["wattnt"]
    WqkvT_in = V["WqkvT"]; WoT_in = V["WoT"]; wffn_in = V["wffnt"]
    WgateT_in = V["WgateT"]; WupT_in = V["WupT"]; WoutT_in = V["WoutT"]
    identf_in = V["identf"]; identb_in = V["identb"]
    mask_band_in = V["mask_band_l1"]; mask_first_in = V["mask_first"]

    xd0 = P.dram("xd0", [T, D])
    xd1 = P.dram("xd1", [T, D])
    xd2 = P.dram("xd2", [T, D])
    xd3 = P.dram("xd3", [T, D])

    if bank is None:
        bank = [P.psum(f"bank{i}", [128, 512], F32) for i in range(8)]
    BK = lambda i: ("bank", i)
    NCH = 8
    CHT = 512

    with ExitStack() as s0:
        A = lambda n, sh, dt=F32: s0.enter_context(nc.sbuf_tensor(P.uniq(n), list(sh), dt))
        xc0 = A("xc0", [128, 2, D])
        lns0 = {k: A("l0" + k, [128, 2, D if k in ("sq", "xn") else 1]) for k in
               ("sq", "s", "s2", "mu", "m2", "var", "lnv", "rstd", "xn")}
        lns0["tag"] = "lnA"
        for t in range(NT):
            pa = t % 2
            em.dma("sync", xc0[:, pa, :], x_in[t * 128:(t + 1) * 128, :], writes=[("x", pa)], tag="ld_x")
            xk = emit_ln_tile(P, lns0, pa, xc0[:, pa, :], ("x", pa))
            em.dma("sync", xd0[t * 128:(t + 1) * 128, :], lns0["xn"][:, pa, :], reads=[xk],
                   writes=[("xd0", t)], tag="st0")
    em.barrier()

    with ExitStack() as s1:
        A = lambda n, sh, dt=F32: s1.enter_context(nc.sbuf_tensor(P.uniq(n), list(sh), dt))
        WgT = A("WgT", [128, ND, D], BF16)
        WvT = A("WvT", [128, ND, D], BF16)
        WdT = A("WdT", [128, ND, D], BF16)
        wmix = A("wmix", [128, ND])
        bgt = A("bgt", [128, ND])
        bvt = A("bvt", [128, ND])
        bdt = A("bdt", [128, ND])
        identa = A("identa", [128, 128])
        identba = A("identb", [128, 128], BF16)
        xra = A("xra", [128, 2, 4, D])
        rmsa = {k: A("r1" + k, [128, 2, D if k in ("sq", "xn") else 1]) for k in
               ("sq", "s", "s2", "mu", "m2", "var", "lnv", "rstd", "xn")}
        rmsa["tag"] = "rmsA"
        xmTa = A("xmTa", [128, 2, ND, CHT], BF16)
        sg = A("sg", [128, 2, CHT])
        tv = A("tv", [128, 2, CHT])
        sd = A("sd", [128, 2, CHT])
        aa = A("aa", [128, 2, CHT])
        xs = A("xs", [128, 2, CHT], BF16)
        hh = A("hh", [128, 2, CHT], BF16)
        carry = A("carry", [128, ND, 1])
        for (wt, win, wkey) in ((WgT, WgT_in, "WgT"), (WvT, WvT_in, "WvT"), (WdT, WdT_in, "WdT")):
            em.dma_fn("gpsimd", lambda e, wt=wt, win=win: e.dma_start(
                out=wt[:, :, :], in_=win.rearrange("(k p) c -> p k c", p=128)),
                writes=[(wkey,)], tag="ldw")
        em.dma("sync", wmix[:, :], wmix_in, writes=[("wmix",)])
        em.dma("sync", bgt[:, :], bg_in, writes=[("bgt",)])
        em.dma("sync", bvt[:, :], bv_in, writes=[("bvt",)])
        em.dma("sync", bdt[:, :], bd_in, writes=[("bdt",)])
        em.dma("sync", identa[:, :], identf_in, writes=[("identa",)])
        em.dma("sync", identba[:, :], identb_in, writes=[("identb",)])
        for n in range(NCH):
            pa = n % 2
            tok0 = n * CHT
            for s_ in range(4):
                t = n * 4 + s_
                em.dma("sync", xra[:, pa, s_, :], xd0[t * 128:(t + 1) * 128, :],
                       writes=[("xra", pa, s_)], tag="ld_c")
                xk = emit_ln_tile(P, rmsa, pa, xra[:, pa, s_, :], ("xra", pa, s_), rms=True, eps=1e-6)
                xn = rmsa["xn"][:, pa, :]
                for k in range(ND):
                    em.op("tensor", lambda e, k=k, xn=xn: e.transpose(bank[6][:, 0:128], xn[:, k * 128:(k + 1) * 128], identa[:, :]),
                          reads=[xk, ("identa",)], writes=[BK(6)], tag="trm")
                    em.op("vector", lambda e, k=k, pa=pa, s_=s_: e.tensor_scalar(
                        xmTa[:, pa, k, s_ * 128:(s_ + 1) * 128], bank[6][:, 0:128], wmix[:, k:k + 1], None, ALU.mult),
                        reads=[BK(6), ("wmix",)], writes=[("xmTa", pa, k)], tag="evm")
            xmr = [("xmTa", pa, k) for k in range(ND)]
            for m in range(ND):
                for (wt, wkey, bk) in ((WgT, "WgT", 0), (WvT, "WvT", 1), (WdT, "WdT", 2)):
                    emit_mm_group(P, bank[bk][:, :], BK(bk),
                                  [(wt[:, k, m * 128:(m + 1) * 128], xmTa[:, pa, k, :]) for k in range(ND)],
                                  reads=xmr + [(wkey,)], tag="mm_gvd")
                em.op("scalar", lambda e, m=m, pa=pa: e.activation(sg[:, pa, :], bank[0][:, :], AF.Sigmoid,
                                                                   bias=bgt[:, m:m + 1]),
                      reads=[BK(0), ("bgt",)], writes=[("sg", pa)], tag="ev_g")
                em.op("scalar", lambda e, m=m, pa=pa: e.activation(tv[:, pa, :], bank[1][:, :], AF.Tanh,
                                                                   bias=bvt[:, m:m + 1]),
                      reads=[BK(1), ("bvt",)], writes=[("tv", pa)], tag="ev_v")
                em.op("scalar", lambda e, m=m, pa=pa: e.activation(sd[:, pa, :], bank[2][:, :], AF.Sigmoid,
                                                                   bias=bdt[:, m:m + 1]),
                      reads=[BK(2), ("bdt",)], writes=[("sd", pa)], tag="ev_d")
                em.op("vector", lambda e, pa=pa: e.tensor_mul(xs[:, pa, :], sg[:, pa, :], tv[:, pa, :]),
                      reads=[("sg", pa), ("tv", pa)], writes=[("xs", pa)], tag="xs")
                em.op("vector", lambda e, pa=pa: e.tensor_scalar(aa[:, pa, :], sd[:, pa, :], 0.998, 0.001,
                                                                 ALU.mult, ALU.add),
                      reads=[("sd", pa)], writes=[("aa", pa)], tag="aa")
                if n % 4 == 0:
                    em.op("vector", lambda e, m=m, pa=pa: e.tensor_tensor_scan(
                        hh[:, pa, :], aa[:, pa, :], xs[:, pa, :], 0.0, ALU.mult, ALU.add),
                        reads=[("aa", pa), ("xs", pa)], writes=[("hh", pa)], tag="scan")
                else:
                    em.op("vector", lambda e, m=m, pa=pa: e.tensor_tensor_scan(
                        hh[:, pa, :], aa[:, pa, :], xs[:, pa, :], carry[:, m, :], ALU.mult, ALU.add),
                        reads=[("aa", pa), ("xs", pa), ("carry", m)], writes=[("hh", pa)], tag="scan")
                em.op("vector", lambda e, m=m, pa=pa: e.tensor_copy(carry[:, m, :], hh[:, pa, CHT - 1:CHT]),
                      reads=[("hh", pa)], writes=[("carry", m)], tag="carry")
                for s_ in range(4):
                    em.op("tensor", lambda e, m=m, pa=pa, s_=s_: e.transpose(
                        bank[7][:, :].bitcast(BF16)[:, 0:128], hh[:, pa, s_ * 128:(s_ + 1) * 128], identba[:, :]),
                        reads=[("hh", pa), ("identb",)], writes=[BK(7)], tag="trh")
                    em.op("vector", lambda e, m=m, pa=pa, s_=s_: e.tensor_tensor(
                        xra[:, pa, s_, m * 128:(m + 1) * 128], xra[:, pa, s_, m * 128:(m + 1) * 128],
                        bank[7][:, :].bitcast(BF16)[:, 0:128], ALU.add),
                        reads=[BK(7), ("xra", pa, s_)], writes=[("xra", pa, s_)], tag="addh")
            for s_ in range(4):
                t = n * 4 + s_
                em.dma("sync", xd1[t * 128:(t + 1) * 128, :], xra[:, pa, s_, :],
                       reads=[("xra", pa, s_)], writes=[("xd1", t)], tag="st1")
    em.barrier()

    with ExitStack() as s2:
        A = lambda n, sh, dt=F32: s2.enter_context(nc.sbuf_tensor(P.uniq(n), list(sh), dt))
        WqkvT = A("WqkvT", [128, ND, 3 * D], BF16)
        WoT = A("WoT", [128, ND, D], BF16)
        wattn = A("wattn", [128, ND])
        identb2 = A("identb2", [128, 128])
        identbb = A("identbb", [128, 128], BF16)
        mband = A("mband", [128, 256])
        mfirst = A("mfirst", [128, 128])
        xcb = A("xcb", [128, 2, D])
        rmsb = {k: A("r2" + k, [128, 2, D if k in ("sq", "xn") else 1]) for k in
               ("sq", "s", "s2", "mu", "m2", "var", "lnv", "rstd", "xn")}
        rmsb["tag"] = "rmsB"
        xmTb = A("xmTb", [128, 2, ND, 128], BF16)
        qTb = A("qTb", [128, 2, ND, 128], BF16)
        kTb = A("kTb", [128, 2, ND, 128], BF16)
        v_rollb = A("v_rollb", [128, 2, D], BF16)
        probsb = A("probsb", [128, 2, 256])
        probsb_bf = A("probsb_bf", [128, 2, 256], BF16)
        probsb_bf2 = A("probsb_bf2", [128, 2, 256], BF16)
        pTb2 = A("pTb2", [128, 2, 2, 128], BF16)
        nmb = A("nmb", [128, 2, 1])
        sumeb = A("sumeb", [128, 2, 1])
        recb = A("recb", [128, 2, 1])
        yTb = A("yTb", [128, 2, ND, 128], BF16)
        xob = A("xob", [128, 2, D])
        em.dma_fn("gpsimd", lambda e: e.dma_start(
            out=WqkvT[:, :, :], in_=WqkvT_in.rearrange("(k p) c -> p k c", p=128)),
            writes=[("WqkvT", k) for k in range(ND)], tag="ldw")
        em.dma_fn("gpsimd", lambda e: e.dma_start(
            out=WoT[:, :, :], in_=WoT_in.rearrange("(k p) c -> p k c", p=128)),
            writes=[("WoT", k) for k in range(ND)], tag="ldw")
        em.dma("sync", wattn[:, :], wattn_in, writes=[("wattn",)])
        em.dma("sync", identb2[:, :], identf_in, writes=[("identb2",)])
        em.dma("sync", identbb[:, :], identb_in, writes=[("identbb",)])
        em.dma("sync", mband[:, :], mask_band_in, writes=[("mband",)])
        em.dma("sync", mfirst[:, :], mask_first_in, writes=[("mfirst",)])
        wqr = [("WqkvT", k) for k in range(ND)]
        for t in range(NT):
            pa = t % 2
            t_loc = t % NT_SEQ
            em.dma("sync", xcb[:, pa, :], xd1[t * 128:(t + 1) * 128, :], writes=[("x", pa)],
                   deps=[], tag="ld_x")
            xk = emit_ln_tile(P, rmsb, pa, xcb[:, pa, :], ("x", pa), rms=True, eps=1e-6)
            xn = rmsb["xn"][:, pa, :]
            for k in range(ND):
                em.op("tensor", lambda e, k=k, xn=xn: e.transpose(bank[6][:, 0:128], xn[:, k * 128:(k + 1) * 128], identb2[:, :]),
                      reads=[xk, ("identb2",)], writes=[BK(6)], tag="trb")
                em.op("vector", lambda e, k=k, pa=pa: e.tensor_scalar(
                    xmTb[:, pa, k, :], bank[6][:, 0:128], wattn[:, k:k + 1], None, ALU.mult),
                    reads=[BK(6), ("wattn",)], writes=[("xmTb", pa, k)], tag="evb")
            xmr = [("xmTb", pa, k) for k in range(ND)]
            for hp in range(ND):
                bk = hp % 2
                emit_mm_group(P, bank[bk][:, 0:128], BK(bk),
                              [(WqkvT[:, k, hp * 128:(hp + 1) * 128], xmTb[:, pa, k, :]) for k in range(ND)],
                              reads=xmr + wqr, tag="mm_q")
                em.op("scalar", lambda e, hp=hp, bk=bk, pa=pa: e.activation(qTb[:, pa, hp, :], bank[bk][:, 0:128],
                                                                            AF.Copy, scale=0.125),
                      reads=[BK(bk)], writes=[("qTb", pa, hp)], tag="ev_q")
                emit_mm_group(P, bank[bk][:, 0:128], BK(bk),
                              [(WqkvT[:, k, D + hp * 128:D + (hp + 1) * 128], xmTb[:, pa, k, :]) for k in range(ND)],
                              reads=xmr + wqr, tag="mm_k")
                em.op("scalar", lambda e, hp=hp, bk=bk, pa=pa: e.copy(kTb[:, pa, hp, :], bank[bk][:, 0:128]),
                      reads=[BK(bk)], writes=[("kTb", pa, hp)], tag="ev_k")
            for c in range(2):
                emit_mm_group(P, bank[2][:, :], BK(2),
                              [(xmTb[:, pa, k, :], WqkvT[:, k, 2 * D + c * 512:2 * D + (c + 1) * 512]) for k in range(ND)],
                              reads=xmr + wqr, tag="mm_v")
                em.op("vector", lambda e, c=c, pa=pa: e.tensor_copy(v_rollb[:, pa, c * 512:(c + 1) * 512], bank[2][:, :]),
                      reads=[BK(2)], writes=[("v", pa, c)], tag="ev_v")
            Wd_ = 128 if t_loc == 0 else 256
            for h in range(16):
                hp = h // 2
                base = (h % 2) * 64
                q_ap = qTb[base:base + 64, pa, hp, :]

                def fn_sc(e, q_ap=q_ap, pa=pa, hp=hp, base=base, t_loc=t_loc):
                    if t_loc == 0:
                        return e.matmul(bank[3][:, 0:128], q_ap, kTb[base:base + 64, pa, hp, :], start=True, stop=True)
                    e.matmul(bank[3][:, 0:128], q_ap, kTb[base:base + 64, 1 - pa, hp, :], start=True, stop=True)
                    return e.matmul(bank[3][:, 128:256], q_ap, kTb[base:base + 64, pa, hp, :], start=True, stop=True)
                sc_reads = [("qTb", pa, hp), ("kTb", pa, hp)] + ([("kTb", 1 - pa, hp)] if t_loc > 0 else [])
                em.op("tensor", fn_sc, reads=sc_reads, writes=[BK(3)], tag="mm_sc")
                pr = probsb[:, pa, :]
                if t_loc == 0:
                    em.op("vector", lambda e, pr=pr: e.tensor_tensor(pr[:, 0:128], bank[3][:, 0:128], mfirst[:, :], ALU.add),
                          reads=[BK(3), ("mfirst",)], writes=[("probsb", pa)], tag="mask0")
                else:
                    em.op("vector", lambda e, pr=pr: e.tensor_tensor(pr[:, 0:256], bank[3][:, 0:256], mband[:, :], ALU.add),
                          reads=[BK(3), ("mband",)], writes=[("probsb", pa)], tag="mask")
                em.op("vector", lambda e, pr=pr, Wd_=Wd_, pa=pa: e.reduce_max(nmb[:, pa, :], pr[:, 0:Wd_], axis=AX, negate=True),
                      reads=[("probsb", pa)], writes=[("nmb", pa)], tag="negmax")
                em.op("scalar", lambda e, pr=pr, Wd_=Wd_, pa=pa: e.activation(probsb_bf[:, pa, 0:Wd_], pr[:, 0:Wd_], AF.Exp,
                                                                              bias=nmb[:, pa, :], accum_out=sumeb[:, pa, :]),
                      reads=[("probsb", pa), ("nmb", pa)], writes=[("probsb_bf", pa), ("sumeb", pa)], tag="exp")
                em.op("vector", lambda e, pa=pa: e.reciprocal(recb[:, pa, :], sumeb[:, pa, :]),
                      reads=[("sumeb", pa)], writes=[("recb", pa)], tag="recip")
                em.op("vector", lambda e, Wd_=Wd_, pa=pa: e.tensor_scalar(probsb_bf2[:, pa, 0:Wd_], probsb_bf[:, pa, 0:Wd_],
                                                                          recb[:, pa, :], None, ALU.mult),
                      reads=[("probsb_bf", pa), ("recb", pa)], writes=[("probsb_bf2", pa)], tag="pscale")
                nch = Wd_ // 128
                for c in range(nch):
                    em.op("tensor", lambda e, c=c, pa=pa: e.transpose(bank[4][:, :].bitcast(BF16)[:, 0:128],
                                                                      probsb_bf2[:, pa, c * 128:(c + 1) * 128], identbb[:, :]),
                          reads=[("probsb_bf2", pa), ("identbb",)], writes=[BK(4)], tag="tr_p")
                    em.op("scalar", lambda e, c=c, pa=pa: e.copy(pTb2[:, pa, c, :], bank[4][:, :].bitcast(BF16)[:, 0:128]),
                          reads=[BK(4)], writes=[("pTb2", pa, c)], tag="ev_p")
                av_pairs = []
                av_reads = []
                if t_loc == 0:
                    av_pairs.append((v_rollb[:, pa, h * 64:(h + 1) * 64], pTb2[:, pa, 0, :]))
                    av_reads += [("v", pa, 0), ("v", pa, 1), ("pTb2", pa, 0)]
                else:
                    av_pairs.append((v_rollb[:, 1 - pa, h * 64:(h + 1) * 64], pTb2[:, pa, 0, :]))
                    av_pairs.append((v_rollb[:, pa, h * 64:(h + 1) * 64], pTb2[:, pa, 1, :]))
                    av_reads += [("v", 1 - pa, 0), ("v", 1 - pa, 1), ("v", pa, 0), ("v", pa, 1),
                                 ("pTb2", pa, 0), ("pTb2", pa, 1)]
                tp = None if base == 0 else (0, 64)
                emit_mm_group(P, bank[5][base:base + 64, 0:128], ("ps_av", h % 2), av_pairs,
                              reads=av_reads, tag="mm_av", tile_position=tp)
                if h % 2 == 1:
                    em.op("vector", lambda e, hp=hp, pa=pa: e.tensor_copy(yTb[:, pa, hp, :], bank[5][:, 0:128]),
                          reads=[("ps_av", 0), ("ps_av", 1)], writes=[("yTb", pa, hp)], tag="ev_av")
            for c in range(2):
                emit_mm_group(P, bank[6][:, :], BK(6),
                              [(yTb[:, pa, k, :], WoT[:, k, c * 512:(c + 1) * 512]) for k in range(ND)],
                              reads=[("yTb", pa, k) for k in range(ND)] + [("WoT", k) for k in range(ND)],
                              tag="mm_pr")
                em.op("vector", lambda e, c=c, pa=pa: e.tensor_tensor(xob[:, pa, c * 512:(c + 1) * 512],
                                                                      bank[6][:, :], xcb[:, pa, c * 512:(c + 1) * 512], ALU.add),
                      reads=[BK(6), ("x", pa)], writes=[("xo", pa, c)], tag="resid")
            em.dma("sync", xd2[t * 128:(t + 1) * 128, :], xob[:, pa, :],
                   reads=[("xo", pa, 0), ("xo", pa, 1)], writes=[("xd2", t)], tag="st2")
    em.barrier()

    with ExitStack() as s3:
        A = lambda n, sh, dt=F32: s3.enter_context(nc.sbuf_tensor(P.uniq(n), list(sh), dt))
        WoutT = A("WoutT", [128, 32, D], BF16)
        wgate_b = A("wgate_b", [128, 2, ND, 128], BF16)
        wup_b = A("wup_b", [128, 2, ND, 128], BF16)
        wffn = A("wffn", [128, ND])
        identc2 = A("identc2", [128, 128])
        xrc = A("xrc", [128, 2, 4, D])
        rmsc = {k: A("r3" + k, [128, 2, D if k in ("sq", "xn") else 1]) for k in
               ("sq", "s", "s2", "mu", "m2", "var", "lnv", "rstd", "xn")}
        rmsc["tag"] = "rmsC"
        xmTc = A("xmTc", [128, 2, ND, 512], BF16)
        gbufc = A("gbufc", [128, 2, 512])
        pTb = A("pTb", [128, 32, 512], BF16)
        for kf in range(32):
            em.dma_fn("gpsimd", lambda e, kf=kf: e.dma_start(out=WoutT[:, kf, :], in_=WoutT_in[kf * 128:(kf + 1) * 128, :]),
                      writes=[("WoutT",)], tag="ld_wo")
        em.dma("sync", wffn[:, :], wffn_in, writes=[("wffn",)])
        em.dma("sync", identc2[:, :], identf_in, writes=[("identc2",)], tag="ldi")
        for n in range(NCH):
            pa = n % 2
            for s_ in range(4):
                t = n * 4 + s_
                em.dma("sync", xrc[:, pa, s_, :], xd2[t * 128:(t + 1) * 128, :],
                       writes=[("xrc", pa, s_)], tag="ld_c")
                xk = emit_ln_tile(P, rmsc, pa, xrc[:, pa, s_, :], ("xrc", pa, s_), rms=True, eps=1e-6)
                xn = rmsc["xn"][:, pa, :]
                for k in range(ND):
                    em.op("tensor", lambda e, k=k, xn=xn: e.transpose(bank[6][:, 0:128], xn[:, k * 128:(k + 1) * 128], identc2[:, :]),
                          reads=[xk, ("identc2",)], writes=[BK(6)], tag="trc")
                    em.op("vector", lambda e, k=k, pa=pa, s_=s_: e.tensor_scalar(
                        xmTc[:, pa, k, s_ * 128:(s_ + 1) * 128], bank[6][:, 0:128], wffn[:, k:k + 1], None, ALU.mult),
                        reads=[BK(6), ("wffn",)], writes=[("xmTc", pa, k)], tag="evc")
            xmr = [("xmTc", pa, k) for k in range(ND)]
            for m in range(32):
                wpa = m % 2
                em.dma_fn("gpsimd", lambda e, m=m, wpa=wpa: e.dma_start(
                    out=wgate_b[:, wpa, :, :],
                    in_=WgateT_in.rearrange("(k p) c -> p k c", p=128)[:, :, m * 128:(m + 1) * 128]),
                    writes=[("wgate_b", wpa, k) for k in range(ND)], tag="ld_wg")
                em.dma_fn("gpsimd", lambda e, m=m, wpa=wpa: e.dma_start(
                    out=wup_b[:, wpa, :, :],
                    in_=WupT_in.rearrange("(k p) c -> p k c", p=128)[:, :, m * 128:(m + 1) * 128]),
                    writes=[("wup_b", wpa, k) for k in range(ND)], tag="ld_wu")
                bk = m % 2
                emit_mm_group(P, bank[bk][:, :], BK(bk),
                              [(wgate_b[:, wpa, k, :], xmTc[:, pa, k, :]) for k in range(ND)],
                              reads=xmr + [("wgate_b", wpa, k) for k in range(ND)], tag="mm_gate")
                em.op("scalar", lambda e, bk=bk, pa=pa: e.activation(gbufc[:, pa, :], bank[bk][:, :], AF.Silu),
                      reads=[BK(bk)], writes=[("gbufc", pa)], tag="ev_gate")
                emit_mm_group(P, bank[2 + bk][:, :], ("bank", 2 + bk),
                              [(wup_b[:, wpa, k, :], xmTc[:, pa, k, :]) for k in range(ND)],
                              reads=xmr + [("wup_b", wpa, k) for k in range(ND)], tag="mm_up")
                em.op("vector", lambda e, m=m, bk=bk, pa=pa: e.tensor_tensor(pTb[:, m, :], gbufc[:, pa, :],
                                                                             bank[2 + bk][:, :], ALU.mult),
                      reads=[("gbufc", pa), ("bank", 2 + bk)], writes=[("pTb", m)], tag="pmul")
            for sub in range(4):
                for dc in range(2):
                    bko = 4 + (sub % 2) * 2 + dc
                    emit_mm_group(P, bank[bko][:, :], ("bank", bko),
                                  [(pTb[:, kf, sub * 128:(sub + 1) * 128], WoutT[:, kf, dc * 512:(dc + 1) * 512])
                                   for kf in range(32)],
                                  reads=[("pTb", kf) for kf in range(32)] + [("WoutT",)], tag="mm_wout")
                    em.op("vector", lambda e, sub=sub, dc=dc, bko=bko, pa=pa: e.tensor_tensor(
                        xrc[:, pa, sub, dc * 512:(dc + 1) * 512], xrc[:, pa, sub, dc * 512:(dc + 1) * 512],
                        bank[bko][:, :], ALU.add),
                        reads=[("bank", bko), ("xrc", pa, sub)], writes=[("xrc", pa, sub)], tag="residc")
            for s_ in range(4):
                t = n * 4 + s_
                em.dma("sync", xd3[t * 128:(t + 1) * 128, :], xrc[:, pa, s_, :],
                       reads=[("xrc", pa, s_)], writes=[("xd3", t)], tag="st3")
    em.barrier()

    with ExitStack() as s4:
        A = lambda n, sh, dt=F32: s4.enter_context(nc.sbuf_tensor(P.uniq(n), list(sh), dt))
        xcd = A("xcd", [128, 2, D])
        x3cd = A("x3cd", [128, 2, D])
        xod = A("xod", [128, 2, D])
        for t in range(NT):
            pa = t % 2
            em.dma("sync", xcd[:, pa, :], x_in[t * 128:(t + 1) * 128, :], writes=[("x", pa)], tag="ld_x")
            em.dma("sync", x3cd[:, pa, :], xd3[t * 128:(t + 1) * 128, :], writes=[("x3", pa)], tag="ld_x3")
            em.op("vector", lambda e, pa=pa: e.tensor_tensor(xod[:, pa, :], xcd[:, pa, :], x3cd[:, pa, :], ALU.add),
                  reads=[("x", pa), ("x3", pa)], writes=[("xo", pa)], tag="resid")
            em.dma("sync", xo_out[t * 128:(t + 1) * 128, :], xod[:, pa, :],
                   reads=[("xo", pa)], writes=[("xo_st", t)], tag="st_x")
    em.barrier()
    return P


def run_l1attn(x_full, w_rms_mix, Wg, bg, Wv, bv, Wd, bd, w_rms_attn, Wqkv, Wo,
               w_rms_ffn, Wgate, Wup, Wout):
    P = build_l1attn()
    nc = P.close_build()
    pmj = lambda v: np.ascontiguousarray(v.reshape(ND, 128).T)
    r = np.arange(128)[:, None]
    j = np.arange(256)[None, :]
    mband = np.where((j >= r + 1) & (j <= 128 + r), 0.0, NEG).astype(np.float32)
    jj = np.arange(128)[None, :]
    mfirst = np.where(jj <= r, 0.0, NEG).astype(np.float32)
    xs = x_full.reshape(NCORES, T, D)
    base = {
        "wmixt": pmj(w_rms_mix), "WgT": np.ascontiguousarray(Wg.T), "bgt": pmj(bg),
        "WvT": np.ascontiguousarray(Wv.T), "bvt": pmj(bv),
        "WdT": np.ascontiguousarray(Wd.T), "bdt": pmj(bd),
        "wattnt": pmj(w_rms_attn), "WqkvT": np.ascontiguousarray(Wqkv.T),
        "WoT": np.ascontiguousarray(Wo.T), "wffnt": pmj(w_rms_ffn),
        "WgateT": np.ascontiguousarray(Wgate.T), "WupT": np.ascontiguousarray(Wup.T),
        "WoutT": np.ascontiguousarray(Wout.T),
        "identf": np.eye(128, dtype=np.float32), "identb": np.eye(128, dtype=BFNP),
        "mask_band": mband, "mask_first": mfirst,
    }
    in_maps = [dict(base, x=np.ascontiguousarray(xs[c])) for c in range(NCORES)]
    res = run_bass_kernel_spmd(nc, in_maps, core_ids=list(range(NCORES)))
    out = np.stack([res.results[c]["xo"] for c in range(NCORES)])
    return out.reshape(16, 2048, D)


L1FFN_DEPTHS = 6
L1FFN_PARTS = "af"
L1FFN_DEBUG = False


def build_l1ffn(P=None, V=None, x_in=None, xo_out=None, bank=None):
    DEPTH6 = 6
    INNER = 1024
    DI2 = 2730
    standalone = P is None
    if standalone:
        P = Prog("l1ffn")
        V = {"Wqkv6": P.inp("Wqkv", [DEPTH6, D, 3 * INNER]), "Wo6": P.inp("Wo", [DEPTH6, INNER, D]),
             "lng": P.inp("lng", [128, DEPTH6, ND]), "lnb": P.inp("lnb", [128, DEPTH6, ND]),
             "W1": P.inp("W1", [DEPTH6, D, 2 * DI2]), "b1a": P.inp("b1a", [128, DEPTH6, 22]),
             "b1g": P.inp("b1g", [128, DEPTH6, 22]), "W2": P.inp("W2", [DEPTH6, DI2, D]),
             "b26": P.inp("b2", [DEPTH6, D]), "qs": P.inp("qs", [DEPTH6, 64]),
             "ks": P.inp("ks", [DEPTH6, 64]),
             "tab_cs": P.inp("tab_cs", [128, 16, 64], F32),
             "tab_sslo": P.inp("tab_sslo", [128, 16, 32], F32),
             "tab_sshi": P.inp("tab_sshi", [128, 16, 32], F32),
             "tab_ci": P.inp("tab_ci", [128, 16, 64], F32),
             "tab_silo": P.inp("tab_silo", [128, 16, 32], F32),
             "tab_sihi": P.inp("tab_sihi", [128, 16, 32], F32),
             "identf": P.inp("identf", [128, 128], F32), "identb": P.inp("identb", [128, 128], BF16),
             "m_full": P.inp("m_full", [128, 384], F32), "m_1": P.inp("m_1", [128, 384], F32),
             "m_0": P.inp("m_0", [128, 384], F32), "ones128": P.inp("ones128", [128, 1], F32),
             "ones1": P.inp("ones1", [1, 128], F32)}
        x_in = P.inp("x", [T, D])
        xo_out = P.out("xo", [T, D])
    em = P.em
    nc = P.nc
    Wqkv_in = V["Wqkv6"]; Wo_in = V["Wo6"]; lng_in = V["lng"]; lnb_in = V["lnb"]
    W1_in = V["W1"]; b1a_in = V["b1a"]; b1g_in = V["b1g"]; W2_in = V["W2"]; b2_in = V["b26"]
    qs_in = V["qs"]; ks_in = V["ks"]
    tab_cs_in = V["tab_cs"]; tab_sslo_in = V["tab_sslo"]; tab_sshi_in = V["tab_sshi"]
    tab_ci_in = V["tab_ci"]; tab_silo_in = V["tab_silo"]; tab_sihi_in = V["tab_sihi"]
    identf_in = V["identf"]; identb_in = V["identb"]
    m_full_in = V["m_full"]; m_1_in = V["m_1"]; m_0_in = V["m_0"]
    ones128_in = V["ones128"]; ones1_in = V["ones1"]

    xd = [P.dram(f"xd{i}", [T, D]) for i in range(2)]
    dbg_mu = P.out("dbg_mu", [1, 1024]) if L1FFN_DEBUG else None
    dbg_rs = P.out("dbg_rs", [1, 1024]) if L1FFN_DEBUG else None
    dbg_xnf = P.out("dbg_xnf", [128, ND, 1024], BF16) if L1FFN_DEBUG else None
    dbg_pb = P.out("dbg_pb", [128, 22, 1024], BF16) if L1FFN_DEBUG else None

    if bank is None:
        bank = [P.psum(f"bank{i}", [128, 512], F32) for i in range(8)]
    BK = lambda i: ("bank", i)

    inst_ctx = ExitStack()
    _sb = lambda n, sh, dt=F32: inst_ctx.enter_context(nc.sbuf_tensor(P.uniq(n), list(sh), dt))
    ident = _sb("ident", [128, 128], F32)
    identb = _sb("identb", [128, 128], BF16)
    tab_cs = _sb("tab_cs", [128, 16, 64], F32)
    tab_sslo = _sb("tab_sslo", [128, 16, 32], F32)
    tab_sshi = _sb("tab_sshi", [128, 16, 32], F32)
    tab_ci = _sb("tab_ci", [128, 16, 64], F32)
    tab_silo = _sb("tab_silo", [128, 16, 32], F32)
    tab_sihi = _sb("tab_sihi", [128, 16, 32], F32)
    m_full = _sb("m_full", [128, 384], F32)
    m_1 = _sb("m_1", [128, 384], F32)
    m_0 = _sb("m_0", [128, 384], F32)
    ones128 = _sb("ones128", [128, 1], F32)
    ones128b = _sb("ones128b", [128, 1], BF16)
    ones1 = _sb("ones1", [1, 128], F32)
    em.dma_fn("gpsimd", lambda e: e.dma_start(out=ones128b[:, :], in_=ones128_in),
              writes=[("o128b",)], tag="ldcb")
    for ap_s, ap_i, key in ((ident, identf_in, "ident"), (identb, identb_in, "identb"),
                            (tab_cs, tab_cs_in, "tcs"), (tab_sslo, tab_sslo_in, "tsl"),
                            (tab_sshi, tab_sshi_in, "tsh"), (tab_ci, tab_ci_in, "tci"),
                            (tab_silo, tab_silo_in, "til"), (tab_sihi, tab_sihi_in, "tih"),
                            (m_full, m_full_in, "mf"), (m_1, m_1_in, "m1m"), (m_0, m_0_in, "m0m"),
                            (ones128, ones128_in, "o128"), (ones1, ones1_in, "o1")):
        em.dma("sync", ap_s[:, :] if ap_s.shape.__len__() == 2 else ap_s[:, :, :], ap_i,
               writes=[(key,)], tag="ldc")

    with ExitStack() as s0:
        A = lambda n, sh, dt=F32: s0.enter_context(nc.sbuf_tensor(P.uniq(n), list(sh), dt))
        xc0 = A("xc0", [128, 2, D])
        lns0 = {k: A("f0" + k, [128, 2, D if k in ("sq", "xn") else 1]) for k in
                ("sq", "s", "s2", "mu", "m2", "var", "lnv", "rstd", "xn")}
        lns0["tag"] = "lnF"
        for t in range(NT):
            pa = t % 2
            em.dma("sync", xc0[:, pa, :], x_in[t * 128:(t + 1) * 128, :], writes=[("x0", pa)], tag="ld")
            xk = emit_ln_tile(P, lns0, pa, xc0[:, pa, :], ("x0", pa))
            em.dma("sync", xd[0][t * 128:(t + 1) * 128, :], lns0["xn"][:, pa, :], reads=[xk],
                   writes=[("xd0", t)], tag="st")
    em.barrier()

    def emit_attn_depth(i, src, dst):
        with ExitStack() as sa:
            A = lambda n, sh, dt=F32: sa.enter_context(nc.sbuf_tensor(P.uniq(n), list(sh), dt))
            wq = A("wq", [128, ND, 3 * INNER], BF16)
            wo = A("wo", [128, ND, D], BF16)
            qsb = A("qsb", [1, 64])
            ksb = A("ksb", [1, 64])
            qkb2 = A("qkb2", [128, 2, 64])
            Aq = A("Aq", [128, 16, 64])
            Bql = A("Bql", [128, 16, 32])
            Bqh = A("Bqh", [128, 16, 32])
            Ak = A("Ak", [128, 16, 64])
            Bkl = A("Bkl", [128, 16, 32])
            Bkh = A("Bkh", [128, 16, 32])
            xat = A("xat", [128, 2, D])
            xT = A("xT", [128, 2, ND, 128], BF16)
            xsT = A("xsT", [128, 2, 4, 128], BF16)
            qb = A("qb", [128, 2, INNER])
            kb = A("kb", [128, 2, INNER])
            sqq = A("sqq", [128, 2, INNER], BF16)
            s2q = A("s2q", [128, 2, 16])
            s2k = A("s2k", [128, 2, 16])
            rsq = A("rsq", [128, 2, 16])
            rsk = A("rsk", [128, 2, 16])
            tmpq = A("tmpq", [128, 2, INNER])
            tmpv = A("tmpv", [128, 2, INNER])
            q2 = A("q2", [128, 2, INNER], BF16)
            k2 = A("k2", [128, 2, INNER], BF16)
            qT = A("qT", [128, 2, ND, 128], BF16)
            kT = A("kT", [128, ND, 6 * 128], BF16)
            v_roll = A("v_roll", [128, 3, INNER], BF16)
            maskb = A("maskb", [128, 3, 384], BF16)
            probs_bf = A("probs_bf", [128, 4, 384], BF16)
            probs_bf2 = A("probs_bf2", [128, 4, 384], BF16)
            pT = A("pT", [128, 4, 3, 128], BF16)
            nm = A("nm", [128, 4, 1])
            sume = A("sume", [128, 4, 1])
            rec = A("rec", [128, 4, 1])
            yT = A("yT", [128, 2, ND, 128], BF16)
            xob = A("xob", [128, 2, D])
            em.dma_fn("gpsimd", lambda e: e.dma_start(
                out=wq[:, :, :], in_=Wqkv_in[i].rearrange("(k p) c -> p k c", p=128)),
                writes=[("wq", k) for k in range(ND)], tag="ldw")
            em.dma_fn("gpsimd", lambda e: e.dma_start(
                out=wo[:, :, :], in_=Wo_in[i].rearrange("(k p) c -> p k c", p=128)),
                writes=[("wo", k) for k in range(ND)], tag="ldw")
            for mi, m_in in enumerate((m_0_in, m_1_in, m_full_in)):
                em.dma_fn("gpsimd", lambda e, mi=mi, m_in=m_in: e.dma_start(out=maskb[:, mi, :], in_=m_in),
                          writes=[("maskb",)], tag="ldmb")
            em.dma("sync", qsb[:, :], qs_in[i:i + 1, :], writes=[("qsb",)], tag="ldq")
            em.dma("sync", ksb[:, :], ks_in[i:i + 1, :], writes=[("ksb",)], tag="ldk")
            emit_mm_group(P, bank[0][:, 0:64], BK(0), [(ones1[:, :], qsb[:, :])],
                          reads=[("o1",), ("qsb",)], tag="mm_bq")
            em.op("vector", lambda e: e.tensor_copy(qkb2[:, 0, :], bank[0][:, 0:64]),
                  reads=[BK(0)], writes=[("qs_b",)], tag="ev_bq")
            emit_mm_group(P, bank[0][:, 0:64], BK(0), [(ones1[:, :], ksb[:, :])],
                          reads=[("o1",), ("ksb",)], tag="mm_bk")
            em.op("vector", lambda e: e.tensor_copy(qkb2[:, 1, :], bank[0][:, 0:64]),
                  reads=[BK(0)], writes=[("ks_b",)], tag="ev_bk")
            qs_b = qkb2[:, 0, :]
            ks_b = qkb2[:, 1, :]
            em.op("vector", lambda e: e.scalar_tensor_tensor(
                Aq[:, :, :], tab_cs[:, :, :], 8.0,
                qs_b.unsqueeze(1).broadcast_to([128, 16, 64]), ALU.mult, ALU.mult),
                reads=[("tcs",), ("qs_b",)], writes=[("Aq",)], tag="tAq")
            em.op("vector", lambda e: e.scalar_tensor_tensor(
                Bql[:, :, :], tab_sslo[:, :, :], 8.0,
                qs_b[:, 32:64].unsqueeze(1).broadcast_to([128, 16, 32]), ALU.mult, ALU.mult),
                reads=[("tsl",), ("qs_b",)], writes=[("Bql",)], tag="tBql")
            em.op("vector", lambda e: e.scalar_tensor_tensor(
                Bqh[:, :, :], tab_sshi[:, :, :], 8.0,
                qs_b[:, 0:32].unsqueeze(1).broadcast_to([128, 16, 32]), ALU.mult, ALU.mult),
                reads=[("tsh",), ("qs_b",)], writes=[("Bqh",)], tag="tBqh")
            em.op("vector", lambda e: e.tensor_tensor(
                Ak[:, :, :], tab_ci[:, :, :], ks_b.unsqueeze(1).broadcast_to([128, 16, 64]), ALU.mult),
                reads=[("tci",), ("ks_b",)], writes=[("Ak",)], tag="tAk")
            em.op("vector", lambda e: e.tensor_tensor(
                Bkl[:, :, :], tab_silo[:, :, :], ks_b[:, 32:64].unsqueeze(1).broadcast_to([128, 16, 32]), ALU.mult),
                reads=[("til",), ("ks_b",)], writes=[("Bkl",)], tag="tBkl")
            em.op("vector", lambda e: e.tensor_tensor(
                Bkh[:, :, :], tab_sihi[:, :, :], ks_b[:, 0:32].unsqueeze(1).broadcast_to([128, 16, 32]), ALU.mult),
                reads=[("tih",), ("ks_b",)], writes=[("Bkh",)], tag="tBkh")

            wqr = [("wq", k) for k in range(ND)]
            for t in range(NT):
                pa = t % 2
                t_loc = t % NT_SEQ
                r3 = t % 3
                em.dma("sync", xat[:, pa, :], src[t * 128:(t + 1) * 128, :], writes=[("xat", pa)], tag="ldx")
                for g4 in range(2):
                    def fn_trx(e, g4=g4, pa=pa):
                        inst = None
                        for j in range(4):
                            inst = e.transpose(bank[6][:, j * 128:(j + 1) * 128],
                                               xat[:, pa, (g4 * 4 + j) * 128:(g4 * 4 + j + 1) * 128], ident[:, :])
                        return inst
                    em.op("tensor", fn_trx, reads=[("xat", pa), ("ident",)], writes=[BK(6)], tag="trx")
                    em.op("vector", lambda e, g4=g4, pa=pa: e.tensor_copy(
                        xT.ap()[:, pa, g4 * 4:(g4 + 1) * 4, :].opt(), bank[6][:, :]),
                        reads=[BK(6)], writes=[("xT", pa, k) for k in range(g4 * 4, g4 * 4 + 4)], tag="evx")
                for k in range(4, ND):
                    if t_loc == 0:
                        em.op("gpsimd", lambda e, k=k, pa=pa: e.memset(xsT[:, pa, k - 4, 0:1], 0.0),
                              reads=[], writes=[("xsT", pa, k)], tag="z_sft")
                    else:
                        em.op("scalar", lambda e, k=k, pa=pa: e.copy(xsT[:, pa, k - 4, 0:1], xT[:, 1 - pa, k, 127:128]),
                              reads=[("xT", 1 - pa, k)], writes=[("xsT", pa, k)], tag="c_sft0")
                    em.op("scalar", lambda e, k=k, pa=pa: e.copy(xsT[:, pa, k - 4, 1:128], xT[:, pa, k, 0:127]),
                          reads=[("xT", pa, k)], writes=[("xsT", pa, k)], deps=(), tag="c_sft")

                def rhs_k(k, pa=pa):
                    return xT[:, pa, k, :] if k < 4 else xsT[:, pa, k - 4, :]
                xs_reads = [("xT", pa, k) for k in range(4)] + [("xsT", pa, k) for k in range(4, ND)]
                for (dst_b, key, col0) in ((qb, "qb", 0), (kb, "kb", INNER)):
                    for c in range(2):
                        emit_mm_group(P, bank[c][:, :], BK(c),
                                      [(rhs_k(k), wq[:, k, col0 + c * 512:col0 + (c + 1) * 512]) for k in range(ND)],
                                      reads=xs_reads + wqr, tag="mm_qk")
                    em.op("vector", lambda e, dst_b=dst_b, pa=pa: e.tensor_copy(dst_b[:, pa, 0:512], bank[0][:, :]),
                          reads=[BK(0)], writes=[(key, pa)], tag="ev_qk")
                    em.op("vector", lambda e, dst_b=dst_b, pa=pa: e.tensor_copy(dst_b[:, pa, 512:1024], bank[1][:, :]),
                          reads=[BK(1)], writes=[(key, pa)], deps=(), tag="ev_qk2")
                for c in range(2):
                    emit_mm_group(P, bank[2][:, :], BK(2),
                                  [(rhs_k(k), wq[:, k, 2 * INNER + c * 512:2 * INNER + (c + 1) * 512]) for k in range(ND)],
                                  reads=xs_reads + wqr, tag="mm_v")
                    em.op("vector", lambda e, c=c, r3=r3: e.tensor_copy(v_roll[:, r3, c * 512:(c + 1) * 512], bank[2][:, :]),
                          reads=[BK(2)], writes=[("v", r3, c)], tag="ev_v")
                for (buf, key, s2b, rsb, Atab, Bltab, Bhtab, outb, okey) in (
                        (qb, "qb", s2q, rsq, Aq, Bql, Bqh, q2, "q2"),
                        (kb, "kb", s2k, rsk, Ak, Bkl, Bkh, k2, "k2")):
                    em.op("scalar", lambda e, buf=buf, pa=pa, s2b=s2b: e.activation(
                        sqq[:, pa, :], buf[:, pa, :], AF.Square),
                        reads=[(key, pa)], writes=[("sqq", pa)], tag="rsq")
                    em.op("vector", lambda e, pa=pa, s2b=s2b: e.reduce_sum(
                        s2b[:, pa, :], sqq.ap()[:, pa, :].rearrange("p (h d) -> p h d", d=64), axis=AX),
                        reads=[("sqq", pa)], writes=[("s2", key, pa)], tag="rs2")
                    em.op("scalar", lambda e, pa=pa, s2b=s2b, rsb=rsb: e.activation(rsb[:, pa, :], s2b[:, pa, :], AF.Ln),
                          reads=[("s2", key, pa)], writes=[("lnv", key, pa)], tag="rln")
                    em.op("scalar", lambda e, pa=pa, rsb=rsb: e.activation(rsb[:, pa, :], rsb[:, pa, :], AF.Exp, scale=0.5),
                          reads=[("lnv", key, pa)], writes=[("st", key, pa)], tag="rexp")
                    em.op("vector", lambda e, pa=pa, rsb=rsb: e.tensor_scalar(rsb[:, pa, :], rsb[:, pa, :], 1e-12, None, ALU.max),
                          reads=[("st", key, pa)], writes=[("mx", key, pa)], tag="rmax")
                    em.op("vector", lambda e, pa=pa, rsb=rsb: e.reciprocal(rsb[:, pa, :], rsb[:, pa, :]),
                          reads=[("mx", key, pa)], writes=[("rs", key, pa)], tag="rrec")
                    b3 = lambda ap: ap
                    em.op("vector", lambda e, buf=buf, pa=pa, Atab=Atab, t_loc=t_loc: e.tensor_tensor(
                        tmpq[:, pa, :].rearrange("p (h d) -> p h d", d=64),
                        buf[:, pa, :].rearrange("p (h d) -> p h d", d=64),
                        Atab[:, t_loc:t_loc + 1, :].broadcast_to([128, 16, 64]), ALU.mult),
                        reads=[(key, pa), ("Aq",), ("Ak",)], writes=[("tmpq", pa)], tag="rotA")
                    em.op("vector", lambda e, buf=buf, pa=pa, Bltab=Bltab, t_loc=t_loc: e.tensor_tensor(
                        tmpv[:, pa, :].rearrange("p (h d) -> p h d", d=64)[:, :, 0:32],
                        buf[:, pa, :].rearrange("p (h d) -> p h d", d=64)[:, :, 32:64],
                        Bltab[:, t_loc:t_loc + 1, :].broadcast_to([128, 16, 32]), ALU.mult),
                        reads=[(key, pa), ("Bql",), ("Bkl",)], writes=[("tmpv", pa)], tag="rotBl")
                    em.op("vector", lambda e, buf=buf, pa=pa, Bhtab=Bhtab, t_loc=t_loc: e.tensor_tensor(
                        tmpv[:, pa, :].rearrange("p (h d) -> p h d", d=64)[:, :, 32:64],
                        buf[:, pa, :].rearrange("p (h d) -> p h d", d=64)[:, :, 0:32],
                        Bhtab[:, t_loc:t_loc + 1, :].broadcast_to([128, 16, 32]), ALU.mult),
                        reads=[(key, pa), ("Bqh",), ("Bkh",)], writes=[("tmpv", pa)], deps=(), tag="rotBh")
                    em.op("vector", lambda e, pa=pa: e.tensor_tensor(tmpq[:, pa, :], tmpq[:, pa, :], tmpv[:, pa, :], ALU.add),
                          reads=[("tmpq", pa), ("tmpv", pa)], writes=[("tmpq", pa)], tag="rotsum")
                    em.op("vector", lambda e, pa=pa, rsb=rsb, outb=outb: e.tensor_tensor(
                        outb[:, pa, :].rearrange("p (h d) -> p h d", d=64),
                        tmpq[:, pa, :].rearrange("p (h d) -> p h d", d=64),
                        rsb[:, pa, :].unsqueeze(2).broadcast_to([128, 16, 64]), ALU.mult),
                        reads=[("tmpq", pa), ("rs", key, pa)], writes=[(okey, pa)], tag="rotout")
                for g4 in range(2):
                    def fn_trq(e, g4=g4, pa=pa):
                        b6 = bank[6][:, :].bitcast(BF16)
                        inst = None
                        for j in range(4):
                            inst = e.transpose(b6[:, j * 128:(j + 1) * 128],
                                               q2[:, pa, (g4 * 4 + j) * 128:(g4 * 4 + j + 1) * 128], identb[:, :])
                        return inst
                    em.op("tensor", fn_trq, reads=[("q2", pa), ("identb",)], writes=[BK(6)], tag="trq")
                    em.op("scalar", lambda e, g4=g4, pa=pa: e.copy(
                        qT.ap()[:, pa, g4 * 4:(g4 + 1) * 4, :].opt(), bank[6][:, :].bitcast(BF16)[:, 0:512]),
                        reads=[BK(6)], writes=[("qT", pa, k) for k in range(g4 * 4, g4 * 4 + 4)], tag="evq")
                    def fn_trk(e, g4=g4, pa=pa):
                        b6 = bank[6][:, :].bitcast(BF16)
                        inst = None
                        for j in range(4):
                            inst = e.transpose(b6[:, (4 + j) * 128:(4 + j + 1) * 128],
                                               k2[:, pa, (g4 * 4 + j) * 128:(g4 * 4 + j + 1) * 128], identb[:, :])
                        return inst
                    em.op("tensor", fn_trk, reads=[("k2", pa), ("identb",)], writes=[BK(6)], tag="trk")
                    em.op("scalar", lambda e, g4=g4, kr=t % 6: e.copy(
                        kT.ap()[:, g4 * 4:(g4 + 1) * 4, kr * 128:(kr + 1) * 128],
                        bank[6][:, :].bitcast(BF16)[:, 512:1024].rearrange("p (k c) -> p k c", c=128)),
                        reads=[BK(6)], writes=[("kT", t % 6, k) for k in range(g4 * 4, g4 * 4 + 4)], tag="evk")
                kr8 = t % 6
                mi = min(t_loc, 2)
                for h in range(16):
                    hp = h // 2
                    base = (h % 2) * 64
                    hb = h % 4
                    scb = 3 if h % 2 == 0 else 6
                    q_ap = qT[base:base + 64, pa, hp, :]

                    def fn_sc(e, q_ap=q_ap, hp=hp, base=base, t_loc=t_loc, kr8=kr8, mi=mi, scb=scb):
                        bb = bank[scb][:, :].bitcast(BF16)
                        e.matmul(bank[scb][:, 0:384], identb[:, :], maskb[:, mi, :], start=True, stop=False)
                        if t_loc >= 2 and kr8 >= 2:
                            return e.matmul(bank[scb][:, 0:384], q_ap,
                                            kT[base:base + 64, hp, (kr8 - 2) * 128:(kr8 + 1) * 128],
                                            start=False, stop=True)
                        inst = None
                        for bi in range(3):
                            tt = t_loc - 2 + bi
                            if tt < 0:
                                continue
                            kk = (kr8 - 2 + bi) % 6
                            inst = e.matmul(bank[scb][:, bi * 128:(bi + 1) * 128], q_ap,
                                            kT[base:base + 64, hp, kk * 128:(kk + 1) * 128], start=False, stop=True)
                        return inst
                    sc_reads = [("qT", pa, hp), ("maskb",), ("identb",)] +                                [("kT", (kr8 - 2 + bi) % 6, hp) for bi in range(3)]
                    em.op("tensor", fn_sc, reads=sc_reads, writes=[BK(scb)], tag="mm_sc")
                    em.op("vector", lambda e, hb=hb, scb=scb: e.reduce_max(nm[:, hb, :], bank[scb][:, 0:384], axis=AX, negate=True),
                          reads=[BK(scb)], writes=[("nm", hb)], tag="negmax")
                    em.op("scalar", lambda e, hb=hb, scb=scb: e.activation(probs_bf[:, hb, 0:384], bank[scb][:, 0:384], AF.Exp,
                                                                           bias=nm[:, hb, :], accum_out=sume[:, hb, :]),
                          reads=[BK(scb), ("nm", hb)], writes=[("probs_bf", hb), ("sume", hb)], tag="exp")
                    em.op("vector", lambda e, hb=hb: e.reciprocal(rec[:, hb, :], sume[:, hb, :]),
                          reads=[("sume", hb)], writes=[("rec", hb)], tag="recip")
                    em.op("vector", lambda e, hb=hb: e.tensor_scalar(probs_bf2[:, hb, 0:384], probs_bf[:, hb, 0:384],
                                                                     rec[:, hb, :], None, ALU.mult),
                          reads=[("probs_bf", hb), ("rec", hb)], writes=[("probs_bf2", hb)], tag="pscale")

                    def fn_trp(e, hb=hb, h2=h % 2):
                        b4 = bank[4][:, :].bitcast(BF16)
                        inst = None
                        for c in range(3):
                            inst = e.transpose(b4[:, (h2 * 3 + c) * 128:(h2 * 3 + c + 1) * 128],
                                               probs_bf2[:, hb, c * 128:(c + 1) * 128], identb[:, :])
                        return inst
                    em.op("tensor", fn_trp, reads=[("probs_bf2", hb), ("identb",)], writes=[BK(4)], tag="tr_p")
                    em.op("scalar", lambda e, hb=hb, h2=h % 2: e.copy(pT.ap()[:, hb, :, :].opt(),
                                                            bank[4][:, :].bitcast(BF16)[:, h2 * 384:h2 * 384 + 384]),
                          reads=[BK(4)], writes=[("pT", hb)], tag="ev_p")
                    av_pairs = []
                    av_reads = [("pT", hb)]
                    for bi in range(3):
                        tt = t_loc - 2 + bi
                        if tt < 0:
                            continue
                        rr = (r3 + 1 + bi) % 3
                        av_pairs.append((v_roll[:, rr, h * 64:(h + 1) * 64], pT[:, hb, bi, :]))
                        av_reads += [("v", rr, 0), ("v", rr, 1)]
                    tp = None if base == 0 else (0, 64)
                    emit_mm_group(P, bank[5][base:base + 64, 0:128], ("ps_av", h % 2), av_pairs,
                                  reads=av_reads, tag="mm_av", tile_position=tp)
                    if h % 2 == 1:
                        em.op("vector", lambda e, hp=hp, pa=pa: e.tensor_copy(yT[:, pa, hp, :], bank[5][:, 0:128]),
                              reads=[("ps_av", 0), ("ps_av", 1)], writes=[("yT", pa, hp)], tag="ev_av")
                for c in range(2):
                    emit_mm_group(P, bank[7][:, :], BK(7),
                                  [(yT[:, pa, k, :], wo[:, k, c * 512:(c + 1) * 512]) for k in range(ND)],
                                  reads=[("yT", pa, k) for k in range(ND)] + [("wo", k) for k in range(ND)],
                                  tag="mm_pr")
                    em.op("vector", lambda e, c=c, pa=pa: e.tensor_tensor(
                        xob[:, pa, c * 512:(c + 1) * 512], bank[7][:, :], xat[:, pa, c * 512:(c + 1) * 512], ALU.add),
                        reads=[BK(7), ("xat", pa)], writes=[("xo", pa, c)], tag="resid")
                em.dma("sync", dst[t * 128:(t + 1) * 128, :], xob[:, pa, :],
                       reads=[("xo", pa, 0), ("xo", pa, 1)], writes=[("dst", t)], tag="stx")
        em.barrier()

    def emit_ffn_depth(i, src, dst):
        NCHF = 4
        CHF = 1024
        NTC = CHF // 128
        with ExitStack() as sf:
            A = lambda n, sh, dt=F32: sf.enter_context(nc.sbuf_tensor(P.uniq(n), list(sh), dt))
            w2r_sb = A("w2r", [128, 22, D], BF16)
            w1a_b = A("w1a", [128, 2, ND, 128], BF16)
            w1g_b = A("w1g", [128, 2, ND, 128], BF16)
            b2sb = A("b2sb", [1, D])
            lngs = A("lngs", [128, ND])
            lnbs = A("lnbs", [128, ND])
            b1as = A("b1as", [128, 22])
            b1gs = A("b1gs", [128, 22])
            xTf = A("xTf", [128, 2, ND, CHF + 1], BF16)
            sqk = A("sqk", [128, 1, CHF + 1], BF16)
            mu_f = A("mu_f", [1, 1, CHF])
            vr_f = A("vr_f", [1, 1, CHF])
            lb_f = A("lb_f", [1, 1, CHF])
            rs_f = A("rs_f", [1, 1, CHF])
            xnf = A("xnf", [128, 1, ND, CHF], BF16)
            asb = A("asb", [128, 1, 512])
            Pb = A("Pb", [128, 22, CHF], BF16)
            xw = A("xw", [128, 2, D])
            xobf = A("xobf", [128, 2, D])
            for kf in range(22):
                kk0 = kf * 128
                kk1 = min(DI2, kk0 + 128)
                em.dma_fn("gpsimd", lambda e, kf=kf, kk0=kk0, kk1=kk1: e.dma_start(
                    out=w2r_sb[0:kk1 - kk0, kf, :], in_=W2_in[i, kk0:kk1, :]),
                    writes=[("w2r",)], tag="ldw2")
            em.dma("sync", b2sb[:, :], b2_in[i:i + 1, :], writes=[("b2s",)], tag="ldb2")
            em.dma("sync", lngs[:, :], lng_in[:, i, :], writes=[("lngs",)], tag="ldg")
            em.dma("sync", lnbs[:, :], lnb_in[:, i, :], writes=[("lnbs",)], tag="ldb")
            em.dma("sync", b1as[:, :], b1a_in[:, i, :], writes=[("b1as",)], tag="ldb1a")
            em.dma("sync", b1gs[:, :], b1g_in[:, i, :], writes=[("b1gs",)], tag="ldb1g")
            for n in range(NCHF):
                pa = n % 2
                seq_start = (n % 2 == 0)
                for s_ in range(NTC):
                    t = n * NTC + s_
                    em.dma("sync", xw[:, pa, :], src[t * 128:(t + 1) * 128, :], writes=[("xw", pa)], tag="ldx")
                    for g4 in range(2):
                        def fn_trf(e, g4=g4, pa=pa):
                            inst = None
                            for j in range(4):
                                inst = e.transpose(bank[6][:, j * 128:(j + 1) * 128],
                                                   xw[:, pa, (g4 * 4 + j) * 128:(g4 * 4 + j + 1) * 128], ident[:, :])
                            return inst
                        em.op("tensor", fn_trf, reads=[("xw", pa), ("ident",)], writes=[BK(6)], tag="trf")
                        em.op("vector", lambda e, g4=g4, pa=pa, s_=s_: e.tensor_copy(
                            xTf.ap()[:, pa, g4 * 4:(g4 + 1) * 4, 1 + s_ * 128:1 + (s_ + 1) * 128],
                            bank[6][:, :].rearrange("p (k c) -> p k c", c=128)),
                            reads=[BK(6)], writes=[("xTf", pa, k) for k in range(g4 * 4, g4 * 4 + 4)], tag="evf")
                for k in range(ND):
                    if seq_start:
                        em.op("gpsimd", lambda e, k=k, pa=pa: e.memset(xTf[:, pa, k, 0:1], 0.0),
                              writes=[("xTf", pa, k)], deps=(), tag="zb")
                    else:
                        em.op("scalar", lambda e, k=k, pa=pa: e.copy(xTf[:, pa, k, 0:1], xTf[:, 1 - pa, k, CHF:CHF + 1]),
                              reads=[("xTf", 1 - pa, k)], writes=[("xTf", pa, k)], tag="cb")
                xr_reads = [("xTf", pa, k) for k in range(ND)]
                def sl(k, hcol):
                    a = 1 if k < 4 else 0
                    return a + hcol * 512
                for hcol in range(2):
                    def fn_sum(e, pa=pa, hcol=hcol):
                        inst = None
                        for k in range(ND):
                            a = sl(k, hcol)
                            inst = e.matmul(bank[hcol][0:1, :], ones128b[:, :], xTf[:, pa, k, a:a + 512],
                                            start=(k == 0), stop=(k == ND - 1))
                        return inst
                    em.op("tensor", fn_sum, reads=xr_reads + [("o128b",)], writes=[BK(hcol)], tag="mm_s")
                    em.op("vector", lambda e, pa=pa, hcol=hcol: e.tensor_scalar(
                        mu_f[:, 0, hcol * 512:(hcol + 1) * 512], bank[hcol][0:1, :], 1.0 / D, None, ALU.mult),
                        reads=[BK(hcol)], writes=[("muk",)], tag="mu")
                sq_reads = []
                for k in range(ND):
                    em.op("scalar", lambda e, k=k, pa=pa: e.activation(sqk[:, 0, :], xTf[:, pa, k, :], AF.Square),
                          reads=[("xTf", pa, k)], writes=[("sqkk",)], tag="sq")
                    sq_reads.append(("sqkk",))
                    for hcol in range(2):
                        a = sl(k, hcol)
                        em.op("tensor", lambda e, k=k, pa=pa, a=a, hcol=hcol: e.matmul(
                            bank[2 + hcol][0:1, :], ones128b[:, :], sqk[:, 0, a:a + 512],
                            start=(k == 0), stop=(k == ND - 1)),
                            reads=[("sqkk",), ("o128b",)], writes=[BK(2 + hcol)], tag="mm_s2")
                for hcol in range(2):
                    hs = slice(hcol * 512, (hcol + 1) * 512)
                    em.op("vector", lambda e, hs=hs: e.tensor_mul(lb_f[:, 0, hs], mu_f[:, 0, hs], mu_f[:, 0, hs]),
                          reads=[("muk",)], writes=[("m2a", hcol)], tag="m2a")
                    em.op("vector", lambda e, hs=hs: e.tensor_scalar(vr_f[:, 0, hs], lb_f[:, 0, hs], -1e-5, None, ALU.add),
                          reads=[("m2a", hcol)], writes=[("m2k", hcol)], tag="m2b")
                    em.op("vector", lambda e, hs=hs, hcol=hcol: e.scalar_tensor_tensor(
                        lb_f[:, 0, hs], bank[2 + hcol][0:1, :], 1.0 / D, vr_f[:, 0, hs], ALU.mult, ALU.subtract),
                        reads=[BK(2 + hcol), ("m2k", hcol)], writes=[("vark", hcol)], tag="var")
                    em.op("scalar", lambda e, hs=hs: e.activation(vr_f[:, 0, hs], lb_f[:, 0, hs], AF.Ln),
                          reads=[("vark", hcol)], writes=[("lnvk", hcol)], tag="lnl")
                    em.op("scalar", lambda e, hs=hs: e.activation(rs_f[:, 0, hs], vr_f[:, 0, hs], AF.Exp, scale=-0.5),
                          reads=[("lnvk", hcol)], writes=[("rstdk", hcol)], tag="expl")
                emit_mm_group(P, bank[2][:, :], BK(2), [(ones1[:, :], mu_f[:, 0, 0:512])],
                              reads=[("o1",), ("muk",)], tag="mm_bmu")
                emit_mm_group(P, bank[3][:, :], BK(3), [(ones1[:, :], mu_f[:, 0, 512:1024])],
                              reads=[("o1",), ("muk",)], tag="mm_bmu2")
                emit_mm_group(P, bank[4][:, :], BK(4), [(ones1[:, :], rs_f[:, 0, 0:512])],
                              reads=[("o1",), ("rstdk", 0)], tag="mm_br")
                emit_mm_group(P, bank[5][:, :], BK(5), [(ones1[:, :], rs_f[:, 0, 512:1024])],
                              reads=[("o1",), ("rstdk", 1)], tag="mm_br2")
                for k in range(ND):
                    for hcol in range(2):
                        a = sl(k, hcol)
                        bmu = bank[2] if hcol == 0 else bank[3]
                        brs = bank[4] if hcol == 0 else bank[5]
                        em.op("vector", lambda e, k=k, pa=pa, a=a, hcol=hcol, bmu=bmu: e.tensor_tensor(
                            xnf[:, 0, k, hcol * 512:(hcol + 1) * 512],
                            xTf[:, pa, k, a:a + 512], bmu[:, :], ALU.subtract),
                            reads=[("xTf", pa, k), BK(2 if hcol == 0 else 3)], writes=[("xnfk", k)], tag="nsub")
                        em.op("vector", lambda e, k=k, pa=pa, hcol=hcol, brs=brs: e.tensor_tensor(
                            xnf[:, 0, k, hcol * 512:(hcol + 1) * 512],
                            xnf[:, 0, k, hcol * 512:(hcol + 1) * 512], brs[:, :], ALU.mult),
                            reads=[("xnfk", k), BK(4 if hcol == 0 else 5)], writes=[("xnfk", k)], tag="nmul")
                    em.op("vector", lambda e, k=k, pa=pa: e.tensor_scalar(
                        xnf[:, 0, k, :], xnf[:, 0, k, :], lngs[:, k:k + 1], lnbs[:, k:k + 1], ALU.mult, ALU.add),
                        reads=[("xnfk", k), ("lngs",), ("lnbs",)], writes=[("xnfk", k)], tag="ngb")
                xn_reads = [("xnfk", k) for k in range(ND)]
                if L1FFN_DEBUG and i == 0 and n == 0:
                    em.dma("gpsimd", dbg_mu, mu_f[:, 0, :], reads=[("muk",)], tag="dbgmu")
                    em.dma("gpsimd", dbg_rs, rs_f[:, 0, :], reads=[("rstdk", 0), ("rstdk", 1)], tag="dbgrs")
                    for k in range(ND):
                        em.dma("gpsimd", dbg_xnf[:, k, :], xnf[:, 0, k, :], reads=[("xnfk", k)], tag="dbgxn")
                for m in range(22):
                    wpa = m % 2
                    c0 = m * 128
                    c1 = min(DI2, c0 + 128)
                    w = c1 - c0
                    em.dma_fn("gpsimd", lambda e, c0=c0, c1=c1, wpa=wpa: e.dma_start(
                        out=w1a_b[:, wpa, :, 0:c1 - c0],
                        in_=W1_in[i].rearrange("(k p) c -> p k c", p=128)[:, :, c0:c1]),
                        writes=[("w1a", wpa, k) for k in range(ND)], tag="ldw1a")
                    em.dma_fn("gpsimd", lambda e, c0=c0, c1=c1, wpa=wpa: e.dma_start(
                        out=w1g_b[:, wpa, :, 0:c1 - c0],
                        in_=W1_in[i].rearrange("(k p) c -> p k c", p=128)[:, :, DI2 + c0:DI2 + c1]),
                        writes=[("w1g", wpa, k) for k in range(ND)], tag="ldw1g")
                    for half in range(2):
                        emit_mm_group(P, bank[4 + half][0:w, :], ("bank", 4 + half),
                                      [(w1a_b[:, wpa, k, 0:w], xnf[:, 0, k, half * 512:(half + 1) * 512]) for k in range(ND)],
                                      reads=xn_reads + [("w1a", wpa, k) for k in range(ND)], tag="mm_w1a")
                        emit_mm_group(P, bank[6 + half][0:w, :], ("bank", 6 + half),
                                      [(w1g_b[:, wpa, k, 0:w], xnf[:, 0, k, half * 512:(half + 1) * 512]) for k in range(ND)],
                                      reads=xn_reads + [("w1g", wpa, k) for k in range(ND)], tag="mm_w1g")
                    for half in range(2):
                        em.op("scalar", lambda e, m=m, w=w, half=half: e.activation(
                            asb[0:w, 0, :], bank[4 + half][0:w, :], AF.Identity,
                            bias=b1as[0:w, m:m + 1]),
                            reads=[("bank", 4 + half), ("b1as",)], writes=[("asbk",)], tag="ev_a")
                        em.op("scalar", lambda e, m=m, w=w, half=half: e.activation(
                            Pb[0:w, m, half * 512:(half + 1) * 512], bank[6 + half][0:w, :], AF.Gelu,
                            bias=b1gs[0:w, m:m + 1]),
                            reads=[("bank", 6 + half), ("b1gs",)], writes=[("Pb", m)], tag="ev_g")
                        em.op("vector", lambda e, m=m, w=w, half=half: e.tensor_tensor(
                            Pb[0:w, m, half * 512:(half + 1) * 512], asb[0:w, 0, :],
                            Pb[0:w, m, half * 512:(half + 1) * 512], ALU.mult),
                            reads=[("asbk",), ("Pb", m)], writes=[("Pb", m)], tag="pmul")
                if L1FFN_DEBUG and i == 0 and n == 0:
                    for kf in range(22):
                        em.dma("gpsimd", dbg_pb[:, kf, :], Pb[:, kf, :], reads=[("Pb", kf)], tag="dbgpb")
                for s_ in range(NTC):
                    t = n * NTC + s_
                    em.dma("sync", xw[:, pa, :], src[t * 128:(t + 1) * 128, :], writes=[("xw", pa)], tag="ldx2")
                    for dc in range(2):
                        bko = dc
                        pairs = []
                        for kf in range(22):
                            kk0 = kf * 128
                            w = min(DI2, kk0 + 128) - kk0
                            pairs.append((Pb[0:w, kf, s_ * 128:(s_ + 1) * 128], w2r_sb[0:w, kf, dc * 512:(dc + 1) * 512]))
                        pairs.append((ones1[:, :], b2sb[:, dc * 512:(dc + 1) * 512]))
                        emit_mm_group(P, bank[bko][:, :], BK(bko), pairs,
                                      reads=[("Pb", kf) for kf in range(22)] + [("w2r",), ("o1",), ("b2s",)],
                                      tag="mm_w2f")
                        em.op("vector", lambda e, dc=dc, pa=pa, bko=bko: e.tensor_tensor(
                            xobf[:, pa, dc * 512:(dc + 1) * 512], bank[bko][:, :],
                            xw[:, pa, dc * 512:(dc + 1) * 512], ALU.add),
                            reads=[BK(bko), ("xw", pa)], writes=[("xof", pa, dc)], tag="residf")
                    em.dma("sync", dst[t * 128:(t + 1) * 128, :], xobf[:, pa, :],
                           reads=[("xof", pa, 0), ("xof", pa, 1)], writes=[("dstf", t)], tag="stxf")
        em.barrier()

    cur = 0
    for i in range(L1FFN_DEPTHS):
        if "a" in L1FFN_PARTS:
            emit_attn_depth(i, xd[cur], xd[1 - cur])
            cur = 1 - cur
        if "f" in L1FFN_PARTS:
            emit_ffn_depth(i, xd[cur], xd[1 - cur])
            cur = 1 - cur

    with ExitStack() as s9:
        A = lambda n, sh, dt=F32: s9.enter_context(nc.sbuf_tensor(P.uniq(n), list(sh), dt))
        xcf = A("xcf", [128, 2, D])
        xsf = A("xsf", [128, 2, D])
        xof2 = A("xof2", [128, 2, D])
        for t in range(NT):
            pa = t % 2
            em.dma("sync", xcf[:, pa, :], x_in[t * 128:(t + 1) * 128, :], writes=[("xf", pa)], tag="ldA")
            em.dma("sync", xsf[:, pa, :], xd[cur][t * 128:(t + 1) * 128, :], writes=[("xs", pa)], tag="ldB")
            em.op("vector", lambda e, pa=pa: e.scalar_tensor_tensor(xof2[:, pa, :], xsf[:, pa, :], 0.5,
                                                                    xcf[:, pa, :], ALU.mult, ALU.add),
                  reads=[("xf", pa), ("xs", pa)], writes=[("xo2", pa)], tag="res")
            em.dma("sync", xo_out[t * 128:(t + 1) * 128, :], xof2[:, pa, :],
                   reads=[("xo2", pa)], writes=[("st", t)], tag="stf")
    inst_ctx.close()
    em.barrier()
    return P


def _l1ffn_tables():
    n, DH, WIN = 2048, 64, 256
    half = np.arange(0, DH, 2, dtype=np.float64)
    inv_freq = 1.0 / 10000.0 ** (half / DH)
    t = np.arange(n, dtype=np.float64)
    freqs = np.concatenate([t[:, None] * inv_freq[None, :]] * 2, -1)
    base_scale = (half + 0.4 * DH) / (1.4 * DH)
    power = (t - n // 2) / (WIN // 2)
    scale = np.concatenate([base_scale[None, :] ** power[:, None]] * 2, -1)
    cs = np.cos(freqs) * scale
    ss = np.sin(freqs) * scale
    ci = np.cos(freqs) / scale
    si = np.sin(freqs) / scale
    lay = lambda v: np.ascontiguousarray(v.reshape(16, 128, v.shape[-1]).transpose(1, 0, 2)).astype(np.float32)
    return {
        "tab_cs": lay(cs), "tab_sslo": lay(-ss[:, 0:32]), "tab_sshi": lay(ss[:, 32:64]),
        "tab_ci": lay(ci), "tab_silo": lay(-si[:, 0:32]), "tab_sihi": lay(si[:, 32:64]),
    }


def _l1ffn_masks():
    r = np.arange(128)[:, None]
    j = np.arange(384)[None, :]
    mf = np.where((j >= r) & (j <= 256 + r), 0.0, NEG).astype(np.float32)
    m1 = mf.copy(); m1[:, 0:128] = NEG
    m0 = mf.copy(); m0[:, 0:256] = NEG
    return mf, m1, m0


def run_l1ffn(x_full, Wqkv, q_scale, k_scale, Wo, ln_g, ln_b, W1, b1, W2, b2):
    P = build_l1ffn()
    nc = P.close_build()
    DI2 = 2730
    pmj6 = lambda v: np.ascontiguousarray(v.reshape(6, ND, 128).transpose(2, 0, 1))
    b1pad = np.zeros((6, 2, 2816), np.float32)
    b1pad[:, 0, 0:DI2] = b1[:, 0:DI2]
    b1pad[:, 1, 0:DI2] = b1[:, DI2:]
    b1a = np.ascontiguousarray(b1pad[:, 0].reshape(6, 22, 128).transpose(2, 0, 1))
    b1g = np.ascontiguousarray(b1pad[:, 1].reshape(6, 22, 128).transpose(2, 0, 1))
    mf, m1, m0 = _l1ffn_masks()
    base = dict(_l1ffn_tables())
    base.update({
        "Wqkv": Wqkv, "Wo": Wo, "lng": pmj6(ln_g), "lnb": pmj6(ln_b),
        "W1": W1, "b1a": b1a, "b1g": b1g, "W2": W2, "b2": b2,
        "qs": q_scale, "ks": k_scale,
        "identf": np.eye(128, dtype=np.float32), "identb": np.eye(128, dtype=BFNP),
        "m_full": mf, "m_1": m1, "m_0": m0,
        "ones128": np.ones((128, 1), np.float32), "ones1": np.ones((1, 128), np.float32),
    })
    xs = x_full.reshape(NCORES, T, D)
    in_maps = [dict(base, x=np.ascontiguousarray(xs[c])) for c in range(NCORES)]
    res = run_bass_kernel_spmd(nc, in_maps, core_ids=list(range(NCORES)))
    out = np.stack([res.results[c]["xo"] for c in range(NCORES)])
    return out.reshape(16, 2048, D)


_PROG_CACHE = {}


def _get_prog(name, builder):
    if name not in _PROG_CACHE:
        P = builder()
        _PROG_CACHE[name] = P.close_build()
    return _PROG_CACHE[name]


def kernel(**inp):
    if FUSED:
        return kernel_fused(**inp)
    x = np.asarray(inp["L0_attn_x"], np.float32)

    x = run_moe_cached(x, inp["L0_ffn_wg"], inp["L0_ffn_w1"], inp["L0_ffn_b1"],
                       inp["L0_ffn_w2"], inp["L0_ffn_b2"])
    x = run_l0attn_cached(x, inp["L0_attn_Wqkv"], inp["L0_attn_Wproj"])
    x = run_moe_cached(x, inp["L0_ffn_wg"], inp["L0_ffn_w1"], inp["L0_ffn_b1"],
                       inp["L0_ffn_w2"], inp["L0_ffn_b2"])
    x = run_l1ffn_cached(x, inp["L1_ffn_Wqkv"], inp["L1_ffn_q_scale"], inp["L1_ffn_k_scale"],
                         inp["L1_ffn_Wo"], inp["L1_ffn_ln_g"], inp["L1_ffn_ln_b"],
                         inp["L1_ffn_W1"], inp["L1_ffn_b1"], inp["L1_ffn_W2"], inp["L1_ffn_b2"])
    x = run_l1attn_cached(x, inp["L1_attn_w_rms_mix"], inp["L1_attn_Wg"], inp["L1_attn_bg"],
                          inp["L1_attn_Wv"], inp["L1_attn_bv"], inp["L1_attn_Wd"], inp["L1_attn_bd"],
                          inp["L1_attn_w_rms_attn"], inp["L1_attn_Wqkv"], inp["L1_attn_Wo"],
                          inp["L1_attn_w_rms_ffn"], inp["L1_attn_Wgate"], inp["L1_attn_Wup"],
                          inp["L1_attn_Wout"])
    x = run_l1ffn_cached(x, inp["L1_ffn_Wqkv"], inp["L1_ffn_q_scale"], inp["L1_ffn_k_scale"],
                         inp["L1_ffn_Wo"], inp["L1_ffn_ln_g"], inp["L1_ffn_ln_b"],
                         inp["L1_ffn_W1"], inp["L1_ffn_b1"], inp["L1_ffn_W2"], inp["L1_ffn_b2"])
    return np.ascontiguousarray(x, dtype=np.float32)


def run_moe_cached(x_full, wg, w1, b1, w2, b2):
    nc = _get_prog("moe", build_moe)
    tri, ones128, ones1, _ = _moe_consts(0)
    ident = np.eye(128, dtype=np.float32)
    b1t = np.ascontiguousarray(b1.reshape(8, 32, 128).transpose(2, 0, 1))
    xs = x_full.reshape(NCORES, T, D)
    in_maps = []
    for c in range(NCORES):
        in_maps.append({
            "x": np.ascontiguousarray(xs[c]), "wg": wg, "w1": w1, "b1t": b1t,
            "w2": w2, "b2": b2, "identf": ident, "tri": tri,
            "ones128": ones128, "ones1": ones1, "prefmask": _moe_consts(c)[3], "ebase": _moe_ebase(),
            "identb": np.eye(128, dtype=BFNP),
        })
    res = run_bass_kernel_spmd(nc, in_maps, core_ids=list(range(NCORES)))
    return np.stack([res.results[c]["xo"] for c in range(NCORES)]).reshape(16, 2048, D)


def run_l0attn_cached(x_full, Wqkv, Wproj):
    nc = _get_prog("l0attn", build_l0attn)
    mask_band, mask_first = _l0attn_masks()
    ident = np.eye(128, dtype=np.float32)
    xs = x_full.reshape(NCORES, T, D)
    in_maps = [{
        "x": np.ascontiguousarray(xs[c]), "Wqkv": Wqkv, "Wproj": Wproj,
        "identf": ident, "identb": np.eye(128, dtype=BFNP),
        "mask_band": mask_band, "mask_first": mask_first,
    } for c in range(NCORES)]
    res = run_bass_kernel_spmd(nc, in_maps, core_ids=list(range(NCORES)))
    return np.stack([res.results[c]["xo"] for c in range(NCORES)]).reshape(16, 2048, D)


def run_l1attn_cached(x_full, w_rms_mix, Wg, bg, Wv, bv, Wd, bd, w_rms_attn, Wqkv, Wo,
                      w_rms_ffn, Wgate, Wup, Wout):
    nc = _get_prog("l1attn", build_l1attn)
    pmj = lambda v: np.ascontiguousarray(v.reshape(ND, 128).T)
    r = np.arange(128)[:, None]
    j = np.arange(256)[None, :]
    mband = np.where((j >= r + 1) & (j <= 128 + r), 0.0, NEG).astype(np.float32)
    jj = np.arange(128)[None, :]
    mfirst = np.where(jj <= r, 0.0, NEG).astype(np.float32)
    xs = x_full.reshape(NCORES, T, D)
    base = {
        "wmixt": pmj(w_rms_mix), "WgT": np.ascontiguousarray(Wg.T), "bgt": pmj(bg),
        "WvT": np.ascontiguousarray(Wv.T), "bvt": pmj(bv),
        "WdT": np.ascontiguousarray(Wd.T), "bdt": pmj(bd),
        "wattnt": pmj(w_rms_attn), "WqkvT": np.ascontiguousarray(Wqkv.T),
        "WoT": np.ascontiguousarray(Wo.T), "wffnt": pmj(w_rms_ffn),
        "WgateT": np.ascontiguousarray(Wgate.T), "WupT": np.ascontiguousarray(Wup.T),
        "WoutT": np.ascontiguousarray(Wout.T),
        "identf": np.eye(128, dtype=np.float32), "identb": np.eye(128, dtype=BFNP),
        "mask_band": mband, "mask_first": mfirst,
    }
    in_maps = [dict(base, x=np.ascontiguousarray(xs[c])) for c in range(NCORES)]
    res = run_bass_kernel_spmd(nc, in_maps, core_ids=list(range(NCORES)))
    return np.stack([res.results[c]["xo"] for c in range(NCORES)]).reshape(16, 2048, D)


def run_l1ffn_cached(x_full, Wqkv, q_scale, k_scale, Wo, ln_g, ln_b, W1, b1, W2, b2):
    nc = _get_prog("l1ffn", build_l1ffn)
    DI2 = 2730
    pmj6 = lambda v: np.ascontiguousarray(v.reshape(6, ND, 128).transpose(2, 0, 1))
    b1pad = np.zeros((6, 2, 2816), np.float32)
    b1pad[:, 0, 0:DI2] = b1[:, 0:DI2]
    b1pad[:, 1, 0:DI2] = b1[:, DI2:]
    b1a = np.ascontiguousarray(b1pad[:, 0].reshape(6, 22, 128).transpose(2, 0, 1))
    b1g = np.ascontiguousarray(b1pad[:, 1].reshape(6, 22, 128).transpose(2, 0, 1))
    mf, m1, m0 = _l1ffn_masks()
    base = dict(_l1ffn_tables())
    base.update({
        "Wqkv": Wqkv, "Wo": Wo, "lng": pmj6(ln_g), "lnb": pmj6(ln_b),
        "W1": W1, "b1a": b1a, "b1g": b1g, "W2": W2, "b2": b2,
        "qs": q_scale, "ks": k_scale,
        "identf": np.eye(128, dtype=np.float32), "identb": np.eye(128, dtype=BFNP),
        "m_full": mf, "m_1": m1, "m_0": m0,
        "ones128": np.ones((128, 1), np.float32), "ones1": np.ones((1, 128), np.float32),
    })
    xs = x_full.reshape(NCORES, T, D)
    in_maps = [dict(base, x=np.ascontiguousarray(xs[c])) for c in range(NCORES)]
    res = run_bass_kernel_spmd(nc, in_maps, core_ids=list(range(NCORES)))
    return np.stack([res.results[c]["xo"] for c in range(NCORES)]).reshape(16, 2048, D)


def build_fused():
    P = Prog("fused")
    DEPTH6, DI2 = 6, 2730
    V = {
        "identf": P.inp("identf", [128, 128], F32),
        "identb": P.inp("identb", [128, 128], BF16),
        "ones128": P.inp("ones128", [128, 1], F32),
        "ones1": P.inp("ones1", [1, 128], F32),
        "tri": P.inp("tri", [128, 128], F32),
        "prefmask": P.inp("prefmask", [128, 1], F32),
        "ebase": P.inp("ebase", [1, 8], F32),
        "mask_band": P.inp("l0_mask_band", [128, 256], F32),
        "mask_first": P.inp("mask_first", [128, 128], F32),
        "mask_band_l1": P.inp("l1_mask_band", [128, 256], F32),
        "m_full": P.inp("m_full", [128, 384], F32),
        "m_1": P.inp("m_1", [128, 384], F32),
        "m_0": P.inp("m_0", [128, 384], F32),
        "tab_cs": P.inp("tab_cs", [128, 16, 64], F32),
        "tab_sslo": P.inp("tab_sslo", [128, 16, 32], F32),
        "tab_sshi": P.inp("tab_sshi", [128, 16, 32], F32),
        "tab_ci": P.inp("tab_ci", [128, 16, 64], F32),
        "tab_silo": P.inp("tab_silo", [128, 16, 32], F32),
        "tab_sihi": P.inp("tab_sihi", [128, 16, 32], F32),
        "wg": P.inp("m_wg", [D, 8]),
        "w1": P.inp("m_w1", [8, D, 4096]),
        "b1t": P.inp("m_b1t", [128, 8, 32]),
        "w2": P.inp("m_w2", [8, 4096, D]),
        "b2": P.inp("m_b2", [8, D]),
        "Wqkv": P.inp("a0_Wqkv", [D, 3 * D]),
        "Wproj": P.inp("a0_Wproj", [D, D]),
        "wmixt": P.inp("A_wmixt", [128, ND]),
        "WgT": P.inp("A_WgT", [D, D]),
        "bgt": P.inp("A_bgt", [128, ND]),
        "WvT": P.inp("A_WvT", [D, D]),
        "bvt": P.inp("A_bvt", [128, ND]),
        "WdT": P.inp("A_WdT", [D, D]),
        "bdt": P.inp("A_bdt", [128, ND]),
        "wattnt": P.inp("A_wattnt", [128, ND]),
        "WqkvT": P.inp("A_WqkvT", [D, 3 * D]),
        "WoT": P.inp("A_WoT", [D, D]),
        "wffnt": P.inp("A_wffnt", [128, ND]),
        "WgateT": P.inp("A_WgateT", [D, 4096]),
        "WupT": P.inp("A_WupT", [D, 4096]),
        "WoutT": P.inp("A_WoutT", [4096, D]),
        "Wqkv6": P.inp("F_Wqkv", [DEPTH6, D, 3 * 1024]),
        "Wo6": P.inp("F_Wo", [DEPTH6, 1024, D]),
        "lng": P.inp("F_lng", [128, DEPTH6, ND]),
        "lnb": P.inp("F_lnb", [128, DEPTH6, ND]),
        "W1": P.inp("F_W1", [DEPTH6, D, 2 * DI2]),
        "b1a": P.inp("F_b1a", [128, DEPTH6, 22]),
        "b1g": P.inp("F_b1g", [128, DEPTH6, 22]),
        "W2": P.inp("F_W2", [DEPTH6, DI2, D]),
        "b26": P.inp("F_b2", [DEPTH6, D]),
        "qs": P.inp("F_qs", [DEPTH6, 64]),
        "ks": P.inp("F_ks", [DEPTH6, 64]),
    }
    x_in = P.inp("x", [T, D])
    xo_out = P.out("xo", [T, D])
    xA = P.dram("xA", [T, D])
    xB = P.dram("xB", [T, D])
    xC = P.dram("xC", [T, D])
    xD = P.dram("xD", [T, D])
    xE = P.dram("xE", [T, D])
    bank = [P.psum(f"bank{i}", [128, 512], F32) for i in range(8)]

    build_moe(P, V, x_in, xA, bank)
    build_l0attn(P, V, xA, xB, bank)
    build_moe(P, V, xB, xC, bank)
    build_l1ffn(P, V, xC, xD, bank)
    build_l1attn(P, V, xD, xE, bank)
    build_l1ffn(P, V, xE, xo_out, bank)
    return P


FUSED = True


def _fused_in_maps(inp):
    DI2 = 2730
    pmj = lambda v: np.ascontiguousarray(v.reshape(ND, 128).T.astype(np.float32))
    pmj6 = lambda v: np.ascontiguousarray(v.reshape(6, ND, 128).transpose(2, 0, 1).astype(np.float32))
    tT = lambda v: np.ascontiguousarray(np.asarray(v, np.float32).T)
    b1 = np.asarray(inp["L1_ffn_b1"], np.float32)
    b1pad = np.zeros((6, 2, 2816), np.float32)
    b1pad[:, 0, 0:DI2] = b1[:, 0:DI2]
    b1pad[:, 1, 0:DI2] = b1[:, DI2:]
    mb0, mf0 = _l0attn_masks()
    r = np.arange(128)[:, None]
    j = np.arange(256)[None, :]
    mb1 = np.where((j >= r + 1) & (j <= 128 + r), 0.0, NEG).astype(np.float32)
    mff, mf1, mf2 = _l1ffn_masks()
    tri, ones128, ones1, _ = _moe_consts(0)
    base = dict(_l1ffn_tables())
    base.update({
        "identf": np.eye(128, dtype=np.float32), "identb": np.eye(128, dtype=BFNP),
        "ones128": ones128, "ones1": ones1, "tri": tri, "ebase": _moe_ebase(),
        "l0_mask_band": mb0, "mask_first": mf0, "l1_mask_band": mb1,
        "m_full": mff, "m_1": mf1, "m_0": mf2,
        "m_wg": np.asarray(inp["L0_ffn_wg"], np.float32),
        "m_w1": np.asarray(inp["L0_ffn_w1"], np.float32),
        "m_b1t": np.ascontiguousarray(np.asarray(inp["L0_ffn_b1"], np.float32).reshape(8, 32, 128).transpose(2, 0, 1)),
        "m_w2": np.asarray(inp["L0_ffn_w2"], np.float32),
        "m_b2": np.asarray(inp["L0_ffn_b2"], np.float32),
        "a0_Wqkv": np.asarray(inp["L0_attn_Wqkv"], np.float32),
        "a0_Wproj": np.asarray(inp["L0_attn_Wproj"], np.float32),
        "A_wmixt": pmj(np.asarray(inp["L1_attn_w_rms_mix"], np.float32)),
        "A_WgT": tT(inp["L1_attn_Wg"]), "A_bgt": pmj(np.asarray(inp["L1_attn_bg"], np.float32)),
        "A_WvT": tT(inp["L1_attn_Wv"]), "A_bvt": pmj(np.asarray(inp["L1_attn_bv"], np.float32)),
        "A_WdT": tT(inp["L1_attn_Wd"]), "A_bdt": pmj(np.asarray(inp["L1_attn_bd"], np.float32)),
        "A_wattnt": pmj(np.asarray(inp["L1_attn_w_rms_attn"], np.float32)),
        "A_WqkvT": tT(inp["L1_attn_Wqkv"]), "A_WoT": tT(inp["L1_attn_Wo"]),
        "A_wffnt": pmj(np.asarray(inp["L1_attn_w_rms_ffn"], np.float32)),
        "A_WgateT": tT(inp["L1_attn_Wgate"]), "A_WupT": tT(inp["L1_attn_Wup"]),
        "A_WoutT": tT(inp["L1_attn_Wout"]),
        "F_Wqkv": np.asarray(inp["L1_ffn_Wqkv"], np.float32),
        "F_Wo": np.asarray(inp["L1_ffn_Wo"], np.float32),
        "F_lng": pmj6(np.asarray(inp["L1_ffn_ln_g"], np.float32)),
        "F_lnb": pmj6(np.asarray(inp["L1_ffn_ln_b"], np.float32)),
        "F_W1": np.asarray(inp["L1_ffn_W1"], np.float32),
        "F_b1a": np.ascontiguousarray(b1pad[:, 0].reshape(6, 22, 128).transpose(2, 0, 1)),
        "F_b1g": np.ascontiguousarray(b1pad[:, 1].reshape(6, 22, 128).transpose(2, 0, 1)),
        "F_W2": np.asarray(inp["L1_ffn_W2"], np.float32),
        "F_b2": np.asarray(inp["L1_ffn_b2"], np.float32),
        "F_qs": np.asarray(inp["L1_ffn_q_scale"], np.float32),
        "F_ks": np.asarray(inp["L1_ffn_k_scale"], np.float32),
    })
    x = np.asarray(inp["L0_attn_x"], np.float32).reshape(NCORES, T, D)
    in_maps = []
    for c in range(NCORES):
        m = dict(base)
        m["x"] = np.ascontiguousarray(x[c])
        m["prefmask"] = _moe_consts(c)[3]
        in_maps.append(m)
    return in_maps


def kernel_fused(**inp):
    nc = _get_prog("fused", build_fused)
    in_maps = _fused_in_maps(inp)
    res = run_bass_kernel_spmd(nc, in_maps, core_ids=list(range(NCORES)))
    out = np.stack([res.results[c]["xo"] for c in range(NCORES)])
    return np.ascontiguousarray(out.reshape(16, 2048, D), dtype=np.float32)
